# version 01
# baseline (speedup 1.0000x reference)
import numpy as np
import ml_dtypes
from contextlib import ExitStack

import concourse.bass as bass
import concourse.mybir as mybir
from concourse.bass_utils import run_bass_kernel_spmd

F32 = mybir.dt.float32
BF16 = mybir.dt.bfloat16
AF = mybir.ActivationFunctionType
ALU = mybir.AluOpType
AX = mybir.AxisListType

D = 2048
T = 2048
KC = D // 128
NH = 16
HD = 128
SCL_L1 = float(128.0 ** 0.5 / 128.0)
EPS_LN = 1e-5
EPS_RMS = float(np.finfo(np.float32).eps)
EPS_BIT = 1e-6
CLIP_HI = float(np.float32(np.float64(128.0) - 1e-6))
CLIP_LO = float(np.float32(np.float64(-128.0) + 1e-6))

ENGINES = ("sync", "tensor", "vector", "scalar", "gpsimd")

_UID = [0]


def sbt(nc, name, shape, dtype):
    _UID[0] += 1
    return nc.sbuf_tensor(f"{name}_{_UID[0]}", shape, dtype)




def chunk_w(w):
    K, M = w.shape
    nk, nm = K // 128, M // 128
    return np.ascontiguousarray(
        w.reshape(nk, 128, nm, 128).transpose(2, 1, 0, 3).reshape(nm, 128, nk * 128))


def nat_chunks(w):
    R, C = w.shape
    return np.ascontiguousarray(w.reshape(R // 128, 128, C))


def make_consts():
    p = np.arange(128, dtype=np.float32)[:, None]
    j = np.arange(128, dtype=np.float32)[None, :]
    c = {}
    c["ident"] = np.eye(128, dtype=np.float32)
    c["ones"] = np.ones((128, 128), dtype=np.float32)
    c["diffD"] = np.ascontiguousarray(np.maximum(j - p, 0.0) + np.zeros((128, 128), np.float32))
    c["tri01"] = (j >= p).astype(np.float32)
    c["iotaF"] = np.ascontiguousarray(j + np.zeros((128, 128), np.float32))
    c["dcol"] = np.concatenate([(128.0 - p), np.full((128, 1), 128.0, np.float32)],
                               axis=1).astype(np.float32)
    jj = np.arange(512, dtype=np.float32)[None, :]
    m = [np.where(jj <= 128.0 * r + p, 0.0, -1e30).astype(np.float32) for r in range(4)]
    c["l1mask"] = np.ascontiguousarray(
        np.concatenate(m, axis=1).astype(ml_dtypes.bfloat16))
    af = ((1.0 / 1024.0) ** np.linspace(0.0, 1.0, 32)).astype(np.float32)
    af64 = np.concatenate([af.astype(np.float64), np.zeros(32)])
    theta = np.arange(T, dtype=np.float64)[:, None] * af64[None, :]
    cosT = np.cos(theta).T
    sinT = np.sin(theta).T
    Call = np.concatenate([cosT, cosT], axis=0)
    Sall = np.concatenate([sinT, -sinT], axis=0)
    c["ropeKC"] = np.ascontiguousarray(Call.astype(np.float32))
    c["ropeKS"] = np.ascontiguousarray(Sall.astype(np.float32))
    c["ropeQC"] = np.ascontiguousarray((Call * SCL_L1).astype(np.float32))
    c["ropeQS"] = np.ascontiguousarray((Sall * SCL_L1).astype(np.float32))
    cv = np.zeros((128, 4), np.float32)
    cv[:, 0] = EPS_LN
    cv[:, 1] = EPS_RMS
    cv[:, 2] = EPS_BIT
    c["cvals"] = cv
    return c


CONST_SHAPES = {
    "ident": [128, 128], "ones": [128, 128], "diffD": [128, 128], "tri01": [128, 128],
    "iotaF": [128, 128], "dcol": [128, 2], "l1mask": [128, 2048],
    "ropeKC": [128, T], "ropeKS": [128, T], "ropeQC": [128, T], "ropeQS": [128, T],
    "cvals": [128, 4],
}


def host_inputs(inputs, core):
    m = {}
    x = np.asarray(inputs["L0_attn_x"][core], dtype=np.float32)
    m["xT"] = np.ascontiguousarray(x.T)
    m["wqkv"] = chunk_w(np.asarray(inputs["L0_attn_Wqkv"], np.float32))
    bqkv = np.asarray(inputs["L0_attn_bqkv"], np.float32)
    m["bqkv_l"] = np.ascontiguousarray(bqkv.reshape(48, 128).T)
    m["vbias_r"] = np.ascontiguousarray(bqkv[4096:6144].reshape(1, 2048))
    m["wout"] = chunk_w(np.asarray(inputs["L0_attn_Wout"], np.float32))
    m["bout_l"] = np.ascontiguousarray(
        np.asarray(inputs["L0_attn_bout"], np.float32).reshape(16, 128).T)
    m["slopes_r"] = np.ascontiguousarray(
        np.asarray(inputs["L0_attn_slopes"], np.float32).reshape(1, 16))
    m["gup_nat"] = nat_chunks(np.asarray(inputs["L0_ffn_w_gate_up"], np.float32))
    m["wpj0_nat"] = nat_chunks(np.asarray(inputs["L0_ffn_w_proj"], np.float32))
    m["wq1"] = chunk_w(np.ascontiguousarray(np.asarray(inputs["L1_attn_Wq"], np.float32).T))
    m["wk1"] = chunk_w(np.ascontiguousarray(np.asarray(inputs["L1_attn_Wk"], np.float32).T))
    m["wv1"] = chunk_w(np.ascontiguousarray(np.asarray(inputs["L1_attn_Wv"], np.float32).T))
    m["wo1"] = chunk_w(np.ascontiguousarray(np.asarray(inputs["L1_attn_Wo"], np.float32).T))
    m["wfc1"] = chunk_w(np.ascontiguousarray(np.asarray(inputs["L1_ffn_w_fc"], np.float32).T))
    m["wpj1"] = chunk_w(np.ascontiguousarray(np.asarray(inputs["L1_ffn_w_proj"], np.float32).T))
    m.update(make_consts())
    return m



class Phase:
    def __init__(self, ctx, name=""):
        self.ctx = ctx
        self.name = name
        self.q = {e: [] for e in ENGINES}

    SELF_SEM = {"vector": "sv", "scalar": "sa", "gpsimd": "sg"}

    def op(self, eng, fn=None, waits=(), inc=None, inc_n=1):
        waits = list(waits)
        ss = self.SELF_SEM.get(eng)
        if ss is not None and fn is not None:
            waits.append((ss, self.ctx.C[ss]))
            if inc is None:
                inc = ss
                inc_n = 1
            else:
                assert inc == ss and inc_n == 1, (eng, inc)
        c = self.ctx.bump(inc, inc_n) if (inc and fn is not None) else None
        self.q[eng].append((fn, tuple(waits), inc, inc_n))
        return c

    def dma(self, out_ap, in_ap, waits=(), sem=None, q="sync"):
        ctx = self.ctx
        nm = ctx.dma_pool[ctx.dma_rr % len(ctx.dma_pool)]
        ctx.dma_rr += 1
        w = list(waits)
        if ctx.C[nm] > 0:
            w.append((nm, ctx.C[nm]))
        self.q[q].append((lambda e, o=out_ap, i=in_ap: e.dma_start(out=o, in_=i),
                          tuple(w), nm, 16))
        return (nm, ctx.bump(nm, 16))

    def run(self):
        ctx = self.ctx
        with ctx.nc.Block() as b:
            for eng in ENGINES:
                ops = self.q[eng]
                if not ops:
                    continue

                def body(e, ops=ops):
                    for fn, waits, inc, inc_n in ops:
                        for wv in waits:
                            if not wv:
                                continue
                            (s, v) = wv
                            if v > 0:
                                e.wait_ge(ctx.sems[s], v)
                        ins = fn(e) if fn is not None else None
                        if inc is not None and ins is not None:
                            ins.then_inc(ctx.sems[inc], inc_n)
                getattr(b, eng)(body)


class Ctx:
    def __init__(self, nc):
        self.nc = nc
        self.es = ExitStack()
        self.sems = {}
        self.C = {}
        for s in ("ld", "st", "sp", "sv", "sa", "sg", "lg"):
            self.sems[s] = self.es.enter_context(nc.semaphore(s))
            self.C[s] = 0
        self.dma_pool = []
        for i in range(12):
            nm = f"dm{i}"
            self.sems[nm] = self.es.enter_context(nc.semaphore(nm))
            self.C[nm] = 0
            self.dma_pool.append(nm)
        self.dma_rr = 0
        self.sb = {}
        self.dram = {}

    def bump(self, s, n=1):
        self.C[s] += n
        return self.C[s]

    def dma_all(self):
        return [(nm, self.C[nm]) for nm in self.dma_pool if self.C[nm] > 0]

    def alloc(self, name, shape, dtype, **kw):
        h = self.es.enter_context(self.nc.sbuf_tensor(name, shape, dtype, **kw))
        self.sb[name] = h
        return h


def declare_io(ctx, debug=False):
    nc = ctx.nc
    d = ctx.dram
    scratch_kind = "ExternalOutput" if debug else "Internal"
    d["xT"] = nc.dram_tensor("xT", [D, T], F32, kind="ExternalInput")
    d["yT"] = nc.dram_tensor("yT", [D, T], F32, kind="ExternalOutput")
    d["wqkv"] = nc.dram_tensor("wqkv", [48, 128, 2048], F32, kind="ExternalInput")
    d["bqkv_l"] = nc.dram_tensor("bqkv_l", [128, 48], F32, kind="ExternalInput")
    d["vbias_r"] = nc.dram_tensor("vbias_r", [1, 2048], F32, kind="ExternalInput")
    d["wout"] = nc.dram_tensor("wout", [16, 128, 2048], F32, kind="ExternalInput")
    d["bout_l"] = nc.dram_tensor("bout_l", [128, 16], F32, kind="ExternalInput")
    d["slopes_r"] = nc.dram_tensor("slopes_r", [1, 16], F32, kind="ExternalInput")
    d["gup_nat"] = nc.dram_tensor("gup_nat", [128, 128, 2048], F32, kind="ExternalInput")
    d["wpj0_nat"] = nc.dram_tensor("wpj0_nat", [16, 128, 8192], F32, kind="ExternalInput")
    for nm in ("wq1", "wk1", "wv1", "wo1"):
        d[nm] = nc.dram_tensor(nm, [16, 128, 2048], F32, kind="ExternalInput")
    d["wfc1"] = nc.dram_tensor("wfc1", [64, 128, 2048], F32, kind="ExternalInput")
    d["wpj1"] = nc.dram_tensor("wpj1", [16, 128, 8192], F32, kind="ExternalInput")
    for nm, shp in CONST_SHAPES.items():
        dt_ = BF16 if nm == "l1mask" else F32
        d[nm] = nc.dram_tensor(nm, shp, dt_, kind="ExternalInput")
    d["xsp"] = nc.dram_tensor("xsp", [KC, 128, T], F32, kind=scratch_kind)
    d["qk_hi"] = nc.dram_tensor("qk_hi", [32, 128, T], BF16, kind=scratch_kind)
    d["qk_lo"] = nc.dram_tensor("qk_lo", [32, 128, T], BF16, kind=scratch_kind)
    d["v_hi"] = nc.dram_tensor("v_hi", [NH, T, HD], BF16, kind=scratch_kind)
    d["v_lo"] = nc.dram_tensor("v_lo", [NH, T, HD], BF16, kind=scratch_kind)
    d["sub_out"] = nc.dram_tensor("sub_out", [KC, 128, T], F32, kind=scratch_kind)
    d["h_f32"] = nc.dram_tensor("h_f32", [64, 128, T], F32, kind=scratch_kind)
    d["hn_hi"] = nc.dram_tensor("hn_hi", [64, 128, T], BF16, kind=scratch_kind)
    d["hn_lo"] = nc.dram_tensor("hn_lo", [64, 128, T], BF16, kind=scratch_kind)
    d["wpj0_t"] = nc.dram_tensor("wpj0_t", [16, 128, 8192], BF16, kind=scratch_kind)
    d["wpj1_t"] = nc.dram_tensor("wpj1_t", [16, 128, 8192], BF16, kind=scratch_kind)
    d["rvec"] = nc.dram_tensor("rvec", [64, 64], F32, kind=scratch_kind)
    d["rv2"] = nc.dram_tensor("rv2", [2, 128], F32, kind=scratch_kind)
    d["rv3"] = nc.dram_tensor("rv3", [64, 32], F32, kind=scratch_kind)
    d["rowsd"] = nc.dram_tensor("rowsd", [4, 2048], F32, kind=scratch_kind)
    d["dbg"] = nc.dram_tensor("dbg", [128, 8192], BF16, kind=scratch_kind)


def alloc_shared(ctx):
    nc = ctx.nc
    ctx.arena = ctx.alloc("arena", [128, KC * T], F32)
    ctx.arena_b = ctx.arena.bitcast(BF16)
    ctx.alloc("c_ident", [128, 128], F32)
    ctx.alloc("c_identb", [128, 128], BF16)
    ctx.alloc("c_ones", [128, 128], F32)
    ctx.alloc("c_cvals", [128, 4], F32)
    ctx.alloc("gcol", [128, 4], F32)
    ctx.alloc("rows", [1, T], F32)
    ctx.alloc("mb", [128, T], F32)
    ctx.alloc("rstdb", [128, T], F32)
    ctx.pb = [ctx.es.enter_context(nc.psum_tensor(f"pb{i}", [128, 512], F32))
              for i in range(8)]
    ctx.pbb = [p.bitcast(BF16) for p in ctx.pb]


def st_c(ctx, c):
    return ctx.arena.ap()[:, c * T:(c + 1) * T]


def ar_hi(ctx, c):
    return ctx.arena_b.ap()[:, c * 2 * T: c * 2 * T + T]


def ar_lo(ctx, c):
    return ctx.arena_b.ap()[:, c * 2 * T + T: (c + 1) * 2 * T]


def rows0(ctx):
    return ctx.sb["rows"].ap()[:, 0:T]


def rsc(ctx):
    return ctx.sb["rstdb"].ap()[0:1, :]



def emit_setup(ctx, load_stream):
    ph = Phase(ctx, "setup")
    d = ctx.dram
    ph.dma(ctx.sb["c_ident"].ap()[:], d["ident"].ap())
    ph.dma(ctx.sb["c_cvals"].ap()[:], d["cvals"].ap())
    ld = ph.dma(ctx.sb["c_ones"].ap()[:], d["ones"].ap())
    if load_stream:
        for c in range(KC):
            ld = ph.dma(st_c(ctx, c), d["xT"].ap()[128 * c:128 * (c + 1), :])
    ph.op("scalar", lambda e: e.copy(ctx.sb["c_identb"].ap()[:], ctx.sb["c_ident"].ap()[:]),
          waits=[ld], inc="sa")
    ph.op("sync", None, waits=ctx.dma_all())
    ph.run()



def emit_stats(ctx, ph, get_chunk, pre_waits, eps, n=D):
    nc = ctx.nc
    ones = ctx.sb["c_ones"]
    sq = ctx.sb["stat_sq"]
    mm_done = [0] * KC
    for c in range(KC):
        ap, w = get_chunk(c)
        w = list(w) + (list(pre_waits) if c == 0 else [])
        if c >= 1:
            w.append(("sp", mm_done[c - 1]))
        ph.op("scalar", lambda e, ap=ap, c=c: e.activation(
            sq.ap()[:], ap, AF.Square), waits=w, inc="sa")
        sq_done = ctx.C["sa"]
        for tt in range(4):
            ph.op("tensor", lambda e, ap=ap, c=c, tt=tt: e.matmul(
                ctx.pb[tt].ap()[0:1, :], ones.ap()[:, 0:1],
                ap[:, 512 * tt:512 * (tt + 1)],
                start=(c == 0), stop=(c == KC - 1)),
                waits=[("sa", sq_done)] if tt == 0 else ())
        for tt in range(4):
            ph.op("tensor", lambda e, c=c, tt=tt: e.matmul(
                ctx.pb[4 + tt].ap()[0:1, :], ones.ap()[:, 0:1],
                sq.ap()[:, 512 * tt:512 * (tt + 1)],
                start=(c == 0), stop=(c == KC - 1)),
                inc="sp" if tt == 3 else None)
        mm_done[c] = ctx.C["sp"]
    allmm = mm_done[KC - 1]
    for tt in range(4):
        ph.op("vector", lambda e, tt=tt: e.tensor_scalar(
            rows0(ctx)[:, 512 * tt:512 * (tt + 1)], ctx.pb[tt].ap()[0:1, :],
            1.0 / n, None, ALU.mult),
            waits=[("sp", allmm)] if tt == 0 else (), inc="sv" if tt == 3 else None)
    mean_done = ctx.C["sv"]
    mb_done = 0
    for tt in range(4):
        ph.op("tensor", lambda e, tt=tt: e.matmul(
            ctx.pb[tt % 2].ap()[:, :], ones.ap()[0:1, :],
            rows0(ctx)[:, 512 * tt:512 * (tt + 1)], start=True, stop=True),
            waits=([("sv", mean_done)] if tt == 0 else
                   ([("sa", mb_done - 1)] if tt >= 2 else ())), inc="sp")
        bd = ctx.C["sp"]
        ph.op("scalar", lambda e, tt=tt: e.copy(
            ctx.sb["mb"].ap()[:, 512 * tt:512 * (tt + 1)], ctx.pb[tt % 2].ap()[:, :]),
            waits=[("sp", bd)], inc="sa")
        mb_done = ctx.C["sa"]
    ph.op("vector", lambda e: e.tensor_mul(rsc(ctx), rows0(ctx), rows0(ctx)),
          waits=[("sa", mb_done)], inc="sv")
    for tt in range(4):
        ph.op("vector", lambda e, tt=tt: e.scalar_tensor_tensor(
            rows0(ctx)[:, 512 * tt:512 * (tt + 1)], ctx.pb[4 + tt].ap()[0:1, :],
            1.0 / n, rsc(ctx)[:, 512 * tt:512 * (tt + 1)], ALU.mult, ALU.subtract),
            inc="sv" if tt == 3 else None)
    var_done = ctx.C["sv"]
    eps_col = {EPS_LN: 0, EPS_RMS: 1, EPS_BIT: 2}[eps]
    ph.op("scalar", lambda e: e.activation(
        rsc(ctx), rows0(ctx), AF.Sqrt,
        bias=ctx.sb["c_cvals"].ap()[0:1, eps_col:eps_col + 1]),
        waits=[("sv", var_done)], inc="sa")
    sdone = ctx.C["sa"]
    ph.op("vector", lambda e: e.reciprocal(rows0(ctx), rsc(ctx)),
          waits=[("sa", sdone)], inc="sv")
    rstd_done = ctx.C["sv"]
    rb_done = 0
    for tt in range(4):
        ph.op("tensor", lambda e, tt=tt: e.matmul(
            ctx.pb[2 + tt % 2].ap()[:, :], ones.ap()[0:1, :],
            rows0(ctx)[:, 512 * tt:512 * (tt + 1)], start=True, stop=True),
            waits=([("sv", rstd_done)] if tt == 0 else
                   ([("sa", rb_done - 1)] if tt >= 2 else ())), inc="sp")
        bd = ctx.C["sp"]
        ph.op("scalar", lambda e, tt=tt: e.copy(
            ctx.sb["rstdb"].ap()[:, 512 * tt:512 * (tt + 1)], ctx.pb[2 + tt % 2].ap()[:, :]),
            waits=[("sp", bd)], inc="sa")
        rb_done = ctx.C["sa"]
    return rb_done



def emit_pre_ln(ctx, split=True, eps=EPS_LN):
    nc = ctx.nc
    d = ctx.dram
    with ExitStack() as es:
        ctx.sb["stat_sq"] = es.enter_context(sbt(nc, "pl_sq", [128, T], F32))
        ut = es.enter_context(sbt(nc, "pl_u", [128, T], F32))
        xnf = es.enter_context(sbt(nc, "pl_xnf", [128, T], F32))

        ph = Phase(ctx, "pre_ln")
        spill_toks = []
        for c in range(KC):
            spill_toks.append(ph.dma(d["xsp"].ap()[c], st_c(ctx, c)))
        bc_done = emit_stats(ctx, ph, lambda c: (st_c(ctx, c), ()), (), eps)
        gd = 0
        for c in range(KC):
            w = [("sa", bc_done)] if c == 0 else [("sg" if split else "sv", gd)]
            if c == 0:
                w += spill_toks
            ph.op("vector", lambda e, c=c: e.tensor_sub(
                ut.ap()[:], st_c(ctx, c), ctx.sb["mb"].ap()[:]), waits=w)
            if split:
                ph.op("vector", lambda e: e.tensor_mul(
                    xnf.ap()[:], ut.ap()[:], ctx.sb["rstdb"].ap()[:]), inc="sv")
                vd = ctx.C["sv"]
                ph.op("scalar", lambda e, c=c: e.copy(ar_hi(ctx, c), xnf.ap()[:]),
                      waits=[("sv", vd)], inc="sa")
                ad = ctx.C["sa"]
                ph.op("gpsimd", lambda e, c=c: e.tensor_sub(
                    ar_lo(ctx, c), xnf.ap()[:], ar_hi(ctx, c)),
                    waits=[("sa", ad)], inc="sg")
                gd = ctx.C["sg"]
            else:
                ph.op("vector", lambda e, c=c: e.tensor_mul(
                    st_c(ctx, c), ut.ap()[:], ctx.sb["rstdb"].ap()[:]), inc="sv")
                gd = ctx.C["sv"]
        ph.op("sync", None, waits=[("sg" if split else "sv", gd)])
        ph.run()
        del ctx.sb["stat_sq"]



def emit_post_ln(ctx, eps=EPS_LN):
    nc = ctx.nc
    d = ctx.dram
    with ExitStack() as es:
        ob = [es.enter_context(sbt(nc, f"po_ob{i}", [128, T], F32)) for i in range(2)]
        ctx.sb["stat_sq"] = es.enter_context(sbt(nc, "po_sq", [128, T], F32))
        xb = [es.enter_context(sbt(nc, f"po_xb{i}", [128, T], F32)) for i in range(2)]
        ut = es.enter_context(sbt(nc, "po_u", [128, T], F32))

        ph = Phase(ctx, "post_ln")
        lds = [0] * KC
        sp0 = ctx.C["sp"]
        for c in range(KC):
            w = [("sp", sp0 + (c - 2) + 1)] if c >= 2 else []
            lds[c] = ph.dma(ob[c % 2].ap()[:], d["sub_out"].ap()[c], waits=w)

        def get_chunk(c):
            return ob[c % 2].ap()[:], [lds[c]]
        bc_done = emit_stats(ctx, ph, get_chunk, (), eps)
        assert ctx.C["sp"] >= sp0 + KC
        prev = 0
        for c in range(KC):
            l1 = ph.dma(ob[c % 2].ap()[:], d["sub_out"].ap()[c],
                        waits=[("sv", prev)] if c >= 2 else [("sa", bc_done)])
            l2 = ph.dma(xb[c % 2].ap()[:], d["xsp"].ap()[c])
            w = [l2, ("sa", bc_done)]
            ph.op("vector", lambda e, c=c: e.tensor_sub(
                ut.ap()[:], ob[c % 2].ap()[:], ctx.sb["mb"].ap()[:]), waits=w)
            ph.op("vector", lambda e: e.tensor_mul(ut.ap()[:], ut.ap()[:],
                                                   ctx.sb["rstdb"].ap()[:]))
            ph.op("vector", lambda e, c=c: e.tensor_add(
                st_c(ctx, c), ut.ap()[:], xb[c % 2].ap()[:]), inc="sv")
            prev = ctx.C["sv"]
        ph.op("sync", None, waits=[("sv", prev)] + ctx.dma_all())
        ph.run()
        del ctx.sb["stat_sq"]


def emit_output(ctx):
    ph = Phase(ctx, "output")
    d = ctx.dram
    last = 0
    for c in range(KC):
        last = ph.dma(d["yT"].ap()[128 * c:128 * (c + 1), :], st_c(ctx, c))
    ph.op("sync", None, waits=ctx.dma_all())
    ph.run()



def emit_proj_stream(ctx, name, ocs, wdram_of, evac, nkc=KC, rhs_hi=None, rhs_lo=None,
                     nwf=2, nws=2):
    nc = ctx.nc
    with ExitStack() as es:
        wf = [es.enter_context(sbt(nc, f"{name}_wf{i}", [128, nkc * 128], F32))
              for i in range(nwf)]
        whi = [es.enter_context(sbt(nc, f"{name}_wh{i}", [128, nkc * 128], BF16))
               for i in range(nws)]
        wlo = [es.enter_context(sbt(nc, f"{name}_wl{i}", [128, nkc * 128], BF16))
               for i in range(nws)]
        ph = Phase(ctx, name)
        if rhs_hi is None:
            rhs_hi = lambda kc, tt: ar_hi(ctx, kc)[:, 512 * tt:512 * (tt + 1)]
            rhs_lo = lambda kc, tt: ar_lo(ctx, kc)[:, 512 * tt:512 * (tt + 1)]
        free = [(0, 0)] * (len(ocs) + 2)
        split_done = [0] * (len(ocs) + 1)
        ev = evac(ph)
        for j, oc in enumerate(ocs):
            wda = wdram_of(oc)
            ld = ph.dma(wf[j % nwf].ap()[:], wda,
                        waits=[("sg", split_done[j - nwf + 1])] if j >= nwf else ())
            wsp = [ld]
            if j >= nws:
                wsp += [("sa", free[j - nws + 1][0]), ("sv", free[j - nws + 1][1])]
            ph.op("scalar", lambda e, j=j: e.copy(whi[j % nws].ap()[:], wf[j % nwf].ap()[:]),
                  waits=wsp, inc="sa")
            ad = ctx.C["sa"]
            ph.op("gpsimd", lambda e, j=j: e.tensor_sub(
                wlo[j % nws].ap()[:], wf[j % nwf].ap()[:], whi[j % nws].ap()[:]),
                waits=[("sa", ad)], inc="sg")
            split_done[j + 1] = ctx.C["sg"]
            sa_f = sv_f = 0
            for tt in range(4):
                bank = (j % 2) * 4 + tt
                w = [("sg", split_done[j + 1])] if tt == 0 else []
                if j >= 2:
                    w += [("sa", free[j - 1][0]), ("sv", free[j - 1][1])]
                first = True
                for kc in range(nkc):
                    for (wa, ra) in ((whi, rhs_hi), (whi, rhs_lo), (wlo, rhs_hi)):
                        ph.op("tensor", lambda e, j=j, tt=tt, kc=kc, wa=wa, ra=ra,
                              first=first, bank=bank: e.matmul(
                                  ctx.pb[bank].ap()[:, :],
                                  wa[j % nws].ap()[:, 128 * kc:128 * (kc + 1)],
                                  ra(kc, tt), start=first,
                                  stop=(kc == nkc - 1 and wa is wlo)),
                              waits=w if first else ())
                        w = []
                        first = False
                ph.q["tensor"][-1] = (ph.q["tensor"][-1][0], ph.q["tensor"][-1][1], "sp", 1)
                ctx.bump("sp")
                mmd = ctx.C["sp"]
                sa_f, sv_f = ev(j, oc, tt, bank, mmd)
            free[j + 1] = (sa_f, sv_f)
        ph.op("sync", None, waits=ctx.dma_all() + [("sa", ctx.C["sa"]),
                                                   ("sv", ctx.C["sv"])])
        ph.run()



def emit_l0_qkv_qk(ctx):
    d = ctx.dram
    nc = ctx.nc
    with ExitStack() as es:
        ohi = [es.enter_context(sbt(nc, f"qoh{i}", [128, T], BF16)) for i in range(2)]
        olo = [es.enter_context(sbt(nc, f"qol{i}", [128, T], BF16)) for i in range(2)]
        bql = es.enter_context(sbt(nc, "bql", [128, 48], F32))
        bq_done = [0]

        def evac(ph):
            bq_done[0] = ph.dma(bql.ap()[:], d["bqkv_l"].ap())
            st_done = [None] * 40

            def ev(j, oc, tt, bank, mmd):
                w = [("sp", mmd)]
                if tt == 0:
                    w.append(bq_done[0])
                    if j >= 2:
                        w.append(st_done[j - 2])
                ph.op("scalar", lambda e, j=j, oc=oc, tt=tt, bank=bank: e.activation(
                    ohi[j % 2].ap()[:, 512 * tt:512 * (tt + 1)], ctx.pb[bank].ap()[:, :],
                    AF.Identity, bias=bql.ap()[:, oc:oc + 1]), waits=w, inc="sa")
                ad = ctx.C["sa"]
                ph.op("vector", lambda e, j=j, oc=oc, tt=tt, bank=bank: e.scalar_tensor_tensor(
                    olo[j % 2].ap()[:, 512 * tt:512 * (tt + 1)], ctx.pb[bank].ap()[:, :],
                    bql.ap()[:, oc:oc + 1], ohi[j % 2].ap()[:, 512 * tt:512 * (tt + 1)],
                    ALU.add, ALU.subtract), waits=[("sa", ad)], inc="sv")
                vd = ctx.C["sv"]
                if tt == 3:
                    ph.dma(d["qk_hi"].ap()[oc], ohi[j % 2].ap()[:],
                           waits=[("sv", vd)])
                    st_done[j] = ph.dma(d["qk_lo"].ap()[oc], olo[j % 2].ap()[:])
                return ad, vd
            return ev

        emit_proj_stream(ctx, "l0qk", list(range(32)), lambda oc: d["wqkv"].ap()[oc], evac)


def emit_v(ctx, name, wd_of, use_bias):
    d = ctx.dram
    nc = ctx.nc
    with ExitStack() as es:
        wf = es.enter_context(sbt(nc, f"{name}_wf", [128, 2048], F32))
        vwh = es.enter_context(sbt(nc, f"{name}_vwh", [128, 4 * 2048], BF16))
        vwl = es.enter_context(sbt(nc, f"{name}_vwl", [128, 4 * 2048], BF16))
        ub = [es.enter_context(sbt(nc, f"{name}_ub{i}", [128, 512], F32))
              for i in range(2)]
        vh = [es.enter_context(sbt(nc, f"{name}_vh{i}", [128, 512], BF16)) for i in range(2)]
        vl = [es.enter_context(sbt(nc, f"{name}_vl{i}", [128, 512], BF16)) for i in range(2)]
        vbb = ctx.sb["mb"]

        ph = Phase(ctx, name)
        ones = ctx.sb["c_ones"]
        bc_done = 0
        if use_bias:
            bl = ph.dma(rows0(ctx), d["vbias_r"].ap())
            for tt in range(4):
                ph.op("tensor", lambda e, tt=tt: e.matmul(
                    ctx.pb[4 + tt].ap()[:, :], ones.ap()[0:1, :],
                    rows0(ctx)[:, 512 * tt:512 * (tt + 1)], start=True, stop=True),
                    waits=[bl] if tt == 0 else (), inc="sp")
                bd = ctx.C["sp"]
                ph.op("scalar", lambda e, tt=tt: e.copy(
                    vbb.ap()[:, 512 * tt:512 * (tt + 1)], ctx.pb[4 + tt].ap()[:, :]),
                    waits=[("sp", bd)], inc="sa")
                bc_done = ctx.C["sa"]
        vwh3 = vwh.ap().rearrange("p (o f) -> p o f", f=2048)
        vwl3 = vwl.ap().rearrange("p (o f) -> p o f", f=2048)
        ev_free = [(0, 0, 0)] * 20
        vh_st = [None, None]
        vl_st = [None, None]
        sg_d = 0
        for vg in range(4):
            for i in range(4):
                ld = ph.dma(wf.ap()[:], wd_of(vg, i),
                            waits=[("sg", sg_d)] if (vg or i) else ())
                ph.op("scalar", lambda e, i=i: e.copy(
                    vwh.ap()[:, 2048 * i:2048 * (i + 1)], wf.ap()[:]),
                    waits=[ld], inc="sa")
                ad = ctx.C["sa"]
                ph.op("gpsimd", lambda e, i=i: e.tensor_sub(
                    vwl.ap()[:, 2048 * i:2048 * (i + 1)], wf.ap()[:],
                    vwh.ap()[:, 2048 * i:2048 * (i + 1)]),
                    waits=[("sa", ad)], inc="sg")
                sg_d = ctx.C["sg"]
            wsplit = sg_d
            for tc in range(16):
                bank = tc % 2
                w = [("sg", wsplit)] if tc == 0 else []
                if tc >= 2:
                    sa_f, sv_f, sg_f = ev_free[tc - 2]
                    w += [("sa", sa_f), ("sv", sv_f), ("sg", sg_f)]
                first = True
                for kc in range(KC):
                    for (xa, wa3) in ((ar_hi, vwh3), (ar_lo, vwh3), (ar_hi, vwl3)):
                        ph.op("tensor", lambda e, tc=tc, kc=kc, xa=xa, wa3=wa3,
                              first=first, bank=bank: e.matmul(
                                  ctx.pb[bank].ap()[:, :],
                                  xa(ctx, kc)[:, 128 * tc:128 * (tc + 1)],
                                  wa3[:, :, 128 * kc:128 * (kc + 1)],
                                  start=first,
                                  stop=(kc == KC - 1 and xa is ar_hi and wa3 is vwl3)),
                              waits=w if first else ())
                        w = []
                        first = False
                ph.q["tensor"][-1] = (ph.q["tensor"][-1][0], ph.q["tensor"][-1][1], "sp", 1)
                ctx.bump("sp")
                mmd = ctx.C["sp"]
                if use_bias:
                    wv = [("sp", mmd), ("sa", bc_done)]
                    if tc >= 2:
                        wv.append(("sg", ev_free[tc - 2][2]))
                    ph.op("vector", lambda e, tc=tc, bank=bank, vg=vg: e.tensor_add(
                        ub[tc % 2].ap()[:], ctx.pb[bank].ap()[:, :],
                        vbb.ap()[:, 512 * vg:512 * (vg + 1)]),
                        waits=wv, inc="sv")
                    vd = ctx.C["sv"]
                    wv2 = [("sv", vd), vh_st[tc % 2]]
                    ph.op("scalar", lambda e, tc=tc: e.copy(vh[tc % 2].ap()[:],
                                                            ub[tc % 2].ap()[:]),
                          waits=wv2, inc="sa")
                    ad = ctx.C["sa"]
                    ph.op("gpsimd", lambda e, tc=tc: e.tensor_sub(
                        vl[tc % 2].ap()[:], ub[tc % 2].ap()[:], vh[tc % 2].ap()[:]),
                        waits=[("sa", ad), vl_st[tc % 2]], inc="sg")
                    gd = ctx.C["sg"]
                else:
                    w2 = [("sp", mmd), vh_st[tc % 2]]
                    ph.op("scalar", lambda e, tc=tc, bank=bank: e.copy(
                        vh[tc % 2].ap()[:], ctx.pb[bank].ap()[:, :]),
                        waits=w2, inc="sa")
                    ad = ctx.C["sa"]
                    ph.op("vector", lambda e, tc=tc, bank=bank: e.tensor_sub(
                        vl[tc % 2].ap()[:], ctx.pb[bank].ap()[:, :], vh[tc % 2].ap()[:]),
                        waits=[("sa", ad), vl_st[tc % 2]], inc="sv")
                    gd = 0
                vd2 = ctx.C["sv"]
                w = [("sa", ctx.C["sa"]), ("sv", vd2)]
                if use_bias:
                    w.append(("sg", gd))
                vh_st[tc % 2] = ph.dma(
                    d["v_hi"].ap()[4 * vg:4 * (vg + 1), 128 * tc:128 * (tc + 1), :]
                    .rearrange("h t e -> t h e"),
                    vh[tc % 2].ap().rearrange("p (h e) -> p h e", e=128), waits=w)
                vl_st[tc % 2] = ph.dma(
                    d["v_lo"].ap()[4 * vg:4 * (vg + 1), 128 * tc:128 * (tc + 1), :]
                    .rearrange("h t e -> t h e"),
                    vl[tc % 2].ap().rearrange("p (h e) -> p h e", e=128))
                ev_free[tc] = (ctx.C["sa"], ctx.C["sv"], ctx.C["sg"])
            ph.op("sync", None, waits=ctx.dma_all())
        ph.run()


def emit_l0_lightning(ctx):
    d = ctx.dram
    nc = ctx.nc
    with ExitStack() as es:
        qh = es.enter_context(sbt(nc, "lt_qh", [128, T], BF16))
        ql = es.enter_context(sbt(nc, "lt_ql", [128, T], BF16))
        kh = es.enter_context(sbt(nc, "lt_kh", [128, T], BF16))
        kl = es.enter_context(sbt(nc, "lt_kl", [128, T], BF16))
        vbh = es.enter_context(sbt(nc, "lt_vh", [128, T], BF16))
        vbl = es.enter_context(sbt(nc, "lt_vl", [128, T], BF16))
        smf = es.enter_context(sbt(nc, "lt_smf", [128, 128], F32))
        smh = es.enter_context(sbt(nc, "lt_smh", [128, 128], BF16))
        sml = es.enter_context(sbt(nc, "lt_sml", [128, 128], BF16))
        kf = es.enter_context(sbt(nc, "lt_kf", [128, 128], F32))
        khh = es.enter_context(sbt(nc, "lt_khh", [128, 128], BF16))
        khl = es.enter_context(sbt(nc, "lt_khl", [128, 128], BF16))
        qf = es.enter_context(sbt(nc, "lt_qf", [128, 128], F32))
        qtf = es.enter_context(sbt(nc, "lt_qtf", [128, 128], F32))
        qth = es.enter_context(sbt(nc, "lt_qth", [128, 128], BF16))
        qtl = es.enter_context(sbt(nc, "lt_qtl", [128, 128], BF16))
        S = es.enter_context(sbt(nc, "lt_S", [128, 128], F32))
        Sh = es.enter_context(sbt(nc, "lt_Sh", [128, 128], BF16))
        Sl = es.enter_context(sbt(nc, "lt_Sl", [128, 128], BF16))
        dTile = es.enter_context(sbt(nc, "lt_dT", [128, 128], F32))
        decayT = es.enter_context(sbt(nc, "lt_decayT", [128, 128], F32))
        qdecb = es.enter_context(sbt(nc, "lt_qdecb", [128, 128], F32))
        kd2c = es.enter_context(sbt(nc, "lt_kd2c", [128, 1], F32))
        blkc = es.enter_context(sbt(nc, "lt_blkc", [128, 1], F32))
        sneg = es.enter_context(sbt(nc, "lt_sneg", [128, 16], F32))
        cdif = es.enter_context(sbt(nc, "lt_cdif", [128, 128], F32))
        ctri = es.enter_context(sbt(nc, "lt_ctri", [128, 128], F32))
        ciot = es.enter_context(sbt(nc, "lt_ciot", [128, 128], F32))
        cdcl = es.enter_context(sbt(nc, "lt_cdcl", [128, 2], F32))
        srow = es.enter_context(sbt(nc, "lt_srow", [1, 16], F32))

        ph = Phase(ctx, "lightning")
        ones = ctx.sb["c_ones"]
        ph.dma(cdif.ap()[:], d["diffD"].ap())
        ph.dma(ctri.ap()[:], d["tri01"].ap())
        ph.dma(ciot.ap()[:], d["iotaF"].ap())
        ph.dma(cdcl.ap()[:], d["dcol"].ap())
        ld = ph.dma(srow.ap()[:], d["slopes_r"].ap())
        ph.op("tensor", lambda e: e.matmul(ctx.pb[0].ap()[:, 0:16], ones.ap()[0:1, :],
                                           srow.ap()[:], start=True, stop=True),
              waits=[ld], inc="sp")
        pd = ctx.C["sp"]
        ph.op("scalar", lambda e: e.activation(sneg.ap()[:], ctx.pb[0].ap()[:, 0:16],
                                               AF.Identity, scale=-1.0),
              waits=[("sp", pd)], inc="sa")
        sneg_done = ctx.C["sa"]

        for h in range(NH):
            head_w = [("sa", sneg_done)] if h == 0 else \
                [("sv", ctx.C["sv"]), ("sg", ctx.C["sg"]), ("sp", ctx.C["sp"])]
            ph.op("scalar", lambda e, h=h: e.activation(
                dTile.ap()[:], cdif.ap()[:], AF.Exp, scale=sneg.ap()[:, h:h + 1]),
                waits=head_w, inc="sa")
            ad = ctx.C["sa"]
            ph.op("vector", lambda e: e.tensor_mul(decayT.ap()[:], dTile.ap()[:],
                                                   ctri.ap()[:]),
                  waits=[("sa", ad)], inc="sv")
            dec_done = ctx.C["sv"]
            ph.op("scalar", lambda e, h=h: e.activation(
                qdecb.ap()[:], ciot.ap()[:], AF.Exp, scale=sneg.ap()[:, h:h + 1]))
            ph.op("scalar", lambda e, h=h: e.activation(
                kd2c.ap()[:], cdcl.ap()[:, 0:1], AF.Exp, scale=sneg.ap()[:, h:h + 1]))
            ph.op("scalar", lambda e, h=h: e.activation(
                blkc.ap()[:], cdcl.ap()[:, 1:2], AF.Exp, scale=sneg.ap()[:, h:h + 1]),
                inc="sa")
            dec2_done = ctx.C["sa"]
            ph.op("vector", lambda e: e.memset(S.ap()[:], 0.0),
                  waits=[("sa", dec2_done)], inc="sv")
            mem_done = ctx.C["sv"]
            ld_toks = [
                ph.dma(qh.ap()[:], d["qk_hi"].ap()[h]),
                ph.dma(ql.ap()[:], d["qk_lo"].ap()[h]),
                ph.dma(kh.ap()[:], d["qk_hi"].ap()[16 + h]),
                ph.dma(kl.ap()[:], d["qk_lo"].ap()[16 + h]),
                ph.dma(vbh.ap().rearrange("p (c e) -> p c e", e=128),
                       d["v_hi"].ap()[h].rearrange("(c p) e -> p c e", p=128)),
                ph.dma(vbl.ap().rearrange("p (c e) -> p c e", e=128),
                       d["v_lo"].ap()[h].rearrange("(c p) e -> p c e", p=128)),
            ]
            Ssplit = (0, 0)
            pe = {}
            for c in range(16):
                cs = slice(128 * c, 128 * (c + 1))
                bs = (c % 2) * 4
                w = ld_toks + [("sv", mem_done), ("sa", dec2_done)] if c == 0 else \
                    [("sv", pe.get(("smf", c - 2), 0))]
                for i, (wa, ra) in enumerate(((kh, qh), (kh, ql), (kl, qh))):
                    ph.op("tensor", lambda e, wa=wa, ra=ra, cs=cs, i=i, bs=bs: e.matmul(
                        ctx.pb[bs].ap()[:, 0:128], wa.ap()[:, cs], ra.ap()[:, cs],
                        start=(i == 0), stop=(i == 2)),
                        waits=w if i == 0 else ())
                    w = []
                ph.q["tensor"][-1] = (ph.q["tensor"][-1][0], ph.q["tensor"][-1][1], "sp", 1)
                ctx.bump("sp")
                sc_done = ctx.C["sp"]
                ph.op("vector", lambda e, bs=bs: e.tensor_mul(
                    smf.ap()[:], ctx.pb[bs].ap()[:, 0:128], decayT.ap()[:]),
                    waits=[("sp", sc_done), ("sa", pe.get(("smh", c - 1), 0))], inc="sv")
                pe[("smf", c)] = ctx.C["sv"]
                ph.op("scalar", lambda e: e.copy(smh.ap()[:], smf.ap()[:]),
                      waits=[("sv", pe[("smf", c)]), ("sp", pe.get(("o", c - 1), 0))],
                      inc="sa")
                pe[("smh", c)] = ctx.C["sa"]
                ph.op("vector", lambda e: e.tensor_sub(sml.ap()[:], smf.ap()[:], smh.ap()[:]),
                      waits=[("sa", pe[("smh", c)]), ("sp", pe.get(("o", c - 1), 0))],
                      inc="sv")
                sm_done = ctx.C["sv"]
                ph.op("vector", lambda e, cs=cs: e.tensor_add(
                    kf.ap()[:], kh.ap()[:, cs], kl.ap()[:, cs]),
                    waits=[("sp", pe.get(("ktr", c - 1), 0))], inc="sv")
                kf_done = ctx.C["sv"]
                ph.op("tensor", lambda e, bs=bs: e.transpose(
                    ctx.pb[bs + 1].ap()[:, 0:128], kf.ap()[:], ctx.sb["c_ident"].ap()[:]),
                    waits=[("sv", kf_done), ("sa", pe.get(("khh", c - 2), 0)),
                           ("sv", pe.get(("khl", c - 2), 0))], inc="sp")
                pe[("ktr", c)] = ctx.C["sp"]
                ph.op("scalar", lambda e, bs=bs: e.activation(
                    khh.ap()[:], ctx.pb[bs + 1].ap()[:, 0:128], AF.Identity,
                    scale=kd2c.ap()[:, 0:1]),
                    waits=[("sp", pe[("ktr", c)]), ("sp", pe.get(("su", c - 1), 0))],
                    inc="sa")
                pe[("khh", c)] = ctx.C["sa"]
                ph.op("vector", lambda e, bs=bs: e.scalar_tensor_tensor(
                    khl.ap()[:], ctx.pb[bs + 1].ap()[:, 0:128], kd2c.ap()[:, 0:1],
                    khh.ap()[:], ALU.mult, ALU.subtract),
                    waits=[("sa", pe[("khh", c)]), ("sp", pe.get(("su", c - 1), 0))],
                    inc="sv")
                pe[("khl", c)] = ctx.C["sv"]
                khat_done = ctx.C["sv"]
                wq = [("sa", pe.get(("qth", c - 1), 0)), ("sg", pe.get(("qtl", c - 1), 0))]
                if c == 0:
                    wq.append(("sa", dec2_done))
                ph.op("vector", lambda e, cs=cs: e.tensor_add(
                    qf.ap()[:], qh.ap()[:, cs], ql.ap()[:, cs]), waits=wq)
                ph.op("vector", lambda e: e.tensor_mul(qtf.ap()[:], qf.ap()[:],
                                                       qdecb.ap()[:]), inc="sv")
                qtf_done = ctx.C["sv"]
                ph.op("scalar", lambda e: e.copy(qth.ap()[:], qtf.ap()[:]),
                      waits=[("sv", qtf_done), ("sp", pe.get(("o", c - 1), 0))], inc="sa")
                pe[("qth", c)] = ctx.C["sa"]
                ph.op("gpsimd", lambda e: e.tensor_sub(qtl.ap()[:], qtf.ap()[:], qth.ap()[:]),
                      waits=[("sa", pe[("qth", c)]), ("sp", pe.get(("o", c - 1), 0))],
                      inc="sg")
                pe[("qtl", c)] = ctx.C["sg"]
                qt_done = ctx.C["sg"]
                mms = [(vbh, smh), (vbh, sml), (vbl, smh)]
                if c > 0:
                    mms += [(Sh, qth), (Sh, qtl), (Sl, qth)]
                w = [("sv", sm_done), ("sg", qt_done),
                     ("sa", pe.get(("oh", c - 2), 0)), ("sv", pe.get(("ol", c - 2), 0))]
                if c > 0:
                    w += [("sa", Ssplit[0]), ("sv", Ssplit[1])]
                for i, (wa, ra) in enumerate(mms):
                    def mmfn(e, wa=wa, ra=ra, i=i, cs=cs, bs=bs, n=len(mms)):
                        lhs = wa.ap()[:, cs] if wa in (vbh, vbl) else wa.ap()[:]
                        return e.matmul(ctx.pb[bs + 2].ap()[:, 0:128], lhs, ra.ap()[:],
                                        start=(i == 0), stop=(i == n - 1))
                    ph.op("tensor", mmfn, waits=w if i == 0 else ())
                    w = []
                ph.q["tensor"][-1] = (ph.q["tensor"][-1][0], ph.q["tensor"][-1][1], "sp", 1)
                ctx.bump("sp")
                pe[("o", c)] = ctx.C["sp"]
                o_done = ctx.C["sp"]
                ph.op("scalar", lambda e, bs=bs, h=h, cs=cs: e.copy(
                    ar_hi(ctx, h)[:, cs], ctx.pb[bs + 2].ap()[:, 0:128]),
                    waits=[("sp", o_done)], inc="sa")
                pe[("oh", c)] = ctx.C["sa"]
                ph.op("vector", lambda e, bs=bs, h=h, cs=cs: e.tensor_sub(
                    ar_lo(ctx, h)[:, cs], ctx.pb[bs + 2].ap()[:, 0:128], ar_hi(ctx, h)[:, cs]),
                    waits=[("sa", pe[("oh", c)])], inc="sv")
                pe[("ol", c)] = ctx.C["sv"]
                w = [("sv", khat_done), ("sv", pe.get(("sustt", c - 2), 0))]
                for i, (wa, ra) in enumerate(((khh, vbh), (khh, vbl), (khl, vbh))):
                    ph.op("tensor", lambda e, wa=wa, ra=ra, i=i, cs=cs, bs=bs: e.matmul(
                        ctx.pb[bs + 3].ap()[:, 0:128], wa.ap()[:], ra.ap()[:, cs],
                        start=(i == 0), stop=(i == 2)),
                        waits=w if i == 0 else ())
                    w = []
                ph.q["tensor"][-1] = (ph.q["tensor"][-1][0], ph.q["tensor"][-1][1], "sp", 1)
                ctx.bump("sp")
                pe[("su", c)] = ctx.C["sp"]
                su_done = ctx.C["sp"]
                ph.op("vector", lambda e, bs=bs: e.scalar_tensor_tensor(
                    S.ap()[:], S.ap()[:], blkc.ap()[:, 0:1], ctx.pb[bs + 3].ap()[:, 0:128],
                    ALU.mult, ALU.add),
                    waits=[("sp", su_done), ("sa", pe.get(("Sh", c - 1), 0))], inc="sv")
                pe[("sustt", c)] = ctx.C["sv"]
                ph.op("scalar", lambda e: e.copy(Sh.ap()[:], S.ap()[:]),
                      waits=[("sv", pe[("sustt", c)])], inc="sa")
                pe[("Sh", c)] = ctx.C["sa"]
                ph.op("vector", lambda e: e.tensor_sub(Sl.ap()[:], S.ap()[:], Sh.ap()[:]),
                      waits=[("sa", pe[("Sh", c)])], inc="sv")
                Ssplit = (pe[("Sh", c)], ctx.C["sv"])
            ph.op("sync", None, waits=[("sp", ctx.C["sp"]), ("sv", ctx.C["sv"]),
                                       ("sa", ctx.C["sa"]), ("sg", ctx.C["sg"])])
        ph.run()


def emit_l0_outproj(ctx):
    d = ctx.dram
    nc = ctx.nc
    with ExitStack() as es:
        ot = [es.enter_context(sbt(nc, f"op_ot{i}", [128, 512], F32)) for i in range(4)]
        bol = es.enter_context(sbt(nc, "op_bol", [128, 16], F32))
        b_done = [0]

        def evac(ph):
            b_done[0] = ph.dma(bol.ap()[:], d["bout_l"].ap())
            st_done = [None] * 70

            def ev(j, oc, tt, bank, mmd):
                ti = (j * 4 + tt) % 4
                w = [("sp", mmd), st_done[max(0, j * 4 + tt - 4)]]
                if j == 0 and tt == 0:
                    w.append(b_done[0])
                ph.op("scalar", lambda e, ti=ti, bank=bank, oc=oc: e.activation(
                    ot[ti].ap()[:], ctx.pb[bank].ap()[:, :], AF.Identity,
                    bias=bol.ap()[:, oc:oc + 1]), waits=w, inc="sa")
                ad = ctx.C["sa"]
                st_done[j * 4 + tt] = ph.dma(
                    d["sub_out"].ap()[oc][:, 512 * tt:512 * (tt + 1)], ot[ti].ap()[:],
                    waits=[("sa", ad)])
                return ad, 0
            return ev

        emit_proj_stream(ctx, "l0op", list(range(16)), lambda oc: d["wout"].ap()[oc], evac)


def emit_l0_attn(ctx):
    emit_pre_ln(ctx, split=True, eps=EPS_LN)
    emit_l0_qkv_qk(ctx)
    emit_v(ctx, "l0v", lambda vg, i: ctx.dram["wqkv"].ap()[32 + 4 * vg + i], use_bias=True)
    emit_l0_lightning(ctx)
    emit_l0_outproj(ctx)
    emit_post_ln(ctx, eps=EPS_LN)



def emit_quant_chain(ctx, ph, wf_ap, wbf, qr, scl, rcp, width, ld_wait, pool_guard):
    ng = width // 64
    ph.op("scalar", lambda e: e.copy(wbf.ap()[:, 0:width], wf_ap),
          waits=ld_wait, inc="sa")
    ad = ctx.C["sa"]
    ph.op("vector", lambda e: e.reduce_sum(
        rcp.ap()[:, 0:ng], wbf.ap()[:, 0:width].rearrange("p (g e) -> p g e", e=64),
        axis=AX.X, apply_absolute_value=True), waits=[("sa", ad)] + list(pool_guard))
    ph.op("vector", lambda e: e.tensor_scalar(
        scl.ap()[:, 0:ng], rcp.ap()[:, 0:ng], 1.0 / 64, 1e-8, ALU.mult, ALU.max))
    ph.op("vector", lambda e: e.reciprocal(rcp.ap()[:, 0:ng], scl.ap()[:, 0:ng]))
    sclb = scl.ap()[:, 0:ng].unsqueeze(2).broadcast_to([128, ng, 64])
    rcpb = rcp.ap()[:, 0:ng].unsqueeze(2).broadcast_to([128, ng, 64])
    w3 = wbf.ap()[:, 0:width].rearrange("p (g e) -> p g e", e=64)
    q3 = qr.ap()[:, 0:width].rearrange("p (g e) -> p g e", e=64)
    ph.op("vector", lambda e: e.tensor_tensor(q3, w3, rcpb, ALU.mult))
    if getattr(ctx, "dbg_quant", False):
        ph.op("vector", lambda e: e.engine_nop(), inc="sv")
        qv = ctx.C["sv"]
        ph.dma(ctx.dram["dbg"].ap()[:, 6144:6144 + width], qr.ap()[:, 0:width],
               waits=[("sv", qv)])
        ph.dma(ctx.dram["dbg"].ap()[:, 6144 - width // 64:6144],
               scl.ap()[:, 0:width // 64])
        ctx.dbg_quant = False
    ph.op("vector", lambda e: e.tensor_scalar(
        wbf.ap()[:, 0:width], qr.ap()[:, 0:width], -0.5, None, ALU.is_lt))
    ph.op("vector", lambda e: e.scalar_tensor_tensor(
        wbf.ap()[:, 0:width], qr.ap()[:, 0:width], 0.5, wbf.ap()[:, 0:width],
        ALU.is_gt, ALU.subtract))
    w3b = wbf.ap()[:, 0:width].rearrange("p (g e) -> p g e", e=64)
    ph.op("vector", lambda e: e.tensor_tensor(q3, w3b, sclb, ALU.mult), inc="sv")
    return ctx.C["sv"]


def emit_l0f_gu(ctx):
    d = ctx.dram
    nc = ctx.nc
    with ExitStack() as es:
        wf = [es.enter_context(sbt(nc, f"gu_wf{i}", [128, 2048], F32)) for i in range(2)]
        wbf = es.enter_context(sbt(nc, "gu_wbf", [128, 2048], BF16))
        qr = es.enter_context(sbt(nc, "gu_qr", [128, 2048], BF16))
        scl = es.enter_context(sbt(nc, "gu_scl", [128, 64], BF16))
        rcp = es.enter_context(sbt(nc, "gu_rcp", [128, 32], F32))
        whT = es.enter_context(sbt(nc, "gu_whT", [128, 2048], BF16))
        silu = es.enter_context(sbt(nc, "gu_silu", [128, 2048], F32))
        hsb = [es.enter_context(sbt(nc, f"gu_h{i}", [128, 2048], F32)) for i in range(2)]
        hsq = es.enter_context(sbt(nc, "gu_hsq", [128, 512], F32))
        ones = ctx.sb["c_ones"]

        ph = Phase(ctx, "l0f_gu")
        prev = {}
        for pi in range(64):
            hstore = 0
            for half, ch in ((0, pi), (1, 64 + pi)):
                j = pi * 2 + half
                ld = ph.dma(wf[j % 2].ap()[:], d["gup_nat"].ap()[ch],
                            waits=[("sa", prev.get(("wbf", j - 2), 0))])
                qd = emit_quant_chain(ctx, ph, wf[j % 2].ap()[:], wbf, qr, scl, rcp,
                                      2048, [ld, ("sa", prev.get(("tev", j - 1), 0))],
                                      [("sv", 0)])
                prev[("wbf", j)] = ctx.C["sa"]
                tevs = [0, 0, 0, 0]
                for kc in range(16):
                    bnk = kc % 4
                    wt = [("sv", qd)] if kc == 0 else []
                    if kc < 4:
                        if bnk == 0:
                            wt.append(("sa", prev.get(("ev", 0, 3, pi - 1 if half == 0 else pi), 0)))
                        elif bnk == 1:
                            wt.append(("sv", prev.get(("ev", 1, 3, pi - 1), 0)))
                        else:
                            wt.append(("sa", prev.get(("tev", j - 1), 0)))
                    else:
                        wt.append(("sa", tevs[bnk]))
                    ph.op("tensor", lambda e, kc=kc, bnk=bnk: e.transpose(
                        ctx.pbb[bnk].ap()[:, 0:128],
                        qr.ap()[:, 128 * kc:128 * (kc + 1)], ctx.sb["c_identb"].ap()[:]),
                        waits=wt, inc="sp")
                    td = ctx.C["sp"]
                    ph.op("scalar", lambda e, kc=kc, bnk=bnk: e.copy(
                        whT.ap()[:, 128 * kc:128 * (kc + 1)], ctx.pbb[bnk].ap()[:, 0:128]),
                        waits=[("sp", td)], inc="sa")
                    tevs[bnk] = ctx.C["sa"]
                prev[("tev", j)] = ctx.C["sa"]
                tev = ctx.C["sa"]
                if pi == 0 and half == 0:
                    ph.dma(d["dbg"].ap()[:, 0:2048], whT.ap()[:], waits=[("sa", tev)])
                    ph.dma(d["dbg"].ap()[:, 2048:4096], qr.ap()[:])
                    ph.dma(d["dbg"].ap()[:, 4096:6144], wbf.ap()[:])
                for tt in range(4):
                    w = [("sa", tev)] if tt == 0 else []
                    if tt >= 1:
                        w.append(("sa" if half == 0 else "sv", prev.get(("ev", half, tt - 1, pi), 0)))
                    else:
                        w.append(("sa" if half == 0 else "sv", prev.get(("ev", half, 3, pi - 1), 0)))
                    first = True
                    for kc in range(KC):
                        for xa in (ar_hi, ar_lo):
                            ph.op("tensor", lambda e, tt=tt, kc=kc, xa=xa, first=first,
                                  half=half: e.matmul(
                                      ctx.pb[half].ap()[:, :],
                                      whT.ap()[:, 128 * kc:128 * (kc + 1)],
                                      xa(ctx, kc)[:, 512 * tt:512 * (tt + 1)],
                                      start=first, stop=(kc == KC - 1 and xa is ar_lo)),
                                  waits=w if first else ())
                            w = []
                            first = False
                    ph.q["tensor"][-1] = (ph.q["tensor"][-1][0], ph.q["tensor"][-1][1], "sp", 1)
                    ctx.bump("sp")
                    mmd = ctx.C["sp"]
                    if half == 0:
                        ph.op("scalar", lambda e, tt=tt: e.activation(
                            silu.ap()[:, 512 * tt:512 * (tt + 1)], ctx.pb[0].ap()[:, :],
                            AF.Silu),
                            waits=[("sp", mmd), ("sv", prev.get(("ev", 1, tt, pi - 1), 0))],
                            inc="sa")
                        prev[("ev", 0, tt, pi)] = ctx.C["sa"]
                    else:
                        wv = [("sp", mmd), ("sa", prev[("ev", 0, tt, pi)]),
                              prev.get(("hst", pi - 2), None)]
                        ph.op("vector", lambda e, tt=tt, pi=pi: e.tensor_mul(
                            hsb[pi % 2].ap()[:, 512 * tt:512 * (tt + 1)],
                            ctx.pb[1].ap()[:, :], silu.ap()[:, 512 * tt:512 * (tt + 1)]),
                            waits=wv, inc="sv")
                        prev[("ev", 1, tt, pi)] = ctx.C["sv"]
                        hd = ctx.C["sv"]
                        ph.op("scalar", lambda e, tt=tt, pi=pi: e.activation(
                            hsq.ap()[:], hsb[pi % 2].ap()[:, 512 * tt:512 * (tt + 1)],
                            AF.Square), waits=[("sv", hd), ("sp", prev.get("sqlast", 0))],
                            inc="sa")
                        sqd = ctx.C["sa"]
                        ph.op("tensor", lambda e, tt=tt, pi=pi: e.matmul(
                            ctx.pb[4 + tt].ap()[0:1, :], ones.ap()[:, 0:1], hsq.ap()[:],
                            start=(pi == 0), stop=(pi == 63)),
                            waits=[("sa", sqd)], inc="sp")
                        prev["sqlast"] = ctx.C["sp"]
            prev[("hst", pi)] = ph.dma(d["h_f32"].ap()[pi], hsb[pi % 2].ap()[:],
                                       waits=[("sv", prev[("ev", 1, 3, pi)])])
        ph.op("sync", None, waits=ctx.dma_all() + [("sp", ctx.C["sp"]),
                                   ("sv", ctx.C["sv"]), ("sa", ctx.C["sa"])])
        ph.run()


def emit_l0f_hn(ctx):
    d = ctx.dram
    nc = ctx.nc
    ones = ctx.sb["c_ones"]
    with ExitStack() as es:
        hb = [es.enter_context(sbt(nc, f"hn_hb{i}", [128, T], F32)) for i in range(2)]
        hnf = es.enter_context(sbt(nc, "hn_f", [128, T], F32))
        hh = [es.enter_context(sbt(nc, f"hn_hh{i}", [128, T], BF16)) for i in range(2)]
        hl = [es.enter_context(sbt(nc, f"hn_hl{i}", [128, T], BF16)) for i in range(2)]
        ph = Phase(ctx, "l0f_hn")
        for tt in range(4):
            ph.op("vector", lambda e, tt=tt: e.tensor_scalar(
                rows0(ctx)[:, 512 * tt:512 * (tt + 1)], ctx.pb[4 + tt].ap()[0:1, :],
                1.0 / 8192.0, None, ALU.mult), inc="sv" if tt == 3 else None)
        md = ctx.C["sv"]
        ph.op("scalar", lambda e: e.activation(
            rsc(ctx), rows0(ctx), AF.Sqrt,
            bias=ctx.sb["c_cvals"].ap()[0:1, 1:2]), waits=[("sv", md)], inc="sa")
        sd = ctx.C["sa"]
        ph.op("vector", lambda e: e.reciprocal(rows0(ctx), rsc(ctx)),
              waits=[("sa", sd)], inc="sv")
        rd = ctx.C["sv"]
        rb = 0
        for tt in range(4):
            ph.op("tensor", lambda e, tt=tt: e.matmul(
                ctx.pb[tt % 2].ap()[:, :], ones.ap()[0:1, :],
                rows0(ctx)[:, 512 * tt:512 * (tt + 1)], start=True, stop=True),
                waits=([("sv", rd)] if tt == 0 else
                       ([("sa", rb - 1)] if tt >= 2 else ())), inc="sp")
            bd = ctx.C["sp"]
            ph.op("scalar", lambda e, tt=tt: e.copy(
                ctx.sb["rstdb"].ap()[:, 512 * tt:512 * (tt + 1)], ctx.pb[tt % 2].ap()[:, :]),
                waits=[("sp", bd)], inc="sa")
            rb = ctx.C["sa"]
        bc = rb
        prev = {}
        for ch in range(64):
            ld = ph.dma(hb[ch % 2].ap()[:], d["h_f32"].ap()[ch],
                        waits=[("sv", prev.get(("hnf", ch - 2), 0))])
            ph.op("vector", lambda e, ch=ch: e.tensor_mul(
                hnf.ap()[:], hb[ch % 2].ap()[:], ctx.sb["rstdb"].ap()[:]),
                waits=[ld, ("sa", bc), ("sg", prev.get(("lo", ch - 1), 0))], inc="sv")
            prev[("hnf", ch)] = ctx.C["sv"]
            ph.op("scalar", lambda e, ch=ch: e.copy(hh[ch % 2].ap()[:], hnf.ap()[:]),
                  waits=[("sv", ctx.C["sv"]), prev.get(("st", ch - 2), None)], inc="sa")
            ad = ctx.C["sa"]
            ph.op("gpsimd", lambda e, ch=ch: e.tensor_sub(
                hl[ch % 2].ap()[:], hnf.ap()[:], hh[ch % 2].ap()[:]),
                waits=[("sa", ad)], inc="sg")
            prev[("lo", ch)] = ctx.C["sg"]
            ph.dma(d["hn_hi"].ap()[ch], hh[ch % 2].ap()[:],
                   waits=[("sg", ctx.C["sg"])])
            prev[("st", ch)] = ph.dma(d["hn_lo"].ap()[ch], hl[ch % 2].ap()[:])
        ph.op("sync", None, waits=ctx.dma_all())
        ph.run()


def emit_l0f_wpj_quant(ctx):
    d = ctx.dram
    nc = ctx.nc
    with ExitStack() as es:
        wf = es.enter_context(sbt(nc, "pq_wf", [128, 4096], F32))
        wbf = es.enter_context(sbt(nc, "pq_wbf", [128, 4096], BF16))
        qr = es.enter_context(sbt(nc, "pq_qr", [128, 4096], BF16))
        scl = es.enter_context(sbt(nc, "pq_scl", [128, 128], BF16))
        rcp = es.enter_context(sbt(nc, "pq_rcp", [128, 64], F32))
        wt = es.enter_context(sbt(nc, "pq_wt", [128, 8192], BF16))
        ph = Phase(ctx, "l0f_pq")
        prev = {}
        for ch in range(16):
            for hf in range(2):
                j = ch * 2 + hf
                ld = ph.dma(wf.ap()[:], d["wpj0_nat"].ap()[ch][:, 4096 * hf:4096 * (hf + 1)],
                            waits=[("sa", prev.get(("wbf", j - 1), 0))])
                qd = emit_quant_chain(ctx, ph, wf.ap()[:], wbf, qr, scl, rcp, 4096,
                                      [ld, ("sa", prev.get(("tev", j - 1), 0))],
                                      [("sv", 0)])
                prev[("wbf", j)] = ctx.C["sa"]
                tevs = [prev.get(("tev", j - 1), 0)] * 8
                for kcH in range(32):
                    bnk = kcH % 8
                    w = [("sv", qd)] if kcH == 0 else []
                    w.append(("sa", tevs[bnk]))
                    ph.op("tensor", lambda e, kcH=kcH, bnk=bnk: e.transpose(
                        ctx.pbb[bnk].ap()[:, 0:128],
                        qr.ap()[:, 128 * kcH:128 * (kcH + 1)], ctx.sb["c_identb"].ap()[:]),
                        waits=w, inc="sp")
                    td = ctx.C["sp"]
                    ph.op("scalar", lambda e, kcH=kcH, bnk=bnk, hf=hf: e.copy(
                        wt.ap()[:, 4096 * hf + 128 * kcH:4096 * hf + 128 * (kcH + 1)],
                        ctx.pbb[bnk].ap()[:, 0:128]),
                        waits=[("sp", td)], inc="sa")
                    tevs[bnk] = ctx.C["sa"]
                prev[("tev", j)] = ctx.C["sa"]
            std = ph.dma(d["wpj0_t"].ap()[ch], wt.ap()[:],
                         waits=[("sa", prev[("tev", ch * 2 + 1)]),
                                prev.get(("st", ch - 1), None)])
            prev[("st", ch)] = std
        ph.op("sync", None, waits=ctx.dma_all())
        ph.run()


def emit_ffn_proj(ctx, name, wt_dram, n_oc, sub_scale=None):
    d = ctx.dram
    nc = ctx.nc
    with ExitStack() as es:
        hh = [es.enter_context(sbt(nc, f"{name}_hh{i}", [128, 512], BF16))
              for i in range(3)]
        hl = [es.enter_context(sbt(nc, f"{name}_hl{i}", [128, 512], BF16))
              for i in range(3)]
        ot = [es.enter_context(sbt(nc, f"{name}_ot{i}", [128, 512], F32))
              for i in range(2)]
        ph = Phase(ctx, name)
        nog = n_oc // 8
        prev = {}
        arb = ctx.arena_b
        for og in range(nog):
            wlds = []
            for i in range(8):
                wlds.append(ph.dma(
                    arb.ap()[:, 8192 * i:8192 * (i + 1)], wt_dram.ap()[og * 8 + i],
                    waits=[("sp", prev.get(("mm", og - 1), 0))] if (og and i == 0) else []))
            for tt in range(4):
                for kcH in range(64):
                    jj = tt * 64 + kcH
                    bi = jj % 3
                    gw = [("sp", prev.get(("hn", og, jj - 3), 0))] if jj >= 3 else []
                    l1 = ph.dma(hh[bi].ap()[:], d["hn_hi"].ap()[kcH][:, 512 * tt:512 * (tt + 1)],
                                waits=gw)
                    l2 = ph.dma(hl[bi].ap()[:], d["hn_lo"].ap()[kcH][:, 512 * tt:512 * (tt + 1)])
                    w = [l1, l2]
                    if kcH == 0:
                        w += wlds
                        if tt >= 1:
                            w.append(("sa", prev.get(("ev", og, tt - 1), 0)))
                        elif og >= 1:
                            w.append(("sa", prev.get(("ev", og - 1, 3), 0)))
                    for oc8 in range(8):
                        for hx in (hh, hl):
                            ph.op("tensor", lambda e, oc8=oc8, hx=hx, bi=bi, kcH=kcH: e.matmul(
                                ctx.pb[oc8].ap()[:, :],
                                arb.ap()[:, 8192 * oc8 + 128 * kcH:8192 * oc8 + 128 * (kcH + 1)],
                                hx[bi].ap()[:], start=(kcH == 0 and hx is hh),
                                stop=(kcH == 63 and hx is hl)),
                                waits=w if (oc8 == 0 and hx is hh) else ())
                            w = []
                    ph.q["tensor"][-1] = (ph.q["tensor"][-1][0], ph.q["tensor"][-1][1], "sp", 1)
                    ctx.bump("sp")
                    prev[("hn", og, jj)] = ctx.C["sp"]
                mmd = ctx.C["sp"]
                prev[("mm", og)] = mmd
                for oc8 in range(8):
                    oc = og * 8 + oc8
                    w = [("sp", mmd), prev.get(("st", og, tt, oc8 - 2), None)]
                    if sub_scale is not None:
                        ph.op("scalar", lambda e, oc8=oc8: e.activation(
                            ot[oc8 % 2].ap()[:], ctx.pb[oc8].ap()[:, :], AF.Identity,
                            scale=sub_scale()), waits=w, inc="sa")
                    else:
                        ph.op("scalar", lambda e, oc8=oc8: e.copy(
                            ot[oc8 % 2].ap()[:], ctx.pb[oc8].ap()[:, :]), waits=w, inc="sa")
                    ad = ctx.C["sa"]
                    prev[("st", og, tt, oc8)] = ph.dma(
                        d["sub_out"].ap()[oc][:, 512 * tt:512 * (tt + 1)],
                        ot[oc8 % 2].ap()[:], waits=[("sa", ad)])
                prev[("ev", og, tt)] = ctx.C["sa"]
        ph.op("sync", None, waits=ctx.dma_all())
        ph.run()


def emit_l0_ffn(ctx):
    emit_pre_ln(ctx, split=True, eps=EPS_LN)
    emit_l0f_gu(ctx)
    emit_l0f_hn(ctx)
    emit_l0f_wpj_quant(ctx)
    emit_ffn_proj(ctx, "l0f_pj", ctx.dram["wpj0_t"], 16)
    emit_post_ln(ctx, eps=EPS_LN)



def emit_l1_qkv_qk(ctx):
    d = ctx.dram
    nc = ctx.nc
    with ExitStack() as es:
        qraw = es.enter_context(sbt(nc, "l1qraw", [128, T], F32))
        xsw = es.enter_context(sbt(nc, "l1xsw", [128, T], F32))
        tc = es.enter_context(sbt(nc, "l1tc", [128, T], F32))
        tsn = es.enter_context(sbt(nc, "l1ts", [128, T], F32))
        hi = [es.enter_context(sbt(nc, "l1hi", [128, 512], BF16)) for i in range(2)]
        lo = [es.enter_context(sbt(nc, "l1lo", [128, 512], BF16)) for i in range(2)]
        state = {"mode": None}

        def evac(ph):
            st_done = [None] * 40
            ev_state = {"tbl": None, "sw": 0}

            def ev(j, oc, tt, bank, mmd):
                is_q = oc < 16
                ts = slice(512 * tt, 512 * (tt + 1))
                if state["mode"] != is_q:
                    state["mode"] = is_q
                    cnm = "ropeQC" if is_q else "ropeKC"
                    snm = "ropeQS" if is_q else "ropeKS"
                    w = [("sv", ctx.C["sv"])]
                    tca = ph.dma(tc.ap()[:], d[cnm].ap(), waits=w)
                    ev_state["tbl"] = ph.dma(tsn.ap()[:], d[snm].ap(), waits=[tca])
                w = [("sp", mmd)]
                if tt == 0 and j >= 1:
                    w.append(st_done[j - 1])
                if tt == 0:
                    w.append(("sv", ev_state["sw"]))
                    w.append(("sg", ev_state.get("sg", 0)))
                ph.op("scalar", lambda e, tt=tt, bank=bank: e.copy(
                    qraw.ap()[:, ts], ctx.pb[bank].ap()[:, :]), waits=w, inc="sa")
                qev = ctx.C["sa"]
                if tt == 3:
                    bounce = d["sub_out"].ap()[0]
                    s1 = ph.dma(bounce[0:64, :], qraw.ap()[64:128, :],
                                waits=[("sa", qev)])
                    s1b = ph.dma(bounce[64:128, :], qraw.ap()[0:64, :], waits=[s1])
                    s2 = ph.dma(xsw.ap()[:], bounce, waits=[s1b])
                    ph.op("vector", lambda e: e.tensor_tensor(
                        qraw.ap()[:], qraw.ap()[:], tc.ap()[:], ALU.mult),
                        waits=[s2, ev_state["tbl"]])
                    ph.op("vector", lambda e: e.tensor_tensor(
                        xsw.ap()[:], xsw.ap()[:], tsn.ap()[:], ALU.mult))
                    ph.op("vector", lambda e: e.tensor_add(
                        qraw.ap()[:], qraw.ap()[:], xsw.ap()[:]), inc="sv")
                    ev_state["sw"] = ctx.C["sv"]
                    yd = ctx.C["sv"]
                    hi_st = ev_state.setdefault("hi_st", [None, None])
                    lo_st = ev_state.setdefault("lo_st", [None, None])
                    for t2 in range(4):
                        t2s = slice(512 * t2, 512 * (t2 + 1))
                        ph.op("scalar", lambda e, t2=t2, t2s=t2s: e.copy(
                            hi[t2 % 2].ap()[:], qraw.ap()[:, t2s]),
                            waits=[("sv", yd), hi_st[t2 % 2]], inc="sa")
                        ad = ctx.C["sa"]
                        ph.op("gpsimd", lambda e, t2=t2, t2s=t2s: e.tensor_sub(
                            lo[t2 % 2].ap()[:], qraw.ap()[:, t2s], hi[t2 % 2].ap()[:]),
                            waits=[("sa", ad), lo_st[t2 % 2]], inc="sg")
                        gd = ctx.C["sg"]
                        hi_st[t2 % 2] = ph.dma(d["qk_hi"].ap()[oc][:, t2s],
                                               hi[t2 % 2].ap()[:], waits=[("sg", gd)])
                        lo_st[t2 % 2] = ph.dma(d["qk_lo"].ap()[oc][:, t2s],
                                               lo[t2 % 2].ap()[:])
                    st_done[j] = lo_st[1]
                    ev_state["sg"] = ctx.C["sg"]
                return ctx.C["sa"], ctx.C["sv"]
            return ev

        emit_proj_stream(ctx, "l1qk", list(range(32)),
                         lambda oc: d["wq1"].ap()[oc] if oc < 16 else d["wk1"].ap()[oc - 16],
                         evac, nwf=1, nws=1)


def emit_l1_flash(ctx):
    d = ctx.dram
    nc = ctx.nc
    with ExitStack() as es:
        qh = es.enter_context(sbt(nc, "fl_qh", [128, T], BF16))
        ql = es.enter_context(sbt(nc, "fl_ql", [128, T], BF16))
        kh = es.enter_context(sbt(nc, "fl_kh", [128, T], BF16))
        kl = es.enter_context(sbt(nc, "fl_kl", [128, T], BF16))
        vbh = es.enter_context(sbt(nc, "fl_vh", [128, T], BF16))
        vbl = es.enter_context(sbt(nc, "fl_vl", [128, T], BF16))
        sco = es.enter_context(sbt(nc, "fl_sco", [128, T], F32))
        msk = es.enter_context(sbt(nc, "fl_msk", [128, T], BF16))
        pth = es.enter_context(sbt(nc, "fl_pth", [128, 16 * 256], BF16))
        ptl = es.enter_context(sbt(nc, "fl_ptl", [128, 16 * 256], BF16))
        acc = es.enter_context(sbt(nc, "fl_acc", [128, 8], F32))
        mneg = es.enter_context(sbt(nc, "fl_mn", [128, 2], F32))
        rcol = es.enter_context(sbt(nc, "fl_rc", [128, 2], F32))
        rr = es.enter_context(sbt(nc, "fl_rr", [2, 128], F32))
        rr2 = es.enter_context(sbt(nc, "fl_rr2", [1, 256], F32))
        pbuf = ctx.sb["mb"]
        rcb = ctx.sb["rstdb"]
        otile = ctx.sb["rstdb"].ap()[:, 256:512]
        ph = Phase(ctx, "l1flash")
        mld = ph.dma(msk.ap()[:], d["l1mask"].ap())
        prev = {}
        for h in range(NH):
            w0 = [("sp", ctx.C["sp"]), ("sv", ctx.C["sv"]), ("sa", ctx.C["sa"]),
                  ("sg", ctx.C["sg"])] if h else [mld]
            ph.op("sync", None, waits=w0)
            ld_toks = [
                ph.dma(qh.ap()[:], d["qk_hi"].ap()[h]),
                ph.dma(ql.ap()[:], d["qk_lo"].ap()[h]),
                ph.dma(kh.ap()[:], d["qk_hi"].ap()[16 + h]),
                ph.dma(kl.ap()[:], d["qk_lo"].ap()[16 + h]),
                ph.dma(vbh.ap().rearrange("p (c e) -> p c e", e=128),
                       d["v_hi"].ap()[h].rearrange("(c p) e -> p c e", p=128)),
                ph.dma(vbl.ap().rearrange("p (c e) -> p c e", e=128),
                       d["v_lo"].ap()[h].rearrange("(c p) e -> p c e", p=128)),
            ]
            for g2 in range(8):
                for iq in range(2):
                    qt = 2 * g2 + iq
                    nkt = qt // 4 + 1
                    qs = slice(128 * qt, 128 * (qt + 1))
                    for kt in range(nkt):
                        bank = kt % 2
                        w = list(ld_toks) if (g2 == 0 and iq == 0 and kt == 0) else \
                            [("sa", prev.get(("sev", bank), 0))]
                        for i, (wa, ra) in enumerate(((qh, kh), (qh, kl), (ql, kh))):
                            ph.op("tensor", lambda e, wa=wa, ra=ra, qs=qs, kt=kt, i=i,
                                  bank=bank: e.matmul(
                                      ctx.pb[bank].ap()[:, :], wa.ap()[:, qs],
                                      ra.ap()[:, 512 * kt:512 * (kt + 1)],
                                      start=(i == 0), stop=(i == 2)),
                                  waits=w if i == 0 else ())
                            w = []
                        ph.q["tensor"][-1] = (ph.q["tensor"][-1][0], ph.q["tensor"][-1][1],
                                              "sp", 1)
                        ctx.bump("sp")
                        smd = ctx.C["sp"]
                        wse = [("sp", smd)]
                        if kt == 0:
                            wse.append(("sv", prev.get("mnd", 0)))
                        ph.op("scalar", lambda e, kt=kt, bank=bank: e.copy(
                            sco.ap()[:, 512 * kt:512 * (kt + 1)], ctx.pb[bank].ap()[:, :]),
                            waits=wse, inc="sa")
                        prev[("sev", bank)] = ctx.C["sa"]
                    sc_done = ctx.C["sa"]
                    r = qt % 4
                    ph.op("vector", lambda e, r=r, nkt=nkt: e.tensor_tensor(
                        sco.ap()[:, 512 * (nkt - 1):512 * nkt],
                        sco.ap()[:, 512 * (nkt - 1):512 * nkt],
                        msk.ap()[:, 512 * r:512 * (r + 1)], ALU.add),
                        waits=[("sa", sc_done)])
                    ph.op("vector", lambda e, nkt=nkt, iq=iq: e.reduce_max(
                        mneg.ap()[:, iq:iq + 1], sco.ap()[:, 0:512 * nkt], axis=AX.X),
                        waits=[("sa", prev.get(("exd", iq), 0))])
                    ph.op("vector", lambda e, iq=iq: e.tensor_scalar(
                        mneg.ap()[:, iq:iq + 1], mneg.ap()[:, iq:iq + 1], -1.0, None,
                        ALU.mult), inc="sv")
                    mn_done = ctx.C["sv"]
                    prev["mnd"] = mn_done
                    for kt in range(nkt):
                        wex = [("sv", mn_done), ("sp", prev.get("ptd", 0))] if kt == 0 else ()
                        ph.op("scalar", lambda e, kt=kt, iq=iq: e.activation(
                            pbuf.ap()[:, 512 * kt:512 * (kt + 1)],
                            sco.ap()[:, 512 * kt:512 * (kt + 1)], AF.Exp,
                            bias=mneg.ap()[:, iq:iq + 1],
                            accum_out=acc.ap()[:, 4 * iq + kt:4 * iq + kt + 1]),
                            waits=wex, inc="sa")
                    ex_done = ctx.C["sa"]
                    prev[("exd", iq)] = ex_done
                    ph.op("vector", lambda e, nkt=nkt, iq=iq: e.reduce_sum(
                        rcol.ap()[:, iq:iq + 1], acc.ap()[:, 4 * iq:4 * iq + nkt], axis=AX.X),
                        waits=[("sa", ex_done)])
                    ph.op("vector", lambda e, iq=iq: e.reciprocal(
                        rcol.ap()[:, iq:iq + 1], rcol.ap()[:, iq:iq + 1]), inc="sv")
                    for kc in range(4 * nkt):
                        bnk = 2 + kc % 2
                        w = [("sa", ex_done), ("sa", prev.get(("ptv", bnk), 0)),
                             ("sv", prev.get(("ptlv", bnk), 0))]
                        if kc == 0:
                            w.append(("sp", prev.get("odone", 0)))
                        ph.op("tensor", lambda e, kc=kc, bnk=bnk: e.transpose(
                            ctx.pb[bnk].ap()[:, 0:128],
                            pbuf.ap()[:, 128 * kc:128 * (kc + 1)], ctx.sb["c_ident"].ap()[:]),
                            waits=w, inc="sp")
                        td = ctx.C["sp"]
                        dst = 256 * kc + 128 * iq
                        ph.op("scalar", lambda e, bnk=bnk, dst=dst: e.copy(
                            pth.ap()[:, dst:dst + 128], ctx.pb[bnk].ap()[:, 0:128]),
                            waits=[("sp", td)], inc="sa")
                        prev[("ptv", bnk)] = ctx.C["sa"]
                        ph.op("vector", lambda e, bnk=bnk, dst=dst: e.tensor_sub(
                            ptl.ap()[:, dst:dst + 128], ctx.pb[bnk].ap()[:, 0:128],
                            pth.ap()[:, dst:dst + 128]),
                            waits=[("sa", ctx.C["sa"])], inc="sv")
                        prev[("ptlv", bnk)] = ctx.C["sv"]
                    prev["ptd"] = ctx.C["sp"]
                pt_done_a, pt_done_v = ctx.C["sa"], ctx.C["sv"]
                ph.op("tensor", lambda e: e.transpose(
                    ctx.pb[6].ap()[0:2, 0:128], rcol.ap()[:], ctx.sb["c_ident"].ap()[:]),
                    waits=[("sv", ctx.C["sv"]), ("sa", prev.get("rce", 0))], inc="sp")
                rtd = ctx.C["sp"]
                ph.op("scalar", lambda e: e.copy(rr.ap()[0:2, 0:128],
                                                 ctx.pb[6].ap()[0:2, 0:128]),
                      waits=[("sp", rtd)], inc="sa")
                prev["rce"] = ctx.C["sa"]
                b1 = ph.dma(d["rv2"].ap(), rr.ap()[0:2, 0:128],
                            waits=[("sa", ctx.C["sa"])])
                b2 = ph.dma(rr2.ap()[:],
                            d["rv2"].ap().rearrange("a b -> (a b)").unsqueeze(0),
                            waits=[b1])
                ph.op("tensor", lambda e: e.matmul(
                    ctx.pb[5].ap()[:, 0:256], ctx.sb["c_ones"].ap()[0:1, :],
                    rr2.ap()[:], start=True, stop=True),
                    waits=[b2, ("sa", prev.get("rcb", 0))], inc="sp")
                rbd = ctx.C["sp"]
                ph.op("scalar", lambda e: e.copy(rcb.ap()[:, 0:256],
                                                 ctx.pb[5].ap()[:, 0:256]),
                      waits=[("sp", rbd)], inc="sa")
                prev["rcb"] = ctx.C["sa"]
                rb_done = ctx.C["sa"]
                w = [("sa", pt_done_a), ("sv", pt_done_v),
                     ("sv", prev.get("oev", 0))]
                first = True
                for kc in range(2 * (g2 + 1)):
                    for (wa, ra) in ((vbh, pth), (vbh, ptl), (vbl, pth)):
                        ph.op("tensor", lambda e, wa=wa, ra=ra, kc=kc, first=first,
                              g2=g2: e.matmul(
                                  ctx.pb[4].ap()[:, 0:256], wa.ap()[:, 128 * kc:128 * (kc + 1)],
                                  ra.ap()[:, 256 * kc:256 * (kc + 1)], start=first,
                                  stop=(kc == 2 * g2 + 1 and ra is ptl and False) or
                                       (kc == 2 * g2 + 1 and wa is vbl)),
                              waits=w if first else ())
                        w = []
                        first = False
                ph.q["tensor"][-1] = (ph.q["tensor"][-1][0], ph.q["tensor"][-1][1], "sp", 1)
                ctx.bump("sp")
                o_done = ctx.C["sp"]
                prev["odone"] = o_done
                ph.op("vector", lambda e: e.tensor_tensor(
                    otile, ctx.pb[4].ap()[:, 0:256], rcb.ap()[:, 0:256], ALU.mult),
                    waits=[("sp", o_done), ("sa", rb_done),
                           ("sg", prev.get("olo", 0))], inc="sv")
                ov = ctx.C["sv"]
                prev["oev"] = ov
                ph.op("scalar", lambda e, h=h, g2=g2: e.copy(
                    ar_hi(ctx, h)[:, 256 * g2:256 * (g2 + 1)], otile),
                    waits=[("sv", ov)], inc="sa")
                ad = ctx.C["sa"]
                ph.op("gpsimd", lambda e, h=h, g2=g2: e.tensor_sub(
                    ar_lo(ctx, h)[:, 256 * g2:256 * (g2 + 1)], otile,
                    ar_hi(ctx, h)[:, 256 * g2:256 * (g2 + 1)]),
                    waits=[("sa", ad)], inc="sg")
                prev["olo"] = ctx.C["sg"]
        ph.op("sync", None, waits=[("sg", ctx.C["sg"])] + ctx.dma_all())
        ph.run()


def emit_l1_outproj(ctx):
    d = ctx.dram
    nc = ctx.nc
    with ExitStack() as es:
        ot = [es.enter_context(sbt(nc, "l1op_ot", [128, 512], F32)) for i in range(4)]

        def evac(ph):
            st_done = [0] * 70

            def ev(j, oc, tt, bank, mmd):
                ti = (j * 4 + tt) % 4
                w = [("sp", mmd), st_done[max(0, j * 4 + tt - 4)]]
                ph.op("scalar", lambda e, ti=ti, bank=bank: e.activation(
                    ot[ti].ap()[:], ctx.pb[bank].ap()[:, :], AF.Identity,
                    scale=1.0 / 3.0), waits=w, inc="sa")
                ad = ctx.C["sa"]
                st_done[j * 4 + tt] = ph.dma(
                    d["sub_out"].ap()[oc][:, 512 * tt:512 * (tt + 1)], ot[ti].ap()[:],
                    waits=[("sa", ad)])
                return ad, 0
            return ev

        emit_proj_stream(ctx, "l1op", list(range(16)), lambda oc: d["wo1"].ap()[oc], evac)


def emit_l1_attn(ctx):
    emit_pre_ln(ctx, split=True, eps=EPS_LN)
    emit_l1_qkv_qk(ctx)
    emit_v(ctx, "l1v", lambda vg, i: ctx.dram["wv1"].ap()[4 * vg + i], use_bias=False)
    emit_l1_flash(ctx)
    emit_l1_outproj(ctx)
    emit_post_ln(ctx, eps=EPS_LN)



def emit_pdance(ctx, ph, acc, tr, r32, sb64, sw_ap, out_row, op, guards=()):
    d = ctx.dram
    bounce = d["sub_out"].ap()[2]
    b1 = ph.dma(bounce[0:64, :], acc.ap()[64:128, :], waits=guards)
    b2 = ph.dma(sw_ap[0:64, :], bounce[0:64, :], waits=[b1])
    ph.op("vector", lambda e: e.tensor_tensor(
        acc.ap()[0:64, :], acc.ap()[0:64, :], sw_ap[0:64, :], op),
        waits=[b2] + list(guards), inc="sv")
    v1 = ctx.C["sv"]
    b3 = ph.dma(bounce[0:32, :], acc.ap()[32:64, :], waits=[("sv", v1), b2])
    b4 = ph.dma(sw_ap[0:32, :], bounce[0:32, :], waits=[b3])
    ph.op("vector", lambda e: e.tensor_tensor(
        acc.ap()[0:32, :], acc.ap()[0:32, :], sw_ap[0:32, :], op), waits=[b4])
    ph.op("vector", lambda e: e.transpose(tr.ap()[:], acc.ap()[0:32, :]))
    ph.op("vector", lambda e: e.tensor_reduce(
        r32.ap()[:], tr.ap()[:].rearrange("p (g f) -> p g f", f=32), AX.X, op),
        inc="sv")
    vd = ctx.C["sv"]
    ph.op("tensor", lambda e: e.transpose(
        ctx.pb[7].ap()[0:64, 0:32], r32.ap()[:], ctx.sb["c_ident"].ap()[0:32, 0:32]),
        waits=[("sv", vd), ("sa", getattr(ctx, "_pd_sa", 0))], inc="sp")
    td = ctx.C["sp"]
    ph.op("scalar", lambda e: e.copy(sb64.ap()[:], ctx.pb[7].ap()[0:64, 0:32]),
          waits=[("sp", td)], inc="sa")
    ctx._pd_sa = ctx.C["sa"]
    b5 = ph.dma(d["rv3"].ap(), sb64.ap()[:], waits=[("sa", ctx.C["sa"])])
    return ph.dma(out_row, d["rv3"].ap().rearrange("a b -> (a b)").unsqueeze(0),
                  waits=[b5])


def emit_bit_affine(ctx, ph, mean_row, rstd_row, mx_row, mn_row, r0, r1, r2, cb,
                    pre_waits):
    ones = ctx.sb["c_ones"]
    ph.op("vector", lambda e: e.tensor_sub(r0, mx_row, mean_row), waits=pre_waits)
    ph.op("vector", lambda e: e.tensor_sub(r1, mean_row, mn_row))
    ph.op("vector", lambda e: e.tensor_tensor(r0, r0, r1, ALU.max))
    ph.op("vector", lambda e: e.tensor_tensor(r0, r0, rstd_row, ALU.mult))
    ph.op("vector", lambda e: e.reciprocal(r1, r0))
    ph.op("vector", lambda e: e.tensor_tensor(r1, rstd_row, r1, ALU.mult))
    ph.op("vector", lambda e: e.tensor_scalar(r1, r1, 128.0, None, ALU.mult))
    ph.op("vector", lambda e: e.tensor_tensor(r2, mean_row, r1, ALU.mult))
    ph.op("vector", lambda e: e.tensor_scalar(r0, r0, 1.0 / 128.0, None,
                                              ALU.mult), inc="sv")
    rdone = ctx.C["sv"]
    bc = 0
    for k, (src, dst) in enumerate(((r1, "mb"), (r2, "rstdb"), (r0, None))):
        dtile = ctx.sb[dst] if dst else cb
        for tt in range(4):
            i = k * 4 + tt
            ph.op("tensor", lambda e, src=src, tt=tt, k=k: e.matmul(
                ctx.pb[2 * (k % 2) + tt % 2].ap()[:, :], ones.ap()[0:1, :],
                src[:, 512 * tt:512 * (tt + 1)], start=True, stop=True),
                waits=([("sv", rdone)] if i == 0 else
                       ([("sa", bc - 1)] if i >= 2 else ())), inc="sp")
            bd = ctx.C["sp"]
            ph.op("scalar", lambda e, dtile=dtile, tt=tt, k=k: e.copy(
                dtile.ap()[:, 512 * tt:512 * (tt + 1)],
                ctx.pb[2 * (k % 2) + tt % 2].ap()[:, :]),
                waits=[("sp", bd)], inc="sa")
            bc = ctx.C["sa"]
    return bc


def emit_clip_chain(ctx, ph, src_ap, hi_ap, lo_ap, ut_ap, cb, waits,
                    extra_hi=None, extra_lo=None):
    ph.op("vector", lambda e: e.tensor_tensor(ut_ap, src_ap, ctx.sb["mb"].ap()[:],
                                              ALU.mult), waits=waits)
    ph.op("vector", lambda e: e.tensor_tensor(ut_ap, ut_ap,
                                              ctx.sb["rstdb"].ap()[:], ALU.subtract))
    ph.op("vector", lambda e: e.tensor_scalar(ut_ap, ut_ap, CLIP_LO, CLIP_HI,
                                              ALU.max, ALU.min))
    ph.op("vector", lambda e: e.tensor_tensor(ut_ap, ut_ap, cb.ap()[:],
                                              ALU.mult), inc="sv")
    vd = ctx.C["sv"]
    ph.op("scalar", lambda e: e.copy(hi_ap, ut_ap),
          waits=[("sv", vd), extra_hi], inc="sa")
    ad = ctx.C["sa"]
    ph.op("gpsimd", lambda e: e.tensor_sub(lo_ap, ut_ap, hi_ap),
          waits=[("sa", ad), extra_lo], inc="sg")
    return ctx.C["sg"]


def emit_l1f_xhat(ctx):
    nc = ctx.nc
    d = ctx.dram
    with ExitStack() as es:
        sq = es.enter_context(sbt(nc, "bx_sq", [128, T], F32))
        ctx.sb["stat_sq"] = sq
        mxa = es.enter_context(sbt(nc, "bx_mx", [128, T], F32))
        mna = es.enter_context(sbt(nc, "bx_mn", [128, T], F32))
        tr = es.enter_context(sbt(nc, "bx_tr", [32, T], F32))
        r32 = es.enter_context(sbt(nc, "bx_r32", [32, 64], F32))
        sb64 = es.enter_context(sbt(nc, "bx_s64", [64, 32], F32))
        cb = es.enter_context(sbt(nc, "bx_cb", [128, T], F32))
        ph = Phase(ctx, "l1f_xhat")
        bc_done = emit_stats(ctx, ph, lambda c: (st_c(ctx, c), ()), (), EPS_BIT)
        ph.op("vector", lambda e: e.tensor_copy(mxa.ap()[:], st_c(ctx, 0)))
        ph.op("vector", lambda e: e.tensor_copy(mna.ap()[:], st_c(ctx, 0)))
        for c in range(1, KC):
            ph.op("vector", lambda e, c=c: e.tensor_tensor(
                mxa.ap()[:], mxa.ap()[:], st_c(ctx, c), ALU.max), inc="sv")
            ph.op("vector", lambda e, c=c: e.tensor_tensor(
                mna.ap()[:], mna.ap()[:], st_c(ctx, c), ALU.min), inc="sv")
        mm_done_v = ctx.C["sv"]
        mxt = emit_pdance(ctx, ph, mxa, tr, r32, sb64, cb.ap()[0:64, :],
                          mxa.ap()[0:1, :], ALU.max, guards=[("sv", mm_done_v)])
        mnt = emit_pdance(ctx, ph, mna, tr, r32, sb64, cb.ap()[0:64, :],
                          mna.ap()[0:1, :], ALU.min, guards=[mxt])
        bc2 = emit_bit_affine(ctx, ph, ctx.sb["mb"].ap()[0:1, :], rows0(ctx),
                              mxa.ap()[0:1, :], mna.ap()[0:1, :],
                              tr.ap()[0:1, 0:T], sq.ap()[0:1, 0:T], cb.ap()[0:1, 0:T],
                              cb, [mnt, ("sa", bc_done)])
        gd = 0
        for c in range(KC):
            w = [("sa", bc2)] if c == 0 else [("sg", gd)]
            gd = emit_clip_chain(ctx, ph, st_c(ctx, c), ar_hi(ctx, c), ar_lo(ctx, c),
                                 sq.ap()[:], cb, w)
        ph.op("sync", None, waits=[("sg", gd)] + ctx.dma_all())
        ph.run()
        del ctx.sb["stat_sq"]


def _emit_gamma_w(ctx, ph, wdram, n_oc, numel, wg, col, grow, toks):
    ones = ctx.sb["c_ones"]
    ph.op("vector", lambda e: e.memset(col.ap()[:, 0:1], 0.0))
    nb = len(toks)
    rtoks = [0] * nb
    for oc in range(n_oc):
        ld = ph.dma(wg[oc % nb].ap()[:], wdram(oc),
                    waits=[toks[oc % nb], ("sv", rtoks[oc % nb])])
        toks[oc % nb] = ld
        ph.op("vector", lambda e, oc=oc: e.reduce_sum(
            col.ap()[:, 1:2], wg[oc % nb].ap()[:].unsqueeze(1),
            axis=AX.X, apply_absolute_value=True), waits=[ld])
        ph.op("vector", lambda e: e.tensor_add(col.ap()[:, 0:1], col.ap()[:, 0:1],
                                               col.ap()[:, 1:2]), inc="sv")
        rtoks[oc % nb] = ctx.C["sv"]
    acc_done = ctx.C["sv"]
    ctx._gw_rtoks = rtoks
    ph.op("tensor", lambda e: e.matmul(
        ctx.pb[6].ap()[0:1, 0:1], col.ap()[:, 0:1], ones.ap()[:, 0:1],
        start=True, stop=True), waits=[("sv", acc_done)], inc="sp")
    gd1 = ctx.C["sp"]
    ph.op("vector", lambda e: e.tensor_scalar(
        grow.ap()[:, 0:1], ctx.pb[6].ap()[0:1, 0:1], 1.0 / numel, None,
        ALU.mult), waits=[("sp", gd1)])
    ph.op("vector", lambda e: e.tensor_scalar(
        grow.ap()[:, 1:2], grow.ap()[:, 0:1], EPS_BIT, 0.5, ALU.add, ALU.mult))
    ph.op("vector", lambda e: e.tensor_scalar(
        grow.ap()[:, 2:3], grow.ap()[:, 1:2], -1.0, None, ALU.mult))
    ph.op("vector", lambda e: e.memset(grow.ap()[:, 3:4], 0.0), inc="sv")
    gr_done = ctx.C["sv"]
    ph.op("tensor", lambda e: e.matmul(
        ctx.pb[6].ap()[:, 0:4], ones.ap()[0:1, :], grow.ap()[:, 0:4],
        start=True, stop=True), waits=[("sv", gr_done)], inc="sp")
    gd2 = ctx.C["sp"]
    ph.op("scalar", lambda e: e.copy(ctx.sb["gcol"].ap()[:], ctx.pb[6].ap()[:, 0:4]),
          waits=[("sp", gd2)], inc="sa")
    return ctx.C["sa"]


def emit_l1f_fc(ctx):
    nc = ctx.nc
    d = ctx.dram
    ones = ctx.sb["c_ones"]
    with ExitStack() as es:
        wg = [es.enter_context(sbt(nc, "fc_wg", [128, 2048], F32)) for i in range(2)]
        col = es.enter_context(sbt(nc, "fc_col", [128, 4], F32))
        grow = es.enter_context(sbt(nc, "fc_grow", [1, 4], F32))
        tb1 = es.enter_context(sbt(nc, "fc_tb1", [128, 2048], BF16))
        tb2 = es.enter_context(sbt(nc, "fc_tb2", [128, 2048], BF16))
        h2b = es.enter_context(sbt(nc, "fc_h2b", [128, 1024], F32))
        sq = es.enter_context(sbt(nc, "fc_sq", [128, 512], F32))
        mxa = es.enter_context(sbt(nc, "fc_mx", [128, T], F32))
        mna = es.enter_context(sbt(nc, "fc_mn", [128, T], F32))
        rowm = es.enter_context(sbt(nc, "fc_rowm", [1, T], F32))
        ph = Phase(ctx, "l1f_fc")
        toks = [None, None]
        gcol_done = _emit_gamma_w(ctx, ph, lambda oc: d["wfc1"].ap()[oc], 64,
                                  16777216.0, wg, col, grow, toks)
        prev = {}
        h2st = {}
        wrd = list(ctx._gw_rtoks)
        for half in range(2):
            for oc in range(64):
                ld = ph.dma(wg[oc % 2].ap()[:], d["wfc1"].ap()[oc],
                            waits=[toks[oc % 2], ("sv", wrd[oc % 2])])
                toks[oc % 2] = ld
                ph.op("vector", lambda e, oc=oc: e.tensor_scalar(
                    tb1.ap()[:], wg[oc % 2].ap()[:], ctx.sb["gcol"].ap()[:, 2:3], None,
                    ALU.is_lt), waits=[ld, ("sa", gcol_done),
                                       ("sp", prev.get("mmlast", 0))])
                ph.op("vector", lambda e, oc=oc: e.scalar_tensor_tensor(
                    tb2.ap()[:], wg[oc % 2].ap()[:], ctx.sb["gcol"].ap()[:, 1:2],
                    tb1.ap()[:], ALU.is_gt, ALU.subtract), inc="sv")
                tern_done = ctx.C["sv"]
                wrd[oc % 2] = tern_done
                for t2 in range(2):
                    tt = 2 * half + t2
                    bank = t2
                    w = [("sv", tern_done), ("sa", prev.get(("gel", bank), 0))]
                    first = True
                    for kc in range(KC):
                        for xa in (ar_hi, ar_lo):
                            ph.op("tensor", lambda e, kc=kc, xa=xa, tt=tt, first=first,
                                  bank=bank: e.matmul(
                                      ctx.pb[bank].ap()[:, :],
                                      tb2.ap()[:, 128 * kc:128 * (kc + 1)],
                                      xa(ctx, kc)[:, 512 * tt:512 * (tt + 1)],
                                      start=first, stop=(kc == KC - 1 and xa is ar_lo)),
                                  waits=w if first else ())
                            w = []
                            first = False
                    ph.q["tensor"][-1] = (ph.q["tensor"][-1][0], ph.q["tensor"][-1][1],
                                          "sp", 1)
                    ctx.bump("sp")
                    mmd = ctx.C["sp"]
                    prev["mmlast"] = mmd
                    ph.op("scalar", lambda e, t2=t2, bank=bank: e.activation(
                        h2b.ap()[:, 512 * t2:512 * (t2 + 1)], ctx.pb[bank].ap()[:, :],
                        AF.Gelu, scale=ctx.sb["gcol"].ap()[:, 0:1]),
                        waits=[("sp", mmd), h2st.get("last"),
                               ("sv", prev.get(("mxd", bank), 0)),
                               ("sv", prev.get(("mnd", bank), 0))], inc="sa")
                    prev[("gel", bank)] = ctx.C["sa"]
                    ph.op("scalar", lambda e, t2=t2: e.activation(
                        sq.ap()[:], h2b.ap()[:, 512 * t2:512 * (t2 + 1)], AF.Square),
                        waits=[("sp", prev.get(("sqm", t2), 0))], inc="sa")
                    sqd = ctx.C["sa"]
                    wst = [("sa", prev[("gel", bank)])]
                    if oc == 0:
                        wst.append(("sv", prev.get("rowmd", 0)))
                    ph.op("tensor", lambda e, t2=t2, oc=oc: e.matmul(
                        ctx.pb[2 + t2].ap()[0:1, :], ones.ap()[:, 0:1],
                        h2b.ap()[:, 512 * t2:512 * (t2 + 1)],
                        start=(oc == 0), stop=(oc == 63)), waits=wst)
                    ph.op("tensor", lambda e, t2=t2, oc=oc: e.matmul(
                        ctx.pb[4 + t2].ap()[0:1, :], ones.ap()[:, 0:1], sq.ap()[:],
                        start=(oc == 0), stop=(oc == 63)),
                        waits=[("sa", sqd)], inc="sp")
                    prev[("sqm", t2)] = ctx.C["sp"]
                    ts2 = slice(512 * tt, 512 * (tt + 1))
                    hslc = slice(512 * t2, 512 * (t2 + 1))
                    if oc == 0:
                        ph.op("vector", lambda e, ts2=ts2, hslc=hslc: e.tensor_copy(
                            mxa.ap()[:, ts2], h2b.ap()[:, hslc]),
                            waits=[("sa", prev[("gel", bank)])], inc="sv")
                        ph.op("vector", lambda e, ts2=ts2, hslc=hslc: e.tensor_copy(
                            mna.ap()[:, ts2], h2b.ap()[:, hslc]),
                            waits=[("sa", prev[("gel", bank)])], inc="sv")
                    else:
                        ph.op("vector", lambda e, ts2=ts2, hslc=hslc: e.tensor_tensor(
                            mxa.ap()[:, ts2], mxa.ap()[:, ts2], h2b.ap()[:, hslc],
                            ALU.max), waits=[("sa", prev[("gel", bank)])], inc="sv")
                        ph.op("vector", lambda e, ts2=ts2, hslc=hslc: e.tensor_tensor(
                            mna.ap()[:, ts2], mna.ap()[:, ts2], h2b.ap()[:, hslc],
                            ALU.min), waits=[("sa", prev[("gel", bank)])], inc="sv")
                    prev[("mxd", bank)] = ctx.C["sv"]
                    prev[("mnd", bank)] = ctx.C["sv"]
                h2st["last"] = ph.dma(
                    d["h_f32"].ap()[oc][:, 1024 * half:1024 * (half + 1)], h2b.ap()[:],
                    waits=[("sv", prev[("mxd", 1)]), ("sv", prev[("mnd", 1)])])
            for t2 in range(2):
                wrow = [("sp", prev[("sqm", t2)])]
                if t2 == 0:
                    wrow.append(prev.get("rowst"))
                ph.op("vector", lambda e, t2=t2: e.tensor_copy(
                    rowm.ap()[:, 512 * t2:512 * (t2 + 1)], ctx.pb[2 + t2].ap()[0:1, :]),
                    waits=wrow)
                ph.op("vector", lambda e, t2=t2: e.tensor_copy(
                    rowm.ap()[:, 1024 + 512 * t2:1024 + 512 * (t2 + 1)],
                    ctx.pb[4 + t2].ap()[0:1, :]), inc="sv")
            prev["rowmd"] = ctx.C["sv"]
            prev["rowst"] = ph.dma(d["rowsd"].ap()[half:half + 1, :], rowm.ap()[:],
                                   waits=[("sv", prev["rowmd"])])
        ph.dma(d["sub_out"].ap()[0], mxa.ap()[:], waits=[("sv", ctx.C["sv"])])
        ph.dma(d["sub_out"].ap()[1], mna.ap()[:], waits=[("sg", ctx.C["sg"])])
        ph.op("sync", None, waits=ctx.dma_all())
        ph.run()


def emit_l1f_rows(ctx):
    nc = ctx.nc
    d = ctx.dram
    with ExitStack() as es:
        mxa = es.enter_context(sbt(nc, "fr_mx", [128, T], F32))
        mna = es.enter_context(sbt(nc, "fr_mn", [128, T], F32))
        tr = es.enter_context(sbt(nc, "fr_tr", [32, T], F32))
        r32 = es.enter_context(sbt(nc, "fr_r32", [32, 64], F32))
        sb64 = es.enter_context(sbt(nc, "fr_s64", [64, 32], F32))
        cb = es.enter_context(sbt(nc, "fr_cb", [128, T], F32))
        rowb = es.enter_context(sbt(nc, "fr_rowb", [1, T], F32))
        rowc = es.enter_context(sbt(nc, "fr_rowc", [1, T], F32))
        rowa_ap = cb.ap()[0:1, :]
        ph = Phase(ctx, "l1f_rows")
        l1 = ph.dma(mxa.ap()[:], d["sub_out"].ap()[0])
        l2 = ph.dma(mna.ap()[:], d["sub_out"].ap()[1])
        mxt = emit_pdance(ctx, ph, mxa, tr, r32, sb64, cb.ap()[0:64, :],
                          mxa.ap()[0:1, :], ALU.max, guards=[l1])
        mnt = emit_pdance(ctx, ph, mna, tr, r32, sb64, cb.ap()[0:64, :],
                          mna.ap()[0:1, :], ALU.min, guards=[mxt, l2])
        l3 = ph.dma(rowa_ap[:, 0:1024], d["rowsd"].ap()[0:1, 0:1024], waits=[mnt])
        l4 = ph.dma(rowa_ap[:, 1024:2048], d["rowsd"].ap()[1:2, 0:1024])
        l5 = ph.dma(rowb.ap()[:, 0:1024], d["rowsd"].ap()[0:1, 1024:2048])
        l6 = ph.dma(rowb.ap()[:, 1024:2048], d["rowsd"].ap()[1:2, 1024:2048])
        ph.op("vector", lambda e: e.tensor_scalar(
            rowc.ap()[:], rowa_ap, 1.0 / 8192.0, None, ALU.mult),
            waits=[l3, l4, mnt])
        ph.op("vector", lambda e: e.tensor_tensor(
            rowa_ap, rowc.ap()[:], rowc.ap()[:], ALU.mult), waits=[l5, l6])
        ph.op("vector", lambda e: e.scalar_tensor_tensor(
            rowb.ap()[:], rowb.ap()[:], 1.0 / 8192.0, rowa_ap,
            ALU.mult, ALU.subtract), inc="sv")
        vard = ctx.C["sv"]
        ph.op("scalar", lambda e: e.activation(
            rowa_ap, rowb.ap()[:], AF.Sqrt,
            bias=ctx.sb["c_cvals"].ap()[0:1, 2:3]), waits=[("sv", vard)], inc="sa")
        sqd = ctx.C["sa"]
        ph.op("vector", lambda e: e.reciprocal(rowb.ap()[:], rowa_ap),
              waits=[("sa", sqd)], inc="sv")
        bc2 = emit_bit_affine(ctx, ph, rowc.ap()[:], rowb.ap()[:],
                              mxa.ap()[0:1, :], mna.ap()[0:1, :],
                              tr.ap()[0:1, 0:T], mna.ap()[0:1, :], rowa_ap,
                              cb, [("sv", ctx.C["sv"])])
        ph.dma(d["rvec"].ap().rearrange("a b -> (a b)").unsqueeze(0)[:, 0:T],
               tr.ap()[0:1, 0:T], waits=[("sa", bc2)])
        ph.op("sync", None, waits=ctx.dma_all() + [("sa", ctx.C["sa"])])
        ph.run()


def emit_l1f_xhat2(ctx):
    nc = ctx.nc
    d = ctx.dram
    ones = ctx.sb["c_ones"]
    with ExitStack() as es:
        hb = [es.enter_context(sbt(nc, "x2_hb", [128, T], F32)) for i in range(2)]
        ut = es.enter_context(sbt(nc, "x2_ut", [128, T], F32))
        hh = [es.enter_context(sbt(nc, "x2_hh", [128, T], BF16)) for i in range(2)]
        hl = [es.enter_context(sbt(nc, "x2_hl", [128, T], BF16)) for i in range(2)]
        cb = es.enter_context(sbt(nc, "x2_cb", [128, T], F32))
        crow = cb.ap()[0:1, :]
        ph = Phase(ctx, "l1f_xhat2")
        lc = ph.dma(crow,
                    d["rvec"].ap().rearrange("a b -> (a b)").unsqueeze(0)[:, 0:T])
        bcc = 0
        for tt in range(4):
            ph.op("tensor", lambda e, tt=tt: e.matmul(
                ctx.pb[tt % 2].ap()[:, :], ones.ap()[0:1, :],
                crow[:, 512 * tt:512 * (tt + 1)], start=True, stop=True),
                waits=([lc] if tt == 0 else
                       ([("sa", bcc - 1)] if tt >= 2 else ())), inc="sp")
            bd = ctx.C["sp"]
            ph.op("scalar", lambda e, tt=tt: e.copy(
                cb.ap()[:, 512 * tt:512 * (tt + 1)], ctx.pb[tt % 2].ap()[:, :]),
                waits=[("sp", bd)], inc="sa")
            bcc = ctx.C["sa"]
        prev = {}
        for ch in range(64):
            ld = ph.dma(hb[ch % 2].ap()[:], d["h_f32"].ap()[ch],
                        waits=[prev.get(("ld", ch - 2))])
            w = [ld, ("sa", bcc)] if ch == 0 else [ld, ("sg", prev.get("gd", 0))]
            gd = emit_clip_chain(ctx, ph, hb[ch % 2].ap()[:], hh[ch % 2].ap()[:],
                                 hl[ch % 2].ap()[:], ut.ap()[:], cb, w,
                                 extra_hi=prev.get(("sth", ch - 2)),
                                 extra_lo=prev.get(("stl", ch - 2)))
            prev["gd"] = gd
            prev[("sth", ch)] = ph.dma(d["hn_hi"].ap()[ch], hh[ch % 2].ap()[:],
                                       waits=[("sg", gd)])
            prev[("stl", ch)] = ph.dma(d["hn_lo"].ap()[ch], hl[ch % 2].ap()[:])
            prev[("ld", ch)] = prev[("stl", ch)]
        ph.op("sync", None, waits=ctx.dma_all())
        ph.run()


def emit_l1f_wpj_tern(ctx):
    nc = ctx.nc
    d = ctx.dram
    with ExitStack() as es:
        wg = [es.enter_context(sbt(nc, "pj_wg", [128, 4096], F32)) for i in range(2)]
        col = es.enter_context(sbt(nc, "pj_col", [128, 4], F32))
        grow = es.enter_context(sbt(nc, "pj_grow", [1, 4], F32))
        tb1 = es.enter_context(sbt(nc, "pj_tb1", [128, 4096], BF16))
        tb2 = es.enter_context(sbt(nc, "pj_tb2", [128, 4096], BF16))
        ph = Phase(ctx, "l1f_pjt")
        toks = [None, None]
        gcol_done = _emit_gamma_w(
            ctx, ph,
            lambda j: d["wpj1"].ap()[j // 2][:, 4096 * (j % 2):4096 * (j % 2 + 1)],
            32, 16777216.0, wg, col, grow, toks)
        prev = {}
        wrd = list(ctx._gw_rtoks)
        for j in range(32):
            ld = ph.dma(wg[j % 2].ap()[:],
                        d["wpj1"].ap()[j // 2][:, 4096 * (j % 2):4096 * (j % 2 + 1)],
                        waits=[toks[j % 2], ("sv", wrd[j % 2])])
            toks[j % 2] = ld
            ph.op("vector", lambda e, j=j: e.tensor_scalar(
                tb1.ap()[:], wg[j % 2].ap()[:], ctx.sb["gcol"].ap()[:, 2:3], None,
                ALU.is_lt), waits=[ld, ("sa", gcol_done), prev.get(("st", j - 1))])
            ph.op("vector", lambda e, j=j: e.scalar_tensor_tensor(
                tb2.ap()[:], wg[j % 2].ap()[:], ctx.sb["gcol"].ap()[:, 1:2],
                tb1.ap()[:], ALU.is_gt, ALU.subtract), inc="sv")
            td = ctx.C["sv"]
            wrd[j % 2] = td
            prev[("st", j)] = ph.dma(
                d["wpj1_t"].ap()[j // 2][:, 4096 * (j % 2):4096 * (j % 2 + 1)],
                tb2.ap()[:], waits=[("sv", td)])
        ph.op("sync", None, waits=ctx.dma_all())
        ph.run()


def emit_l1_ffn(ctx):
    emit_pre_ln(ctx, split=False, eps=EPS_LN)
    emit_l1f_xhat(ctx)
    emit_l1f_fc(ctx)
    emit_l1f_rows(ctx)
    emit_l1f_xhat2(ctx)
    emit_l1f_wpj_tern(ctx)
    emit_ffn_proj(ctx, "l1f_pj", ctx.dram["wpj1_t"], 16,
                  sub_scale=lambda: ctx.sb["gcol"].ap()[:, 0:1])
    emit_post_ln(ctx, eps=EPS_LN)


SUB_BLOCKS = {}


def _register():
    SUB_BLOCKS["l0_attn"] = emit_l0_attn
    SUB_BLOCKS["l0_ffn"] = emit_l0_ffn
    SUB_BLOCKS["l1_attn"] = emit_l1_attn
    SUB_BLOCKS["l1_ffn"] = emit_l1_ffn


_register()


def build_program(parts, debug=False):
    nc = bass.Bass("TRN2", target_bir_lowering=False)
    ctx = Ctx(nc)
    declare_io(ctx, debug=debug)
    alloc_shared(ctx)
    emit_setup(ctx, load_stream=True)
    for p in parts:
        SUB_BLOCKS[p](ctx)
    emit_output(ctx)
    ctx.es.close()
    return nc


ALL_PARTS = ["l0_attn", "l0_ffn", "l1_attn", "l1_ffn"]


def kernel(**inputs):
    per_core = [host_inputs(inputs, c) for c in range(8)]
    nc = build_program(ALL_PARTS)
    res = run_bass_kernel_spmd(nc, per_core, core_ids=list(range(8)))
    return np.ascontiguousarray(
        np.stack([np.asarray(r["yT"]).T for r in res.results]).astype(np.float32))


# version 02
# speedup vs baseline: 1.0146x; 1.0146x over previous
import numpy as np
import ml_dtypes
from contextlib import ExitStack

import concourse.bass as bass
import concourse.mybir as mybir
from concourse.bass_utils import run_bass_kernel_spmd

F32 = mybir.dt.float32
BF16 = mybir.dt.bfloat16
AF = mybir.ActivationFunctionType
ALU = mybir.AluOpType
AX = mybir.AxisListType

D = 2048
T = 2048
KC = D // 128
NH = 16
HD = 128
SCL_L1 = float(128.0 ** 0.5 / 128.0)
EPS_LN = 1e-5
EPS_RMS = float(np.finfo(np.float32).eps)
EPS_BIT = 1e-6
CLIP_HI = float(np.float32(np.float64(128.0) - 1e-6))
CLIP_LO = float(np.float32(np.float64(-128.0) + 1e-6))

ENGINES = ("sync", "tensor", "vector", "scalar", "gpsimd")

_UID = [0]


def sbt(nc, name, shape, dtype):
    _UID[0] += 1
    return nc.sbuf_tensor(f"{name}_{_UID[0]}", shape, dtype)




def chunk_w(w):
    K, M = w.shape
    nk, nm = K // 128, M // 128
    return np.ascontiguousarray(
        w.reshape(nk, 128, nm, 128).transpose(2, 1, 0, 3).reshape(nm, 128, nk * 128))


def nat_chunks(w):
    R, C = w.shape
    return np.ascontiguousarray(w.reshape(R // 128, 128, C))


def make_consts():
    p = np.arange(128, dtype=np.float32)[:, None]
    j = np.arange(128, dtype=np.float32)[None, :]
    c = {}
    c["ident"] = np.eye(128, dtype=np.float32)
    c["ones"] = np.ones((128, 128), dtype=np.float32)
    c["diffD"] = np.ascontiguousarray(np.maximum(j - p, 0.0) + np.zeros((128, 128), np.float32))
    c["tri01"] = (j >= p).astype(np.float32)
    c["iotaF"] = np.ascontiguousarray(j + np.zeros((128, 128), np.float32))
    c["dcol"] = np.concatenate([(128.0 - p), np.full((128, 1), 128.0, np.float32)],
                               axis=1).astype(np.float32)
    jj = np.arange(512, dtype=np.float32)[None, :]
    m = [np.where(jj <= 128.0 * r + p, 0.0, -1e30).astype(np.float32) for r in range(4)]
    c["l1mask"] = np.ascontiguousarray(
        np.concatenate(m, axis=1).astype(ml_dtypes.bfloat16))
    af = ((1.0 / 1024.0) ** np.linspace(0.0, 1.0, 32)).astype(np.float32)
    af64 = np.concatenate([af.astype(np.float64), np.zeros(32)])
    theta = np.arange(T, dtype=np.float64)[:, None] * af64[None, :]
    cosT = np.cos(theta).T
    sinT = np.sin(theta).T
    Call = np.concatenate([cosT, cosT], axis=0)
    Sall = np.concatenate([sinT, -sinT], axis=0)
    c["ropeKC"] = np.ascontiguousarray(Call.astype(np.float32))
    c["ropeKS"] = np.ascontiguousarray(Sall.astype(np.float32))
    c["ropeQC"] = np.ascontiguousarray((Call * SCL_L1).astype(np.float32))
    c["ropeQS"] = np.ascontiguousarray((Sall * SCL_L1).astype(np.float32))
    cv = np.zeros((128, 4), np.float32)
    cv[:, 0] = EPS_LN
    cv[:, 1] = EPS_RMS
    cv[:, 2] = EPS_BIT
    c["cvals"] = cv
    return c


CONST_SHAPES = {
    "ident": [128, 128], "ones": [128, 128], "diffD": [128, 128], "tri01": [128, 128],
    "iotaF": [128, 128], "dcol": [128, 2], "l1mask": [128, 2048],
    "ropeKC": [128, T], "ropeKS": [128, T], "ropeQC": [128, T], "ropeQS": [128, T],
    "cvals": [128, 4],
}


def host_inputs(inputs, core):
    m = {}
    x = np.asarray(inputs["L0_attn_x"][core], dtype=np.float32)
    m["xT"] = np.ascontiguousarray(x.T)
    m["wqkv"] = chunk_w(np.asarray(inputs["L0_attn_Wqkv"], np.float32))
    bqkv = np.asarray(inputs["L0_attn_bqkv"], np.float32)
    m["bqkv_l"] = np.ascontiguousarray(bqkv.reshape(48, 128).T)
    m["vbias_r"] = np.ascontiguousarray(bqkv[4096:6144].reshape(1, 2048))
    m["wout"] = chunk_w(np.asarray(inputs["L0_attn_Wout"], np.float32))
    m["bout_l"] = np.ascontiguousarray(
        np.asarray(inputs["L0_attn_bout"], np.float32).reshape(16, 128).T)
    m["slopes_r"] = np.ascontiguousarray(
        np.asarray(inputs["L0_attn_slopes"], np.float32).reshape(1, 16))
    m["gup_nat"] = nat_chunks(np.asarray(inputs["L0_ffn_w_gate_up"], np.float32))
    m["wpj0_nat"] = nat_chunks(np.asarray(inputs["L0_ffn_w_proj"], np.float32))
    m["wq1"] = chunk_w(np.ascontiguousarray(np.asarray(inputs["L1_attn_Wq"], np.float32).T))
    m["wk1"] = chunk_w(np.ascontiguousarray(np.asarray(inputs["L1_attn_Wk"], np.float32).T))
    m["wv1"] = chunk_w(np.ascontiguousarray(np.asarray(inputs["L1_attn_Wv"], np.float32).T))
    m["wo1"] = chunk_w(np.ascontiguousarray(np.asarray(inputs["L1_attn_Wo"], np.float32).T))
    m["wfc1"] = chunk_w(np.ascontiguousarray(np.asarray(inputs["L1_ffn_w_fc"], np.float32).T))
    m["wpj1"] = chunk_w(np.ascontiguousarray(np.asarray(inputs["L1_ffn_w_proj"], np.float32).T))
    m.update(make_consts())
    return m



class Phase:
    def __init__(self, ctx, name=""):
        self.ctx = ctx
        self.name = name
        self.q = {e: [] for e in ENGINES}

    SELF_SEM = {"vector": "sv", "scalar": "sa", "gpsimd": "sg"}

    def op(self, eng, fn=None, waits=(), inc=None, inc_n=1):
        waits = list(waits)
        ss = self.SELF_SEM.get(eng)
        if ss is not None and fn is not None:
            waits.append((ss, self.ctx.C[ss]))
            if inc is None:
                inc = ss
                inc_n = 1
            else:
                assert inc == ss and inc_n == 1, (eng, inc)
        c = self.ctx.bump(inc, inc_n) if (inc and fn is not None) else None
        self.q[eng].append((fn, tuple(waits), inc, inc_n))
        return c

    def dma(self, out_ap, in_ap, waits=(), sem=None, q="sync"):
        ctx = self.ctx
        nm = ctx.dma_pool[ctx.dma_rr % len(ctx.dma_pool)]
        ctx.dma_rr += 1
        w = list(waits)
        if ctx.C[nm] > 0:
            w.append((nm, ctx.C[nm]))
        self.q[q].append((lambda e, o=out_ap, i=in_ap: e.dma_start(out=o, in_=i),
                          tuple(w), nm, 16))
        return (nm, ctx.bump(nm, 16))

    def run(self):
        ctx = self.ctx
        with ctx.nc.Block() as b:
            for eng in ENGINES:
                ops = self.q[eng]
                if not ops:
                    continue

                def body(e, ops=ops):
                    for fn, waits, inc, inc_n in ops:
                        for wv in waits:
                            if not wv:
                                continue
                            (s, v) = wv
                            if v > 0:
                                e.wait_ge(ctx.sems[s], v)
                        ins = fn(e) if fn is not None else None
                        if inc is not None and ins is not None:
                            ins.then_inc(ctx.sems[inc], inc_n)
                getattr(b, eng)(body)


class Ctx:
    def __init__(self, nc):
        self.nc = nc
        self.es = ExitStack()
        self.sems = {}
        self.C = {}
        for s in ("ld", "st", "sp", "sv", "sa", "sg", "lg"):
            self.sems[s] = self.es.enter_context(nc.semaphore(s))
            self.C[s] = 0
        self.dma_pool = []
        for i in range(12):
            nm = f"dm{i}"
            self.sems[nm] = self.es.enter_context(nc.semaphore(nm))
            self.C[nm] = 0
            self.dma_pool.append(nm)
        self.dma_rr = 0
        self.sb = {}
        self.dram = {}

    def bump(self, s, n=1):
        self.C[s] += n
        return self.C[s]

    def dma_all(self):
        return [(nm, self.C[nm]) for nm in self.dma_pool if self.C[nm] > 0]

    def alloc(self, name, shape, dtype, **kw):
        h = self.es.enter_context(self.nc.sbuf_tensor(name, shape, dtype, **kw))
        self.sb[name] = h
        return h


def declare_io(ctx, debug=False):
    nc = ctx.nc
    d = ctx.dram
    scratch_kind = "ExternalOutput" if debug else "Internal"
    d["xT"] = nc.dram_tensor("xT", [D, T], F32, kind="ExternalInput")
    d["yT"] = nc.dram_tensor("yT", [D, T], F32, kind="ExternalOutput")
    d["wqkv"] = nc.dram_tensor("wqkv", [48, 128, 2048], F32, kind="ExternalInput")
    d["bqkv_l"] = nc.dram_tensor("bqkv_l", [128, 48], F32, kind="ExternalInput")
    d["vbias_r"] = nc.dram_tensor("vbias_r", [1, 2048], F32, kind="ExternalInput")
    d["wout"] = nc.dram_tensor("wout", [16, 128, 2048], F32, kind="ExternalInput")
    d["bout_l"] = nc.dram_tensor("bout_l", [128, 16], F32, kind="ExternalInput")
    d["slopes_r"] = nc.dram_tensor("slopes_r", [1, 16], F32, kind="ExternalInput")
    d["gup_nat"] = nc.dram_tensor("gup_nat", [128, 128, 2048], F32, kind="ExternalInput")
    d["wpj0_nat"] = nc.dram_tensor("wpj0_nat", [16, 128, 8192], F32, kind="ExternalInput")
    for nm in ("wq1", "wk1", "wv1", "wo1"):
        d[nm] = nc.dram_tensor(nm, [16, 128, 2048], F32, kind="ExternalInput")
    d["wfc1"] = nc.dram_tensor("wfc1", [64, 128, 2048], F32, kind="ExternalInput")
    d["wpj1"] = nc.dram_tensor("wpj1", [16, 128, 8192], F32, kind="ExternalInput")
    for nm, shp in CONST_SHAPES.items():
        dt_ = BF16 if nm == "l1mask" else F32
        d[nm] = nc.dram_tensor(nm, shp, dt_, kind="ExternalInput")
    d["xsp"] = nc.dram_tensor("xsp", [KC, 128, T], F32, kind=scratch_kind)
    d["qk_hi"] = nc.dram_tensor("qk_hi", [32, 128, T], BF16, kind=scratch_kind)
    d["qk_lo"] = nc.dram_tensor("qk_lo", [32, 128, T], BF16, kind=scratch_kind)
    d["v_hi"] = nc.dram_tensor("v_hi", [NH, T, HD], BF16, kind=scratch_kind)
    d["v_lo"] = nc.dram_tensor("v_lo", [NH, T, HD], BF16, kind=scratch_kind)
    d["sub_out"] = nc.dram_tensor("sub_out", [KC, 128, T], F32, kind=scratch_kind)
    d["h_f32"] = nc.dram_tensor("h_f32", [64, 128, T], F32, kind=scratch_kind)
    d["hn_hi"] = nc.dram_tensor("hn_hi", [64, 128, T], BF16, kind=scratch_kind)
    d["hn_lo"] = nc.dram_tensor("hn_lo", [64, 128, T], BF16, kind=scratch_kind)
    d["wpj0_t"] = nc.dram_tensor("wpj0_t", [16, 128, 8192], BF16, kind=scratch_kind)
    d["wpj1_t"] = nc.dram_tensor("wpj1_t", [16, 128, 8192], BF16, kind=scratch_kind)
    d["rvec"] = nc.dram_tensor("rvec", [64, 64], F32, kind=scratch_kind)
    d["rv2"] = nc.dram_tensor("rv2", [2, 128], F32, kind=scratch_kind)
    d["rv3"] = nc.dram_tensor("rv3", [64, 32], F32, kind=scratch_kind)
    d["rowsd"] = nc.dram_tensor("rowsd", [4, 2048], F32, kind=scratch_kind)
    d["dbg"] = nc.dram_tensor("dbg", [128, 8192], BF16, kind=scratch_kind)


def alloc_shared(ctx):
    nc = ctx.nc
    ctx.arena = ctx.alloc("arena", [128, KC * T], F32)
    ctx.arena_b = ctx.arena.bitcast(BF16)
    ctx.alloc("c_ident", [128, 128], F32)
    ctx.alloc("c_identb", [128, 128], BF16)
    ctx.alloc("c_ones", [128, 128], F32)
    ctx.alloc("c_cvals", [128, 4], F32)
    ctx.alloc("gcol", [128, 4], F32)
    ctx.alloc("rows", [1, T], F32)
    ctx.alloc("mb", [128, T], F32)
    ctx.alloc("rstdb", [128, T], F32)
    ctx.pb = [ctx.es.enter_context(nc.psum_tensor(f"pb{i}", [128, 512], F32))
              for i in range(8)]
    ctx.pbb = [p.bitcast(BF16) for p in ctx.pb]


def st_c(ctx, c):
    return ctx.arena.ap()[:, c * T:(c + 1) * T]


def ar_hi(ctx, c):
    return ctx.arena_b.ap()[:, c * 2 * T: c * 2 * T + T]


def ar_lo(ctx, c):
    return ctx.arena_b.ap()[:, c * 2 * T + T: (c + 1) * 2 * T]


def rows0(ctx):
    return ctx.sb["rows"].ap()[:, 0:T]


def rsc(ctx):
    return ctx.sb["rstdb"].ap()[0:1, :]



def emit_setup(ctx, load_stream):
    ph = Phase(ctx, "setup")
    d = ctx.dram
    ph.dma(ctx.sb["c_ident"].ap()[:], d["ident"].ap())
    ph.dma(ctx.sb["c_cvals"].ap()[:], d["cvals"].ap())
    ld = ph.dma(ctx.sb["c_ones"].ap()[:], d["ones"].ap())
    if load_stream:
        for c in range(KC):
            ld = ph.dma(st_c(ctx, c), d["xT"].ap()[128 * c:128 * (c + 1), :])
    ph.op("scalar", lambda e: e.copy(ctx.sb["c_identb"].ap()[:], ctx.sb["c_ident"].ap()[:]),
          waits=[ld], inc="sa")
    ph.op("sync", None, waits=ctx.dma_all())
    ph.run()



def emit_stats(ctx, ph, get_chunk, pre_waits, eps, n=D):
    nc = ctx.nc
    ones = ctx.sb["c_ones"]
    sq = ctx.sb["stat_sq"]
    mm_done = [0] * KC
    for c in range(KC):
        ap, w = get_chunk(c)
        w = list(w) + (list(pre_waits) if c == 0 else [])
        if c >= 1:
            w.append(("sp", mm_done[c - 1]))
        ph.op("scalar", lambda e, ap=ap, c=c: e.activation(
            sq.ap()[:], ap, AF.Square), waits=w, inc="sa")
        sq_done = ctx.C["sa"]
        for tt in range(4):
            ph.op("tensor", lambda e, ap=ap, c=c, tt=tt: e.matmul(
                ctx.pb[tt].ap()[0:1, :], ones.ap()[:, 0:1],
                ap[:, 512 * tt:512 * (tt + 1)],
                start=(c == 0), stop=(c == KC - 1)),
                waits=[("sa", sq_done)] if tt == 0 else ())
        for tt in range(4):
            ph.op("tensor", lambda e, c=c, tt=tt: e.matmul(
                ctx.pb[4 + tt].ap()[0:1, :], ones.ap()[:, 0:1],
                sq.ap()[:, 512 * tt:512 * (tt + 1)],
                start=(c == 0), stop=(c == KC - 1)),
                inc="sp" if tt == 3 else None)
        mm_done[c] = ctx.C["sp"]
    allmm = mm_done[KC - 1]
    for tt in range(4):
        ph.op("vector", lambda e, tt=tt: e.tensor_scalar(
            rows0(ctx)[:, 512 * tt:512 * (tt + 1)], ctx.pb[tt].ap()[0:1, :],
            1.0 / n, None, ALU.mult),
            waits=[("sp", allmm)] if tt == 0 else (), inc="sv" if tt == 3 else None)
    mean_done = ctx.C["sv"]
    mb_done = 0
    for tt in range(4):
        ph.op("tensor", lambda e, tt=tt: e.matmul(
            ctx.pb[tt % 2].ap()[:, :], ones.ap()[0:1, :],
            rows0(ctx)[:, 512 * tt:512 * (tt + 1)], start=True, stop=True),
            waits=([("sv", mean_done)] if tt == 0 else
                   ([("sa", mb_done - 1)] if tt >= 2 else ())), inc="sp")
        bd = ctx.C["sp"]
        ph.op("scalar", lambda e, tt=tt: e.copy(
            ctx.sb["mb"].ap()[:, 512 * tt:512 * (tt + 1)], ctx.pb[tt % 2].ap()[:, :]),
            waits=[("sp", bd)], inc="sa")
        mb_done = ctx.C["sa"]
    ph.op("vector", lambda e: e.tensor_mul(rsc(ctx), rows0(ctx), rows0(ctx)),
          waits=[("sa", mb_done)], inc="sv")
    for tt in range(4):
        ph.op("vector", lambda e, tt=tt: e.scalar_tensor_tensor(
            rows0(ctx)[:, 512 * tt:512 * (tt + 1)], ctx.pb[4 + tt].ap()[0:1, :],
            1.0 / n, rsc(ctx)[:, 512 * tt:512 * (tt + 1)], ALU.mult, ALU.subtract),
            inc="sv" if tt == 3 else None)
    var_done = ctx.C["sv"]
    eps_col = {EPS_LN: 0, EPS_RMS: 1, EPS_BIT: 2}[eps]
    ph.op("scalar", lambda e: e.activation(
        rsc(ctx), rows0(ctx), AF.Sqrt,
        bias=ctx.sb["c_cvals"].ap()[0:1, eps_col:eps_col + 1]),
        waits=[("sv", var_done)], inc="sa")
    sdone = ctx.C["sa"]
    ph.op("vector", lambda e: e.reciprocal(rows0(ctx), rsc(ctx)),
          waits=[("sa", sdone)], inc="sv")
    rstd_done = ctx.C["sv"]
    rb_done = 0
    for tt in range(4):
        ph.op("tensor", lambda e, tt=tt: e.matmul(
            ctx.pb[2 + tt % 2].ap()[:, :], ones.ap()[0:1, :],
            rows0(ctx)[:, 512 * tt:512 * (tt + 1)], start=True, stop=True),
            waits=([("sv", rstd_done)] if tt == 0 else
                   ([("sa", rb_done - 1)] if tt >= 2 else ())), inc="sp")
        bd = ctx.C["sp"]
        ph.op("scalar", lambda e, tt=tt: e.copy(
            ctx.sb["rstdb"].ap()[:, 512 * tt:512 * (tt + 1)], ctx.pb[2 + tt % 2].ap()[:, :]),
            waits=[("sp", bd)], inc="sa")
        rb_done = ctx.C["sa"]
    return rb_done



def emit_pre_ln(ctx, split=True, eps=EPS_LN):
    nc = ctx.nc
    d = ctx.dram
    with ExitStack() as es:
        ctx.sb["stat_sq"] = es.enter_context(sbt(nc, "pl_sq", [128, T], F32))
        ut = es.enter_context(sbt(nc, "pl_u", [128, T], F32))
        xnf = es.enter_context(sbt(nc, "pl_xnf", [128, T], F32))

        ph = Phase(ctx, "pre_ln")
        spill_toks = []
        for c in range(KC):
            spill_toks.append(ph.dma(d["xsp"].ap()[c], st_c(ctx, c)))
        bc_done = emit_stats(ctx, ph, lambda c: (st_c(ctx, c), ()), (), eps)
        gd = 0
        for c in range(KC):
            w = [("sa", bc_done)] if c == 0 else [("sg" if split else "sv", gd)]
            if c == 0:
                w += spill_toks
            ph.op("vector", lambda e, c=c: e.tensor_sub(
                ut.ap()[:], st_c(ctx, c), ctx.sb["mb"].ap()[:]), waits=w)
            if split:
                ph.op("vector", lambda e: e.tensor_mul(
                    xnf.ap()[:], ut.ap()[:], ctx.sb["rstdb"].ap()[:]), inc="sv")
                vd = ctx.C["sv"]
                ph.op("scalar", lambda e, c=c: e.copy(ar_hi(ctx, c), xnf.ap()[:]),
                      waits=[("sv", vd)], inc="sa")
                ad = ctx.C["sa"]
                ph.op("gpsimd", lambda e, c=c: e.tensor_sub(
                    ar_lo(ctx, c), xnf.ap()[:], ar_hi(ctx, c)),
                    waits=[("sa", ad)], inc="sg")
                gd = ctx.C["sg"]
            else:
                ph.op("vector", lambda e, c=c: e.tensor_mul(
                    st_c(ctx, c), ut.ap()[:], ctx.sb["rstdb"].ap()[:]), inc="sv")
                gd = ctx.C["sv"]
        ph.op("sync", None, waits=[("sg" if split else "sv", gd)])
        ph.run()
        del ctx.sb["stat_sq"]



def emit_post_ln(ctx, eps=EPS_LN):
    nc = ctx.nc
    d = ctx.dram
    with ExitStack() as es:
        ob = [es.enter_context(sbt(nc, f"po_ob{i}", [128, T], F32)) for i in range(2)]
        ctx.sb["stat_sq"] = es.enter_context(sbt(nc, "po_sq", [128, T], F32))
        xb = [es.enter_context(sbt(nc, f"po_xb{i}", [128, T], F32)) for i in range(2)]
        ut = es.enter_context(sbt(nc, "po_u", [128, T], F32))

        ph = Phase(ctx, "post_ln")
        lds = [0] * KC
        sp0 = ctx.C["sp"]
        for c in range(KC):
            w = [("sp", sp0 + (c - 2) + 1)] if c >= 2 else []
            lds[c] = ph.dma(ob[c % 2].ap()[:], d["sub_out"].ap()[c], waits=w)

        def get_chunk(c):
            return ob[c % 2].ap()[:], [lds[c]]
        bc_done = emit_stats(ctx, ph, get_chunk, (), eps)
        assert ctx.C["sp"] >= sp0 + KC
        prev = 0
        for c in range(KC):
            l1 = ph.dma(ob[c % 2].ap()[:], d["sub_out"].ap()[c],
                        waits=[("sv", prev)] if c >= 2 else [("sa", bc_done)])
            l2 = ph.dma(xb[c % 2].ap()[:], d["xsp"].ap()[c])
            w = [l1, l2, ("sa", bc_done)]
            ph.op("vector", lambda e, c=c: e.tensor_sub(
                ut.ap()[:], ob[c % 2].ap()[:], ctx.sb["mb"].ap()[:]), waits=w)
            ph.op("vector", lambda e: e.tensor_mul(ut.ap()[:], ut.ap()[:],
                                                   ctx.sb["rstdb"].ap()[:]))
            ph.op("vector", lambda e, c=c: e.tensor_add(
                st_c(ctx, c), ut.ap()[:], xb[c % 2].ap()[:]), inc="sv")
            prev = ctx.C["sv"]
        ph.op("sync", None, waits=[("sv", prev)] + ctx.dma_all())
        ph.run()
        del ctx.sb["stat_sq"]


def emit_output(ctx):
    ph = Phase(ctx, "output")
    d = ctx.dram
    last = 0
    for c in range(KC):
        last = ph.dma(d["yT"].ap()[128 * c:128 * (c + 1), :], st_c(ctx, c))
    ph.op("sync", None, waits=ctx.dma_all())
    ph.run()



def emit_proj_stream(ctx, name, ocs, wdram_of, evac, nkc=KC, rhs_hi=None, rhs_lo=None,
                     nwf=2, nws=2):
    nc = ctx.nc
    with ExitStack() as es:
        wf = [es.enter_context(sbt(nc, f"{name}_wf{i}", [128, nkc * 128], F32))
              for i in range(nwf)]
        whi = [es.enter_context(sbt(nc, f"{name}_wh{i}", [128, nkc * 128], BF16))
               for i in range(nws)]
        wlo = [es.enter_context(sbt(nc, f"{name}_wl{i}", [128, nkc * 128], BF16))
               for i in range(nws)]
        ph = Phase(ctx, name)
        if rhs_hi is None:
            rhs_hi = lambda kc, tt: ar_hi(ctx, kc)[:, 512 * tt:512 * (tt + 1)]
            rhs_lo = lambda kc, tt: ar_lo(ctx, kc)[:, 512 * tt:512 * (tt + 1)]
        free = [(0, 0)] * (len(ocs) + 2)
        split_done = [0] * (len(ocs) + 1)
        ev = evac(ph)
        for j, oc in enumerate(ocs):
            wda = wdram_of(oc)
            ld = ph.dma(wf[j % nwf].ap()[:], wda,
                        waits=[("sg", split_done[j - nwf + 1])] if j >= nwf else ())
            wsp = [ld]
            if j >= nws:
                wsp += [("sa", free[j - nws + 1][0]), ("sv", free[j - nws + 1][1])]
            ph.op("scalar", lambda e, j=j: e.copy(whi[j % nws].ap()[:], wf[j % nwf].ap()[:]),
                  waits=wsp, inc="sa")
            ad = ctx.C["sa"]
            ph.op("gpsimd", lambda e, j=j: e.tensor_sub(
                wlo[j % nws].ap()[:], wf[j % nwf].ap()[:], whi[j % nws].ap()[:]),
                waits=[("sa", ad)], inc="sg")
            split_done[j + 1] = ctx.C["sg"]
            sa_f = sv_f = 0
            for tt in range(4):
                bank = (j % 2) * 4 + tt
                w = [("sg", split_done[j + 1])] if tt == 0 else []
                if j >= 2:
                    w += [("sa", free[j - 1][0]), ("sv", free[j - 1][1])]
                first = True
                for kc in range(nkc):
                    for (wa, ra) in ((whi, rhs_hi), (whi, rhs_lo), (wlo, rhs_hi)):
                        ph.op("tensor", lambda e, j=j, tt=tt, kc=kc, wa=wa, ra=ra,
                              first=first, bank=bank: e.matmul(
                                  ctx.pb[bank].ap()[:, :],
                                  wa[j % nws].ap()[:, 128 * kc:128 * (kc + 1)],
                                  ra(kc, tt), start=first,
                                  stop=(kc == nkc - 1 and wa is wlo)),
                              waits=w if first else ())
                        w = []
                        first = False
                ph.q["tensor"][-1] = (ph.q["tensor"][-1][0], ph.q["tensor"][-1][1], "sp", 1)
                ctx.bump("sp")
                mmd = ctx.C["sp"]
                sa_f, sv_f = ev(j, oc, tt, bank, mmd)
            free[j + 1] = (sa_f, sv_f)
        ph.op("sync", None, waits=ctx.dma_all() + [("sa", ctx.C["sa"]),
                                                   ("sv", ctx.C["sv"])])
        ph.run()



def emit_l0_qkv_qk(ctx):
    d = ctx.dram
    nc = ctx.nc
    with ExitStack() as es:
        ohi = [es.enter_context(sbt(nc, f"qoh{i}", [128, T], BF16)) for i in range(2)]
        olo = [es.enter_context(sbt(nc, f"qol{i}", [128, T], BF16)) for i in range(2)]
        bql = es.enter_context(sbt(nc, "bql", [128, 48], F32))
        bq_done = [0]

        def evac(ph):
            bq_done[0] = ph.dma(bql.ap()[:], d["bqkv_l"].ap())
            st_done = [None] * 40

            def ev(j, oc, tt, bank, mmd):
                w = [("sp", mmd)]
                if tt == 0:
                    w.append(bq_done[0])
                    if j >= 2:
                        w.extend(st_done[j - 2])
                ph.op("scalar", lambda e, j=j, oc=oc, tt=tt, bank=bank: e.activation(
                    ohi[j % 2].ap()[:, 512 * tt:512 * (tt + 1)], ctx.pb[bank].ap()[:, :],
                    AF.Identity, bias=bql.ap()[:, oc:oc + 1]), waits=w, inc="sa")
                ad = ctx.C["sa"]
                ph.op("vector", lambda e, j=j, oc=oc, tt=tt, bank=bank: e.scalar_tensor_tensor(
                    olo[j % 2].ap()[:, 512 * tt:512 * (tt + 1)], ctx.pb[bank].ap()[:, :],
                    bql.ap()[:, oc:oc + 1], ohi[j % 2].ap()[:, 512 * tt:512 * (tt + 1)],
                    ALU.add, ALU.subtract), waits=[("sa", ad)], inc="sv")
                vd = ctx.C["sv"]
                if tt == 3:
                    hi_tok = ph.dma(d["qk_hi"].ap()[oc], ohi[j % 2].ap()[:],
                                    waits=[("sv", vd)])
                    lo_tok = ph.dma(d["qk_lo"].ap()[oc], olo[j % 2].ap()[:])
                    st_done[j] = (hi_tok, lo_tok)
                return ad, vd
            return ev

        emit_proj_stream(ctx, "l0qk", list(range(32)), lambda oc: d["wqkv"].ap()[oc], evac)


def emit_v(ctx, name, wd_of, use_bias):
    d = ctx.dram
    nc = ctx.nc
    with ExitStack() as es:
        wf = es.enter_context(sbt(nc, f"{name}_wf", [128, 2048], F32))
        vwh = es.enter_context(sbt(nc, f"{name}_vwh", [128, 4 * 2048], BF16))
        vwl = es.enter_context(sbt(nc, f"{name}_vwl", [128, 4 * 2048], BF16))
        ub = [es.enter_context(sbt(nc, f"{name}_ub{i}", [128, 512], F32))
              for i in range(2)]
        vh = [es.enter_context(sbt(nc, f"{name}_vh{i}", [128, 512], BF16)) for i in range(2)]
        vl = [es.enter_context(sbt(nc, f"{name}_vl{i}", [128, 512], BF16)) for i in range(2)]
        vbb = ctx.sb["mb"]

        ph = Phase(ctx, name)
        ones = ctx.sb["c_ones"]
        bc_done = 0
        if use_bias:
            bl = ph.dma(rows0(ctx), d["vbias_r"].ap())
            for tt in range(4):
                ph.op("tensor", lambda e, tt=tt: e.matmul(
                    ctx.pb[4 + tt].ap()[:, :], ones.ap()[0:1, :],
                    rows0(ctx)[:, 512 * tt:512 * (tt + 1)], start=True, stop=True),
                    waits=[bl] if tt == 0 else (), inc="sp")
                bd = ctx.C["sp"]
                ph.op("scalar", lambda e, tt=tt: e.copy(
                    vbb.ap()[:, 512 * tt:512 * (tt + 1)], ctx.pb[4 + tt].ap()[:, :]),
                    waits=[("sp", bd)], inc="sa")
                bc_done = ctx.C["sa"]
        vwh3 = vwh.ap().rearrange("p (o f) -> p o f", f=2048)
        vwl3 = vwl.ap().rearrange("p (o f) -> p o f", f=2048)
        ev_free = [(0, 0, 0)] * 20
        vh_st = [None, None]
        vl_st = [None, None]
        sg_d = 0
        for vg in range(4):
            for i in range(4):
                ld = ph.dma(wf.ap()[:], wd_of(vg, i),
                            waits=[("sg", sg_d)] if (vg or i) else ())
                ph.op("scalar", lambda e, i=i: e.copy(
                    vwh.ap()[:, 2048 * i:2048 * (i + 1)], wf.ap()[:]),
                    waits=[ld], inc="sa")
                ad = ctx.C["sa"]
                ph.op("gpsimd", lambda e, i=i: e.tensor_sub(
                    vwl.ap()[:, 2048 * i:2048 * (i + 1)], wf.ap()[:],
                    vwh.ap()[:, 2048 * i:2048 * (i + 1)]),
                    waits=[("sa", ad)], inc="sg")
                sg_d = ctx.C["sg"]
            wsplit = sg_d
            for tc in range(16):
                bank = tc % 2
                w = [("sg", wsplit)] if tc == 0 else []
                if tc >= 2:
                    sa_f, sv_f, sg_f = ev_free[tc - 2]
                    w += [("sa", sa_f), ("sv", sv_f), ("sg", sg_f)]
                first = True
                for kc in range(KC):
                    for (xa, wa3) in ((ar_hi, vwh3), (ar_lo, vwh3), (ar_hi, vwl3)):
                        ph.op("tensor", lambda e, tc=tc, kc=kc, xa=xa, wa3=wa3,
                              first=first, bank=bank: e.matmul(
                                  ctx.pb[bank].ap()[:, :],
                                  xa(ctx, kc)[:, 128 * tc:128 * (tc + 1)],
                                  wa3[:, :, 128 * kc:128 * (kc + 1)],
                                  start=first,
                                  stop=(kc == KC - 1 and xa is ar_hi and wa3 is vwl3)),
                              waits=w if first else ())
                        w = []
                        first = False
                ph.q["tensor"][-1] = (ph.q["tensor"][-1][0], ph.q["tensor"][-1][1], "sp", 1)
                ctx.bump("sp")
                mmd = ctx.C["sp"]
                if use_bias:
                    wv = [("sp", mmd), ("sa", bc_done)]
                    if tc >= 2:
                        wv.append(("sg", ev_free[tc - 2][2]))
                    ph.op("vector", lambda e, tc=tc, bank=bank, vg=vg: e.tensor_add(
                        ub[tc % 2].ap()[:], ctx.pb[bank].ap()[:, :],
                        vbb.ap()[:, 512 * vg:512 * (vg + 1)]),
                        waits=wv, inc="sv")
                    vd = ctx.C["sv"]
                    wv2 = [("sv", vd), vh_st[tc % 2]]
                    ph.op("scalar", lambda e, tc=tc: e.copy(vh[tc % 2].ap()[:],
                                                            ub[tc % 2].ap()[:]),
                          waits=wv2, inc="sa")
                    ad = ctx.C["sa"]
                    ph.op("gpsimd", lambda e, tc=tc: e.tensor_sub(
                        vl[tc % 2].ap()[:], ub[tc % 2].ap()[:], vh[tc % 2].ap()[:]),
                        waits=[("sa", ad), vl_st[tc % 2]], inc="sg")
                    gd = ctx.C["sg"]
                else:
                    w2 = [("sp", mmd), vh_st[tc % 2]]
                    ph.op("scalar", lambda e, tc=tc, bank=bank: e.copy(
                        vh[tc % 2].ap()[:], ctx.pb[bank].ap()[:, :]),
                        waits=w2, inc="sa")
                    ad = ctx.C["sa"]
                    ph.op("vector", lambda e, tc=tc, bank=bank: e.tensor_sub(
                        vl[tc % 2].ap()[:], ctx.pb[bank].ap()[:, :], vh[tc % 2].ap()[:]),
                        waits=[("sa", ad), vl_st[tc % 2]], inc="sv")
                    gd = 0
                vd2 = ctx.C["sv"]
                w = [("sa", ctx.C["sa"]), ("sv", vd2)]
                if use_bias:
                    w.append(("sg", gd))
                vh_st[tc % 2] = ph.dma(
                    d["v_hi"].ap()[4 * vg:4 * (vg + 1), 128 * tc:128 * (tc + 1), :]
                    .rearrange("h t e -> t h e"),
                    vh[tc % 2].ap().rearrange("p (h e) -> p h e", e=128), waits=w)
                vl_st[tc % 2] = ph.dma(
                    d["v_lo"].ap()[4 * vg:4 * (vg + 1), 128 * tc:128 * (tc + 1), :]
                    .rearrange("h t e -> t h e"),
                    vl[tc % 2].ap().rearrange("p (h e) -> p h e", e=128))
                ev_free[tc] = (ctx.C["sa"], ctx.C["sv"], ctx.C["sg"])
            ph.op("sync", None, waits=ctx.dma_all())
        ph.run()


def emit_l0_lightning(ctx):
    d = ctx.dram
    nc = ctx.nc
    with ExitStack() as es:
        qh = es.enter_context(sbt(nc, "lt_qh", [128, T], BF16))
        ql = es.enter_context(sbt(nc, "lt_ql", [128, T], BF16))
        kh = es.enter_context(sbt(nc, "lt_kh", [128, T], BF16))
        kl = es.enter_context(sbt(nc, "lt_kl", [128, T], BF16))
        vbh = es.enter_context(sbt(nc, "lt_vh", [128, T], BF16))
        vbl = es.enter_context(sbt(nc, "lt_vl", [128, T], BF16))
        smf = es.enter_context(sbt(nc, "lt_smf", [128, 128], F32))
        smh = es.enter_context(sbt(nc, "lt_smh", [128, 128], BF16))
        sml = es.enter_context(sbt(nc, "lt_sml", [128, 128], BF16))
        kf = es.enter_context(sbt(nc, "lt_kf", [128, 128], F32))
        khh = es.enter_context(sbt(nc, "lt_khh", [128, 128], BF16))
        khl = es.enter_context(sbt(nc, "lt_khl", [128, 128], BF16))
        qf = es.enter_context(sbt(nc, "lt_qf", [128, 128], F32))
        qtf = es.enter_context(sbt(nc, "lt_qtf", [128, 128], F32))
        qth = es.enter_context(sbt(nc, "lt_qth", [128, 128], BF16))
        qtl = es.enter_context(sbt(nc, "lt_qtl", [128, 128], BF16))
        S = es.enter_context(sbt(nc, "lt_S", [128, 128], F32))
        Sh = es.enter_context(sbt(nc, "lt_Sh", [128, 128], BF16))
        Sl = es.enter_context(sbt(nc, "lt_Sl", [128, 128], BF16))
        dTile = es.enter_context(sbt(nc, "lt_dT", [128, 128], F32))
        decayT = es.enter_context(sbt(nc, "lt_decayT", [128, 128], F32))
        qdecb = es.enter_context(sbt(nc, "lt_qdecb", [128, 128], F32))
        kd2c = es.enter_context(sbt(nc, "lt_kd2c", [128, 1], F32))
        blkc = es.enter_context(sbt(nc, "lt_blkc", [128, 1], F32))
        sneg = es.enter_context(sbt(nc, "lt_sneg", [128, 16], F32))
        cdif = es.enter_context(sbt(nc, "lt_cdif", [128, 128], F32))
        ctri = es.enter_context(sbt(nc, "lt_ctri", [128, 128], F32))
        ciot = es.enter_context(sbt(nc, "lt_ciot", [128, 128], F32))
        cdcl = es.enter_context(sbt(nc, "lt_cdcl", [128, 2], F32))
        srow = es.enter_context(sbt(nc, "lt_srow", [1, 16], F32))

        ph = Phase(ctx, "lightning")
        ones = ctx.sb["c_ones"]
        cld = [ph.dma(cdif.ap()[:], d["diffD"].ap()),
               ph.dma(ctri.ap()[:], d["tri01"].ap()),
               ph.dma(ciot.ap()[:], d["iotaF"].ap()),
               ph.dma(cdcl.ap()[:], d["dcol"].ap())]
        ld = ph.dma(srow.ap()[:], d["slopes_r"].ap())
        ph.op("tensor", lambda e: e.matmul(ctx.pb[0].ap()[:, 0:16], ones.ap()[0:1, :],
                                           srow.ap()[:], start=True, stop=True),
              waits=[ld], inc="sp")
        pd = ctx.C["sp"]
        ph.op("scalar", lambda e: e.activation(sneg.ap()[:], ctx.pb[0].ap()[:, 0:16],
                                               AF.Identity, scale=-1.0),
              waits=[("sp", pd)], inc="sa")
        sneg_done = ctx.C["sa"]

        for h in range(NH):
            head_w = [("sa", sneg_done)] + cld if h == 0 else \
                [("sv", ctx.C["sv"]), ("sg", ctx.C["sg"]), ("sp", ctx.C["sp"])]
            ph.op("scalar", lambda e, h=h: e.activation(
                dTile.ap()[:], cdif.ap()[:], AF.Exp, scale=sneg.ap()[:, h:h + 1]),
                waits=head_w, inc="sa")
            ad = ctx.C["sa"]
            ph.op("vector", lambda e: e.tensor_mul(decayT.ap()[:], dTile.ap()[:],
                                                   ctri.ap()[:]),
                  waits=[("sa", ad)], inc="sv")
            dec_done = ctx.C["sv"]
            ph.op("scalar", lambda e, h=h: e.activation(
                qdecb.ap()[:], ciot.ap()[:], AF.Exp, scale=sneg.ap()[:, h:h + 1]))
            ph.op("scalar", lambda e, h=h: e.activation(
                kd2c.ap()[:], cdcl.ap()[:, 0:1], AF.Exp, scale=sneg.ap()[:, h:h + 1]))
            ph.op("scalar", lambda e, h=h: e.activation(
                blkc.ap()[:], cdcl.ap()[:, 1:2], AF.Exp, scale=sneg.ap()[:, h:h + 1]),
                inc="sa")
            dec2_done = ctx.C["sa"]
            ph.op("vector", lambda e: e.memset(S.ap()[:], 0.0),
                  waits=[("sa", dec2_done)], inc="sv")
            mem_done = ctx.C["sv"]
            ld_toks = [
                ph.dma(qh.ap()[:], d["qk_hi"].ap()[h]),
                ph.dma(ql.ap()[:], d["qk_lo"].ap()[h]),
                ph.dma(kh.ap()[:], d["qk_hi"].ap()[16 + h]),
                ph.dma(kl.ap()[:], d["qk_lo"].ap()[16 + h]),
                ph.dma(vbh.ap().rearrange("p (c e) -> p c e", e=128),
                       d["v_hi"].ap()[h].rearrange("(c p) e -> p c e", p=128)),
                ph.dma(vbl.ap().rearrange("p (c e) -> p c e", e=128),
                       d["v_lo"].ap()[h].rearrange("(c p) e -> p c e", p=128)),
            ]
            Ssplit = (0, 0)
            pe = {}
            for c in range(16):
                cs = slice(128 * c, 128 * (c + 1))
                bs = (c % 2) * 4
                w = ld_toks + [("sv", mem_done), ("sa", dec2_done)] if c == 0 else \
                    [("sv", pe.get(("smf", c - 2), 0))]
                for i, (wa, ra) in enumerate(((kh, qh), (kh, ql), (kl, qh))):
                    ph.op("tensor", lambda e, wa=wa, ra=ra, cs=cs, i=i, bs=bs: e.matmul(
                        ctx.pb[bs].ap()[:, 0:128], wa.ap()[:, cs], ra.ap()[:, cs],
                        start=(i == 0), stop=(i == 2)),
                        waits=w if i == 0 else ())
                    w = []
                ph.q["tensor"][-1] = (ph.q["tensor"][-1][0], ph.q["tensor"][-1][1], "sp", 1)
                ctx.bump("sp")
                sc_done = ctx.C["sp"]
                ph.op("vector", lambda e, bs=bs: e.tensor_mul(
                    smf.ap()[:], ctx.pb[bs].ap()[:, 0:128], decayT.ap()[:]),
                    waits=[("sp", sc_done), ("sa", pe.get(("smh", c - 1), 0))], inc="sv")
                pe[("smf", c)] = ctx.C["sv"]
                ph.op("scalar", lambda e: e.copy(smh.ap()[:], smf.ap()[:]),
                      waits=[("sv", pe[("smf", c)]), ("sp", pe.get(("o", c - 1), 0))],
                      inc="sa")
                pe[("smh", c)] = ctx.C["sa"]
                ph.op("vector", lambda e: e.tensor_sub(sml.ap()[:], smf.ap()[:], smh.ap()[:]),
                      waits=[("sa", pe[("smh", c)]), ("sp", pe.get(("o", c - 1), 0))],
                      inc="sv")
                sm_done = ctx.C["sv"]
                ph.op("vector", lambda e, cs=cs: e.tensor_add(
                    kf.ap()[:], kh.ap()[:, cs], kl.ap()[:, cs]),
                    waits=[("sp", pe.get(("ktr", c - 1), 0))], inc="sv")
                kf_done = ctx.C["sv"]
                ph.op("tensor", lambda e, bs=bs: e.transpose(
                    ctx.pb[bs + 1].ap()[:, 0:128], kf.ap()[:], ctx.sb["c_ident"].ap()[:]),
                    waits=[("sv", kf_done), ("sa", pe.get(("khh", c - 2), 0)),
                           ("sv", pe.get(("khl", c - 2), 0))], inc="sp")
                pe[("ktr", c)] = ctx.C["sp"]
                ph.op("scalar", lambda e, bs=bs: e.activation(
                    khh.ap()[:], ctx.pb[bs + 1].ap()[:, 0:128], AF.Identity,
                    scale=kd2c.ap()[:, 0:1]),
                    waits=[("sp", pe[("ktr", c)]), ("sp", pe.get(("su", c - 1), 0))],
                    inc="sa")
                pe[("khh", c)] = ctx.C["sa"]
                ph.op("vector", lambda e, bs=bs: e.scalar_tensor_tensor(
                    khl.ap()[:], ctx.pb[bs + 1].ap()[:, 0:128], kd2c.ap()[:, 0:1],
                    khh.ap()[:], ALU.mult, ALU.subtract),
                    waits=[("sa", pe[("khh", c)]), ("sp", pe.get(("su", c - 1), 0))],
                    inc="sv")
                pe[("khl", c)] = ctx.C["sv"]
                khat_done = ctx.C["sv"]
                wq = [("sa", pe.get(("qth", c - 1), 0)), ("sg", pe.get(("qtl", c - 1), 0))]
                if c == 0:
                    wq.append(("sa", dec2_done))
                ph.op("vector", lambda e, cs=cs: e.tensor_add(
                    qf.ap()[:], qh.ap()[:, cs], ql.ap()[:, cs]), waits=wq)
                ph.op("vector", lambda e: e.tensor_mul(qtf.ap()[:], qf.ap()[:],
                                                       qdecb.ap()[:]), inc="sv")
                qtf_done = ctx.C["sv"]
                ph.op("scalar", lambda e: e.copy(qth.ap()[:], qtf.ap()[:]),
                      waits=[("sv", qtf_done), ("sp", pe.get(("o", c - 1), 0))], inc="sa")
                pe[("qth", c)] = ctx.C["sa"]
                ph.op("gpsimd", lambda e: e.tensor_sub(qtl.ap()[:], qtf.ap()[:], qth.ap()[:]),
                      waits=[("sa", pe[("qth", c)]), ("sp", pe.get(("o", c - 1), 0))],
                      inc="sg")
                pe[("qtl", c)] = ctx.C["sg"]
                qt_done = ctx.C["sg"]
                mms = [(vbh, smh), (vbh, sml), (vbl, smh)]
                if c > 0:
                    mms += [(Sh, qth), (Sh, qtl), (Sl, qth)]
                w = [("sv", sm_done), ("sg", qt_done),
                     ("sa", pe.get(("oh", c - 2), 0)), ("sv", pe.get(("ol", c - 2), 0))]
                if c > 0:
                    w += [("sa", Ssplit[0]), ("sv", Ssplit[1])]
                for i, (wa, ra) in enumerate(mms):
                    def mmfn(e, wa=wa, ra=ra, i=i, cs=cs, bs=bs, n=len(mms)):
                        lhs = wa.ap()[:, cs] if wa in (vbh, vbl) else wa.ap()[:]
                        return e.matmul(ctx.pb[bs + 2].ap()[:, 0:128], lhs, ra.ap()[:],
                                        start=(i == 0), stop=(i == n - 1))
                    ph.op("tensor", mmfn, waits=w if i == 0 else ())
                    w = []
                ph.q["tensor"][-1] = (ph.q["tensor"][-1][0], ph.q["tensor"][-1][1], "sp", 1)
                ctx.bump("sp")
                pe[("o", c)] = ctx.C["sp"]
                o_done = ctx.C["sp"]
                ph.op("scalar", lambda e, bs=bs, h=h, cs=cs: e.copy(
                    ar_hi(ctx, h)[:, cs], ctx.pb[bs + 2].ap()[:, 0:128]),
                    waits=[("sp", o_done)], inc="sa")
                pe[("oh", c)] = ctx.C["sa"]
                ph.op("vector", lambda e, bs=bs, h=h, cs=cs: e.tensor_sub(
                    ar_lo(ctx, h)[:, cs], ctx.pb[bs + 2].ap()[:, 0:128], ar_hi(ctx, h)[:, cs]),
                    waits=[("sa", pe[("oh", c)])], inc="sv")
                pe[("ol", c)] = ctx.C["sv"]
                w = [("sv", khat_done), ("sv", pe.get(("sustt", c - 2), 0))]
                for i, (wa, ra) in enumerate(((khh, vbh), (khh, vbl), (khl, vbh))):
                    ph.op("tensor", lambda e, wa=wa, ra=ra, i=i, cs=cs, bs=bs: e.matmul(
                        ctx.pb[bs + 3].ap()[:, 0:128], wa.ap()[:], ra.ap()[:, cs],
                        start=(i == 0), stop=(i == 2)),
                        waits=w if i == 0 else ())
                    w = []
                ph.q["tensor"][-1] = (ph.q["tensor"][-1][0], ph.q["tensor"][-1][1], "sp", 1)
                ctx.bump("sp")
                pe[("su", c)] = ctx.C["sp"]
                su_done = ctx.C["sp"]
                ph.op("vector", lambda e, bs=bs: e.scalar_tensor_tensor(
                    S.ap()[:], S.ap()[:], blkc.ap()[:, 0:1], ctx.pb[bs + 3].ap()[:, 0:128],
                    ALU.mult, ALU.add),
                    waits=[("sp", su_done), ("sa", pe.get(("Sh", c - 1), 0))], inc="sv")
                pe[("sustt", c)] = ctx.C["sv"]
                ph.op("scalar", lambda e: e.copy(Sh.ap()[:], S.ap()[:]),
                      waits=[("sv", pe[("sustt", c)])], inc="sa")
                pe[("Sh", c)] = ctx.C["sa"]
                ph.op("vector", lambda e: e.tensor_sub(Sl.ap()[:], S.ap()[:], Sh.ap()[:]),
                      waits=[("sa", pe[("Sh", c)])], inc="sv")
                Ssplit = (pe[("Sh", c)], ctx.C["sv"])
            ph.op("sync", None, waits=[("sp", ctx.C["sp"]), ("sv", ctx.C["sv"]),
                                       ("sa", ctx.C["sa"]), ("sg", ctx.C["sg"])])
        ph.run()


def emit_l0_outproj(ctx):
    d = ctx.dram
    nc = ctx.nc
    with ExitStack() as es:
        ot = [es.enter_context(sbt(nc, f"op_ot{i}", [128, 512], F32)) for i in range(4)]
        bol = es.enter_context(sbt(nc, "op_bol", [128, 16], F32))
        b_done = [0]

        def evac(ph):
            b_done[0] = ph.dma(bol.ap()[:], d["bout_l"].ap())
            st_done = [None] * 70

            def ev(j, oc, tt, bank, mmd):
                ti = (j * 4 + tt) % 4
                w = [("sp", mmd), st_done[max(0, j * 4 + tt - 4)]]
                if j == 0 and tt == 0:
                    w.append(b_done[0])
                ph.op("scalar", lambda e, ti=ti, bank=bank, oc=oc: e.activation(
                    ot[ti].ap()[:], ctx.pb[bank].ap()[:, :], AF.Identity,
                    bias=bol.ap()[:, oc:oc + 1]), waits=w, inc="sa")
                ad = ctx.C["sa"]
                st_done[j * 4 + tt] = ph.dma(
                    d["sub_out"].ap()[oc][:, 512 * tt:512 * (tt + 1)], ot[ti].ap()[:],
                    waits=[("sa", ad)])
                return ad, 0
            return ev

        emit_proj_stream(ctx, "l0op", list(range(16)), lambda oc: d["wout"].ap()[oc], evac)


def emit_l0_attn(ctx):
    emit_pre_ln(ctx, split=True, eps=EPS_LN)
    emit_l0_qkv_qk(ctx)
    emit_v(ctx, "l0v", lambda vg, i: ctx.dram["wqkv"].ap()[32 + 4 * vg + i], use_bias=True)
    emit_l0_lightning(ctx)
    emit_l0_outproj(ctx)
    emit_post_ln(ctx, eps=EPS_LN)



def emit_quant_chain(ctx, ph, wf_ap, wbf, qr, scl, rcp, width, ld_wait, pool_guard):
    ng = width // 64
    ph.op("scalar", lambda e: e.copy(wbf.ap()[:, 0:width], wf_ap),
          waits=ld_wait, inc="sa")
    ad = ctx.C["sa"]
    ph.op("vector", lambda e: e.reduce_sum(
        rcp.ap()[:, 0:ng], wbf.ap()[:, 0:width].rearrange("p (g e) -> p g e", e=64),
        axis=AX.X, apply_absolute_value=True), waits=[("sa", ad)] + list(pool_guard))
    ph.op("vector", lambda e: e.tensor_scalar(
        scl.ap()[:, 0:ng], rcp.ap()[:, 0:ng], 1.0 / 64, 1e-8, ALU.mult, ALU.max))
    ph.op("vector", lambda e: e.reciprocal(rcp.ap()[:, 0:ng], scl.ap()[:, 0:ng]))
    sclb = scl.ap()[:, 0:ng].unsqueeze(2).broadcast_to([128, ng, 64])
    rcpb = rcp.ap()[:, 0:ng].unsqueeze(2).broadcast_to([128, ng, 64])
    w3 = wbf.ap()[:, 0:width].rearrange("p (g e) -> p g e", e=64)
    q3 = qr.ap()[:, 0:width].rearrange("p (g e) -> p g e", e=64)
    ph.op("vector", lambda e: e.tensor_tensor(q3, w3, rcpb, ALU.mult))
    if getattr(ctx, "dbg_quant", False):
        ph.op("vector", lambda e: e.engine_nop(), inc="sv")
        qv = ctx.C["sv"]
        ph.dma(ctx.dram["dbg"].ap()[:, 6144:6144 + width], qr.ap()[:, 0:width],
               waits=[("sv", qv)])
        ph.dma(ctx.dram["dbg"].ap()[:, 6144 - width // 64:6144],
               scl.ap()[:, 0:width // 64])
        ctx.dbg_quant = False
    ph.op("vector", lambda e: e.tensor_scalar(
        wbf.ap()[:, 0:width], qr.ap()[:, 0:width], -0.5, None, ALU.is_lt))
    ph.op("vector", lambda e: e.scalar_tensor_tensor(
        wbf.ap()[:, 0:width], qr.ap()[:, 0:width], 0.5, wbf.ap()[:, 0:width],
        ALU.is_gt, ALU.subtract))
    w3b = wbf.ap()[:, 0:width].rearrange("p (g e) -> p g e", e=64)
    ph.op("vector", lambda e: e.tensor_tensor(q3, w3b, sclb, ALU.mult), inc="sv")
    return ctx.C["sv"]


def emit_l0f_gu(ctx):
    d = ctx.dram
    nc = ctx.nc
    with ExitStack() as es:
        wf = [es.enter_context(sbt(nc, f"gu_wf{i}", [128, 2048], F32)) for i in range(2)]
        wbf = es.enter_context(sbt(nc, "gu_wbf", [128, 2048], BF16))
        qr = es.enter_context(sbt(nc, "gu_qr", [128, 2048], BF16))
        scl = es.enter_context(sbt(nc, "gu_scl", [128, 64], BF16))
        rcp = es.enter_context(sbt(nc, "gu_rcp", [128, 32], F32))
        whT = es.enter_context(sbt(nc, "gu_whT", [128, 2048], BF16))
        silu = es.enter_context(sbt(nc, "gu_silu", [128, 2048], F32))
        hsb = [es.enter_context(sbt(nc, f"gu_h{i}", [128, 2048], F32)) for i in range(2)]
        hsq = es.enter_context(sbt(nc, "gu_hsq", [128, 512], F32))
        ones = ctx.sb["c_ones"]

        ph = Phase(ctx, "l0f_gu")
        prev = {}
        for pi in range(64):
            hstore = 0
            for half, ch in ((0, pi), (1, 64 + pi)):
                j = pi * 2 + half
                ld = ph.dma(wf[j % 2].ap()[:], d["gup_nat"].ap()[ch],
                            waits=[("sa", prev.get(("wbf", j - 2), 0))])
                qd = emit_quant_chain(ctx, ph, wf[j % 2].ap()[:], wbf, qr, scl, rcp,
                                      2048, [ld, ("sa", prev.get(("tev", j - 1), 0))],
                                      [("sv", 0)])
                prev[("wbf", j)] = ctx.C["sa"]
                tevs = [0, 0, 0, 0]
                for kc in range(16):
                    bnk = kc % 4
                    wt = [("sv", qd)] if kc == 0 else []
                    if kc < 4:
                        if bnk == 0:
                            wt.append(("sa", prev.get(("ev", 0, 3, pi - 1 if half == 0 else pi), 0)))
                        elif bnk == 1:
                            wt.append(("sv", prev.get(("ev", 1, 3, pi - 1), 0)))
                        else:
                            wt.append(("sa", prev.get(("tev", j - 1), 0)))
                    else:
                        wt.append(("sa", tevs[bnk]))
                    ph.op("tensor", lambda e, kc=kc, bnk=bnk: e.transpose(
                        ctx.pbb[bnk].ap()[:, 0:128],
                        qr.ap()[:, 128 * kc:128 * (kc + 1)], ctx.sb["c_identb"].ap()[:]),
                        waits=wt, inc="sp")
                    td = ctx.C["sp"]
                    ph.op("scalar", lambda e, kc=kc, bnk=bnk: e.copy(
                        whT.ap()[:, 128 * kc:128 * (kc + 1)], ctx.pbb[bnk].ap()[:, 0:128]),
                        waits=[("sp", td)], inc="sa")
                    tevs[bnk] = ctx.C["sa"]
                prev[("tev", j)] = ctx.C["sa"]
                tev = ctx.C["sa"]
                for tt in range(4):
                    w = [("sa", tev)] if tt == 0 else []
                    if tt >= 1:
                        w.append(("sa" if half == 0 else "sv", prev.get(("ev", half, tt - 1, pi), 0)))
                    else:
                        w.append(("sa" if half == 0 else "sv", prev.get(("ev", half, 3, pi - 1), 0)))
                    first = True
                    for kc in range(KC):
                        for xa in (ar_hi, ar_lo):
                            ph.op("tensor", lambda e, tt=tt, kc=kc, xa=xa, first=first,
                                  half=half: e.matmul(
                                      ctx.pb[half].ap()[:, :],
                                      whT.ap()[:, 128 * kc:128 * (kc + 1)],
                                      xa(ctx, kc)[:, 512 * tt:512 * (tt + 1)],
                                      start=first, stop=(kc == KC - 1 and xa is ar_lo)),
                                  waits=w if first else ())
                            w = []
                            first = False
                    ph.q["tensor"][-1] = (ph.q["tensor"][-1][0], ph.q["tensor"][-1][1], "sp", 1)
                    ctx.bump("sp")
                    mmd = ctx.C["sp"]
                    if half == 0:
                        ph.op("scalar", lambda e, tt=tt: e.activation(
                            silu.ap()[:, 512 * tt:512 * (tt + 1)], ctx.pb[0].ap()[:, :],
                            AF.Silu),
                            waits=[("sp", mmd), ("sv", prev.get(("ev", 1, tt, pi - 1), 0))],
                            inc="sa")
                        prev[("ev", 0, tt, pi)] = ctx.C["sa"]
                    else:
                        wv = [("sp", mmd), ("sa", prev[("ev", 0, tt, pi)]),
                              prev.get(("hst", pi - 2), None)]
                        ph.op("vector", lambda e, tt=tt, pi=pi: e.tensor_mul(
                            hsb[pi % 2].ap()[:, 512 * tt:512 * (tt + 1)],
                            ctx.pb[1].ap()[:, :], silu.ap()[:, 512 * tt:512 * (tt + 1)]),
                            waits=wv, inc="sv")
                        prev[("ev", 1, tt, pi)] = ctx.C["sv"]
                        hd = ctx.C["sv"]
                        ph.op("scalar", lambda e, tt=tt, pi=pi: e.activation(
                            hsq.ap()[:], hsb[pi % 2].ap()[:, 512 * tt:512 * (tt + 1)],
                            AF.Square), waits=[("sv", hd), ("sp", prev.get("sqlast", 0))],
                            inc="sa")
                        sqd = ctx.C["sa"]
                        ph.op("tensor", lambda e, tt=tt, pi=pi: e.matmul(
                            ctx.pb[4 + tt].ap()[0:1, :], ones.ap()[:, 0:1], hsq.ap()[:],
                            start=(pi == 0), stop=(pi == 63)),
                            waits=[("sa", sqd)], inc="sp")
                        prev["sqlast"] = ctx.C["sp"]
            prev[("hst", pi)] = ph.dma(d["h_f32"].ap()[pi], hsb[pi % 2].ap()[:],
                                       waits=[("sv", prev[("ev", 1, 3, pi)])])
        ph.op("sync", None, waits=ctx.dma_all() + [("sp", ctx.C["sp"]),
                                   ("sv", ctx.C["sv"]), ("sa", ctx.C["sa"])])
        ph.run()


def emit_l0f_hn(ctx):
    d = ctx.dram
    nc = ctx.nc
    ones = ctx.sb["c_ones"]
    with ExitStack() as es:
        hb = [es.enter_context(sbt(nc, f"hn_hb{i}", [128, T], F32)) for i in range(2)]
        hnf = es.enter_context(sbt(nc, "hn_f", [128, T], F32))
        hh = [es.enter_context(sbt(nc, f"hn_hh{i}", [128, T], BF16)) for i in range(2)]
        hl = [es.enter_context(sbt(nc, f"hn_hl{i}", [128, T], BF16)) for i in range(2)]
        ph = Phase(ctx, "l0f_hn")
        for tt in range(4):
            ph.op("vector", lambda e, tt=tt: e.tensor_scalar(
                rows0(ctx)[:, 512 * tt:512 * (tt + 1)], ctx.pb[4 + tt].ap()[0:1, :],
                1.0 / 8192.0, None, ALU.mult), inc="sv" if tt == 3 else None)
        md = ctx.C["sv"]
        ph.op("scalar", lambda e: e.activation(
            rsc(ctx), rows0(ctx), AF.Sqrt,
            bias=ctx.sb["c_cvals"].ap()[0:1, 1:2]), waits=[("sv", md)], inc="sa")
        sd = ctx.C["sa"]
        ph.op("vector", lambda e: e.reciprocal(rows0(ctx), rsc(ctx)),
              waits=[("sa", sd)], inc="sv")
        rd = ctx.C["sv"]
        rb = 0
        for tt in range(4):
            ph.op("tensor", lambda e, tt=tt: e.matmul(
                ctx.pb[tt % 2].ap()[:, :], ones.ap()[0:1, :],
                rows0(ctx)[:, 512 * tt:512 * (tt + 1)], start=True, stop=True),
                waits=([("sv", rd)] if tt == 0 else
                       ([("sa", rb - 1)] if tt >= 2 else ())), inc="sp")
            bd = ctx.C["sp"]
            ph.op("scalar", lambda e, tt=tt: e.copy(
                ctx.sb["rstdb"].ap()[:, 512 * tt:512 * (tt + 1)], ctx.pb[tt % 2].ap()[:, :]),
                waits=[("sp", bd)], inc="sa")
            rb = ctx.C["sa"]
        bc = rb
        prev = {}
        for ch in range(64):
            ld = ph.dma(hb[ch % 2].ap()[:], d["h_f32"].ap()[ch],
                        waits=[("sv", prev.get(("hnf", ch - 2), 0))])
            ph.op("vector", lambda e, ch=ch: e.tensor_mul(
                hnf.ap()[:], hb[ch % 2].ap()[:], ctx.sb["rstdb"].ap()[:]),
                waits=[ld, ("sa", bc), ("sg", prev.get(("lo", ch - 1), 0))], inc="sv")
            prev[("hnf", ch)] = ctx.C["sv"]
            ph.op("scalar", lambda e, ch=ch: e.copy(hh[ch % 2].ap()[:], hnf.ap()[:]),
                  waits=[("sv", ctx.C["sv"])] + list(prev.get(("st", ch - 2), ())),
                  inc="sa")
            ad = ctx.C["sa"]
            ph.op("gpsimd", lambda e, ch=ch: e.tensor_sub(
                hl[ch % 2].ap()[:], hnf.ap()[:], hh[ch % 2].ap()[:]),
                waits=[("sa", ad)], inc="sg")
            prev[("lo", ch)] = ctx.C["sg"]
            hi_tok = ph.dma(d["hn_hi"].ap()[ch], hh[ch % 2].ap()[:],
                            waits=[("sg", ctx.C["sg"])])
            lo_tok = ph.dma(d["hn_lo"].ap()[ch], hl[ch % 2].ap()[:])
            prev[("st", ch)] = (hi_tok, lo_tok)
        ph.op("sync", None, waits=ctx.dma_all())
        ph.run()


def emit_l0f_wpj_quant(ctx):
    d = ctx.dram
    nc = ctx.nc
    with ExitStack() as es:
        wf = es.enter_context(sbt(nc, "pq_wf", [128, 4096], F32))
        wbf = es.enter_context(sbt(nc, "pq_wbf", [128, 4096], BF16))
        qr = es.enter_context(sbt(nc, "pq_qr", [128, 4096], BF16))
        scl = es.enter_context(sbt(nc, "pq_scl", [128, 128], BF16))
        rcp = es.enter_context(sbt(nc, "pq_rcp", [128, 64], F32))
        wt = es.enter_context(sbt(nc, "pq_wt", [128, 8192], BF16))
        ph = Phase(ctx, "l0f_pq")
        prev = {}
        for ch in range(16):
            for hf in range(2):
                j = ch * 2 + hf
                ld = ph.dma(wf.ap()[:], d["wpj0_nat"].ap()[ch][:, 4096 * hf:4096 * (hf + 1)],
                            waits=[("sa", prev.get(("wbf", j - 1), 0))])
                qd = emit_quant_chain(ctx, ph, wf.ap()[:], wbf, qr, scl, rcp, 4096,
                                      [ld, ("sa", prev.get(("tev", j - 1), 0))],
                                      [("sv", 0)])
                prev[("wbf", j)] = ctx.C["sa"]
                tevs = [prev.get(("tev", j - 1), 0)] * 8
                for kcH in range(32):
                    bnk = kcH % 8
                    w = [("sv", qd)] if kcH == 0 else []
                    w.append(("sa", tevs[bnk]))
                    ph.op("tensor", lambda e, kcH=kcH, bnk=bnk: e.transpose(
                        ctx.pbb[bnk].ap()[:, 0:128],
                        qr.ap()[:, 128 * kcH:128 * (kcH + 1)], ctx.sb["c_identb"].ap()[:]),
                        waits=w, inc="sp")
                    td = ctx.C["sp"]
                    ev_w = [("sp", td)]
                    if kcH == 0 and hf == 0 and ("st", ch - 1) in prev:
                        ev_w.append(prev[("st", ch - 1)])
                    ph.op("scalar", lambda e, kcH=kcH, bnk=bnk, hf=hf: e.copy(
                        wt.ap()[:, 4096 * hf + 128 * kcH:4096 * hf + 128 * (kcH + 1)],
                        ctx.pbb[bnk].ap()[:, 0:128]),
                        waits=ev_w, inc="sa")
                    tevs[bnk] = ctx.C["sa"]
                prev[("tev", j)] = ctx.C["sa"]
            std = ph.dma(d["wpj0_t"].ap()[ch], wt.ap()[:],
                         waits=[("sa", prev[("tev", ch * 2 + 1)]),
                                prev.get(("st", ch - 1), None)])
            prev[("st", ch)] = std
        ph.op("sync", None, waits=ctx.dma_all())
        ph.run()


def emit_ffn_proj(ctx, name, wt_dram, n_oc, sub_scale=None):
    d = ctx.dram
    nc = ctx.nc
    with ExitStack() as es:
        hh = [es.enter_context(sbt(nc, f"{name}_hh{i}", [128, 512], BF16))
              for i in range(3)]
        hl = [es.enter_context(sbt(nc, f"{name}_hl{i}", [128, 512], BF16))
              for i in range(3)]
        ot = [es.enter_context(sbt(nc, f"{name}_ot{i}", [128, 512], F32))
              for i in range(2)]
        ph = Phase(ctx, name)
        nog = n_oc // 8
        prev = {}
        arb = ctx.arena_b
        for og in range(nog):
            wlds = []
            for i in range(8):
                wlds.append(ph.dma(
                    arb.ap()[:, 8192 * i:8192 * (i + 1)], wt_dram.ap()[og * 8 + i],
                    waits=[("sp", prev.get(("mm", og - 1), 0))] if (og and i == 0) else []))
            for tt in range(4):
                for kcH in range(64):
                    jj = tt * 64 + kcH
                    bi = jj % 3
                    gw = [("sp", prev.get(("hn", og, jj - 3), 0))] if jj >= 3 else []
                    l1 = ph.dma(hh[bi].ap()[:], d["hn_hi"].ap()[kcH][:, 512 * tt:512 * (tt + 1)],
                                waits=gw)
                    l2 = ph.dma(hl[bi].ap()[:], d["hn_lo"].ap()[kcH][:, 512 * tt:512 * (tt + 1)])
                    w = [l1, l2]
                    if kcH == 0:
                        w += wlds
                        if tt >= 1:
                            w.append(("sa", prev.get(("ev", og, tt - 1), 0)))
                        elif og >= 1:
                            w.append(("sa", prev.get(("ev", og - 1, 3), 0)))
                    for oc8 in range(8):
                        for hx in (hh, hl):
                            ph.op("tensor", lambda e, oc8=oc8, hx=hx, bi=bi, kcH=kcH: e.matmul(
                                ctx.pb[oc8].ap()[:, :],
                                arb.ap()[:, 8192 * oc8 + 128 * kcH:8192 * oc8 + 128 * (kcH + 1)],
                                hx[bi].ap()[:], start=(kcH == 0 and hx is hh),
                                stop=(kcH == 63 and hx is hl)),
                                waits=w if (oc8 == 0 and hx is hh) else ())
                            w = []
                    ph.q["tensor"][-1] = (ph.q["tensor"][-1][0], ph.q["tensor"][-1][1], "sp", 1)
                    ctx.bump("sp")
                    prev[("hn", og, jj)] = ctx.C["sp"]
                mmd = ctx.C["sp"]
                prev[("mm", og)] = mmd
                for oc8 in range(8):
                    oc = og * 8 + oc8
                    w = [("sp", mmd), prev.get(("st", og, tt, oc8 - 2), None)]
                    if sub_scale is not None:
                        ph.op("scalar", lambda e, oc8=oc8: e.activation(
                            ot[oc8 % 2].ap()[:], ctx.pb[oc8].ap()[:, :], AF.Identity,
                            scale=sub_scale()), waits=w, inc="sa")
                    else:
                        ph.op("scalar", lambda e, oc8=oc8: e.copy(
                            ot[oc8 % 2].ap()[:], ctx.pb[oc8].ap()[:, :]), waits=w, inc="sa")
                    ad = ctx.C["sa"]
                    prev[("st", og, tt, oc8)] = ph.dma(
                        d["sub_out"].ap()[oc][:, 512 * tt:512 * (tt + 1)],
                        ot[oc8 % 2].ap()[:], waits=[("sa", ad)])
                prev[("ev", og, tt)] = ctx.C["sa"]
        ph.op("sync", None, waits=ctx.dma_all())
        ph.run()


def emit_l0_ffn(ctx):
    emit_pre_ln(ctx, split=True, eps=EPS_LN)
    emit_l0f_gu(ctx)
    emit_l0f_hn(ctx)
    emit_l0f_wpj_quant(ctx)
    emit_ffn_proj(ctx, "l0f_pj", ctx.dram["wpj0_t"], 16)
    emit_post_ln(ctx, eps=EPS_LN)



def emit_l1_qkv_qk(ctx):
    d = ctx.dram
    nc = ctx.nc
    with ExitStack() as es:
        qraw = es.enter_context(sbt(nc, "l1qraw", [128, T], F32))
        xsw = es.enter_context(sbt(nc, "l1xsw", [128, T], F32))
        tc = es.enter_context(sbt(nc, "l1tc", [128, T], F32))
        tsn = es.enter_context(sbt(nc, "l1ts", [128, T], F32))
        hi = [es.enter_context(sbt(nc, "l1hi", [128, 512], BF16)) for i in range(2)]
        lo = [es.enter_context(sbt(nc, "l1lo", [128, 512], BF16)) for i in range(2)]
        state = {"mode": None}

        def evac(ph):
            st_done = [None] * 40
            ev_state = {"tbl": None, "sw": 0}

            def ev(j, oc, tt, bank, mmd):
                is_q = oc < 16
                ts = slice(512 * tt, 512 * (tt + 1))
                if state["mode"] != is_q:
                    state["mode"] = is_q
                    cnm = "ropeQC" if is_q else "ropeKC"
                    snm = "ropeQS" if is_q else "ropeKS"
                    w = [("sv", ctx.C["sv"])]
                    tca = ph.dma(tc.ap()[:], d[cnm].ap(), waits=w)
                    ev_state["tbl"] = ph.dma(tsn.ap()[:], d[snm].ap(), waits=[tca])
                w = [("sp", mmd)]
                if tt == 0 and j >= 1:
                    w.append(st_done[j - 1])
                if tt == 0:
                    w.append(("sv", ev_state["sw"]))
                    w.append(("sg", ev_state.get("sg", 0)))
                ph.op("scalar", lambda e, tt=tt, bank=bank: e.copy(
                    qraw.ap()[:, ts], ctx.pb[bank].ap()[:, :]), waits=w, inc="sa")
                qev = ctx.C["sa"]
                if tt == 3:
                    bounce = d["sub_out"].ap()[0]
                    s1 = ph.dma(bounce[0:64, :], qraw.ap()[64:128, :],
                                waits=[("sa", qev)])
                    s1b = ph.dma(bounce[64:128, :], qraw.ap()[0:64, :], waits=[s1])
                    s2 = ph.dma(xsw.ap()[:], bounce, waits=[s1b])
                    ph.op("vector", lambda e: e.tensor_tensor(
                        qraw.ap()[:], qraw.ap()[:], tc.ap()[:], ALU.mult),
                        waits=[s2, ev_state["tbl"]])
                    ph.op("vector", lambda e: e.tensor_tensor(
                        xsw.ap()[:], xsw.ap()[:], tsn.ap()[:], ALU.mult))
                    ph.op("vector", lambda e: e.tensor_add(
                        qraw.ap()[:], qraw.ap()[:], xsw.ap()[:]), inc="sv")
                    ev_state["sw"] = ctx.C["sv"]
                    yd = ctx.C["sv"]
                    hi_st = ev_state.setdefault("hi_st", [None, None])
                    lo_st = ev_state.setdefault("lo_st", [None, None])
                    for t2 in range(4):
                        t2s = slice(512 * t2, 512 * (t2 + 1))
                        ph.op("scalar", lambda e, t2=t2, t2s=t2s: e.copy(
                            hi[t2 % 2].ap()[:], qraw.ap()[:, t2s]),
                            waits=[("sv", yd), hi_st[t2 % 2]], inc="sa")
                        ad = ctx.C["sa"]
                        ph.op("gpsimd", lambda e, t2=t2, t2s=t2s: e.tensor_sub(
                            lo[t2 % 2].ap()[:], qraw.ap()[:, t2s], hi[t2 % 2].ap()[:]),
                            waits=[("sa", ad), lo_st[t2 % 2]], inc="sg")
                        gd = ctx.C["sg"]
                        hi_st[t2 % 2] = ph.dma(d["qk_hi"].ap()[oc][:, t2s],
                                               hi[t2 % 2].ap()[:], waits=[("sg", gd)])
                        lo_st[t2 % 2] = ph.dma(d["qk_lo"].ap()[oc][:, t2s],
                                               lo[t2 % 2].ap()[:])
                    st_done[j] = lo_st[1]
                    ev_state["sg"] = ctx.C["sg"]
                return ctx.C["sa"], ctx.C["sv"]
            return ev

        emit_proj_stream(ctx, "l1qk", list(range(32)),
                         lambda oc: d["wq1"].ap()[oc] if oc < 16 else d["wk1"].ap()[oc - 16],
                         evac, nwf=1, nws=1)


def emit_l1_flash(ctx):
    d = ctx.dram
    nc = ctx.nc
    with ExitStack() as es:
        qh = es.enter_context(sbt(nc, "fl_qh", [128, T], BF16))
        ql = es.enter_context(sbt(nc, "fl_ql", [128, T], BF16))
        kh = es.enter_context(sbt(nc, "fl_kh", [128, T], BF16))
        kl = es.enter_context(sbt(nc, "fl_kl", [128, T], BF16))
        vbh = es.enter_context(sbt(nc, "fl_vh", [128, T], BF16))
        vbl = es.enter_context(sbt(nc, "fl_vl", [128, T], BF16))
        sco = es.enter_context(sbt(nc, "fl_sco", [128, T], F32))
        msk = es.enter_context(sbt(nc, "fl_msk", [128, T], BF16))
        pth = es.enter_context(sbt(nc, "fl_pth", [128, 16 * 256], BF16))
        ptl = es.enter_context(sbt(nc, "fl_ptl", [128, 16 * 256], BF16))
        acc = es.enter_context(sbt(nc, "fl_acc", [128, 8], F32))
        mneg = es.enter_context(sbt(nc, "fl_mn", [128, 2], F32))
        rcol = es.enter_context(sbt(nc, "fl_rc", [128, 2], F32))
        rr = es.enter_context(sbt(nc, "fl_rr", [2, 128], F32))
        rr2 = es.enter_context(sbt(nc, "fl_rr2", [1, 256], F32))
        pbuf = ctx.sb["mb"]
        rcb = ctx.sb["rstdb"]
        otile = ctx.sb["rstdb"].ap()[:, 256:512]
        ph = Phase(ctx, "l1flash")
        mld = ph.dma(msk.ap()[:], d["l1mask"].ap())
        prev = {}
        for h in range(NH):
            w0 = [("sp", ctx.C["sp"]), ("sv", ctx.C["sv"]), ("sa", ctx.C["sa"]),
                  ("sg", ctx.C["sg"])] if h else [mld]
            ph.op("sync", None, waits=w0)
            ld_toks = [
                ph.dma(qh.ap()[:], d["qk_hi"].ap()[h]),
                ph.dma(ql.ap()[:], d["qk_lo"].ap()[h]),
                ph.dma(kh.ap()[:], d["qk_hi"].ap()[16 + h]),
                ph.dma(kl.ap()[:], d["qk_lo"].ap()[16 + h]),
                ph.dma(vbh.ap().rearrange("p (c e) -> p c e", e=128),
                       d["v_hi"].ap()[h].rearrange("(c p) e -> p c e", p=128)),
                ph.dma(vbl.ap().rearrange("p (c e) -> p c e", e=128),
                       d["v_lo"].ap()[h].rearrange("(c p) e -> p c e", p=128)),
            ]
            for g2 in range(8):
                for iq in range(2):
                    qt = 2 * g2 + iq
                    nkt = qt // 4 + 1
                    qs = slice(128 * qt, 128 * (qt + 1))
                    for kt in range(nkt):
                        bank = kt % 2
                        w = list(ld_toks) if (g2 == 0 and iq == 0 and kt == 0) else \
                            [("sa", prev.get(("sev", bank), 0))]
                        for i, (wa, ra) in enumerate(((qh, kh), (qh, kl), (ql, kh))):
                            ph.op("tensor", lambda e, wa=wa, ra=ra, qs=qs, kt=kt, i=i,
                                  bank=bank: e.matmul(
                                      ctx.pb[bank].ap()[:, :], wa.ap()[:, qs],
                                      ra.ap()[:, 512 * kt:512 * (kt + 1)],
                                      start=(i == 0), stop=(i == 2)),
                                  waits=w if i == 0 else ())
                            w = []
                        ph.q["tensor"][-1] = (ph.q["tensor"][-1][0], ph.q["tensor"][-1][1],
                                              "sp", 1)
                        ctx.bump("sp")
                        smd = ctx.C["sp"]
                        wse = [("sp", smd)]
                        if kt == 0:
                            wse.append(("sv", prev.get("mnd", 0)))
                        ph.op("scalar", lambda e, kt=kt, bank=bank: e.copy(
                            sco.ap()[:, 512 * kt:512 * (kt + 1)], ctx.pb[bank].ap()[:, :]),
                            waits=wse, inc="sa")
                        prev[("sev", bank)] = ctx.C["sa"]
                    sc_done = ctx.C["sa"]
                    r = qt % 4
                    ph.op("vector", lambda e, r=r, nkt=nkt: e.tensor_tensor(
                        sco.ap()[:, 512 * (nkt - 1):512 * nkt],
                        sco.ap()[:, 512 * (nkt - 1):512 * nkt],
                        msk.ap()[:, 512 * r:512 * (r + 1)], ALU.add),
                        waits=[("sa", sc_done)])
                    ph.op("vector", lambda e, nkt=nkt, iq=iq: e.reduce_max(
                        mneg.ap()[:, iq:iq + 1], sco.ap()[:, 0:512 * nkt], axis=AX.X),
                        waits=[("sa", prev.get(("exd", iq), 0))])
                    ph.op("vector", lambda e, iq=iq: e.tensor_scalar(
                        mneg.ap()[:, iq:iq + 1], mneg.ap()[:, iq:iq + 1], -1.0, None,
                        ALU.mult), inc="sv")
                    mn_done = ctx.C["sv"]
                    prev["mnd"] = mn_done
                    for kt in range(nkt):
                        wex = [("sv", mn_done), ("sp", prev.get("ptd", 0))] if kt == 0 else ()
                        ph.op("scalar", lambda e, kt=kt, iq=iq: e.activation(
                            pbuf.ap()[:, 512 * kt:512 * (kt + 1)],
                            sco.ap()[:, 512 * kt:512 * (kt + 1)], AF.Exp,
                            bias=mneg.ap()[:, iq:iq + 1],
                            accum_out=acc.ap()[:, 4 * iq + kt:4 * iq + kt + 1]),
                            waits=wex, inc="sa")
                    ex_done = ctx.C["sa"]
                    prev[("exd", iq)] = ex_done
                    ph.op("vector", lambda e, nkt=nkt, iq=iq: e.reduce_sum(
                        rcol.ap()[:, iq:iq + 1], acc.ap()[:, 4 * iq:4 * iq + nkt], axis=AX.X),
                        waits=[("sa", ex_done)])
                    ph.op("vector", lambda e, iq=iq: e.reciprocal(
                        rcol.ap()[:, iq:iq + 1], rcol.ap()[:, iq:iq + 1]), inc="sv")
                    for kc in range(4 * nkt):
                        bnk = 2 + kc % 2
                        w = [("sa", ex_done), ("sa", prev.get(("ptv", bnk), 0)),
                             ("sv", prev.get(("ptlv", bnk), 0))]
                        if kc == 0:
                            w.append(("sp", prev.get("odone", 0)))
                        ph.op("tensor", lambda e, kc=kc, bnk=bnk: e.transpose(
                            ctx.pb[bnk].ap()[:, 0:128],
                            pbuf.ap()[:, 128 * kc:128 * (kc + 1)], ctx.sb["c_ident"].ap()[:]),
                            waits=w, inc="sp")
                        td = ctx.C["sp"]
                        dst = 256 * kc + 128 * iq
                        ph.op("scalar", lambda e, bnk=bnk, dst=dst: e.copy(
                            pth.ap()[:, dst:dst + 128], ctx.pb[bnk].ap()[:, 0:128]),
                            waits=[("sp", td)], inc="sa")
                        prev[("ptv", bnk)] = ctx.C["sa"]
                        ph.op("vector", lambda e, bnk=bnk, dst=dst: e.tensor_sub(
                            ptl.ap()[:, dst:dst + 128], ctx.pb[bnk].ap()[:, 0:128],
                            pth.ap()[:, dst:dst + 128]),
                            waits=[("sa", ctx.C["sa"])], inc="sv")
                        prev[("ptlv", bnk)] = ctx.C["sv"]
                    prev["ptd"] = ctx.C["sp"]
                pt_done_a, pt_done_v = ctx.C["sa"], ctx.C["sv"]
                ph.op("tensor", lambda e: e.transpose(
                    ctx.pb[6].ap()[0:2, 0:128], rcol.ap()[:], ctx.sb["c_ident"].ap()[:]),
                    waits=[("sv", ctx.C["sv"]), ("sa", prev.get("rce", 0))], inc="sp")
                rtd = ctx.C["sp"]
                ph.op("scalar", lambda e: e.copy(rr.ap()[0:2, 0:128],
                                                 ctx.pb[6].ap()[0:2, 0:128]),
                      waits=[("sp", rtd)], inc="sa")
                prev["rce"] = ctx.C["sa"]
                b1 = ph.dma(d["rv2"].ap(), rr.ap()[0:2, 0:128],
                            waits=[("sa", ctx.C["sa"])])
                b2 = ph.dma(rr2.ap()[:],
                            d["rv2"].ap().rearrange("a b -> (a b)").unsqueeze(0),
                            waits=[b1])
                ph.op("tensor", lambda e: e.matmul(
                    ctx.pb[5].ap()[:, 0:256], ctx.sb["c_ones"].ap()[0:1, :],
                    rr2.ap()[:], start=True, stop=True),
                    waits=[b2, ("sa", prev.get("rcb", 0))], inc="sp")
                rbd = ctx.C["sp"]
                ph.op("scalar", lambda e: e.copy(rcb.ap()[:, 0:256],
                                                 ctx.pb[5].ap()[:, 0:256]),
                      waits=[("sp", rbd)], inc="sa")
                prev["rcb"] = ctx.C["sa"]
                rb_done = ctx.C["sa"]
                w = [("sa", pt_done_a), ("sv", pt_done_v),
                     ("sv", prev.get("oev", 0))]
                first = True
                for kc in range(2 * (g2 + 1)):
                    for (wa, ra) in ((vbh, pth), (vbh, ptl), (vbl, pth)):
                        ph.op("tensor", lambda e, wa=wa, ra=ra, kc=kc, first=first,
                              g2=g2: e.matmul(
                                  ctx.pb[4].ap()[:, 0:256], wa.ap()[:, 128 * kc:128 * (kc + 1)],
                                  ra.ap()[:, 256 * kc:256 * (kc + 1)], start=first,
                                  stop=(kc == 2 * g2 + 1 and ra is ptl and False) or
                                       (kc == 2 * g2 + 1 and wa is vbl)),
                              waits=w if first else ())
                        w = []
                        first = False
                ph.q["tensor"][-1] = (ph.q["tensor"][-1][0], ph.q["tensor"][-1][1], "sp", 1)
                ctx.bump("sp")
                o_done = ctx.C["sp"]
                prev["odone"] = o_done
                ph.op("vector", lambda e: e.tensor_tensor(
                    otile, ctx.pb[4].ap()[:, 0:256], rcb.ap()[:, 0:256], ALU.mult),
                    waits=[("sp", o_done), ("sa", rb_done),
                           ("sg", prev.get("olo", 0))], inc="sv")
                ov = ctx.C["sv"]
                prev["oev"] = ov
                ph.op("scalar", lambda e, h=h, g2=g2: e.copy(
                    ar_hi(ctx, h)[:, 256 * g2:256 * (g2 + 1)], otile),
                    waits=[("sv", ov)], inc="sa")
                ad = ctx.C["sa"]
                ph.op("gpsimd", lambda e, h=h, g2=g2: e.tensor_sub(
                    ar_lo(ctx, h)[:, 256 * g2:256 * (g2 + 1)], otile,
                    ar_hi(ctx, h)[:, 256 * g2:256 * (g2 + 1)]),
                    waits=[("sa", ad)], inc="sg")
                prev["olo"] = ctx.C["sg"]
        ph.op("sync", None, waits=[("sg", ctx.C["sg"])] + ctx.dma_all())
        ph.run()


def emit_l1_outproj(ctx):
    d = ctx.dram
    nc = ctx.nc
    with ExitStack() as es:
        ot = [es.enter_context(sbt(nc, "l1op_ot", [128, 512], F32)) for i in range(4)]

        def evac(ph):
            st_done = [0] * 70

            def ev(j, oc, tt, bank, mmd):
                ti = (j * 4 + tt) % 4
                w = [("sp", mmd), st_done[max(0, j * 4 + tt - 4)]]
                ph.op("scalar", lambda e, ti=ti, bank=bank: e.activation(
                    ot[ti].ap()[:], ctx.pb[bank].ap()[:, :], AF.Identity,
                    scale=1.0 / 3.0), waits=w, inc="sa")
                ad = ctx.C["sa"]
                st_done[j * 4 + tt] = ph.dma(
                    d["sub_out"].ap()[oc][:, 512 * tt:512 * (tt + 1)], ot[ti].ap()[:],
                    waits=[("sa", ad)])
                return ad, 0
            return ev

        emit_proj_stream(ctx, "l1op", list(range(16)), lambda oc: d["wo1"].ap()[oc], evac)


def emit_l1_attn(ctx):
    emit_pre_ln(ctx, split=True, eps=EPS_LN)
    emit_l1_qkv_qk(ctx)
    emit_v(ctx, "l1v", lambda vg, i: ctx.dram["wv1"].ap()[4 * vg + i], use_bias=False)
    emit_l1_flash(ctx)
    emit_l1_outproj(ctx)
    emit_post_ln(ctx, eps=EPS_LN)



def emit_pdance(ctx, ph, acc, tr, r32, sb64, sw_ap, out_row, op, guards=()):
    d = ctx.dram
    bounce = d["sub_out"].ap()[2]
    b1 = ph.dma(bounce[0:64, :], acc.ap()[64:128, :], waits=guards)
    b2 = ph.dma(sw_ap[0:64, :], bounce[0:64, :], waits=[b1])
    ph.op("vector", lambda e: e.tensor_tensor(
        acc.ap()[0:64, :], acc.ap()[0:64, :], sw_ap[0:64, :], op),
        waits=[b2] + list(guards), inc="sv")
    v1 = ctx.C["sv"]
    b3 = ph.dma(bounce[0:32, :], acc.ap()[32:64, :], waits=[("sv", v1), b2])
    b4 = ph.dma(sw_ap[0:32, :], bounce[0:32, :], waits=[b3])
    ph.op("vector", lambda e: e.tensor_tensor(
        acc.ap()[0:32, :], acc.ap()[0:32, :], sw_ap[0:32, :], op), waits=[b4])
    ph.op("vector", lambda e: e.transpose(tr.ap()[:], acc.ap()[0:32, :]))
    ph.op("vector", lambda e: e.tensor_reduce(
        r32.ap()[:], tr.ap()[:].rearrange("p (g f) -> p g f", f=32), AX.X, op),
        inc="sv")
    vd = ctx.C["sv"]
    ph.op("tensor", lambda e: e.transpose(
        ctx.pb[7].ap()[0:64, 0:32], r32.ap()[:], ctx.sb["c_ident"].ap()[0:32, 0:32]),
        waits=[("sv", vd), ("sa", getattr(ctx, "_pd_sa", 0))], inc="sp")
    td = ctx.C["sp"]
    ph.op("scalar", lambda e: e.copy(sb64.ap()[:], ctx.pb[7].ap()[0:64, 0:32]),
          waits=[("sp", td)], inc="sa")
    ctx._pd_sa = ctx.C["sa"]
    b5 = ph.dma(d["rv3"].ap(), sb64.ap()[:], waits=[("sa", ctx.C["sa"])])
    return ph.dma(out_row, d["rv3"].ap().rearrange("a b -> (a b)").unsqueeze(0),
                  waits=[b5])


def emit_bit_affine(ctx, ph, mean_row, rstd_row, mx_row, mn_row, r0, r1, r2, cb,
                    pre_waits):
    ones = ctx.sb["c_ones"]
    ph.op("vector", lambda e: e.tensor_sub(r0, mx_row, mean_row), waits=pre_waits)
    ph.op("vector", lambda e: e.tensor_sub(r1, mean_row, mn_row))
    ph.op("vector", lambda e: e.tensor_tensor(r0, r0, r1, ALU.max))
    ph.op("vector", lambda e: e.tensor_tensor(r0, r0, rstd_row, ALU.mult))
    ph.op("vector", lambda e: e.reciprocal(r1, r0))
    ph.op("vector", lambda e: e.tensor_tensor(r1, rstd_row, r1, ALU.mult))
    ph.op("vector", lambda e: e.tensor_scalar(r1, r1, 128.0, None, ALU.mult))
    ph.op("vector", lambda e: e.tensor_tensor(r2, mean_row, r1, ALU.mult))
    ph.op("vector", lambda e: e.tensor_scalar(r0, r0, 1.0 / 128.0, None,
                                              ALU.mult), inc="sv")
    rdone = ctx.C["sv"]
    bc = 0
    for k, (src, dst) in enumerate(((r1, "mb"), (r2, "rstdb"), (r0, None))):
        dtile = ctx.sb[dst] if dst else cb
        for tt in range(4):
            i = k * 4 + tt
            ph.op("tensor", lambda e, src=src, tt=tt, k=k: e.matmul(
                ctx.pb[2 * (k % 2) + tt % 2].ap()[:, :], ones.ap()[0:1, :],
                src[:, 512 * tt:512 * (tt + 1)], start=True, stop=True),
                waits=([("sv", rdone)] if i == 0 else
                       ([("sa", bc - 1)] if i >= 2 else ())), inc="sp")
            bd = ctx.C["sp"]
            ph.op("scalar", lambda e, dtile=dtile, tt=tt, k=k: e.copy(
                dtile.ap()[:, 512 * tt:512 * (tt + 1)],
                ctx.pb[2 * (k % 2) + tt % 2].ap()[:, :]),
                waits=[("sp", bd)], inc="sa")
            bc = ctx.C["sa"]
    return bc


def emit_clip_chain(ctx, ph, src_ap, hi_ap, lo_ap, ut_ap, cb, waits,
                    extra_hi=None, extra_lo=None):
    ph.op("vector", lambda e: e.tensor_tensor(ut_ap, src_ap, ctx.sb["mb"].ap()[:],
                                              ALU.mult), waits=waits)
    ph.op("vector", lambda e: e.tensor_tensor(ut_ap, ut_ap,
                                              ctx.sb["rstdb"].ap()[:], ALU.subtract))
    ph.op("vector", lambda e: e.tensor_scalar(ut_ap, ut_ap, CLIP_LO, CLIP_HI,
                                              ALU.max, ALU.min))
    ph.op("vector", lambda e: e.tensor_tensor(ut_ap, ut_ap, cb.ap()[:],
                                              ALU.mult), inc="sv")
    vd = ctx.C["sv"]
    ph.op("scalar", lambda e: e.copy(hi_ap, ut_ap),
          waits=[("sv", vd), extra_hi], inc="sa")
    ad = ctx.C["sa"]
    ph.op("gpsimd", lambda e: e.tensor_sub(lo_ap, ut_ap, hi_ap),
          waits=[("sa", ad), extra_lo], inc="sg")
    return ctx.C["sg"]


def emit_l1f_xhat(ctx):
    nc = ctx.nc
    d = ctx.dram
    with ExitStack() as es:
        sq = es.enter_context(sbt(nc, "bx_sq", [128, T], F32))
        ctx.sb["stat_sq"] = sq
        mxa = es.enter_context(sbt(nc, "bx_mx", [128, T], F32))
        mna = es.enter_context(sbt(nc, "bx_mn", [128, T], F32))
        tr = es.enter_context(sbt(nc, "bx_tr", [32, T], F32))
        r32 = es.enter_context(sbt(nc, "bx_r32", [32, 64], F32))
        sb64 = es.enter_context(sbt(nc, "bx_s64", [64, 32], F32))
        cb = es.enter_context(sbt(nc, "bx_cb", [128, T], F32))
        ph = Phase(ctx, "l1f_xhat")
        bc_done = emit_stats(ctx, ph, lambda c: (st_c(ctx, c), ()), (), EPS_BIT)
        ph.op("vector", lambda e: e.tensor_copy(mxa.ap()[:], st_c(ctx, 0)))
        ph.op("vector", lambda e: e.tensor_copy(mna.ap()[:], st_c(ctx, 0)))
        for c in range(1, KC):
            ph.op("vector", lambda e, c=c: e.tensor_tensor(
                mxa.ap()[:], mxa.ap()[:], st_c(ctx, c), ALU.max), inc="sv")
            ph.op("vector", lambda e, c=c: e.tensor_tensor(
                mna.ap()[:], mna.ap()[:], st_c(ctx, c), ALU.min), inc="sv")
        mm_done_v = ctx.C["sv"]
        mxt = emit_pdance(ctx, ph, mxa, tr, r32, sb64, cb.ap()[0:64, :],
                          mxa.ap()[0:1, :], ALU.max, guards=[("sv", mm_done_v)])
        mnt = emit_pdance(ctx, ph, mna, tr, r32, sb64, cb.ap()[0:64, :],
                          mna.ap()[0:1, :], ALU.min, guards=[mxt])
        bc2 = emit_bit_affine(ctx, ph, ctx.sb["mb"].ap()[0:1, :], rows0(ctx),
                              mxa.ap()[0:1, :], mna.ap()[0:1, :],
                              tr.ap()[0:1, 0:T], sq.ap()[0:1, 0:T], cb.ap()[0:1, 0:T],
                              cb, [mnt, ("sa", bc_done)])
        gd = 0
        for c in range(KC):
            w = [("sa", bc2)] if c == 0 else [("sg", gd)]
            gd = emit_clip_chain(ctx, ph, st_c(ctx, c), ar_hi(ctx, c), ar_lo(ctx, c),
                                 sq.ap()[:], cb, w)
        ph.op("sync", None, waits=[("sg", gd)] + ctx.dma_all())
        ph.run()
        del ctx.sb["stat_sq"]


def _emit_gamma_w(ctx, ph, wdram, n_oc, numel, wg, col, grow, toks):
    ones = ctx.sb["c_ones"]
    ph.op("vector", lambda e: e.memset(col.ap()[:, 0:1], 0.0))
    nb = len(toks)
    rtoks = [0] * nb
    for oc in range(n_oc):
        ld = ph.dma(wg[oc % nb].ap()[:], wdram(oc),
                    waits=[toks[oc % nb], ("sv", rtoks[oc % nb])])
        toks[oc % nb] = ld
        ph.op("vector", lambda e, oc=oc: e.reduce_sum(
            col.ap()[:, 1:2], wg[oc % nb].ap()[:].unsqueeze(1),
            axis=AX.X, apply_absolute_value=True), waits=[ld])
        ph.op("vector", lambda e: e.tensor_add(col.ap()[:, 0:1], col.ap()[:, 0:1],
                                               col.ap()[:, 1:2]), inc="sv")
        rtoks[oc % nb] = ctx.C["sv"]
    acc_done = ctx.C["sv"]
    ctx._gw_rtoks = rtoks
    ph.op("tensor", lambda e: e.matmul(
        ctx.pb[6].ap()[0:1, 0:1], col.ap()[:, 0:1], ones.ap()[:, 0:1],
        start=True, stop=True), waits=[("sv", acc_done)], inc="sp")
    gd1 = ctx.C["sp"]
    ph.op("vector", lambda e: e.tensor_scalar(
        grow.ap()[:, 0:1], ctx.pb[6].ap()[0:1, 0:1], 1.0 / numel, None,
        ALU.mult), waits=[("sp", gd1)])
    ph.op("vector", lambda e: e.tensor_scalar(
        grow.ap()[:, 1:2], grow.ap()[:, 0:1], EPS_BIT, 0.5, ALU.add, ALU.mult))
    ph.op("vector", lambda e: e.tensor_scalar(
        grow.ap()[:, 2:3], grow.ap()[:, 1:2], -1.0, None, ALU.mult))
    ph.op("vector", lambda e: e.memset(grow.ap()[:, 3:4], 0.0), inc="sv")
    gr_done = ctx.C["sv"]
    ph.op("tensor", lambda e: e.matmul(
        ctx.pb[6].ap()[:, 0:4], ones.ap()[0:1, :], grow.ap()[:, 0:4],
        start=True, stop=True), waits=[("sv", gr_done)], inc="sp")
    gd2 = ctx.C["sp"]
    ph.op("scalar", lambda e: e.copy(ctx.sb["gcol"].ap()[:], ctx.pb[6].ap()[:, 0:4]),
          waits=[("sp", gd2)], inc="sa")
    return ctx.C["sa"]


def emit_l1f_fc(ctx):
    nc = ctx.nc
    d = ctx.dram
    ones = ctx.sb["c_ones"]
    with ExitStack() as es:
        wg = [es.enter_context(sbt(nc, "fc_wg", [128, 2048], F32)) for i in range(2)]
        col = es.enter_context(sbt(nc, "fc_col", [128, 4], F32))
        grow = es.enter_context(sbt(nc, "fc_grow", [1, 4], F32))
        tb1 = es.enter_context(sbt(nc, "fc_tb1", [128, 2048], BF16))
        tb2 = es.enter_context(sbt(nc, "fc_tb2", [128, 2048], BF16))
        h2b = es.enter_context(sbt(nc, "fc_h2b", [128, 1024], F32))
        sq = es.enter_context(sbt(nc, "fc_sq", [128, 512], F32))
        mxa = es.enter_context(sbt(nc, "fc_mx", [128, T], F32))
        mna = es.enter_context(sbt(nc, "fc_mn", [128, T], F32))
        rowm = es.enter_context(sbt(nc, "fc_rowm", [1, T], F32))
        ph = Phase(ctx, "l1f_fc")
        toks = [None, None]
        gcol_done = _emit_gamma_w(ctx, ph, lambda oc: d["wfc1"].ap()[oc], 64,
                                  16777216.0, wg, col, grow, toks)
        prev = {}
        h2st = {}
        wrd = list(ctx._gw_rtoks)
        for half in range(2):
            for oc in range(64):
                ld = ph.dma(wg[oc % 2].ap()[:], d["wfc1"].ap()[oc],
                            waits=[toks[oc % 2], ("sv", wrd[oc % 2])])
                toks[oc % 2] = ld
                ph.op("vector", lambda e, oc=oc: e.tensor_scalar(
                    tb1.ap()[:], wg[oc % 2].ap()[:], ctx.sb["gcol"].ap()[:, 2:3], None,
                    ALU.is_lt), waits=[ld, ("sa", gcol_done),
                                       ("sp", prev.get("mmlast", 0))])
                ph.op("vector", lambda e, oc=oc: e.scalar_tensor_tensor(
                    tb2.ap()[:], wg[oc % 2].ap()[:], ctx.sb["gcol"].ap()[:, 1:2],
                    tb1.ap()[:], ALU.is_gt, ALU.subtract), inc="sv")
                tern_done = ctx.C["sv"]
                wrd[oc % 2] = tern_done
                for t2 in range(2):
                    tt = 2 * half + t2
                    bank = t2
                    w = [("sv", tern_done), ("sa", prev.get(("gel", bank), 0))]
                    first = True
                    for kc in range(KC):
                        for xa in (ar_hi, ar_lo):
                            ph.op("tensor", lambda e, kc=kc, xa=xa, tt=tt, first=first,
                                  bank=bank: e.matmul(
                                      ctx.pb[bank].ap()[:, :],
                                      tb2.ap()[:, 128 * kc:128 * (kc + 1)],
                                      xa(ctx, kc)[:, 512 * tt:512 * (tt + 1)],
                                      start=first, stop=(kc == KC - 1 and xa is ar_lo)),
                                  waits=w if first else ())
                            w = []
                            first = False
                    ph.q["tensor"][-1] = (ph.q["tensor"][-1][0], ph.q["tensor"][-1][1],
                                          "sp", 1)
                    ctx.bump("sp")
                    mmd = ctx.C["sp"]
                    prev["mmlast"] = mmd
                    ph.op("scalar", lambda e, t2=t2, bank=bank: e.activation(
                        h2b.ap()[:, 512 * t2:512 * (t2 + 1)], ctx.pb[bank].ap()[:, :],
                        AF.Gelu, scale=ctx.sb["gcol"].ap()[:, 0:1]),
                        waits=[("sp", mmd), h2st.get("last"),
                               ("sv", prev.get(("mxd", bank), 0)),
                               ("sv", prev.get(("mnd", bank), 0))], inc="sa")
                    prev[("gel", bank)] = ctx.C["sa"]
                    ph.op("scalar", lambda e, t2=t2: e.activation(
                        sq.ap()[:], h2b.ap()[:, 512 * t2:512 * (t2 + 1)], AF.Square),
                        waits=[("sp", prev.get(("sqm", t2), 0))], inc="sa")
                    sqd = ctx.C["sa"]
                    wst = [("sa", prev[("gel", bank)])]
                    if oc == 0:
                        wst.append(("sv", prev.get("rowmd", 0)))
                    ph.op("tensor", lambda e, t2=t2, oc=oc: e.matmul(
                        ctx.pb[2 + t2].ap()[0:1, :], ones.ap()[:, 0:1],
                        h2b.ap()[:, 512 * t2:512 * (t2 + 1)],
                        start=(oc == 0), stop=(oc == 63)), waits=wst)
                    ph.op("tensor", lambda e, t2=t2, oc=oc: e.matmul(
                        ctx.pb[4 + t2].ap()[0:1, :], ones.ap()[:, 0:1], sq.ap()[:],
                        start=(oc == 0), stop=(oc == 63)),
                        waits=[("sa", sqd)], inc="sp")
                    prev[("sqm", t2)] = ctx.C["sp"]
                    ts2 = slice(512 * tt, 512 * (tt + 1))
                    hslc = slice(512 * t2, 512 * (t2 + 1))
                    if oc == 0:
                        ph.op("vector", lambda e, ts2=ts2, hslc=hslc: e.tensor_copy(
                            mxa.ap()[:, ts2], h2b.ap()[:, hslc]),
                            waits=[("sa", prev[("gel", bank)])], inc="sv")
                        ph.op("vector", lambda e, ts2=ts2, hslc=hslc: e.tensor_copy(
                            mna.ap()[:, ts2], h2b.ap()[:, hslc]),
                            waits=[("sa", prev[("gel", bank)])], inc="sv")
                    else:
                        ph.op("vector", lambda e, ts2=ts2, hslc=hslc: e.tensor_tensor(
                            mxa.ap()[:, ts2], mxa.ap()[:, ts2], h2b.ap()[:, hslc],
                            ALU.max), waits=[("sa", prev[("gel", bank)])], inc="sv")
                        ph.op("vector", lambda e, ts2=ts2, hslc=hslc: e.tensor_tensor(
                            mna.ap()[:, ts2], mna.ap()[:, ts2], h2b.ap()[:, hslc],
                            ALU.min), waits=[("sa", prev[("gel", bank)])], inc="sv")
                    prev[("mxd", bank)] = ctx.C["sv"]
                    prev[("mnd", bank)] = ctx.C["sv"]
                h2st["last"] = ph.dma(
                    d["h_f32"].ap()[oc][:, 1024 * half:1024 * (half + 1)], h2b.ap()[:],
                    waits=[("sv", prev[("mxd", 1)]), ("sv", prev[("mnd", 1)])])
            for t2 in range(2):
                wrow = [("sp", prev[("sqm", t2)])]
                if t2 == 0:
                    wrow.append(prev.get("rowst"))
                ph.op("vector", lambda e, t2=t2: e.tensor_copy(
                    rowm.ap()[:, 512 * t2:512 * (t2 + 1)], ctx.pb[2 + t2].ap()[0:1, :]),
                    waits=wrow)
                ph.op("vector", lambda e, t2=t2: e.tensor_copy(
                    rowm.ap()[:, 1024 + 512 * t2:1024 + 512 * (t2 + 1)],
                    ctx.pb[4 + t2].ap()[0:1, :]), inc="sv")
            prev["rowmd"] = ctx.C["sv"]
            prev["rowst"] = ph.dma(d["rowsd"].ap()[half:half + 1, :], rowm.ap()[:],
                                   waits=[("sv", prev["rowmd"])])
        ph.dma(d["sub_out"].ap()[0], mxa.ap()[:], waits=[("sv", ctx.C["sv"])])
        ph.dma(d["sub_out"].ap()[1], mna.ap()[:], waits=[("sg", ctx.C["sg"])])
        ph.op("sync", None, waits=ctx.dma_all())
        ph.run()


def emit_l1f_rows(ctx):
    nc = ctx.nc
    d = ctx.dram
    with ExitStack() as es:
        mxa = es.enter_context(sbt(nc, "fr_mx", [128, T], F32))
        mna = es.enter_context(sbt(nc, "fr_mn", [128, T], F32))
        tr = es.enter_context(sbt(nc, "fr_tr", [32, T], F32))
        r32 = es.enter_context(sbt(nc, "fr_r32", [32, 64], F32))
        sb64 = es.enter_context(sbt(nc, "fr_s64", [64, 32], F32))
        cb = es.enter_context(sbt(nc, "fr_cb", [128, T], F32))
        rowb = es.enter_context(sbt(nc, "fr_rowb", [1, T], F32))
        rowc = es.enter_context(sbt(nc, "fr_rowc", [1, T], F32))
        rowa_ap = cb.ap()[0:1, :]
        ph = Phase(ctx, "l1f_rows")
        l1 = ph.dma(mxa.ap()[:], d["sub_out"].ap()[0])
        l2 = ph.dma(mna.ap()[:], d["sub_out"].ap()[1])
        mxt = emit_pdance(ctx, ph, mxa, tr, r32, sb64, cb.ap()[0:64, :],
                          mxa.ap()[0:1, :], ALU.max, guards=[l1])
        mnt = emit_pdance(ctx, ph, mna, tr, r32, sb64, cb.ap()[0:64, :],
                          mna.ap()[0:1, :], ALU.min, guards=[mxt, l2])
        l3 = ph.dma(rowa_ap[:, 0:1024], d["rowsd"].ap()[0:1, 0:1024], waits=[mnt])
        l4 = ph.dma(rowa_ap[:, 1024:2048], d["rowsd"].ap()[1:2, 0:1024])
        l5 = ph.dma(rowb.ap()[:, 0:1024], d["rowsd"].ap()[0:1, 1024:2048])
        l6 = ph.dma(rowb.ap()[:, 1024:2048], d["rowsd"].ap()[1:2, 1024:2048])
        ph.op("vector", lambda e: e.tensor_scalar(
            rowc.ap()[:], rowa_ap, 1.0 / 8192.0, None, ALU.mult),
            waits=[l3, l4, mnt])
        ph.op("vector", lambda e: e.tensor_tensor(
            rowa_ap, rowc.ap()[:], rowc.ap()[:], ALU.mult), waits=[l5, l6])
        ph.op("vector", lambda e: e.scalar_tensor_tensor(
            rowb.ap()[:], rowb.ap()[:], 1.0 / 8192.0, rowa_ap,
            ALU.mult, ALU.subtract), inc="sv")
        vard = ctx.C["sv"]
        ph.op("scalar", lambda e: e.activation(
            rowa_ap, rowb.ap()[:], AF.Sqrt,
            bias=ctx.sb["c_cvals"].ap()[0:1, 2:3]), waits=[("sv", vard)], inc="sa")
        sqd = ctx.C["sa"]
        ph.op("vector", lambda e: e.reciprocal(rowb.ap()[:], rowa_ap),
              waits=[("sa", sqd)], inc="sv")
        bc2 = emit_bit_affine(ctx, ph, rowc.ap()[:], rowb.ap()[:],
                              mxa.ap()[0:1, :], mna.ap()[0:1, :],
                              tr.ap()[0:1, 0:T], mna.ap()[0:1, :], rowa_ap,
                              cb, [("sv", ctx.C["sv"])])
        ph.dma(d["rvec"].ap().rearrange("a b -> (a b)").unsqueeze(0)[:, 0:T],
               tr.ap()[0:1, 0:T], waits=[("sa", bc2)])
        ph.op("sync", None, waits=ctx.dma_all() + [("sa", ctx.C["sa"])])
        ph.run()


def emit_l1f_xhat2(ctx):
    nc = ctx.nc
    d = ctx.dram
    ones = ctx.sb["c_ones"]
    with ExitStack() as es:
        hb = [es.enter_context(sbt(nc, "x2_hb", [128, T], F32)) for i in range(2)]
        ut = es.enter_context(sbt(nc, "x2_ut", [128, T], F32))
        hh = [es.enter_context(sbt(nc, "x2_hh", [128, T], BF16)) for i in range(2)]
        hl = [es.enter_context(sbt(nc, "x2_hl", [128, T], BF16)) for i in range(2)]
        cb = es.enter_context(sbt(nc, "x2_cb", [128, T], F32))
        crow = cb.ap()[0:1, :]
        ph = Phase(ctx, "l1f_xhat2")
        lc = ph.dma(crow,
                    d["rvec"].ap().rearrange("a b -> (a b)").unsqueeze(0)[:, 0:T])
        bcc = 0
        for tt in range(4):
            ph.op("tensor", lambda e, tt=tt: e.matmul(
                ctx.pb[tt % 2].ap()[:, :], ones.ap()[0:1, :],
                crow[:, 512 * tt:512 * (tt + 1)], start=True, stop=True),
                waits=([lc] if tt == 0 else
                       ([("sa", bcc - 1)] if tt >= 2 else ())), inc="sp")
            bd = ctx.C["sp"]
            ph.op("scalar", lambda e, tt=tt: e.copy(
                cb.ap()[:, 512 * tt:512 * (tt + 1)], ctx.pb[tt % 2].ap()[:, :]),
                waits=[("sp", bd)], inc="sa")
            bcc = ctx.C["sa"]
        prev = {}
        for ch in range(64):
            ld = ph.dma(hb[ch % 2].ap()[:], d["h_f32"].ap()[ch],
                        waits=[prev.get(("ld", ch - 2))])
            w = [ld, ("sa", bcc)] if ch == 0 else [ld, ("sg", prev.get("gd", 0))]
            gd = emit_clip_chain(ctx, ph, hb[ch % 2].ap()[:], hh[ch % 2].ap()[:],
                                 hl[ch % 2].ap()[:], ut.ap()[:], cb, w,
                                 extra_hi=prev.get(("sth", ch - 2)),
                                 extra_lo=prev.get(("stl", ch - 2)))
            prev["gd"] = gd
            prev[("sth", ch)] = ph.dma(d["hn_hi"].ap()[ch], hh[ch % 2].ap()[:],
                                       waits=[("sg", gd)])
            prev[("stl", ch)] = ph.dma(d["hn_lo"].ap()[ch], hl[ch % 2].ap()[:])
            prev[("ld", ch)] = prev[("stl", ch)]
        ph.op("sync", None, waits=ctx.dma_all())
        ph.run()


def emit_l1f_wpj_tern(ctx):
    nc = ctx.nc
    d = ctx.dram
    with ExitStack() as es:
        wg = [es.enter_context(sbt(nc, "pj_wg", [128, 4096], F32)) for i in range(2)]
        col = es.enter_context(sbt(nc, "pj_col", [128, 4], F32))
        grow = es.enter_context(sbt(nc, "pj_grow", [1, 4], F32))
        tb1 = es.enter_context(sbt(nc, "pj_tb1", [128, 4096], BF16))
        tb2 = es.enter_context(sbt(nc, "pj_tb2", [128, 4096], BF16))
        ph = Phase(ctx, "l1f_pjt")
        toks = [None, None]
        gcol_done = _emit_gamma_w(
            ctx, ph,
            lambda j: d["wpj1"].ap()[j // 2][:, 4096 * (j % 2):4096 * (j % 2 + 1)],
            32, 16777216.0, wg, col, grow, toks)
        prev = {}
        wrd = list(ctx._gw_rtoks)
        for j in range(32):
            ld = ph.dma(wg[j % 2].ap()[:],
                        d["wpj1"].ap()[j // 2][:, 4096 * (j % 2):4096 * (j % 2 + 1)],
                        waits=[toks[j % 2], ("sv", wrd[j % 2])])
            toks[j % 2] = ld
            ph.op("vector", lambda e, j=j: e.tensor_scalar(
                tb1.ap()[:], wg[j % 2].ap()[:], ctx.sb["gcol"].ap()[:, 2:3], None,
                ALU.is_lt), waits=[ld, ("sa", gcol_done), prev.get(("st", j - 1))])
            ph.op("vector", lambda e, j=j: e.scalar_tensor_tensor(
                tb2.ap()[:], wg[j % 2].ap()[:], ctx.sb["gcol"].ap()[:, 1:2],
                tb1.ap()[:], ALU.is_gt, ALU.subtract), inc="sv")
            td = ctx.C["sv"]
            wrd[j % 2] = td
            prev[("st", j)] = ph.dma(
                d["wpj1_t"].ap()[j // 2][:, 4096 * (j % 2):4096 * (j % 2 + 1)],
                tb2.ap()[:], waits=[("sv", td)])
        ph.op("sync", None, waits=ctx.dma_all())
        ph.run()


def emit_l1_ffn(ctx):
    emit_pre_ln(ctx, split=False, eps=EPS_LN)
    emit_l1f_xhat(ctx)
    emit_l1f_fc(ctx)
    emit_l1f_rows(ctx)
    emit_l1f_xhat2(ctx)
    emit_l1f_wpj_tern(ctx)
    emit_ffn_proj(ctx, "l1f_pj", ctx.dram["wpj1_t"], 16,
                  sub_scale=lambda: ctx.sb["gcol"].ap()[:, 0:1])
    emit_post_ln(ctx, eps=EPS_LN)


SUB_BLOCKS = {}


def _register():
    SUB_BLOCKS["l0_attn"] = emit_l0_attn
    SUB_BLOCKS["l0_ffn"] = emit_l0_ffn
    SUB_BLOCKS["l1_attn"] = emit_l1_attn
    SUB_BLOCKS["l1_ffn"] = emit_l1_ffn


_register()


def build_program(parts, debug=False):
    nc = bass.Bass("TRN2", target_bir_lowering=False)
    ctx = Ctx(nc)
    declare_io(ctx, debug=debug)
    alloc_shared(ctx)
    emit_setup(ctx, load_stream=True)
    for p in parts:
        SUB_BLOCKS[p](ctx)
    emit_output(ctx)
    ctx.es.close()
    return nc


ALL_PARTS = ["l0_attn", "l0_ffn", "l1_attn", "l1_ffn"]


def kernel(**inputs):
    per_core = [host_inputs(inputs, c) for c in range(8)]
    nc = build_program(ALL_PARTS)
    res = run_bass_kernel_spmd(nc, per_core, core_ids=list(range(8)))
    return np.ascontiguousarray(
        np.stack([np.asarray(r["yT"]).T for r in res.results]).astype(np.float32))


# version 03 (final)
# speedup vs baseline: 1.0153x; 1.0007x over previous
import numpy as np
import ml_dtypes
from contextlib import ExitStack

import concourse.bass as bass
import concourse.mybir as mybir
from concourse.bass_utils import run_bass_kernel_spmd

F32 = mybir.dt.float32
BF16 = mybir.dt.bfloat16
AF = mybir.ActivationFunctionType
ALU = mybir.AluOpType
AX = mybir.AxisListType

D = 2048
T = 2048
KC = D // 128
NH = 16
HD = 128
SCL_L1 = float(128.0 ** 0.5 / 128.0)
EPS_LN = 1e-5
EPS_RMS = float(np.finfo(np.float32).eps)
EPS_BIT = 1e-6
CLIP_HI = float(np.float32(np.float64(128.0) - 1e-6))
CLIP_LO = float(np.float32(np.float64(-128.0) + 1e-6))

ENGINES = ("sync", "tensor", "vector", "scalar", "gpsimd")

_UID = [0]


def sbt(nc, name, shape, dtype):
    _UID[0] += 1
    return nc.sbuf_tensor(f"{name}_{_UID[0]}", shape, dtype)




def chunk_w(w):
    K, M = w.shape
    nk, nm = K // 128, M // 128
    return np.ascontiguousarray(
        w.reshape(nk, 128, nm, 128).transpose(2, 1, 0, 3).reshape(nm, 128, nk * 128))


def nat_chunks(w):
    R, C = w.shape
    return np.ascontiguousarray(w.reshape(R // 128, 128, C))


def make_consts():
    p = np.arange(128, dtype=np.float32)[:, None]
    j = np.arange(128, dtype=np.float32)[None, :]
    c = {}
    c["ident"] = np.eye(128, dtype=np.float32)
    c["ones"] = np.ones((128, 128), dtype=np.float32)
    c["diffD"] = np.ascontiguousarray(np.maximum(j - p, 0.0) + np.zeros((128, 128), np.float32))
    c["tri01"] = (j >= p).astype(np.float32)
    c["iotaF"] = np.ascontiguousarray(j + np.zeros((128, 128), np.float32))
    c["dcol"] = np.concatenate([(128.0 - p), np.full((128, 1), 128.0, np.float32)],
                               axis=1).astype(np.float32)
    jj = np.arange(512, dtype=np.float32)[None, :]
    m = [np.where(jj <= 128.0 * r + p, 0.0, -1e30).astype(np.float32) for r in range(4)]
    c["l1mask"] = np.ascontiguousarray(
        np.concatenate(m, axis=1).astype(ml_dtypes.bfloat16))
    af = ((1.0 / 1024.0) ** np.linspace(0.0, 1.0, 32)).astype(np.float32)
    af64 = np.concatenate([af.astype(np.float64), np.zeros(32)])
    theta = np.arange(T, dtype=np.float64)[:, None] * af64[None, :]
    cosT = np.cos(theta).T
    sinT = np.sin(theta).T
    Call = np.concatenate([cosT, cosT], axis=0)
    Sall = np.concatenate([sinT, -sinT], axis=0)
    c["ropeKC"] = np.ascontiguousarray(Call.astype(np.float32))
    c["ropeKS"] = np.ascontiguousarray(Sall.astype(np.float32))
    c["ropeQC"] = np.ascontiguousarray((Call * SCL_L1).astype(np.float32))
    c["ropeQS"] = np.ascontiguousarray((Sall * SCL_L1).astype(np.float32))
    cv = np.zeros((128, 4), np.float32)
    cv[:, 0] = EPS_LN
    cv[:, 1] = EPS_RMS
    cv[:, 2] = EPS_BIT
    c["cvals"] = cv
    return c


CONST_SHAPES = {
    "ident": [128, 128], "ones": [128, 128], "diffD": [128, 128], "tri01": [128, 128],
    "iotaF": [128, 128], "dcol": [128, 2], "l1mask": [128, 2048],
    "ropeKC": [128, T], "ropeKS": [128, T], "ropeQC": [128, T], "ropeQS": [128, T],
    "cvals": [128, 4],
}


def host_inputs(inputs, core):
    m = {}
    x = np.asarray(inputs["L0_attn_x"][core], dtype=np.float32)
    m["xT"] = np.ascontiguousarray(x.T)
    m["wqkv"] = chunk_w(np.asarray(inputs["L0_attn_Wqkv"], np.float32))
    bqkv = np.asarray(inputs["L0_attn_bqkv"], np.float32)
    m["bqkv_l"] = np.ascontiguousarray(bqkv.reshape(48, 128).T)
    m["vbias_r"] = np.ascontiguousarray(bqkv[4096:6144].reshape(1, 2048))
    m["wout"] = chunk_w(np.asarray(inputs["L0_attn_Wout"], np.float32))
    m["bout_l"] = np.ascontiguousarray(
        np.asarray(inputs["L0_attn_bout"], np.float32).reshape(16, 128).T)
    m["slopes_r"] = np.ascontiguousarray(
        np.asarray(inputs["L0_attn_slopes"], np.float32).reshape(1, 16))
    m["gup_nat"] = nat_chunks(np.asarray(inputs["L0_ffn_w_gate_up"], np.float32))
    m["wpj0_nat"] = nat_chunks(np.asarray(inputs["L0_ffn_w_proj"], np.float32))
    m["wq1"] = chunk_w(np.ascontiguousarray(np.asarray(inputs["L1_attn_Wq"], np.float32).T))
    m["wk1"] = chunk_w(np.ascontiguousarray(np.asarray(inputs["L1_attn_Wk"], np.float32).T))
    m["wv1"] = chunk_w(np.ascontiguousarray(np.asarray(inputs["L1_attn_Wv"], np.float32).T))
    m["wo1"] = chunk_w(np.ascontiguousarray(np.asarray(inputs["L1_attn_Wo"], np.float32).T))
    m["wfc1"] = chunk_w(np.ascontiguousarray(np.asarray(inputs["L1_ffn_w_fc"], np.float32).T))
    m["wpj1"] = chunk_w(np.ascontiguousarray(np.asarray(inputs["L1_ffn_w_proj"], np.float32).T))
    m.update(make_consts())
    return m



class Phase:
    def __init__(self, ctx, name=""):
        self.ctx = ctx
        self.name = name
        self.q = {e: [] for e in ENGINES}

    SELF_SEM = {"vector": "sv", "scalar": "sa", "gpsimd": "sg"}

    def op(self, eng, fn=None, waits=(), inc=None, inc_n=1):
        waits = list(waits)
        ss = self.SELF_SEM.get(eng)
        if ss is not None and fn is not None:
            waits.append((ss, self.ctx.C[ss]))
            if inc is None:
                inc = ss
                inc_n = 1
            else:
                assert inc == ss and inc_n == 1, (eng, inc)
        c = self.ctx.bump(inc, inc_n) if (inc and fn is not None) else None
        self.q[eng].append((fn, tuple(waits), inc, inc_n))
        return c

    def dma(self, out_ap, in_ap, waits=(), sem=None, q="sync"):
        ctx = self.ctx
        nm = ctx.dma_pool[ctx.dma_rr % len(ctx.dma_pool)]
        ctx.dma_rr += 1
        w = list(waits)
        if ctx.C[nm] > 0:
            w.append((nm, ctx.C[nm]))
        self.q[q].append((lambda e, o=out_ap, i=in_ap: e.dma_start(out=o, in_=i),
                          tuple(w), nm, 16))
        return (nm, ctx.bump(nm, 16))

    def run(self):
        ctx = self.ctx
        with ctx.nc.Block() as b:
            for eng in ENGINES:
                ops = self.q[eng]
                if not ops:
                    continue

                def body(e, ops=ops):
                    for fn, waits, inc, inc_n in ops:
                        for wv in waits:
                            if not wv:
                                continue
                            (s, v) = wv
                            if v > 0:
                                e.wait_ge(ctx.sems[s], v)
                        ins = fn(e) if fn is not None else None
                        if inc is not None and ins is not None:
                            ins.then_inc(ctx.sems[inc], inc_n)
                getattr(b, eng)(body)


class Ctx:
    def __init__(self, nc):
        self.nc = nc
        self.es = ExitStack()
        self.sems = {}
        self.C = {}
        for s in ("ld", "st", "sp", "sv", "sa", "sg", "lg"):
            self.sems[s] = self.es.enter_context(nc.semaphore(s))
            self.C[s] = 0
        self.dma_pool = []
        for i in range(12):
            nm = f"dm{i}"
            self.sems[nm] = self.es.enter_context(nc.semaphore(nm))
            self.C[nm] = 0
            self.dma_pool.append(nm)
        self.dma_rr = 0
        self.sb = {}
        self.dram = {}

    def bump(self, s, n=1):
        self.C[s] += n
        return self.C[s]

    def dma_all(self):
        return [(nm, self.C[nm]) for nm in self.dma_pool if self.C[nm] > 0]

    def alloc(self, name, shape, dtype, **kw):
        h = self.es.enter_context(self.nc.sbuf_tensor(name, shape, dtype, **kw))
        self.sb[name] = h
        return h


def declare_io(ctx, debug=False):
    nc = ctx.nc
    d = ctx.dram
    scratch_kind = "ExternalOutput" if debug else "Internal"
    d["xT"] = nc.dram_tensor("xT", [D, T], F32, kind="ExternalInput")
    d["yT"] = nc.dram_tensor("yT", [D, T], F32, kind="ExternalOutput")
    d["wqkv"] = nc.dram_tensor("wqkv", [48, 128, 2048], F32, kind="ExternalInput")
    d["bqkv_l"] = nc.dram_tensor("bqkv_l", [128, 48], F32, kind="ExternalInput")
    d["vbias_r"] = nc.dram_tensor("vbias_r", [1, 2048], F32, kind="ExternalInput")
    d["wout"] = nc.dram_tensor("wout", [16, 128, 2048], F32, kind="ExternalInput")
    d["bout_l"] = nc.dram_tensor("bout_l", [128, 16], F32, kind="ExternalInput")
    d["slopes_r"] = nc.dram_tensor("slopes_r", [1, 16], F32, kind="ExternalInput")
    d["gup_nat"] = nc.dram_tensor("gup_nat", [128, 128, 2048], F32, kind="ExternalInput")
    d["wpj0_nat"] = nc.dram_tensor("wpj0_nat", [16, 128, 8192], F32, kind="ExternalInput")
    for nm in ("wq1", "wk1", "wv1", "wo1"):
        d[nm] = nc.dram_tensor(nm, [16, 128, 2048], F32, kind="ExternalInput")
    d["wfc1"] = nc.dram_tensor("wfc1", [64, 128, 2048], F32, kind="ExternalInput")
    d["wpj1"] = nc.dram_tensor("wpj1", [16, 128, 8192], F32, kind="ExternalInput")
    for nm, shp in CONST_SHAPES.items():
        dt_ = BF16 if nm == "l1mask" else F32
        d[nm] = nc.dram_tensor(nm, shp, dt_, kind="ExternalInput")
    d["xsp"] = nc.dram_tensor("xsp", [KC, 128, T], F32, kind=scratch_kind)
    d["qk_hi"] = nc.dram_tensor("qk_hi", [32, 128, T], BF16, kind=scratch_kind)
    d["qk_lo"] = nc.dram_tensor("qk_lo", [32, 128, T], BF16, kind=scratch_kind)
    d["v_hi"] = nc.dram_tensor("v_hi", [NH, T, HD], BF16, kind=scratch_kind)
    d["v_lo"] = nc.dram_tensor("v_lo", [NH, T, HD], BF16, kind=scratch_kind)
    d["sub_out"] = nc.dram_tensor("sub_out", [KC, 128, T], F32, kind=scratch_kind)
    d["h_f32"] = nc.dram_tensor("h_f32", [64, 128, T], F32, kind=scratch_kind)
    d["hn_hi"] = nc.dram_tensor("hn_hi", [64, 128, T], BF16, kind=scratch_kind)
    d["hn_lo"] = nc.dram_tensor("hn_lo", [64, 128, T], BF16, kind=scratch_kind)
    d["wpj0_t"] = nc.dram_tensor("wpj0_t", [16, 128, 8192], BF16, kind=scratch_kind)
    d["wpj1_t"] = nc.dram_tensor("wpj1_t", [16, 128, 8192], BF16, kind=scratch_kind)
    d["rvec"] = nc.dram_tensor("rvec", [64, 64], F32, kind=scratch_kind)
    d["rv2"] = nc.dram_tensor("rv2", [2, 128], F32, kind=scratch_kind)
    d["rv3"] = nc.dram_tensor("rv3", [64, 32], F32, kind=scratch_kind)
    d["rowsd"] = nc.dram_tensor("rowsd", [4, 2048], F32, kind=scratch_kind)
    d["dbg"] = nc.dram_tensor("dbg", [128, 8192], BF16, kind=scratch_kind)


def alloc_shared(ctx):
    nc = ctx.nc
    ctx.arena = ctx.alloc("arena", [128, KC * T], F32)
    ctx.arena_b = ctx.arena.bitcast(BF16)
    ctx.alloc("c_ident", [128, 128], F32)
    ctx.alloc("c_identb", [128, 128], BF16)
    ctx.alloc("c_ones", [128, 128], F32)
    ctx.alloc("c_cvals", [128, 4], F32)
    ctx.alloc("gcol", [128, 4], F32)
    ctx.alloc("rows", [1, T], F32)
    ctx.alloc("mb", [128, T], F32)
    ctx.alloc("rstdb", [128, T], F32)
    ctx.pb = [ctx.es.enter_context(nc.psum_tensor(f"pb{i}", [128, 512], F32))
              for i in range(8)]
    ctx.pbb = [p.bitcast(BF16) for p in ctx.pb]


def st_c(ctx, c):
    return ctx.arena.ap()[:, c * T:(c + 1) * T]


def ar_hi(ctx, c):
    return ctx.arena_b.ap()[:, c * 2 * T: c * 2 * T + T]


def ar_lo(ctx, c):
    return ctx.arena_b.ap()[:, c * 2 * T + T: (c + 1) * 2 * T]


def rows0(ctx):
    return ctx.sb["rows"].ap()[:, 0:T]


def rsc(ctx):
    return ctx.sb["rstdb"].ap()[0:1, :]



def emit_setup(ctx, load_stream):
    ph = Phase(ctx, "setup")
    d = ctx.dram
    ph.dma(ctx.sb["c_ident"].ap()[:], d["ident"].ap())
    ph.dma(ctx.sb["c_cvals"].ap()[:], d["cvals"].ap())
    ld = ph.dma(ctx.sb["c_ones"].ap()[:], d["ones"].ap())
    if load_stream:
        for c in range(KC):
            ld = ph.dma(st_c(ctx, c), d["xT"].ap()[128 * c:128 * (c + 1), :])
    ph.op("scalar", lambda e: e.copy(ctx.sb["c_identb"].ap()[:], ctx.sb["c_ident"].ap()[:]),
          waits=[ld], inc="sa")
    ph.op("sync", None, waits=ctx.dma_all())
    ph.run()



def emit_stats(ctx, ph, get_chunk, pre_waits, eps, n=D):
    nc = ctx.nc
    ones = ctx.sb["c_ones"]
    sq = ctx.sb["stat_sq"]
    mm_done = [0] * KC
    for c in range(KC):
        ap, w = get_chunk(c)
        w = list(w) + (list(pre_waits) if c == 0 else [])
        if c >= 1:
            w.append(("sp", mm_done[c - 1]))
        ph.op("scalar", lambda e, ap=ap, c=c: e.activation(
            sq.ap()[:], ap, AF.Square), waits=w, inc="sa")
        sq_done = ctx.C["sa"]
        for tt in range(4):
            ph.op("tensor", lambda e, ap=ap, c=c, tt=tt: e.matmul(
                ctx.pb[tt].ap()[0:1, :], ones.ap()[:, 0:1],
                ap[:, 512 * tt:512 * (tt + 1)],
                start=(c == 0), stop=(c == KC - 1)),
                waits=[("sa", sq_done)] if tt == 0 else ())
        for tt in range(4):
            ph.op("tensor", lambda e, c=c, tt=tt: e.matmul(
                ctx.pb[4 + tt].ap()[0:1, :], ones.ap()[:, 0:1],
                sq.ap()[:, 512 * tt:512 * (tt + 1)],
                start=(c == 0), stop=(c == KC - 1)),
                inc="sp" if tt == 3 else None)
        mm_done[c] = ctx.C["sp"]
    allmm = mm_done[KC - 1]
    for tt in range(4):
        ph.op("vector", lambda e, tt=tt: e.tensor_scalar(
            rows0(ctx)[:, 512 * tt:512 * (tt + 1)], ctx.pb[tt].ap()[0:1, :],
            1.0 / n, None, ALU.mult),
            waits=[("sp", allmm)] if tt == 0 else (), inc="sv" if tt == 3 else None)
    mean_done = ctx.C["sv"]
    mb_done = 0
    for tt in range(4):
        ph.op("tensor", lambda e, tt=tt: e.matmul(
            ctx.pb[tt % 2].ap()[:, :], ones.ap()[0:1, :],
            rows0(ctx)[:, 512 * tt:512 * (tt + 1)], start=True, stop=True),
            waits=([("sv", mean_done)] if tt == 0 else
                   ([("sa", mb_done - 1)] if tt >= 2 else ())), inc="sp")
        bd = ctx.C["sp"]
        ph.op("scalar", lambda e, tt=tt: e.copy(
            ctx.sb["mb"].ap()[:, 512 * tt:512 * (tt + 1)], ctx.pb[tt % 2].ap()[:, :]),
            waits=[("sp", bd)], inc="sa")
        mb_done = ctx.C["sa"]
    ph.op("vector", lambda e: e.tensor_mul(rsc(ctx), rows0(ctx), rows0(ctx)),
          waits=[("sa", mb_done)], inc="sv")
    for tt in range(4):
        ph.op("vector", lambda e, tt=tt: e.scalar_tensor_tensor(
            rows0(ctx)[:, 512 * tt:512 * (tt + 1)], ctx.pb[4 + tt].ap()[0:1, :],
            1.0 / n, rsc(ctx)[:, 512 * tt:512 * (tt + 1)], ALU.mult, ALU.subtract),
            inc="sv" if tt == 3 else None)
    var_done = ctx.C["sv"]
    eps_col = {EPS_LN: 0, EPS_RMS: 1, EPS_BIT: 2}[eps]
    ph.op("scalar", lambda e: e.activation(
        rsc(ctx), rows0(ctx), AF.Sqrt,
        bias=ctx.sb["c_cvals"].ap()[0:1, eps_col:eps_col + 1]),
        waits=[("sv", var_done)], inc="sa")
    sdone = ctx.C["sa"]
    ph.op("vector", lambda e: e.reciprocal(rows0(ctx), rsc(ctx)),
          waits=[("sa", sdone)], inc="sv")
    rstd_done = ctx.C["sv"]
    rb_done = 0
    for tt in range(4):
        ph.op("tensor", lambda e, tt=tt: e.matmul(
            ctx.pb[2 + tt % 2].ap()[:, :], ones.ap()[0:1, :],
            rows0(ctx)[:, 512 * tt:512 * (tt + 1)], start=True, stop=True),
            waits=([("sv", rstd_done)] if tt == 0 else
                   ([("sa", rb_done - 1)] if tt >= 2 else ())), inc="sp")
        bd = ctx.C["sp"]
        ph.op("scalar", lambda e, tt=tt: e.copy(
            ctx.sb["rstdb"].ap()[:, 512 * tt:512 * (tt + 1)], ctx.pb[2 + tt % 2].ap()[:, :]),
            waits=[("sp", bd)], inc="sa")
        rb_done = ctx.C["sa"]
    return rb_done



def emit_pre_ln(ctx, split=True, eps=EPS_LN):
    nc = ctx.nc
    d = ctx.dram
    with ExitStack() as es:
        ctx.sb["stat_sq"] = es.enter_context(sbt(nc, "pl_sq", [128, T], F32))
        ut = es.enter_context(sbt(nc, "pl_u", [128, T], F32))
        xnf = es.enter_context(sbt(nc, "pl_xnf", [128, T], F32))

        ph = Phase(ctx, "pre_ln")
        spill_toks = []
        for c in range(KC):
            spill_toks.append(ph.dma(d["xsp"].ap()[c], st_c(ctx, c)))
        bc_done = emit_stats(ctx, ph, lambda c: (st_c(ctx, c), ()), (), eps)
        gd = 0
        for c in range(KC):
            w = [("sa", bc_done)] if c == 0 else [("sg" if split else "sv", gd)]
            if c == 0:
                w += spill_toks
            ph.op("vector", lambda e, c=c: e.tensor_sub(
                ut.ap()[:], st_c(ctx, c), ctx.sb["mb"].ap()[:]), waits=w)
            if split:
                ph.op("vector", lambda e: e.tensor_mul(
                    xnf.ap()[:], ut.ap()[:], ctx.sb["rstdb"].ap()[:]), inc="sv")
                vd = ctx.C["sv"]
                ph.op("scalar", lambda e, c=c: e.copy(ar_hi(ctx, c), xnf.ap()[:]),
                      waits=[("sv", vd)], inc="sa")
                ad = ctx.C["sa"]
                ph.op("gpsimd", lambda e, c=c: e.tensor_sub(
                    ar_lo(ctx, c), xnf.ap()[:], ar_hi(ctx, c)),
                    waits=[("sa", ad)], inc="sg")
                gd = ctx.C["sg"]
            else:
                ph.op("vector", lambda e, c=c: e.tensor_mul(
                    st_c(ctx, c), ut.ap()[:], ctx.sb["rstdb"].ap()[:]), inc="sv")
                gd = ctx.C["sv"]
        ph.op("sync", None, waits=[("sg" if split else "sv", gd)])
        ph.run()
        del ctx.sb["stat_sq"]



def emit_post_ln(ctx, eps=EPS_LN):
    nc = ctx.nc
    d = ctx.dram
    with ExitStack() as es:
        ob = [es.enter_context(sbt(nc, f"po_ob{i}", [128, T], F32)) for i in range(2)]
        ctx.sb["stat_sq"] = es.enter_context(sbt(nc, "po_sq", [128, T], F32))
        xb = [es.enter_context(sbt(nc, f"po_xb{i}", [128, T], F32)) for i in range(2)]
        ut = es.enter_context(sbt(nc, "po_u", [128, T], F32))

        ph = Phase(ctx, "post_ln")
        lds = [0] * KC
        sp0 = ctx.C["sp"]
        for c in range(KC):
            w = [("sp", sp0 + (c - 2) + 1)] if c >= 2 else []
            lds[c] = ph.dma(ob[c % 2].ap()[:], d["sub_out"].ap()[c], waits=w)

        def get_chunk(c):
            return ob[c % 2].ap()[:], [lds[c]]
        bc_done = emit_stats(ctx, ph, get_chunk, (), eps)
        assert ctx.C["sp"] >= sp0 + KC
        prev = 0
        for c in range(KC):
            l1 = ph.dma(ob[c % 2].ap()[:], d["sub_out"].ap()[c],
                        waits=[("sv", prev)] if c >= 2 else [("sa", bc_done)])
            l2 = ph.dma(xb[c % 2].ap()[:], d["xsp"].ap()[c])
            w = [l1, l2, ("sa", bc_done)]
            ph.op("vector", lambda e, c=c: e.tensor_sub(
                ut.ap()[:], ob[c % 2].ap()[:], ctx.sb["mb"].ap()[:]), waits=w)
            ph.op("vector", lambda e: e.tensor_mul(ut.ap()[:], ut.ap()[:],
                                                   ctx.sb["rstdb"].ap()[:]))
            ph.op("vector", lambda e, c=c: e.tensor_add(
                st_c(ctx, c), ut.ap()[:], xb[c % 2].ap()[:]), inc="sv")
            prev = ctx.C["sv"]
        ph.op("sync", None, waits=[("sv", prev)] + ctx.dma_all())
        ph.run()
        del ctx.sb["stat_sq"]


def emit_output(ctx):
    ph = Phase(ctx, "output")
    d = ctx.dram
    last = 0
    for c in range(KC):
        last = ph.dma(d["yT"].ap()[128 * c:128 * (c + 1), :], st_c(ctx, c))
    ph.op("sync", None, waits=ctx.dma_all())
    ph.run()



def emit_proj_stream(ctx, name, ocs, wdram_of, evac, nkc=KC, rhs_hi=None, rhs_lo=None,
                     nwf=2, nws=2):
    nc = ctx.nc
    with ExitStack() as es:
        wf = [es.enter_context(sbt(nc, f"{name}_wf{i}", [128, nkc * 128], F32))
              for i in range(nwf)]
        whi = [es.enter_context(sbt(nc, f"{name}_wh{i}", [128, nkc * 128], BF16))
               for i in range(nws)]
        wlo = [es.enter_context(sbt(nc, f"{name}_wl{i}", [128, nkc * 128], BF16))
               for i in range(nws)]
        ph = Phase(ctx, name)
        if rhs_hi is None:
            rhs_hi = lambda kc, tt: ar_hi(ctx, kc)[:, 512 * tt:512 * (tt + 1)]
            rhs_lo = lambda kc, tt: ar_lo(ctx, kc)[:, 512 * tt:512 * (tt + 1)]
        free = [(0, 0)] * (len(ocs) + 2)
        split_done = [0] * (len(ocs) + 1)
        ev = evac(ph)
        for j, oc in enumerate(ocs):
            wda = wdram_of(oc)
            ld = ph.dma(wf[j % nwf].ap()[:], wda,
                        waits=[("sg", split_done[j - nwf + 1])] if j >= nwf else ())
            wsp = [ld]
            if j >= nws:
                wsp += [("sa", free[j - nws + 1][0]), ("sv", free[j - nws + 1][1])]
            ph.op("scalar", lambda e, j=j: e.copy(whi[j % nws].ap()[:], wf[j % nwf].ap()[:]),
                  waits=wsp, inc="sa")
            ad = ctx.C["sa"]
            ph.op("gpsimd", lambda e, j=j: e.tensor_sub(
                wlo[j % nws].ap()[:], wf[j % nwf].ap()[:], whi[j % nws].ap()[:]),
                waits=[("sa", ad)], inc="sg")
            split_done[j + 1] = ctx.C["sg"]
            sa_f = sv_f = 0
            for tt in range(4):
                bank = (j % 2) * 4 + tt
                w = [("sg", split_done[j + 1])] if tt == 0 else []
                if j >= 2:
                    w += [("sa", free[j - 1][0]), ("sv", free[j - 1][1])]
                first = True
                for kc in range(nkc):
                    for (wa, ra) in ((whi, rhs_hi), (whi, rhs_lo), (wlo, rhs_hi)):
                        ph.op("tensor", lambda e, j=j, tt=tt, kc=kc, wa=wa, ra=ra,
                              first=first, bank=bank: e.matmul(
                                  ctx.pb[bank].ap()[:, :],
                                  wa[j % nws].ap()[:, 128 * kc:128 * (kc + 1)],
                                  ra(kc, tt), start=first,
                                  stop=(kc == nkc - 1 and wa is wlo)),
                              waits=w if first else ())
                        w = []
                        first = False
                ph.q["tensor"][-1] = (ph.q["tensor"][-1][0], ph.q["tensor"][-1][1], "sp", 1)
                ctx.bump("sp")
                mmd = ctx.C["sp"]
                sa_f, sv_f = ev(j, oc, tt, bank, mmd)
            free[j + 1] = (sa_f, sv_f)
        ph.op("sync", None, waits=ctx.dma_all() + [("sa", ctx.C["sa"]),
                                                   ("sv", ctx.C["sv"])])
        ph.run()



def emit_l0_qkv_qk(ctx):
    d = ctx.dram
    nc = ctx.nc
    with ExitStack() as es:
        ohi = [es.enter_context(sbt(nc, f"qoh{i}", [128, T], BF16)) for i in range(2)]
        olo = [es.enter_context(sbt(nc, f"qol{i}", [128, T], BF16)) for i in range(2)]
        bql = es.enter_context(sbt(nc, "bql", [128, 48], F32))
        bq_done = [0]

        def evac(ph):
            bq_done[0] = ph.dma(bql.ap()[:], d["bqkv_l"].ap())
            st_done = [None] * 40

            def ev(j, oc, tt, bank, mmd):
                w = [("sp", mmd)]
                if tt == 0:
                    w.append(bq_done[0])
                    if j >= 2:
                        w.extend(st_done[j - 2])
                ph.op("scalar", lambda e, j=j, oc=oc, tt=tt, bank=bank: e.activation(
                    ohi[j % 2].ap()[:, 512 * tt:512 * (tt + 1)], ctx.pb[bank].ap()[:, :],
                    AF.Identity, bias=bql.ap()[:, oc:oc + 1]), waits=w, inc="sa")
                ad = ctx.C["sa"]
                ph.op("vector", lambda e, j=j, oc=oc, tt=tt, bank=bank: e.scalar_tensor_tensor(
                    olo[j % 2].ap()[:, 512 * tt:512 * (tt + 1)], ctx.pb[bank].ap()[:, :],
                    bql.ap()[:, oc:oc + 1], ohi[j % 2].ap()[:, 512 * tt:512 * (tt + 1)],
                    ALU.add, ALU.subtract), waits=[("sa", ad)], inc="sv")
                vd = ctx.C["sv"]
                if tt == 3:
                    hi_tok = ph.dma(d["qk_hi"].ap()[oc], ohi[j % 2].ap()[:],
                                    waits=[("sv", vd)])
                    lo_tok = ph.dma(d["qk_lo"].ap()[oc], olo[j % 2].ap()[:])
                    st_done[j] = (hi_tok, lo_tok)
                return ad, vd
            return ev

        emit_proj_stream(ctx, "l0qk", list(range(32)), lambda oc: d["wqkv"].ap()[oc], evac)


def emit_v(ctx, name, wd_of, use_bias):
    d = ctx.dram
    nc = ctx.nc
    with ExitStack() as es:
        wf = es.enter_context(sbt(nc, f"{name}_wf", [128, 2048], F32))
        vwh = es.enter_context(sbt(nc, f"{name}_vwh", [128, 4 * 2048], BF16))
        vwl = es.enter_context(sbt(nc, f"{name}_vwl", [128, 4 * 2048], BF16))
        ub = [es.enter_context(sbt(nc, f"{name}_ub{i}", [128, 512], F32))
              for i in range(2)]
        vh = [es.enter_context(sbt(nc, f"{name}_vh{i}", [128, 512], BF16)) for i in range(2)]
        vl = [es.enter_context(sbt(nc, f"{name}_vl{i}", [128, 512], BF16)) for i in range(2)]
        vbb = ctx.sb["mb"]

        ph = Phase(ctx, name)
        ones = ctx.sb["c_ones"]
        bc_done = 0
        if use_bias:
            bl = ph.dma(rows0(ctx), d["vbias_r"].ap())
            for tt in range(4):
                ph.op("tensor", lambda e, tt=tt: e.matmul(
                    ctx.pb[4 + tt].ap()[:, :], ones.ap()[0:1, :],
                    rows0(ctx)[:, 512 * tt:512 * (tt + 1)], start=True, stop=True),
                    waits=[bl] if tt == 0 else (), inc="sp")
                bd = ctx.C["sp"]
                ph.op("scalar", lambda e, tt=tt: e.copy(
                    vbb.ap()[:, 512 * tt:512 * (tt + 1)], ctx.pb[4 + tt].ap()[:, :]),
                    waits=[("sp", bd)], inc="sa")
                bc_done = ctx.C["sa"]
        vwh3 = vwh.ap().rearrange("p (o f) -> p o f", f=2048)
        vwl3 = vwl.ap().rearrange("p (o f) -> p o f", f=2048)
        ev_free = [(0, 0, 0)] * 20
        vh_st = [None, None]
        vl_st = [None, None]
        sg_d = 0
        for vg in range(4):
            for i in range(4):
                ld = ph.dma(wf.ap()[:], wd_of(vg, i),
                            waits=[("sg", sg_d)] if (vg or i) else ())
                ph.op("scalar", lambda e, i=i: e.copy(
                    vwh.ap()[:, 2048 * i:2048 * (i + 1)], wf.ap()[:]),
                    waits=[ld], inc="sa")
                ad = ctx.C["sa"]
                ph.op("gpsimd", lambda e, i=i: e.tensor_sub(
                    vwl.ap()[:, 2048 * i:2048 * (i + 1)], wf.ap()[:],
                    vwh.ap()[:, 2048 * i:2048 * (i + 1)]),
                    waits=[("sa", ad)], inc="sg")
                sg_d = ctx.C["sg"]
            wsplit = sg_d
            for tc in range(16):
                bank = tc % 2
                w = [("sg", wsplit)] if tc == 0 else []
                if tc >= 2:
                    sa_f, sv_f, sg_f = ev_free[tc - 2]
                    w += [("sa", sa_f), ("sv", sv_f), ("sg", sg_f)]
                first = True
                for kc in range(KC):
                    for (xa, wa3) in ((ar_hi, vwh3), (ar_lo, vwh3), (ar_hi, vwl3)):
                        ph.op("tensor", lambda e, tc=tc, kc=kc, xa=xa, wa3=wa3,
                              first=first, bank=bank: e.matmul(
                                  ctx.pb[bank].ap()[:, :],
                                  xa(ctx, kc)[:, 128 * tc:128 * (tc + 1)],
                                  wa3[:, :, 128 * kc:128 * (kc + 1)],
                                  start=first,
                                  stop=(kc == KC - 1 and xa is ar_hi and wa3 is vwl3)),
                              waits=w if first else ())
                        w = []
                        first = False
                ph.q["tensor"][-1] = (ph.q["tensor"][-1][0], ph.q["tensor"][-1][1], "sp", 1)
                ctx.bump("sp")
                mmd = ctx.C["sp"]
                if use_bias:
                    wv = [("sp", mmd), ("sa", bc_done)]
                    if tc >= 2:
                        wv.append(("sg", ev_free[tc - 2][2]))
                    ph.op("vector", lambda e, tc=tc, bank=bank, vg=vg: e.tensor_add(
                        ub[tc % 2].ap()[:], ctx.pb[bank].ap()[:, :],
                        vbb.ap()[:, 512 * vg:512 * (vg + 1)]),
                        waits=wv, inc="sv")
                    vd = ctx.C["sv"]
                    wv2 = [("sv", vd), vh_st[tc % 2]]
                    ph.op("scalar", lambda e, tc=tc: e.copy(vh[tc % 2].ap()[:],
                                                            ub[tc % 2].ap()[:]),
                          waits=wv2, inc="sa")
                    ad = ctx.C["sa"]
                    ph.op("gpsimd", lambda e, tc=tc: e.tensor_sub(
                        vl[tc % 2].ap()[:], ub[tc % 2].ap()[:], vh[tc % 2].ap()[:]),
                        waits=[("sa", ad), vl_st[tc % 2]], inc="sg")
                    gd = ctx.C["sg"]
                else:
                    w2 = [("sp", mmd), vh_st[tc % 2]]
                    ph.op("scalar", lambda e, tc=tc, bank=bank: e.copy(
                        vh[tc % 2].ap()[:], ctx.pb[bank].ap()[:, :]),
                        waits=w2, inc="sa")
                    ad = ctx.C["sa"]
                    ph.op("vector", lambda e, tc=tc, bank=bank: e.tensor_sub(
                        vl[tc % 2].ap()[:], ctx.pb[bank].ap()[:, :], vh[tc % 2].ap()[:]),
                        waits=[("sa", ad), vl_st[tc % 2]], inc="sv")
                    gd = 0
                vd2 = ctx.C["sv"]
                w = [("sa", ctx.C["sa"]), ("sv", vd2)]
                if use_bias:
                    w.append(("sg", gd))
                vh_st[tc % 2] = ph.dma(
                    d["v_hi"].ap()[4 * vg:4 * (vg + 1), 128 * tc:128 * (tc + 1), :]
                    .rearrange("h t e -> t h e"),
                    vh[tc % 2].ap().rearrange("p (h e) -> p h e", e=128), waits=w)
                vl_st[tc % 2] = ph.dma(
                    d["v_lo"].ap()[4 * vg:4 * (vg + 1), 128 * tc:128 * (tc + 1), :]
                    .rearrange("h t e -> t h e"),
                    vl[tc % 2].ap().rearrange("p (h e) -> p h e", e=128))
                ev_free[tc] = (ctx.C["sa"], ctx.C["sv"], ctx.C["sg"])
            ph.op("sync", None, waits=ctx.dma_all())
        ph.run()


def emit_l0_lightning(ctx):
    d = ctx.dram
    nc = ctx.nc
    with ExitStack() as es:
        qh = es.enter_context(sbt(nc, "lt_qh", [128, T], BF16))
        ql = es.enter_context(sbt(nc, "lt_ql", [128, T], BF16))
        kh = es.enter_context(sbt(nc, "lt_kh", [128, T], BF16))
        kl = es.enter_context(sbt(nc, "lt_kl", [128, T], BF16))
        vbh = es.enter_context(sbt(nc, "lt_vh", [128, T], BF16))
        vbl = es.enter_context(sbt(nc, "lt_vl", [128, T], BF16))
        smf = es.enter_context(sbt(nc, "lt_smf", [128, 128], F32))
        smh = es.enter_context(sbt(nc, "lt_smh", [128, 128], BF16))
        sml = es.enter_context(sbt(nc, "lt_sml", [128, 128], BF16))
        kf = es.enter_context(sbt(nc, "lt_kf", [128, 128], F32))
        khh = es.enter_context(sbt(nc, "lt_khh", [128, 128], BF16))
        khl = es.enter_context(sbt(nc, "lt_khl", [128, 128], BF16))
        qf = es.enter_context(sbt(nc, "lt_qf", [128, 128], F32))
        qtf = es.enter_context(sbt(nc, "lt_qtf", [128, 128], F32))
        qth = es.enter_context(sbt(nc, "lt_qth", [128, 128], BF16))
        qtl = es.enter_context(sbt(nc, "lt_qtl", [128, 128], BF16))
        S = es.enter_context(sbt(nc, "lt_S", [128, 128], F32))
        Sh = es.enter_context(sbt(nc, "lt_Sh", [128, 128], BF16))
        Sl = es.enter_context(sbt(nc, "lt_Sl", [128, 128], BF16))
        dTile = es.enter_context(sbt(nc, "lt_dT", [128, 128], F32))
        decayT = es.enter_context(sbt(nc, "lt_decayT", [128, 128], F32))
        qdecb = es.enter_context(sbt(nc, "lt_qdecb", [128, 128], F32))
        kd2c = es.enter_context(sbt(nc, "lt_kd2c", [128, 1], F32))
        blkc = es.enter_context(sbt(nc, "lt_blkc", [128, 1], F32))
        sneg = es.enter_context(sbt(nc, "lt_sneg", [128, 16], F32))
        cdif = es.enter_context(sbt(nc, "lt_cdif", [128, 128], F32))
        ctri = es.enter_context(sbt(nc, "lt_ctri", [128, 128], F32))
        ciot = es.enter_context(sbt(nc, "lt_ciot", [128, 128], F32))
        cdcl = es.enter_context(sbt(nc, "lt_cdcl", [128, 2], F32))
        srow = es.enter_context(sbt(nc, "lt_srow", [1, 16], F32))

        ph = Phase(ctx, "lightning")
        ones = ctx.sb["c_ones"]
        cld = [ph.dma(cdif.ap()[:], d["diffD"].ap()),
               ph.dma(ctri.ap()[:], d["tri01"].ap()),
               ph.dma(ciot.ap()[:], d["iotaF"].ap()),
               ph.dma(cdcl.ap()[:], d["dcol"].ap())]
        ld = ph.dma(srow.ap()[:], d["slopes_r"].ap())
        ph.op("tensor", lambda e: e.matmul(ctx.pb[0].ap()[:, 0:16], ones.ap()[0:1, :],
                                           srow.ap()[:], start=True, stop=True),
              waits=[ld], inc="sp")
        pd = ctx.C["sp"]
        ph.op("scalar", lambda e: e.activation(sneg.ap()[:], ctx.pb[0].ap()[:, 0:16],
                                               AF.Identity, scale=-1.0),
              waits=[("sp", pd)], inc="sa")
        sneg_done = ctx.C["sa"]

        for h in range(NH):
            head_w = [("sa", sneg_done)] + cld if h == 0 else \
                [("sv", ctx.C["sv"]), ("sg", ctx.C["sg"]), ("sp", ctx.C["sp"])]
            ph.op("scalar", lambda e, h=h: e.activation(
                dTile.ap()[:], cdif.ap()[:], AF.Exp, scale=sneg.ap()[:, h:h + 1]),
                waits=head_w, inc="sa")
            ad = ctx.C["sa"]
            ph.op("vector", lambda e: e.tensor_mul(decayT.ap()[:], dTile.ap()[:],
                                                   ctri.ap()[:]),
                  waits=[("sa", ad)], inc="sv")
            dec_done = ctx.C["sv"]
            ph.op("scalar", lambda e, h=h: e.activation(
                qdecb.ap()[:], ciot.ap()[:], AF.Exp, scale=sneg.ap()[:, h:h + 1]))
            ph.op("scalar", lambda e, h=h: e.activation(
                kd2c.ap()[:], cdcl.ap()[:, 0:1], AF.Exp, scale=sneg.ap()[:, h:h + 1]))
            ph.op("scalar", lambda e, h=h: e.activation(
                blkc.ap()[:], cdcl.ap()[:, 1:2], AF.Exp, scale=sneg.ap()[:, h:h + 1]),
                inc="sa")
            dec2_done = ctx.C["sa"]
            ph.op("vector", lambda e: e.memset(S.ap()[:], 0.0),
                  waits=[("sa", dec2_done)], inc="sv")
            mem_done = ctx.C["sv"]
            ld_toks = [
                ph.dma(qh.ap()[:], d["qk_hi"].ap()[h]),
                ph.dma(ql.ap()[:], d["qk_lo"].ap()[h]),
                ph.dma(kh.ap()[:], d["qk_hi"].ap()[16 + h]),
                ph.dma(kl.ap()[:], d["qk_lo"].ap()[16 + h]),
                ph.dma(vbh.ap().rearrange("p (c e) -> p c e", e=128),
                       d["v_hi"].ap()[h].rearrange("(c p) e -> p c e", p=128)),
                ph.dma(vbl.ap().rearrange("p (c e) -> p c e", e=128),
                       d["v_lo"].ap()[h].rearrange("(c p) e -> p c e", p=128)),
            ]
            Ssplit = (0, 0)
            pe = {}
            for c in range(16):
                cs = slice(128 * c, 128 * (c + 1))
                bs = (c % 2) * 4
                w = ld_toks + [("sv", mem_done), ("sa", dec2_done)] if c == 0 else \
                    [("sv", pe.get(("smf", c - 2), 0))]
                for i, (wa, ra) in enumerate(((kh, qh), (kh, ql), (kl, qh))):
                    ph.op("tensor", lambda e, wa=wa, ra=ra, cs=cs, i=i, bs=bs: e.matmul(
                        ctx.pb[bs].ap()[:, 0:128], wa.ap()[:, cs], ra.ap()[:, cs],
                        start=(i == 0), stop=(i == 2)),
                        waits=w if i == 0 else ())
                    w = []
                ph.q["tensor"][-1] = (ph.q["tensor"][-1][0], ph.q["tensor"][-1][1], "sp", 1)
                ctx.bump("sp")
                sc_done = ctx.C["sp"]
                ph.op("vector", lambda e, bs=bs: e.tensor_mul(
                    smf.ap()[:], ctx.pb[bs].ap()[:, 0:128], decayT.ap()[:]),
                    waits=[("sp", sc_done), ("sa", pe.get(("smh", c - 1), 0))], inc="sv")
                pe[("smf", c)] = ctx.C["sv"]
                ph.op("scalar", lambda e: e.copy(smh.ap()[:], smf.ap()[:]),
                      waits=[("sv", pe[("smf", c)]), ("sp", pe.get(("o", c - 1), 0))],
                      inc="sa")
                pe[("smh", c)] = ctx.C["sa"]
                ph.op("vector", lambda e: e.tensor_sub(sml.ap()[:], smf.ap()[:], smh.ap()[:]),
                      waits=[("sa", pe[("smh", c)]), ("sp", pe.get(("o", c - 1), 0))],
                      inc="sv")
                sm_done = ctx.C["sv"]
                ph.op("vector", lambda e, cs=cs: e.tensor_add(
                    kf.ap()[:], kh.ap()[:, cs], kl.ap()[:, cs]),
                    waits=[("sp", pe.get(("ktr", c - 1), 0))], inc="sv")
                kf_done = ctx.C["sv"]
                ph.op("tensor", lambda e, bs=bs: e.transpose(
                    ctx.pb[bs + 1].ap()[:, 0:128], kf.ap()[:], ctx.sb["c_ident"].ap()[:]),
                    waits=[("sv", kf_done), ("sa", pe.get(("khh", c - 2), 0)),
                           ("sv", pe.get(("khl", c - 2), 0))], inc="sp")
                pe[("ktr", c)] = ctx.C["sp"]
                ph.op("scalar", lambda e, bs=bs: e.activation(
                    khh.ap()[:], ctx.pb[bs + 1].ap()[:, 0:128], AF.Identity,
                    scale=kd2c.ap()[:, 0:1]),
                    waits=[("sp", pe[("ktr", c)]), ("sp", pe.get(("su", c - 1), 0))],
                    inc="sa")
                pe[("khh", c)] = ctx.C["sa"]
                ph.op("vector", lambda e, bs=bs: e.scalar_tensor_tensor(
                    khl.ap()[:], ctx.pb[bs + 1].ap()[:, 0:128], kd2c.ap()[:, 0:1],
                    khh.ap()[:], ALU.mult, ALU.subtract),
                    waits=[("sa", pe[("khh", c)]), ("sp", pe.get(("su", c - 1), 0))],
                    inc="sv")
                pe[("khl", c)] = ctx.C["sv"]
                khat_done = ctx.C["sv"]
                wq = [("sa", pe.get(("qth", c - 1), 0)), ("sg", pe.get(("qtl", c - 1), 0))]
                if c == 0:
                    wq.append(("sa", dec2_done))
                ph.op("vector", lambda e, cs=cs: e.tensor_add(
                    qf.ap()[:], qh.ap()[:, cs], ql.ap()[:, cs]), waits=wq)
                ph.op("vector", lambda e: e.tensor_mul(qtf.ap()[:], qf.ap()[:],
                                                       qdecb.ap()[:]), inc="sv")
                qtf_done = ctx.C["sv"]
                ph.op("scalar", lambda e: e.copy(qth.ap()[:], qtf.ap()[:]),
                      waits=[("sv", qtf_done), ("sp", pe.get(("o", c - 1), 0))], inc="sa")
                pe[("qth", c)] = ctx.C["sa"]
                ph.op("gpsimd", lambda e: e.tensor_sub(qtl.ap()[:], qtf.ap()[:], qth.ap()[:]),
                      waits=[("sa", pe[("qth", c)]), ("sp", pe.get(("o", c - 1), 0))],
                      inc="sg")
                pe[("qtl", c)] = ctx.C["sg"]
                qt_done = ctx.C["sg"]
                mms = [(vbh, smh), (vbh, sml), (vbl, smh)]
                if c > 0:
                    mms += [(Sh, qth), (Sh, qtl), (Sl, qth)]
                w = [("sv", sm_done), ("sg", qt_done),
                     ("sa", pe.get(("oh", c - 2), 0)), ("sv", pe.get(("ol", c - 2), 0))]
                if c > 0:
                    w += [("sa", Ssplit[0]), ("sv", Ssplit[1])]
                for i, (wa, ra) in enumerate(mms):
                    def mmfn(e, wa=wa, ra=ra, i=i, cs=cs, bs=bs, n=len(mms)):
                        lhs = wa.ap()[:, cs] if wa in (vbh, vbl) else wa.ap()[:]
                        return e.matmul(ctx.pb[bs + 2].ap()[:, 0:128], lhs, ra.ap()[:],
                                        start=(i == 0), stop=(i == n - 1))
                    ph.op("tensor", mmfn, waits=w if i == 0 else ())
                    w = []
                ph.q["tensor"][-1] = (ph.q["tensor"][-1][0], ph.q["tensor"][-1][1], "sp", 1)
                ctx.bump("sp")
                pe[("o", c)] = ctx.C["sp"]
                o_done = ctx.C["sp"]
                ph.op("scalar", lambda e, bs=bs, h=h, cs=cs: e.copy(
                    ar_hi(ctx, h)[:, cs], ctx.pb[bs + 2].ap()[:, 0:128]),
                    waits=[("sp", o_done)], inc="sa")
                pe[("oh", c)] = ctx.C["sa"]
                ph.op("vector", lambda e, bs=bs, h=h, cs=cs: e.tensor_sub(
                    ar_lo(ctx, h)[:, cs], ctx.pb[bs + 2].ap()[:, 0:128], ar_hi(ctx, h)[:, cs]),
                    waits=[("sa", pe[("oh", c)])], inc="sv")
                pe[("ol", c)] = ctx.C["sv"]
                w = [("sv", khat_done), ("sv", pe.get(("sustt", c - 2), 0))]
                for i, (wa, ra) in enumerate(((khh, vbh), (khh, vbl), (khl, vbh))):
                    ph.op("tensor", lambda e, wa=wa, ra=ra, i=i, cs=cs, bs=bs: e.matmul(
                        ctx.pb[bs + 3].ap()[:, 0:128], wa.ap()[:], ra.ap()[:, cs],
                        start=(i == 0), stop=(i == 2)),
                        waits=w if i == 0 else ())
                    w = []
                ph.q["tensor"][-1] = (ph.q["tensor"][-1][0], ph.q["tensor"][-1][1], "sp", 1)
                ctx.bump("sp")
                pe[("su", c)] = ctx.C["sp"]
                su_done = ctx.C["sp"]
                ph.op("vector", lambda e, bs=bs: e.scalar_tensor_tensor(
                    S.ap()[:], S.ap()[:], blkc.ap()[:, 0:1], ctx.pb[bs + 3].ap()[:, 0:128],
                    ALU.mult, ALU.add),
                    waits=[("sp", su_done), ("sa", pe.get(("Sh", c - 1), 0))], inc="sv")
                pe[("sustt", c)] = ctx.C["sv"]
                ph.op("scalar", lambda e: e.copy(Sh.ap()[:], S.ap()[:]),
                      waits=[("sv", pe[("sustt", c)])], inc="sa")
                pe[("Sh", c)] = ctx.C["sa"]
                ph.op("vector", lambda e: e.tensor_sub(Sl.ap()[:], S.ap()[:], Sh.ap()[:]),
                      waits=[("sa", pe[("Sh", c)])], inc="sv")
                Ssplit = (pe[("Sh", c)], ctx.C["sv"])
            ph.op("sync", None, waits=[("sp", ctx.C["sp"]), ("sv", ctx.C["sv"]),
                                       ("sa", ctx.C["sa"]), ("sg", ctx.C["sg"])])
        ph.run()


def emit_l0_outproj(ctx):
    d = ctx.dram
    nc = ctx.nc
    with ExitStack() as es:
        ot = [es.enter_context(sbt(nc, f"op_ot{i}", [128, 512], F32)) for i in range(4)]
        bol = es.enter_context(sbt(nc, "op_bol", [128, 16], F32))
        b_done = [0]

        def evac(ph):
            b_done[0] = ph.dma(bol.ap()[:], d["bout_l"].ap())
            st_done = [None] * 70

            def ev(j, oc, tt, bank, mmd):
                ti = (j * 4 + tt) % 4
                w = [("sp", mmd), st_done[max(0, j * 4 + tt - 4)]]
                if j == 0 and tt == 0:
                    w.append(b_done[0])
                ph.op("scalar", lambda e, ti=ti, bank=bank, oc=oc: e.activation(
                    ot[ti].ap()[:], ctx.pb[bank].ap()[:, :], AF.Identity,
                    bias=bol.ap()[:, oc:oc + 1]), waits=w, inc="sa")
                ad = ctx.C["sa"]
                st_done[j * 4 + tt] = ph.dma(
                    d["sub_out"].ap()[oc][:, 512 * tt:512 * (tt + 1)], ot[ti].ap()[:],
                    waits=[("sa", ad)])
                return ad, 0
            return ev

        emit_proj_stream(ctx, "l0op", list(range(16)), lambda oc: d["wout"].ap()[oc], evac)


def emit_l0_attn(ctx):
    emit_pre_ln(ctx, split=True, eps=EPS_LN)
    emit_l0_qkv_qk(ctx)
    emit_v(ctx, "l0v", lambda vg, i: ctx.dram["wqkv"].ap()[32 + 4 * vg + i], use_bias=True)
    emit_l0_lightning(ctx)
    emit_l0_outproj(ctx)
    emit_post_ln(ctx, eps=EPS_LN)



def emit_quant_chain(ctx, ph, wf_ap, wbf, qr, scl, rcp, width, ld_wait, pool_guard):
    ng = width // 64
    ph.op("scalar", lambda e: e.copy(wbf.ap()[:, 0:width], wf_ap),
          waits=ld_wait, inc="sa")
    ad = ctx.C["sa"]
    ph.op("vector", lambda e: e.reduce_sum(
        rcp.ap()[:, 0:ng], wbf.ap()[:, 0:width].rearrange("p (g e) -> p g e", e=64),
        axis=AX.X, apply_absolute_value=True), waits=[("sa", ad)] + list(pool_guard))
    ph.op("vector", lambda e: e.tensor_scalar(
        scl.ap()[:, 0:ng], rcp.ap()[:, 0:ng], 1.0 / 64, 1e-8, ALU.mult, ALU.max))
    ph.op("vector", lambda e: e.reciprocal(rcp.ap()[:, 0:ng], scl.ap()[:, 0:ng]))
    sclb = scl.ap()[:, 0:ng].unsqueeze(2).broadcast_to([128, ng, 64])
    rcpb = rcp.ap()[:, 0:ng].unsqueeze(2).broadcast_to([128, ng, 64])
    w3 = wbf.ap()[:, 0:width].rearrange("p (g e) -> p g e", e=64)
    q3 = qr.ap()[:, 0:width].rearrange("p (g e) -> p g e", e=64)
    ph.op("vector", lambda e: e.tensor_tensor(q3, w3, rcpb, ALU.mult))
    if getattr(ctx, "dbg_quant", False):
        ph.op("vector", lambda e: e.engine_nop(), inc="sv")
        qv = ctx.C["sv"]
        ph.dma(ctx.dram["dbg"].ap()[:, 6144:6144 + width], qr.ap()[:, 0:width],
               waits=[("sv", qv)])
        ph.dma(ctx.dram["dbg"].ap()[:, 6144 - width // 64:6144],
               scl.ap()[:, 0:width // 64])
        ctx.dbg_quant = False
    ph.op("vector", lambda e: e.tensor_scalar(
        wbf.ap()[:, 0:width], qr.ap()[:, 0:width], -0.5, None, ALU.is_lt))
    ph.op("vector", lambda e: e.scalar_tensor_tensor(
        wbf.ap()[:, 0:width], qr.ap()[:, 0:width], 0.5, wbf.ap()[:, 0:width],
        ALU.is_gt, ALU.subtract))
    w3b = wbf.ap()[:, 0:width].rearrange("p (g e) -> p g e", e=64)
    ph.op("vector", lambda e: e.tensor_tensor(q3, w3b, sclb, ALU.mult), inc="sv")
    return ctx.C["sv"]


def emit_l0f_gu(ctx):
    d = ctx.dram
    nc = ctx.nc
    with ExitStack() as es:
        wf = [es.enter_context(sbt(nc, f"gu_wf{i}", [128, 2048], F32)) for i in range(2)]
        wbf = es.enter_context(sbt(nc, "gu_wbf", [128, 2048], BF16))
        qr = es.enter_context(sbt(nc, "gu_qr", [128, 2048], BF16))
        scl = es.enter_context(sbt(nc, "gu_scl", [128, 64], BF16))
        rcp = es.enter_context(sbt(nc, "gu_rcp", [128, 32], F32))
        whT = es.enter_context(sbt(nc, "gu_whT", [128, 2048], BF16))
        silu = es.enter_context(sbt(nc, "gu_silu", [128, 2048], F32))
        hsb = [es.enter_context(sbt(nc, f"gu_h{i}", [128, 2048], F32)) for i in range(2)]
        hsq = es.enter_context(sbt(nc, "gu_hsq", [128, 512], F32))
        ones = ctx.sb["c_ones"]

        ph = Phase(ctx, "l0f_gu")
        prev = {}
        for pi in range(64):
            hstore = 0
            for half, ch in ((0, pi), (1, 64 + pi)):
                j = pi * 2 + half
                ld = ph.dma(wf[j % 2].ap()[:], d["gup_nat"].ap()[ch],
                            waits=[("sa", prev.get(("wbf", j - 2), 0))])
                qd = emit_quant_chain(ctx, ph, wf[j % 2].ap()[:], wbf, qr, scl, rcp,
                                      2048, [ld, ("sv", prev.get(("tev", j - 1), 0))],
                                      [("sv", 0)])
                prev[("wbf", j)] = ctx.C["sa"]
                tevs = {2: prev.get(("tev", j - 1), 0),
                        3: prev.get(("tev", j - 1), 0)}
                for kc in range(16):
                    bnk = 2 + kc % 2
                    wt = [("sv", qd)] if kc == 0 else []
                    wt.append(("sv", tevs[bnk]))
                    ph.op("tensor", lambda e, kc=kc, bnk=bnk: e.transpose(
                        ctx.pbb[bnk].ap()[:, 0:128],
                        qr.ap()[:, 128 * kc:128 * (kc + 1)], ctx.sb["c_identb"].ap()[:]),
                        waits=wt, inc="sp")
                    td = ctx.C["sp"]
                    ph.op("vector", lambda e, kc=kc, bnk=bnk: e.tensor_copy(
                        whT.ap()[:, 128 * kc:128 * (kc + 1)], ctx.pbb[bnk].ap()[:, 0:128]),
                        waits=[("sp", td)], inc="sv")
                    tevs[bnk] = ctx.C["sv"]
                prev[("tev", j)] = ctx.C["sv"]
                tev = ctx.C["sv"]
                for tt in range(4):
                    w = [("sv", tev)] if tt == 0 else []
                    if tt >= 1:
                        w.append(("sa" if half == 0 else "sv", prev.get(("ev", half, tt - 1, pi), 0)))
                    else:
                        w.append(("sa" if half == 0 else "sv", prev.get(("ev", half, 3, pi - 1), 0)))
                    first = True
                    for kc in range(KC):
                        for xa in (ar_hi, ar_lo):
                            ph.op("tensor", lambda e, tt=tt, kc=kc, xa=xa, first=first,
                                  half=half: e.matmul(
                                      ctx.pb[half].ap()[:, :],
                                      whT.ap()[:, 128 * kc:128 * (kc + 1)],
                                      xa(ctx, kc)[:, 512 * tt:512 * (tt + 1)],
                                      start=first, stop=(kc == KC - 1 and xa is ar_lo)),
                                  waits=w if first else ())
                            w = []
                            first = False
                    ph.q["tensor"][-1] = (ph.q["tensor"][-1][0], ph.q["tensor"][-1][1], "sp", 1)
                    ctx.bump("sp")
                    mmd = ctx.C["sp"]
                    if half == 0:
                        ph.op("scalar", lambda e, tt=tt: e.activation(
                            silu.ap()[:, 512 * tt:512 * (tt + 1)], ctx.pb[0].ap()[:, :],
                            AF.Silu),
                            waits=[("sp", mmd), ("sv", prev.get(("ev", 1, tt, pi - 1), 0))],
                            inc="sa")
                        prev[("ev", 0, tt, pi)] = ctx.C["sa"]
                    else:
                        wv = [("sp", mmd), ("sa", prev[("ev", 0, tt, pi)]),
                              prev.get(("hst", pi - 2), None)]
                        ph.op("vector", lambda e, tt=tt, pi=pi: e.tensor_mul(
                            hsb[pi % 2].ap()[:, 512 * tt:512 * (tt + 1)],
                            ctx.pb[1].ap()[:, :], silu.ap()[:, 512 * tt:512 * (tt + 1)]),
                            waits=wv, inc="sv")
                        prev[("ev", 1, tt, pi)] = ctx.C["sv"]
                        hd = ctx.C["sv"]
                        ph.op("scalar", lambda e, tt=tt, pi=pi: e.activation(
                            hsq.ap()[:], hsb[pi % 2].ap()[:, 512 * tt:512 * (tt + 1)],
                            AF.Square), waits=[("sv", hd), ("sp", prev.get("sqlast", 0))],
                            inc="sa")
                        sqd = ctx.C["sa"]
                        ph.op("tensor", lambda e, tt=tt, pi=pi: e.matmul(
                            ctx.pb[4 + tt].ap()[0:1, :], ones.ap()[:, 0:1], hsq.ap()[:],
                            start=(pi == 0), stop=(pi == 63)),
                            waits=[("sa", sqd)], inc="sp")
                        prev["sqlast"] = ctx.C["sp"]
            prev[("hst", pi)] = ph.dma(d["h_f32"].ap()[pi], hsb[pi % 2].ap()[:],
                                       waits=[("sv", prev[("ev", 1, 3, pi)])])
        ph.op("sync", None, waits=ctx.dma_all() + [("sp", ctx.C["sp"]),
                                   ("sv", ctx.C["sv"]), ("sa", ctx.C["sa"])])
        ph.run()


def emit_l0f_hn(ctx):
    d = ctx.dram
    nc = ctx.nc
    ones = ctx.sb["c_ones"]
    with ExitStack() as es:
        hb = [es.enter_context(sbt(nc, f"hn_hb{i}", [128, T], F32)) for i in range(2)]
        hnf = es.enter_context(sbt(nc, "hn_f", [128, T], F32))
        hh = [es.enter_context(sbt(nc, f"hn_hh{i}", [128, T], BF16)) for i in range(2)]
        hl = [es.enter_context(sbt(nc, f"hn_hl{i}", [128, T], BF16)) for i in range(2)]
        ph = Phase(ctx, "l0f_hn")
        for tt in range(4):
            ph.op("vector", lambda e, tt=tt: e.tensor_scalar(
                rows0(ctx)[:, 512 * tt:512 * (tt + 1)], ctx.pb[4 + tt].ap()[0:1, :],
                1.0 / 8192.0, None, ALU.mult), inc="sv" if tt == 3 else None)
        md = ctx.C["sv"]
        ph.op("scalar", lambda e: e.activation(
            rsc(ctx), rows0(ctx), AF.Sqrt,
            bias=ctx.sb["c_cvals"].ap()[0:1, 1:2]), waits=[("sv", md)], inc="sa")
        sd = ctx.C["sa"]
        ph.op("vector", lambda e: e.reciprocal(rows0(ctx), rsc(ctx)),
              waits=[("sa", sd)], inc="sv")
        rd = ctx.C["sv"]
        rb = 0
        for tt in range(4):
            ph.op("tensor", lambda e, tt=tt: e.matmul(
                ctx.pb[tt % 2].ap()[:, :], ones.ap()[0:1, :],
                rows0(ctx)[:, 512 * tt:512 * (tt + 1)], start=True, stop=True),
                waits=([("sv", rd)] if tt == 0 else
                       ([("sa", rb - 1)] if tt >= 2 else ())), inc="sp")
            bd = ctx.C["sp"]
            ph.op("scalar", lambda e, tt=tt: e.copy(
                ctx.sb["rstdb"].ap()[:, 512 * tt:512 * (tt + 1)], ctx.pb[tt % 2].ap()[:, :]),
                waits=[("sp", bd)], inc="sa")
            rb = ctx.C["sa"]
        bc = rb
        prev = {}
        for ch in range(64):
            ld = ph.dma(hb[ch % 2].ap()[:], d["h_f32"].ap()[ch],
                        waits=[("sv", prev.get(("hnf", ch - 2), 0))])
            ph.op("vector", lambda e, ch=ch: e.tensor_mul(
                hnf.ap()[:], hb[ch % 2].ap()[:], ctx.sb["rstdb"].ap()[:]),
                waits=[ld, ("sa", bc), ("sg", prev.get(("lo", ch - 1), 0))], inc="sv")
            prev[("hnf", ch)] = ctx.C["sv"]
            ph.op("scalar", lambda e, ch=ch: e.copy(hh[ch % 2].ap()[:], hnf.ap()[:]),
                  waits=[("sv", ctx.C["sv"])] + list(prev.get(("st", ch - 2), ())),
                  inc="sa")
            ad = ctx.C["sa"]
            ph.op("gpsimd", lambda e, ch=ch: e.tensor_sub(
                hl[ch % 2].ap()[:], hnf.ap()[:], hh[ch % 2].ap()[:]),
                waits=[("sa", ad)], inc="sg")
            prev[("lo", ch)] = ctx.C["sg"]
            hi_tok = ph.dma(d["hn_hi"].ap()[ch], hh[ch % 2].ap()[:],
                            waits=[("sg", ctx.C["sg"])])
            lo_tok = ph.dma(d["hn_lo"].ap()[ch], hl[ch % 2].ap()[:])
            prev[("st", ch)] = (hi_tok, lo_tok)
        ph.op("sync", None, waits=ctx.dma_all())
        ph.run()


def emit_l0f_wpj_quant(ctx):
    d = ctx.dram
    nc = ctx.nc
    with ExitStack() as es:
        wf = es.enter_context(sbt(nc, "pq_wf", [128, 4096], F32))
        wbf = es.enter_context(sbt(nc, "pq_wbf", [128, 4096], BF16))
        qr = es.enter_context(sbt(nc, "pq_qr", [128, 4096], BF16))
        scl = es.enter_context(sbt(nc, "pq_scl", [128, 128], BF16))
        rcp = es.enter_context(sbt(nc, "pq_rcp", [128, 64], F32))
        wt = es.enter_context(sbt(nc, "pq_wt", [128, 8192], BF16))
        ph = Phase(ctx, "l0f_pq")
        prev = {}
        for ch in range(16):
            for hf in range(2):
                j = ch * 2 + hf
                ld = ph.dma(wf.ap()[:], d["wpj0_nat"].ap()[ch][:, 4096 * hf:4096 * (hf + 1)],
                            waits=[("sa", prev.get(("wbf", j - 1), 0))])
                tprev = [t for t in prev.get(("tev", j - 1), ()) if t]
                qd = emit_quant_chain(ctx, ph, wf.ap()[:], wbf, qr, scl, rcp, 4096,
                                      [ld] + tprev,
                                      [("sv", 0)])
                prev[("wbf", j)] = ctx.C["sa"]
                tevs = [prev.get(("tev", j - 1), (None,) * 8)[b] for b in range(8)]
                for kcH in range(32):
                    bnk = kcH % 8
                    eng = "scalar" if bnk < 4 else "vector"
                    sem = "sa" if bnk < 4 else "sv"
                    w = [("sv", qd)] if kcH == 0 else []
                    if tevs[bnk]:
                        w.append(tevs[bnk])
                    ph.op("tensor", lambda e, kcH=kcH, bnk=bnk: e.transpose(
                        ctx.pbb[bnk].ap()[:, 0:128],
                        qr.ap()[:, 128 * kcH:128 * (kcH + 1)], ctx.sb["c_identb"].ap()[:]),
                        waits=w, inc="sp")
                    td = ctx.C["sp"]
                    ev_w = [("sp", td)]
                    if kcH == 0 and hf == 0 and ("st", ch - 1) in prev:
                        ev_w.append(prev[("st", ch - 1)])
                    if eng == "scalar":
                        ph.op(eng, lambda e, kcH=kcH, bnk=bnk, hf=hf: e.copy(
                            wt.ap()[:, 4096 * hf + 128 * kcH:4096 * hf + 128 * (kcH + 1)],
                            ctx.pbb[bnk].ap()[:, 0:128]),
                            waits=ev_w, inc=sem)
                    else:
                        ph.op(eng, lambda e, kcH=kcH, bnk=bnk, hf=hf: e.tensor_copy(
                            wt.ap()[:, 4096 * hf + 128 * kcH:4096 * hf + 128 * (kcH + 1)],
                            ctx.pbb[bnk].ap()[:, 0:128]),
                            waits=ev_w, inc=sem)
                    tevs[bnk] = (sem, ctx.C[sem])
                prev[("tev", j)] = tuple(tevs)
            std = ph.dma(d["wpj0_t"].ap()[ch], wt.ap()[:],
                         waits=list(prev[("tev", ch * 2 + 1)]) +
                               [prev.get(("st", ch - 1), None)])
            prev[("st", ch)] = std
        ph.op("sync", None, waits=ctx.dma_all())
        ph.run()


def emit_ffn_proj(ctx, name, wt_dram, n_oc, sub_scale=None):
    d = ctx.dram
    nc = ctx.nc
    with ExitStack() as es:
        hh = [es.enter_context(sbt(nc, f"{name}_hh{i}", [128, 512], BF16))
              for i in range(3)]
        hl = [es.enter_context(sbt(nc, f"{name}_hl{i}", [128, 512], BF16))
              for i in range(3)]
        ot = [es.enter_context(sbt(nc, f"{name}_ot{i}", [128, 512], F32))
              for i in range(2)]
        ph = Phase(ctx, name)
        nog = n_oc // 8
        prev = {}
        arb = ctx.arena_b
        for og in range(nog):
            wlds = []
            for i in range(8):
                wlds.append(ph.dma(
                    arb.ap()[:, 8192 * i:8192 * (i + 1)], wt_dram.ap()[og * 8 + i],
                    waits=[("sp", prev.get(("mm", og - 1), 0))] if (og and i == 0) else []))
            for tt in range(4):
                for kcH in range(64):
                    jj = tt * 64 + kcH
                    bi = jj % 3
                    gw = [("sp", prev.get(("hn", og, jj - 3), 0))] if jj >= 3 else []
                    l1 = ph.dma(hh[bi].ap()[:], d["hn_hi"].ap()[kcH][:, 512 * tt:512 * (tt + 1)],
                                waits=gw)
                    l2 = ph.dma(hl[bi].ap()[:], d["hn_lo"].ap()[kcH][:, 512 * tt:512 * (tt + 1)])
                    w = [l1, l2]
                    if kcH == 0:
                        w += wlds
                        if tt >= 1:
                            w.append(("sa", prev.get(("ev", og, tt - 1), 0)))
                        elif og >= 1:
                            w.append(("sa", prev.get(("ev", og - 1, 3), 0)))
                    for oc8 in range(8):
                        for hx in (hh, hl):
                            ph.op("tensor", lambda e, oc8=oc8, hx=hx, bi=bi, kcH=kcH: e.matmul(
                                ctx.pb[oc8].ap()[:, :],
                                arb.ap()[:, 8192 * oc8 + 128 * kcH:8192 * oc8 + 128 * (kcH + 1)],
                                hx[bi].ap()[:], start=(kcH == 0 and hx is hh),
                                stop=(kcH == 63 and hx is hl)),
                                waits=w if (oc8 == 0 and hx is hh) else ())
                            w = []
                    ph.q["tensor"][-1] = (ph.q["tensor"][-1][0], ph.q["tensor"][-1][1], "sp", 1)
                    ctx.bump("sp")
                    prev[("hn", og, jj)] = ctx.C["sp"]
                mmd = ctx.C["sp"]
                prev[("mm", og)] = mmd
                for oc8 in range(8):
                    oc = og * 8 + oc8
                    w = [("sp", mmd), prev.get(("st", og, tt, oc8 - 2), None)]
                    if sub_scale is not None:
                        ph.op("scalar", lambda e, oc8=oc8: e.activation(
                            ot[oc8 % 2].ap()[:], ctx.pb[oc8].ap()[:, :], AF.Identity,
                            scale=sub_scale()), waits=w, inc="sa")
                    else:
                        ph.op("scalar", lambda e, oc8=oc8: e.copy(
                            ot[oc8 % 2].ap()[:], ctx.pb[oc8].ap()[:, :]), waits=w, inc="sa")
                    ad = ctx.C["sa"]
                    prev[("st", og, tt, oc8)] = ph.dma(
                        d["sub_out"].ap()[oc][:, 512 * tt:512 * (tt + 1)],
                        ot[oc8 % 2].ap()[:], waits=[("sa", ad)])
                prev[("ev", og, tt)] = ctx.C["sa"]
        ph.op("sync", None, waits=ctx.dma_all())
        ph.run()


def emit_l0_ffn(ctx):
    emit_pre_ln(ctx, split=True, eps=EPS_LN)
    emit_l0f_gu(ctx)
    emit_l0f_hn(ctx)
    emit_l0f_wpj_quant(ctx)
    emit_ffn_proj(ctx, "l0f_pj", ctx.dram["wpj0_t"], 16)
    emit_post_ln(ctx, eps=EPS_LN)



def emit_l1_qkv_qk(ctx):
    d = ctx.dram
    nc = ctx.nc
    with ExitStack() as es:
        qraw = es.enter_context(sbt(nc, "l1qraw", [128, T], F32))
        xsw = es.enter_context(sbt(nc, "l1xsw", [128, T], F32))
        tc = es.enter_context(sbt(nc, "l1tc", [128, T], F32))
        tsn = es.enter_context(sbt(nc, "l1ts", [128, T], F32))
        hi = [es.enter_context(sbt(nc, "l1hi", [128, 512], BF16)) for i in range(2)]
        lo = [es.enter_context(sbt(nc, "l1lo", [128, 512], BF16)) for i in range(2)]
        state = {"mode": None}

        def evac(ph):
            st_done = [None] * 40
            ev_state = {"tbl": None, "sw": 0}

            def ev(j, oc, tt, bank, mmd):
                is_q = oc < 16
                ts = slice(512 * tt, 512 * (tt + 1))
                if state["mode"] != is_q:
                    state["mode"] = is_q
                    cnm = "ropeQC" if is_q else "ropeKC"
                    snm = "ropeQS" if is_q else "ropeKS"
                    w = [("sv", ctx.C["sv"])]
                    tca = ph.dma(tc.ap()[:], d[cnm].ap(), waits=w)
                    ev_state["tbl"] = ph.dma(tsn.ap()[:], d[snm].ap(), waits=[tca])
                w = [("sp", mmd)]
                if tt == 0 and j >= 1:
                    w.append(st_done[j - 1])
                if tt == 0:
                    w.append(("sv", ev_state["sw"]))
                    w.append(("sg", ev_state.get("sg", 0)))
                ph.op("scalar", lambda e, tt=tt, bank=bank: e.copy(
                    qraw.ap()[:, ts], ctx.pb[bank].ap()[:, :]), waits=w, inc="sa")
                qev = ctx.C["sa"]
                if tt == 3:
                    bounce = d["sub_out"].ap()[0]
                    s1 = ph.dma(bounce[0:64, :], qraw.ap()[64:128, :],
                                waits=[("sa", qev)])
                    s1b = ph.dma(bounce[64:128, :], qraw.ap()[0:64, :], waits=[s1])
                    s2 = ph.dma(xsw.ap()[:], bounce, waits=[s1b])
                    ph.op("vector", lambda e: e.tensor_tensor(
                        qraw.ap()[:], qraw.ap()[:], tc.ap()[:], ALU.mult),
                        waits=[s2, ev_state["tbl"]])
                    ph.op("vector", lambda e: e.tensor_tensor(
                        xsw.ap()[:], xsw.ap()[:], tsn.ap()[:], ALU.mult))
                    ph.op("vector", lambda e: e.tensor_add(
                        qraw.ap()[:], qraw.ap()[:], xsw.ap()[:]), inc="sv")
                    ev_state["sw"] = ctx.C["sv"]
                    yd = ctx.C["sv"]
                    hi_st = ev_state.setdefault("hi_st", [None, None])
                    lo_st = ev_state.setdefault("lo_st", [None, None])
                    for t2 in range(4):
                        t2s = slice(512 * t2, 512 * (t2 + 1))
                        ph.op("scalar", lambda e, t2=t2, t2s=t2s: e.copy(
                            hi[t2 % 2].ap()[:], qraw.ap()[:, t2s]),
                            waits=[("sv", yd), hi_st[t2 % 2]], inc="sa")
                        ad = ctx.C["sa"]
                        ph.op("gpsimd", lambda e, t2=t2, t2s=t2s: e.tensor_sub(
                            lo[t2 % 2].ap()[:], qraw.ap()[:, t2s], hi[t2 % 2].ap()[:]),
                            waits=[("sa", ad), lo_st[t2 % 2]], inc="sg")
                        gd = ctx.C["sg"]
                        hi_st[t2 % 2] = ph.dma(d["qk_hi"].ap()[oc][:, t2s],
                                               hi[t2 % 2].ap()[:], waits=[("sg", gd)])
                        lo_st[t2 % 2] = ph.dma(d["qk_lo"].ap()[oc][:, t2s],
                                               lo[t2 % 2].ap()[:])
                    st_done[j] = lo_st[1]
                    ev_state["sg"] = ctx.C["sg"]
                return ctx.C["sa"], ctx.C["sv"]
            return ev

        emit_proj_stream(ctx, "l1qk", list(range(32)),
                         lambda oc: d["wq1"].ap()[oc] if oc < 16 else d["wk1"].ap()[oc - 16],
                         evac, nwf=1, nws=1)


def emit_l1_flash(ctx):
    d = ctx.dram
    nc = ctx.nc
    with ExitStack() as es:
        qh = es.enter_context(sbt(nc, "fl_qh", [128, T], BF16))
        ql = es.enter_context(sbt(nc, "fl_ql", [128, T], BF16))
        kh = es.enter_context(sbt(nc, "fl_kh", [128, T], BF16))
        kl = es.enter_context(sbt(nc, "fl_kl", [128, T], BF16))
        vbh = es.enter_context(sbt(nc, "fl_vh", [128, T], BF16))
        vbl = es.enter_context(sbt(nc, "fl_vl", [128, T], BF16))
        sco = es.enter_context(sbt(nc, "fl_sco", [128, T], F32))
        msk = es.enter_context(sbt(nc, "fl_msk", [128, T], BF16))
        pth = es.enter_context(sbt(nc, "fl_pth", [128, 16 * 256], BF16))
        ptl = es.enter_context(sbt(nc, "fl_ptl", [128, 16 * 256], BF16))
        acc = es.enter_context(sbt(nc, "fl_acc", [128, 8], F32))
        mneg = es.enter_context(sbt(nc, "fl_mn", [128, 2], F32))
        rcol = es.enter_context(sbt(nc, "fl_rc", [128, 2], F32))
        rr = es.enter_context(sbt(nc, "fl_rr", [2, 128], F32))
        rr2 = es.enter_context(sbt(nc, "fl_rr2", [1, 256], F32))
        pbuf = ctx.sb["mb"]
        rcb = ctx.sb["rstdb"]
        otile = ctx.sb["rstdb"].ap()[:, 256:512]
        ph = Phase(ctx, "l1flash")
        mld = ph.dma(msk.ap()[:], d["l1mask"].ap())
        prev = {}
        for h in range(NH):
            w0 = [("sp", ctx.C["sp"]), ("sv", ctx.C["sv"]), ("sa", ctx.C["sa"]),
                  ("sg", ctx.C["sg"])] if h else [mld]
            ph.op("sync", None, waits=w0)
            ld_toks = [
                ph.dma(qh.ap()[:], d["qk_hi"].ap()[h]),
                ph.dma(ql.ap()[:], d["qk_lo"].ap()[h]),
                ph.dma(kh.ap()[:], d["qk_hi"].ap()[16 + h]),
                ph.dma(kl.ap()[:], d["qk_lo"].ap()[16 + h]),
                ph.dma(vbh.ap().rearrange("p (c e) -> p c e", e=128),
                       d["v_hi"].ap()[h].rearrange("(c p) e -> p c e", p=128)),
                ph.dma(vbl.ap().rearrange("p (c e) -> p c e", e=128),
                       d["v_lo"].ap()[h].rearrange("(c p) e -> p c e", p=128)),
            ]
            for g2 in range(8):
                for iq in range(2):
                    qt = 2 * g2 + iq
                    nkt = qt // 4 + 1
                    qs = slice(128 * qt, 128 * (qt + 1))
                    for kt in range(nkt):
                        bank = kt % 2
                        w = list(ld_toks) if (g2 == 0 and iq == 0 and kt == 0) else \
                            [("sa", prev.get(("sev", bank), 0))]
                        for i, (wa, ra) in enumerate(((qh, kh), (qh, kl), (ql, kh))):
                            ph.op("tensor", lambda e, wa=wa, ra=ra, qs=qs, kt=kt, i=i,
                                  bank=bank: e.matmul(
                                      ctx.pb[bank].ap()[:, :], wa.ap()[:, qs],
                                      ra.ap()[:, 512 * kt:512 * (kt + 1)],
                                      start=(i == 0), stop=(i == 2)),
                                  waits=w if i == 0 else ())
                            w = []
                        ph.q["tensor"][-1] = (ph.q["tensor"][-1][0], ph.q["tensor"][-1][1],
                                              "sp", 1)
                        ctx.bump("sp")
                        smd = ctx.C["sp"]
                        wse = [("sp", smd)]
                        if kt == 0:
                            wse.append(("sv", prev.get("mnd", 0)))
                        ph.op("scalar", lambda e, kt=kt, bank=bank: e.copy(
                            sco.ap()[:, 512 * kt:512 * (kt + 1)], ctx.pb[bank].ap()[:, :]),
                            waits=wse, inc="sa")
                        prev[("sev", bank)] = ctx.C["sa"]
                    sc_done = ctx.C["sa"]
                    r = qt % 4
                    ph.op("vector", lambda e, r=r, nkt=nkt: e.tensor_tensor(
                        sco.ap()[:, 512 * (nkt - 1):512 * nkt],
                        sco.ap()[:, 512 * (nkt - 1):512 * nkt],
                        msk.ap()[:, 512 * r:512 * (r + 1)], ALU.add),
                        waits=[("sa", sc_done)])
                    ph.op("vector", lambda e, nkt=nkt, iq=iq: e.reduce_max(
                        mneg.ap()[:, iq:iq + 1], sco.ap()[:, 0:512 * nkt], axis=AX.X),
                        waits=[("sa", prev.get(("exd", iq), 0))])
                    ph.op("vector", lambda e, iq=iq: e.tensor_scalar(
                        mneg.ap()[:, iq:iq + 1], mneg.ap()[:, iq:iq + 1], -1.0, None,
                        ALU.mult), inc="sv")
                    mn_done = ctx.C["sv"]
                    prev["mnd"] = mn_done
                    for kt in range(nkt):
                        wex = [("sv", mn_done), ("sp", prev.get("ptd", 0))] if kt == 0 else ()
                        ph.op("scalar", lambda e, kt=kt, iq=iq: e.activation(
                            pbuf.ap()[:, 512 * kt:512 * (kt + 1)],
                            sco.ap()[:, 512 * kt:512 * (kt + 1)], AF.Exp,
                            bias=mneg.ap()[:, iq:iq + 1],
                            accum_out=acc.ap()[:, 4 * iq + kt:4 * iq + kt + 1]),
                            waits=wex, inc="sa")
                    ex_done = ctx.C["sa"]
                    prev[("exd", iq)] = ex_done
                    ph.op("vector", lambda e, nkt=nkt, iq=iq: e.reduce_sum(
                        rcol.ap()[:, iq:iq + 1], acc.ap()[:, 4 * iq:4 * iq + nkt], axis=AX.X),
                        waits=[("sa", ex_done)])
                    ph.op("vector", lambda e, iq=iq: e.reciprocal(
                        rcol.ap()[:, iq:iq + 1], rcol.ap()[:, iq:iq + 1]), inc="sv")
                    for kc in range(4 * nkt):
                        bnk = 2 + kc % 2
                        w = [("sa", ex_done), ("sa", prev.get(("ptv", bnk), 0)),
                             ("sv", prev.get(("ptlv", bnk), 0))]
                        if kc == 0:
                            w.append(("sp", prev.get("odone", 0)))
                        ph.op("tensor", lambda e, kc=kc, bnk=bnk: e.transpose(
                            ctx.pb[bnk].ap()[:, 0:128],
                            pbuf.ap()[:, 128 * kc:128 * (kc + 1)], ctx.sb["c_ident"].ap()[:]),
                            waits=w, inc="sp")
                        td = ctx.C["sp"]
                        dst = 256 * kc + 128 * iq
                        ph.op("scalar", lambda e, bnk=bnk, dst=dst: e.copy(
                            pth.ap()[:, dst:dst + 128], ctx.pb[bnk].ap()[:, 0:128]),
                            waits=[("sp", td)], inc="sa")
                        prev[("ptv", bnk)] = ctx.C["sa"]
                        ph.op("vector", lambda e, bnk=bnk, dst=dst: e.tensor_sub(
                            ptl.ap()[:, dst:dst + 128], ctx.pb[bnk].ap()[:, 0:128],
                            pth.ap()[:, dst:dst + 128]),
                            waits=[("sa", ctx.C["sa"])], inc="sv")
                        prev[("ptlv", bnk)] = ctx.C["sv"]
                    prev["ptd"] = ctx.C["sp"]
                pt_done_a, pt_done_v = ctx.C["sa"], ctx.C["sv"]
                ph.op("tensor", lambda e: e.transpose(
                    ctx.pb[6].ap()[0:2, 0:128], rcol.ap()[:], ctx.sb["c_ident"].ap()[:]),
                    waits=[("sv", ctx.C["sv"]), ("sa", prev.get("rce", 0))], inc="sp")
                rtd = ctx.C["sp"]
                ph.op("scalar", lambda e: e.copy(rr.ap()[0:2, 0:128],
                                                 ctx.pb[6].ap()[0:2, 0:128]),
                      waits=[("sp", rtd)], inc="sa")
                prev["rce"] = ctx.C["sa"]
                b1 = ph.dma(d["rv2"].ap(), rr.ap()[0:2, 0:128],
                            waits=[("sa", ctx.C["sa"])])
                b2 = ph.dma(rr2.ap()[:],
                            d["rv2"].ap().rearrange("a b -> (a b)").unsqueeze(0),
                            waits=[b1])
                ph.op("tensor", lambda e: e.matmul(
                    ctx.pb[5].ap()[:, 0:256], ctx.sb["c_ones"].ap()[0:1, :],
                    rr2.ap()[:], start=True, stop=True),
                    waits=[b2, ("sa", prev.get("rcb", 0))], inc="sp")
                rbd = ctx.C["sp"]
                ph.op("scalar", lambda e: e.copy(rcb.ap()[:, 0:256],
                                                 ctx.pb[5].ap()[:, 0:256]),
                      waits=[("sp", rbd)], inc="sa")
                prev["rcb"] = ctx.C["sa"]
                rb_done = ctx.C["sa"]
                w = [("sa", pt_done_a), ("sv", pt_done_v),
                     ("sv", prev.get("oev", 0))]
                first = True
                for kc in range(2 * (g2 + 1)):
                    for (wa, ra) in ((vbh, pth), (vbh, ptl), (vbl, pth)):
                        ph.op("tensor", lambda e, wa=wa, ra=ra, kc=kc, first=first,
                              g2=g2: e.matmul(
                                  ctx.pb[4].ap()[:, 0:256], wa.ap()[:, 128 * kc:128 * (kc + 1)],
                                  ra.ap()[:, 256 * kc:256 * (kc + 1)], start=first,
                                  stop=(kc == 2 * g2 + 1 and ra is ptl and False) or
                                       (kc == 2 * g2 + 1 and wa is vbl)),
                              waits=w if first else ())
                        w = []
                        first = False
                ph.q["tensor"][-1] = (ph.q["tensor"][-1][0], ph.q["tensor"][-1][1], "sp", 1)
                ctx.bump("sp")
                o_done = ctx.C["sp"]
                prev["odone"] = o_done
                ph.op("vector", lambda e: e.tensor_tensor(
                    otile, ctx.pb[4].ap()[:, 0:256], rcb.ap()[:, 0:256], ALU.mult),
                    waits=[("sp", o_done), ("sa", rb_done),
                           ("sg", prev.get("olo", 0))], inc="sv")
                ov = ctx.C["sv"]
                prev["oev"] = ov
                ph.op("scalar", lambda e, h=h, g2=g2: e.copy(
                    ar_hi(ctx, h)[:, 256 * g2:256 * (g2 + 1)], otile),
                    waits=[("sv", ov)], inc="sa")
                ad = ctx.C["sa"]
                ph.op("gpsimd", lambda e, h=h, g2=g2: e.tensor_sub(
                    ar_lo(ctx, h)[:, 256 * g2:256 * (g2 + 1)], otile,
                    ar_hi(ctx, h)[:, 256 * g2:256 * (g2 + 1)]),
                    waits=[("sa", ad)], inc="sg")
                prev["olo"] = ctx.C["sg"]
        ph.op("sync", None, waits=[("sg", ctx.C["sg"])] + ctx.dma_all())
        ph.run()


def emit_l1_outproj(ctx):
    d = ctx.dram
    nc = ctx.nc
    with ExitStack() as es:
        ot = [es.enter_context(sbt(nc, "l1op_ot", [128, 512], F32)) for i in range(4)]

        def evac(ph):
            st_done = [0] * 70

            def ev(j, oc, tt, bank, mmd):
                ti = (j * 4 + tt) % 4
                w = [("sp", mmd), st_done[max(0, j * 4 + tt - 4)]]
                ph.op("scalar", lambda e, ti=ti, bank=bank: e.activation(
                    ot[ti].ap()[:], ctx.pb[bank].ap()[:, :], AF.Identity,
                    scale=1.0 / 3.0), waits=w, inc="sa")
                ad = ctx.C["sa"]
                st_done[j * 4 + tt] = ph.dma(
                    d["sub_out"].ap()[oc][:, 512 * tt:512 * (tt + 1)], ot[ti].ap()[:],
                    waits=[("sa", ad)])
                return ad, 0
            return ev

        emit_proj_stream(ctx, "l1op", list(range(16)), lambda oc: d["wo1"].ap()[oc], evac)


def emit_l1_attn(ctx):
    emit_pre_ln(ctx, split=True, eps=EPS_LN)
    emit_l1_qkv_qk(ctx)
    emit_v(ctx, "l1v", lambda vg, i: ctx.dram["wv1"].ap()[4 * vg + i], use_bias=False)
    emit_l1_flash(ctx)
    emit_l1_outproj(ctx)
    emit_post_ln(ctx, eps=EPS_LN)



def emit_pdance(ctx, ph, acc, tr, r32, sb64, sw_ap, out_row, op, guards=()):
    d = ctx.dram
    bounce = d["sub_out"].ap()[2]
    b1 = ph.dma(bounce[0:64, :], acc.ap()[64:128, :], waits=guards)
    b2 = ph.dma(sw_ap[0:64, :], bounce[0:64, :], waits=[b1])
    ph.op("vector", lambda e: e.tensor_tensor(
        acc.ap()[0:64, :], acc.ap()[0:64, :], sw_ap[0:64, :], op),
        waits=[b2] + list(guards), inc="sv")
    v1 = ctx.C["sv"]
    b3 = ph.dma(bounce[0:32, :], acc.ap()[32:64, :], waits=[("sv", v1), b2])
    b4 = ph.dma(sw_ap[0:32, :], bounce[0:32, :], waits=[b3])
    ph.op("vector", lambda e: e.tensor_tensor(
        acc.ap()[0:32, :], acc.ap()[0:32, :], sw_ap[0:32, :], op), waits=[b4])
    ph.op("vector", lambda e: e.transpose(tr.ap()[:], acc.ap()[0:32, :]))
    ph.op("vector", lambda e: e.tensor_reduce(
        r32.ap()[:], tr.ap()[:].rearrange("p (g f) -> p g f", f=32), AX.X, op),
        inc="sv")
    vd = ctx.C["sv"]
    ph.op("tensor", lambda e: e.transpose(
        ctx.pb[7].ap()[0:64, 0:32], r32.ap()[:], ctx.sb["c_ident"].ap()[0:32, 0:32]),
        waits=[("sv", vd), ("sa", getattr(ctx, "_pd_sa", 0))], inc="sp")
    td = ctx.C["sp"]
    ph.op("scalar", lambda e: e.copy(sb64.ap()[:], ctx.pb[7].ap()[0:64, 0:32]),
          waits=[("sp", td)], inc="sa")
    ctx._pd_sa = ctx.C["sa"]
    b5 = ph.dma(d["rv3"].ap(), sb64.ap()[:], waits=[("sa", ctx.C["sa"])])
    return ph.dma(out_row, d["rv3"].ap().rearrange("a b -> (a b)").unsqueeze(0),
                  waits=[b5])


def emit_bit_affine(ctx, ph, mean_row, rstd_row, mx_row, mn_row, r0, r1, r2, cb,
                    pre_waits):
    ones = ctx.sb["c_ones"]
    ph.op("vector", lambda e: e.tensor_sub(r0, mx_row, mean_row), waits=pre_waits)
    ph.op("vector", lambda e: e.tensor_sub(r1, mean_row, mn_row))
    ph.op("vector", lambda e: e.tensor_tensor(r0, r0, r1, ALU.max))
    ph.op("vector", lambda e: e.tensor_tensor(r0, r0, rstd_row, ALU.mult))
    ph.op("vector", lambda e: e.reciprocal(r1, r0))
    ph.op("vector", lambda e: e.tensor_tensor(r1, rstd_row, r1, ALU.mult))
    ph.op("vector", lambda e: e.tensor_scalar(r1, r1, 128.0, None, ALU.mult))
    ph.op("vector", lambda e: e.tensor_tensor(r2, mean_row, r1, ALU.mult))
    ph.op("vector", lambda e: e.tensor_scalar(r0, r0, 1.0 / 128.0, None,
                                              ALU.mult), inc="sv")
    rdone = ctx.C["sv"]
    bc = 0
    for k, (src, dst) in enumerate(((r1, "mb"), (r2, "rstdb"), (r0, None))):
        dtile = ctx.sb[dst] if dst else cb
        for tt in range(4):
            i = k * 4 + tt
            ph.op("tensor", lambda e, src=src, tt=tt, k=k: e.matmul(
                ctx.pb[2 * (k % 2) + tt % 2].ap()[:, :], ones.ap()[0:1, :],
                src[:, 512 * tt:512 * (tt + 1)], start=True, stop=True),
                waits=([("sv", rdone)] if i == 0 else
                       ([("sa", bc - 1)] if i >= 2 else ())), inc="sp")
            bd = ctx.C["sp"]
            ph.op("scalar", lambda e, dtile=dtile, tt=tt, k=k: e.copy(
                dtile.ap()[:, 512 * tt:512 * (tt + 1)],
                ctx.pb[2 * (k % 2) + tt % 2].ap()[:, :]),
                waits=[("sp", bd)], inc="sa")
            bc = ctx.C["sa"]
    return bc


def emit_clip_chain(ctx, ph, src_ap, hi_ap, lo_ap, ut_ap, cb, waits,
                    extra_hi=None, extra_lo=None):
    ph.op("vector", lambda e: e.tensor_tensor(ut_ap, src_ap, ctx.sb["mb"].ap()[:],
                                              ALU.mult), waits=waits)
    ph.op("vector", lambda e: e.tensor_tensor(ut_ap, ut_ap,
                                              ctx.sb["rstdb"].ap()[:], ALU.subtract))
    ph.op("vector", lambda e: e.tensor_scalar(ut_ap, ut_ap, CLIP_LO, CLIP_HI,
                                              ALU.max, ALU.min))
    ph.op("vector", lambda e: e.tensor_tensor(ut_ap, ut_ap, cb.ap()[:],
                                              ALU.mult), inc="sv")
    vd = ctx.C["sv"]
    ph.op("scalar", lambda e: e.copy(hi_ap, ut_ap),
          waits=[("sv", vd), extra_hi], inc="sa")
    ad = ctx.C["sa"]
    ph.op("gpsimd", lambda e: e.tensor_sub(lo_ap, ut_ap, hi_ap),
          waits=[("sa", ad), extra_lo], inc="sg")
    return ctx.C["sg"]


def emit_l1f_xhat(ctx):
    nc = ctx.nc
    d = ctx.dram
    with ExitStack() as es:
        sq = es.enter_context(sbt(nc, "bx_sq", [128, T], F32))
        ctx.sb["stat_sq"] = sq
        mxa = es.enter_context(sbt(nc, "bx_mx", [128, T], F32))
        mna = es.enter_context(sbt(nc, "bx_mn", [128, T], F32))
        tr = es.enter_context(sbt(nc, "bx_tr", [32, T], F32))
        r32 = es.enter_context(sbt(nc, "bx_r32", [32, 64], F32))
        sb64 = es.enter_context(sbt(nc, "bx_s64", [64, 32], F32))
        cb = es.enter_context(sbt(nc, "bx_cb", [128, T], F32))
        ph = Phase(ctx, "l1f_xhat")
        bc_done = emit_stats(ctx, ph, lambda c: (st_c(ctx, c), ()), (), EPS_BIT)
        ph.op("vector", lambda e: e.tensor_copy(mxa.ap()[:], st_c(ctx, 0)))
        ph.op("vector", lambda e: e.tensor_copy(mna.ap()[:], st_c(ctx, 0)))
        for c in range(1, KC):
            ph.op("vector", lambda e, c=c: e.tensor_tensor(
                mxa.ap()[:], mxa.ap()[:], st_c(ctx, c), ALU.max), inc="sv")
            ph.op("vector", lambda e, c=c: e.tensor_tensor(
                mna.ap()[:], mna.ap()[:], st_c(ctx, c), ALU.min), inc="sv")
        mm_done_v = ctx.C["sv"]
        mxt = emit_pdance(ctx, ph, mxa, tr, r32, sb64, cb.ap()[0:64, :],
                          mxa.ap()[0:1, :], ALU.max, guards=[("sv", mm_done_v)])
        mnt = emit_pdance(ctx, ph, mna, tr, r32, sb64, cb.ap()[0:64, :],
                          mna.ap()[0:1, :], ALU.min, guards=[mxt])
        bc2 = emit_bit_affine(ctx, ph, ctx.sb["mb"].ap()[0:1, :], rows0(ctx),
                              mxa.ap()[0:1, :], mna.ap()[0:1, :],
                              tr.ap()[0:1, 0:T], sq.ap()[0:1, 0:T], cb.ap()[0:1, 0:T],
                              cb, [mnt, ("sa", bc_done)])
        gd = 0
        for c in range(KC):
            w = [("sa", bc2)] if c == 0 else [("sg", gd)]
            gd = emit_clip_chain(ctx, ph, st_c(ctx, c), ar_hi(ctx, c), ar_lo(ctx, c),
                                 sq.ap()[:], cb, w)
        ph.op("sync", None, waits=[("sg", gd)] + ctx.dma_all())
        ph.run()
        del ctx.sb["stat_sq"]


def _emit_gamma_w(ctx, ph, wdram, n_oc, numel, wg, col, grow, toks):
    ones = ctx.sb["c_ones"]
    ph.op("vector", lambda e: e.memset(col.ap()[:, 0:1], 0.0))
    nb = len(toks)
    rtoks = [0] * nb
    for oc in range(n_oc):
        ld = ph.dma(wg[oc % nb].ap()[:], wdram(oc),
                    waits=[toks[oc % nb], ("sv", rtoks[oc % nb])])
        toks[oc % nb] = ld
        ph.op("vector", lambda e, oc=oc: e.reduce_sum(
            col.ap()[:, 1:2], wg[oc % nb].ap()[:].unsqueeze(1),
            axis=AX.X, apply_absolute_value=True), waits=[ld])
        ph.op("vector", lambda e: e.tensor_add(col.ap()[:, 0:1], col.ap()[:, 0:1],
                                               col.ap()[:, 1:2]), inc="sv")
        rtoks[oc % nb] = ctx.C["sv"]
    acc_done = ctx.C["sv"]
    ctx._gw_rtoks = rtoks
    ph.op("tensor", lambda e: e.matmul(
        ctx.pb[6].ap()[0:1, 0:1], col.ap()[:, 0:1], ones.ap()[:, 0:1],
        start=True, stop=True), waits=[("sv", acc_done)], inc="sp")
    gd1 = ctx.C["sp"]
    ph.op("vector", lambda e: e.tensor_scalar(
        grow.ap()[:, 0:1], ctx.pb[6].ap()[0:1, 0:1], 1.0 / numel, None,
        ALU.mult), waits=[("sp", gd1)])
    ph.op("vector", lambda e: e.tensor_scalar(
        grow.ap()[:, 1:2], grow.ap()[:, 0:1], EPS_BIT, 0.5, ALU.add, ALU.mult))
    ph.op("vector", lambda e: e.tensor_scalar(
        grow.ap()[:, 2:3], grow.ap()[:, 1:2], -1.0, None, ALU.mult))
    ph.op("vector", lambda e: e.memset(grow.ap()[:, 3:4], 0.0), inc="sv")
    gr_done = ctx.C["sv"]
    ph.op("tensor", lambda e: e.matmul(
        ctx.pb[6].ap()[:, 0:4], ones.ap()[0:1, :], grow.ap()[:, 0:4],
        start=True, stop=True), waits=[("sv", gr_done)], inc="sp")
    gd2 = ctx.C["sp"]
    ph.op("scalar", lambda e: e.copy(ctx.sb["gcol"].ap()[:], ctx.pb[6].ap()[:, 0:4]),
          waits=[("sp", gd2)], inc="sa")
    return ctx.C["sa"]


def emit_l1f_fc(ctx):
    nc = ctx.nc
    d = ctx.dram
    ones = ctx.sb["c_ones"]
    with ExitStack() as es:
        wg = [es.enter_context(sbt(nc, "fc_wg", [128, 2048], F32)) for i in range(2)]
        col = es.enter_context(sbt(nc, "fc_col", [128, 4], F32))
        grow = es.enter_context(sbt(nc, "fc_grow", [1, 4], F32))
        tb1 = es.enter_context(sbt(nc, "fc_tb1", [128, 2048], BF16))
        tb2 = es.enter_context(sbt(nc, "fc_tb2", [128, 2048], BF16))
        h2b = es.enter_context(sbt(nc, "fc_h2b", [128, 1024], F32))
        sq = es.enter_context(sbt(nc, "fc_sq", [128, 512], F32))
        mxa = es.enter_context(sbt(nc, "fc_mx", [128, T], F32))
        mna = es.enter_context(sbt(nc, "fc_mn", [128, T], F32))
        rowm = es.enter_context(sbt(nc, "fc_rowm", [1, T], F32))
        ph = Phase(ctx, "l1f_fc")
        toks = [None, None]
        gcol_done = _emit_gamma_w(ctx, ph, lambda oc: d["wfc1"].ap()[oc], 64,
                                  16777216.0, wg, col, grow, toks)
        prev = {}
        h2st = {}
        wrd = list(ctx._gw_rtoks)
        for half in range(2):
            for oc in range(64):
                ld = ph.dma(wg[oc % 2].ap()[:], d["wfc1"].ap()[oc],
                            waits=[toks[oc % 2], ("sv", wrd[oc % 2])])
                toks[oc % 2] = ld
                ph.op("vector", lambda e, oc=oc: e.tensor_scalar(
                    tb1.ap()[:], wg[oc % 2].ap()[:], ctx.sb["gcol"].ap()[:, 2:3], None,
                    ALU.is_lt), waits=[ld, ("sa", gcol_done),
                                       ("sp", prev.get("mmlast", 0))])
                ph.op("vector", lambda e, oc=oc: e.scalar_tensor_tensor(
                    tb2.ap()[:], wg[oc % 2].ap()[:], ctx.sb["gcol"].ap()[:, 1:2],
                    tb1.ap()[:], ALU.is_gt, ALU.subtract), inc="sv")
                tern_done = ctx.C["sv"]
                wrd[oc % 2] = tern_done
                for t2 in range(2):
                    tt = 2 * half + t2
                    bank = t2
                    w = [("sv", tern_done), ("sa", prev.get(("gel", bank), 0))]
                    first = True
                    for kc in range(KC):
                        for xa in (ar_hi, ar_lo):
                            ph.op("tensor", lambda e, kc=kc, xa=xa, tt=tt, first=first,
                                  bank=bank: e.matmul(
                                      ctx.pb[bank].ap()[:, :],
                                      tb2.ap()[:, 128 * kc:128 * (kc + 1)],
                                      xa(ctx, kc)[:, 512 * tt:512 * (tt + 1)],
                                      start=first, stop=(kc == KC - 1 and xa is ar_lo)),
                                  waits=w if first else ())
                            w = []
                            first = False
                    ph.q["tensor"][-1] = (ph.q["tensor"][-1][0], ph.q["tensor"][-1][1],
                                          "sp", 1)
                    ctx.bump("sp")
                    mmd = ctx.C["sp"]
                    prev["mmlast"] = mmd
                    ph.op("scalar", lambda e, t2=t2, bank=bank: e.activation(
                        h2b.ap()[:, 512 * t2:512 * (t2 + 1)], ctx.pb[bank].ap()[:, :],
                        AF.Gelu, scale=ctx.sb["gcol"].ap()[:, 0:1]),
                        waits=[("sp", mmd), h2st.get("last"),
                               ("sv", prev.get(("mxd", bank), 0)),
                               ("sv", prev.get(("mnd", bank), 0))], inc="sa")
                    prev[("gel", bank)] = ctx.C["sa"]
                    ph.op("scalar", lambda e, t2=t2: e.activation(
                        sq.ap()[:], h2b.ap()[:, 512 * t2:512 * (t2 + 1)], AF.Square),
                        waits=[("sp", prev.get(("sqm", t2), 0))], inc="sa")
                    sqd = ctx.C["sa"]
                    wst = [("sa", prev[("gel", bank)])]
                    if oc == 0:
                        wst.append(("sv", prev.get("rowmd", 0)))
                    ph.op("tensor", lambda e, t2=t2, oc=oc: e.matmul(
                        ctx.pb[2 + t2].ap()[0:1, :], ones.ap()[:, 0:1],
                        h2b.ap()[:, 512 * t2:512 * (t2 + 1)],
                        start=(oc == 0), stop=(oc == 63)), waits=wst)
                    ph.op("tensor", lambda e, t2=t2, oc=oc: e.matmul(
                        ctx.pb[4 + t2].ap()[0:1, :], ones.ap()[:, 0:1], sq.ap()[:],
                        start=(oc == 0), stop=(oc == 63)),
                        waits=[("sa", sqd)], inc="sp")
                    prev[("sqm", t2)] = ctx.C["sp"]
                    ts2 = slice(512 * tt, 512 * (tt + 1))
                    hslc = slice(512 * t2, 512 * (t2 + 1))
                    if oc == 0:
                        ph.op("vector", lambda e, ts2=ts2, hslc=hslc: e.tensor_copy(
                            mxa.ap()[:, ts2], h2b.ap()[:, hslc]),
                            waits=[("sa", prev[("gel", bank)])], inc="sv")
                        ph.op("vector", lambda e, ts2=ts2, hslc=hslc: e.tensor_copy(
                            mna.ap()[:, ts2], h2b.ap()[:, hslc]),
                            waits=[("sa", prev[("gel", bank)])], inc="sv")
                    else:
                        ph.op("vector", lambda e, ts2=ts2, hslc=hslc: e.tensor_tensor(
                            mxa.ap()[:, ts2], mxa.ap()[:, ts2], h2b.ap()[:, hslc],
                            ALU.max), waits=[("sa", prev[("gel", bank)])], inc="sv")
                        ph.op("vector", lambda e, ts2=ts2, hslc=hslc: e.tensor_tensor(
                            mna.ap()[:, ts2], mna.ap()[:, ts2], h2b.ap()[:, hslc],
                            ALU.min), waits=[("sa", prev[("gel", bank)])], inc="sv")
                    prev[("mxd", bank)] = ctx.C["sv"]
                    prev[("mnd", bank)] = ctx.C["sv"]
                h2st["last"] = ph.dma(
                    d["h_f32"].ap()[oc][:, 1024 * half:1024 * (half + 1)], h2b.ap()[:],
                    waits=[("sv", prev[("mxd", 1)]), ("sv", prev[("mnd", 1)])])
            for t2 in range(2):
                wrow = [("sp", prev[("sqm", t2)])]
                if t2 == 0:
                    wrow.append(prev.get("rowst"))
                ph.op("vector", lambda e, t2=t2: e.tensor_copy(
                    rowm.ap()[:, 512 * t2:512 * (t2 + 1)], ctx.pb[2 + t2].ap()[0:1, :]),
                    waits=wrow)
                ph.op("vector", lambda e, t2=t2: e.tensor_copy(
                    rowm.ap()[:, 1024 + 512 * t2:1024 + 512 * (t2 + 1)],
                    ctx.pb[4 + t2].ap()[0:1, :]), inc="sv")
            prev["rowmd"] = ctx.C["sv"]
            prev["rowst"] = ph.dma(d["rowsd"].ap()[half:half + 1, :], rowm.ap()[:],
                                   waits=[("sv", prev["rowmd"])])
        ph.dma(d["sub_out"].ap()[0], mxa.ap()[:], waits=[("sv", ctx.C["sv"])])
        ph.dma(d["sub_out"].ap()[1], mna.ap()[:], waits=[("sg", ctx.C["sg"])])
        ph.op("sync", None, waits=ctx.dma_all())
        ph.run()


def emit_l1f_rows(ctx):
    nc = ctx.nc
    d = ctx.dram
    with ExitStack() as es:
        mxa = es.enter_context(sbt(nc, "fr_mx", [128, T], F32))
        mna = es.enter_context(sbt(nc, "fr_mn", [128, T], F32))
        tr = es.enter_context(sbt(nc, "fr_tr", [32, T], F32))
        r32 = es.enter_context(sbt(nc, "fr_r32", [32, 64], F32))
        sb64 = es.enter_context(sbt(nc, "fr_s64", [64, 32], F32))
        cb = es.enter_context(sbt(nc, "fr_cb", [128, T], F32))
        rowb = es.enter_context(sbt(nc, "fr_rowb", [1, T], F32))
        rowc = es.enter_context(sbt(nc, "fr_rowc", [1, T], F32))
        rowa_ap = cb.ap()[0:1, :]
        ph = Phase(ctx, "l1f_rows")
        l1 = ph.dma(mxa.ap()[:], d["sub_out"].ap()[0])
        l2 = ph.dma(mna.ap()[:], d["sub_out"].ap()[1])
        mxt = emit_pdance(ctx, ph, mxa, tr, r32, sb64, cb.ap()[0:64, :],
                          mxa.ap()[0:1, :], ALU.max, guards=[l1])
        mnt = emit_pdance(ctx, ph, mna, tr, r32, sb64, cb.ap()[0:64, :],
                          mna.ap()[0:1, :], ALU.min, guards=[mxt, l2])
        l3 = ph.dma(rowa_ap[:, 0:1024], d["rowsd"].ap()[0:1, 0:1024], waits=[mnt])
        l4 = ph.dma(rowa_ap[:, 1024:2048], d["rowsd"].ap()[1:2, 0:1024])
        l5 = ph.dma(rowb.ap()[:, 0:1024], d["rowsd"].ap()[0:1, 1024:2048])
        l6 = ph.dma(rowb.ap()[:, 1024:2048], d["rowsd"].ap()[1:2, 1024:2048])
        ph.op("vector", lambda e: e.tensor_scalar(
            rowc.ap()[:], rowa_ap, 1.0 / 8192.0, None, ALU.mult),
            waits=[l3, l4, mnt])
        ph.op("vector", lambda e: e.tensor_tensor(
            rowa_ap, rowc.ap()[:], rowc.ap()[:], ALU.mult), waits=[l5, l6])
        ph.op("vector", lambda e: e.scalar_tensor_tensor(
            rowb.ap()[:], rowb.ap()[:], 1.0 / 8192.0, rowa_ap,
            ALU.mult, ALU.subtract), inc="sv")
        vard = ctx.C["sv"]
        ph.op("scalar", lambda e: e.activation(
            rowa_ap, rowb.ap()[:], AF.Sqrt,
            bias=ctx.sb["c_cvals"].ap()[0:1, 2:3]), waits=[("sv", vard)], inc="sa")
        sqd = ctx.C["sa"]
        ph.op("vector", lambda e: e.reciprocal(rowb.ap()[:], rowa_ap),
              waits=[("sa", sqd)], inc="sv")
        bc2 = emit_bit_affine(ctx, ph, rowc.ap()[:], rowb.ap()[:],
                              mxa.ap()[0:1, :], mna.ap()[0:1, :],
                              tr.ap()[0:1, 0:T], mna.ap()[0:1, :], rowa_ap,
                              cb, [("sv", ctx.C["sv"])])
        ph.dma(d["rvec"].ap().rearrange("a b -> (a b)").unsqueeze(0)[:, 0:T],
               tr.ap()[0:1, 0:T], waits=[("sa", bc2)])
        ph.op("sync", None, waits=ctx.dma_all() + [("sa", ctx.C["sa"])])
        ph.run()


def emit_l1f_xhat2(ctx):
    nc = ctx.nc
    d = ctx.dram
    ones = ctx.sb["c_ones"]
    with ExitStack() as es:
        hb = [es.enter_context(sbt(nc, "x2_hb", [128, T], F32)) for i in range(2)]
        ut = es.enter_context(sbt(nc, "x2_ut", [128, T], F32))
        hh = [es.enter_context(sbt(nc, "x2_hh", [128, T], BF16)) for i in range(2)]
        hl = [es.enter_context(sbt(nc, "x2_hl", [128, T], BF16)) for i in range(2)]
        cb = es.enter_context(sbt(nc, "x2_cb", [128, T], F32))
        crow = cb.ap()[0:1, :]
        ph = Phase(ctx, "l1f_xhat2")
        lc = ph.dma(crow,
                    d["rvec"].ap().rearrange("a b -> (a b)").unsqueeze(0)[:, 0:T])
        bcc = 0
        for tt in range(4):
            ph.op("tensor", lambda e, tt=tt: e.matmul(
                ctx.pb[tt % 2].ap()[:, :], ones.ap()[0:1, :],
                crow[:, 512 * tt:512 * (tt + 1)], start=True, stop=True),
                waits=([lc] if tt == 0 else
                       ([("sa", bcc - 1)] if tt >= 2 else ())), inc="sp")
            bd = ctx.C["sp"]
            ph.op("scalar", lambda e, tt=tt: e.copy(
                cb.ap()[:, 512 * tt:512 * (tt + 1)], ctx.pb[tt % 2].ap()[:, :]),
                waits=[("sp", bd)], inc="sa")
            bcc = ctx.C["sa"]
        prev = {}
        for ch in range(64):
            ld = ph.dma(hb[ch % 2].ap()[:], d["h_f32"].ap()[ch],
                        waits=[prev.get(("ld", ch - 2))])
            w = [ld, ("sa", bcc)] if ch == 0 else [ld, ("sg", prev.get("gd", 0))]
            gd = emit_clip_chain(ctx, ph, hb[ch % 2].ap()[:], hh[ch % 2].ap()[:],
                                 hl[ch % 2].ap()[:], ut.ap()[:], cb, w,
                                 extra_hi=prev.get(("sth", ch - 2)),
                                 extra_lo=prev.get(("stl", ch - 2)))
            prev["gd"] = gd
            prev[("sth", ch)] = ph.dma(d["hn_hi"].ap()[ch], hh[ch % 2].ap()[:],
                                       waits=[("sg", gd)])
            prev[("stl", ch)] = ph.dma(d["hn_lo"].ap()[ch], hl[ch % 2].ap()[:])
            prev[("ld", ch)] = prev[("stl", ch)]
        ph.op("sync", None, waits=ctx.dma_all())
        ph.run()


def emit_l1f_wpj_tern(ctx):
    nc = ctx.nc
    d = ctx.dram
    with ExitStack() as es:
        wg = [es.enter_context(sbt(nc, "pj_wg", [128, 4096], F32)) for i in range(2)]
        col = es.enter_context(sbt(nc, "pj_col", [128, 4], F32))
        grow = es.enter_context(sbt(nc, "pj_grow", [1, 4], F32))
        tb1 = es.enter_context(sbt(nc, "pj_tb1", [128, 4096], BF16))
        tb2 = es.enter_context(sbt(nc, "pj_tb2", [128, 4096], BF16))
        ph = Phase(ctx, "l1f_pjt")
        toks = [None, None]
        gcol_done = _emit_gamma_w(
            ctx, ph,
            lambda j: d["wpj1"].ap()[j // 2][:, 4096 * (j % 2):4096 * (j % 2 + 1)],
            32, 16777216.0, wg, col, grow, toks)
        prev = {}
        wrd = list(ctx._gw_rtoks)
        for j in range(32):
            ld = ph.dma(wg[j % 2].ap()[:],
                        d["wpj1"].ap()[j // 2][:, 4096 * (j % 2):4096 * (j % 2 + 1)],
                        waits=[toks[j % 2], ("sv", wrd[j % 2])])
            toks[j % 2] = ld
            ph.op("vector", lambda e, j=j: e.tensor_scalar(
                tb1.ap()[:], wg[j % 2].ap()[:], ctx.sb["gcol"].ap()[:, 2:3], None,
                ALU.is_lt), waits=[ld, ("sa", gcol_done), prev.get(("st", j - 1))])
            ph.op("vector", lambda e, j=j: e.scalar_tensor_tensor(
                tb2.ap()[:], wg[j % 2].ap()[:], ctx.sb["gcol"].ap()[:, 1:2],
                tb1.ap()[:], ALU.is_gt, ALU.subtract), inc="sv")
            td = ctx.C["sv"]
            wrd[j % 2] = td
            prev[("st", j)] = ph.dma(
                d["wpj1_t"].ap()[j // 2][:, 4096 * (j % 2):4096 * (j % 2 + 1)],
                tb2.ap()[:], waits=[("sv", td)])
        ph.op("sync", None, waits=ctx.dma_all())
        ph.run()


def emit_l1_ffn(ctx):
    emit_pre_ln(ctx, split=False, eps=EPS_LN)
    emit_l1f_xhat(ctx)
    emit_l1f_fc(ctx)
    emit_l1f_rows(ctx)
    emit_l1f_xhat2(ctx)
    emit_l1f_wpj_tern(ctx)
    emit_ffn_proj(ctx, "l1f_pj", ctx.dram["wpj1_t"], 16,
                  sub_scale=lambda: ctx.sb["gcol"].ap()[:, 0:1])
    emit_post_ln(ctx, eps=EPS_LN)


SUB_BLOCKS = {}


def _register():
    SUB_BLOCKS["l0_attn"] = emit_l0_attn
    SUB_BLOCKS["l0_ffn"] = emit_l0_ffn
    SUB_BLOCKS["l1_attn"] = emit_l1_attn
    SUB_BLOCKS["l1_ffn"] = emit_l1_ffn


_register()


def build_program(parts, debug=False):
    nc = bass.Bass("TRN2", target_bir_lowering=False)
    ctx = Ctx(nc)
    declare_io(ctx, debug=debug)
    alloc_shared(ctx)
    emit_setup(ctx, load_stream=True)
    for p in parts:
        SUB_BLOCKS[p](ctx)
    emit_output(ctx)
    ctx.es.close()
    return nc


ALL_PARTS = ["l0_attn", "l0_ffn", "l1_attn", "l1_ffn"]


def kernel(**inputs):
    per_core = [host_inputs(inputs, c) for c in range(8)]
    nc = build_program(ALL_PARTS)
    res = run_bass_kernel_spmd(nc, per_core, core_ids=list(range(8)))
    return np.ascontiguousarray(
        np.stack([np.asarray(r["yT"]).T for r in res.results]).astype(np.float32))


# version 04
# speedup vs baseline: 1.0175x; 1.0022x over previous
import numpy as np
import ml_dtypes
from contextlib import ExitStack

import concourse.bass as bass
import concourse.mybir as mybir
from concourse.bass_utils import run_bass_kernel_spmd

F32 = mybir.dt.float32
BF16 = mybir.dt.bfloat16
AF = mybir.ActivationFunctionType
ALU = mybir.AluOpType
AX = mybir.AxisListType

D = 2048
T = 2048
KC = D // 128
NH = 16
HD = 128
SCL_L1 = float(128.0 ** 0.5 / 128.0)
EPS_LN = 1e-5
EPS_RMS = float(np.finfo(np.float32).eps)
EPS_BIT = 1e-6
CLIP_HI = float(np.float32(np.float64(128.0) - 1e-6))
CLIP_LO = float(np.float32(np.float64(-128.0) + 1e-6))

ENGINES = ("sync", "tensor", "vector", "scalar", "gpsimd")

_UID = [0]


def sbt(nc, name, shape, dtype):
    _UID[0] += 1
    return nc.sbuf_tensor(f"{name}_{_UID[0]}", shape, dtype)




def chunk_w(w):
    K, M = w.shape
    nk, nm = K // 128, M // 128
    return np.ascontiguousarray(
        w.reshape(nk, 128, nm, 128).transpose(2, 1, 0, 3).reshape(nm, 128, nk * 128))


def nat_chunks(w):
    R, C = w.shape
    return np.ascontiguousarray(w.reshape(R // 128, 128, C))


def make_consts():
    p = np.arange(128, dtype=np.float32)[:, None]
    j = np.arange(128, dtype=np.float32)[None, :]
    c = {}
    c["ident"] = np.eye(128, dtype=np.float32)
    c["ones"] = np.ones((128, 128), dtype=np.float32)
    c["diffD"] = np.ascontiguousarray(np.maximum(j - p, 0.0) + np.zeros((128, 128), np.float32))
    c["tri01"] = (j >= p).astype(np.float32)
    c["iotaF"] = np.ascontiguousarray(j + np.zeros((128, 128), np.float32))
    c["dcol"] = np.concatenate([(128.0 - p), np.full((128, 1), 128.0, np.float32)],
                               axis=1).astype(np.float32)
    jj = np.arange(512, dtype=np.float32)[None, :]
    m = [np.where(jj <= 128.0 * r + p, 0.0, -1e30).astype(np.float32) for r in range(4)]
    c["l1mask"] = np.ascontiguousarray(
        np.concatenate(m, axis=1).astype(ml_dtypes.bfloat16))
    af = ((1.0 / 1024.0) ** np.linspace(0.0, 1.0, 32)).astype(np.float32)
    af64 = np.concatenate([af.astype(np.float64), np.zeros(32)])
    theta = np.arange(T, dtype=np.float64)[:, None] * af64[None, :]
    cosT = np.cos(theta).T
    sinT = np.sin(theta).T
    Call = np.concatenate([cosT, cosT], axis=0)
    Sall = np.concatenate([sinT, -sinT], axis=0)
    c["ropeKC"] = np.ascontiguousarray(Call.astype(np.float32))
    c["ropeKS"] = np.ascontiguousarray(Sall.astype(np.float32))
    c["ropeQC"] = np.ascontiguousarray((Call * SCL_L1).astype(np.float32))
    c["ropeQS"] = np.ascontiguousarray((Sall * SCL_L1).astype(np.float32))
    cv = np.zeros((128, 4), np.float32)
    cv[:, 0] = EPS_LN
    cv[:, 1] = EPS_RMS
    cv[:, 2] = EPS_BIT
    c["cvals"] = cv
    return c


CONST_SHAPES = {
    "ident": [128, 128], "ones": [128, 128], "diffD": [128, 128], "tri01": [128, 128],
    "iotaF": [128, 128], "dcol": [128, 2], "l1mask": [128, 2048],
    "ropeKC": [128, T], "ropeKS": [128, T], "ropeQC": [128, T], "ropeQS": [128, T],
    "cvals": [128, 4],
}


def host_inputs(inputs, core):
    m = {}
    x = np.asarray(inputs["L0_attn_x"][core], dtype=np.float32)
    m["xT"] = np.ascontiguousarray(x.T)
    m["wqkv"] = chunk_w(np.asarray(inputs["L0_attn_Wqkv"], np.float32))
    bqkv = np.asarray(inputs["L0_attn_bqkv"], np.float32)
    m["bqkv_l"] = np.ascontiguousarray(bqkv.reshape(48, 128).T)
    m["vbias_r"] = np.ascontiguousarray(bqkv[4096:6144].reshape(1, 2048))
    m["wout"] = chunk_w(np.asarray(inputs["L0_attn_Wout"], np.float32))
    m["bout_l"] = np.ascontiguousarray(
        np.asarray(inputs["L0_attn_bout"], np.float32).reshape(16, 128).T)
    m["slopes_r"] = np.ascontiguousarray(
        np.asarray(inputs["L0_attn_slopes"], np.float32).reshape(1, 16))
    m["gup_nat"] = nat_chunks(np.asarray(inputs["L0_ffn_w_gate_up"], np.float32))
    m["wpj0_nat"] = nat_chunks(np.asarray(inputs["L0_ffn_w_proj"], np.float32))
    m["wq1"] = chunk_w(np.ascontiguousarray(np.asarray(inputs["L1_attn_Wq"], np.float32).T))
    m["wk1"] = chunk_w(np.ascontiguousarray(np.asarray(inputs["L1_attn_Wk"], np.float32).T))
    m["wv1"] = chunk_w(np.ascontiguousarray(np.asarray(inputs["L1_attn_Wv"], np.float32).T))
    m["wo1"] = chunk_w(np.ascontiguousarray(np.asarray(inputs["L1_attn_Wo"], np.float32).T))
    m["wfc1"] = chunk_w(np.ascontiguousarray(np.asarray(inputs["L1_ffn_w_fc"], np.float32).T))
    m["wpj1"] = chunk_w(np.ascontiguousarray(np.asarray(inputs["L1_ffn_w_proj"], np.float32).T))
    m.update(make_consts())
    return m



class Phase:
    def __init__(self, ctx, name=""):
        self.ctx = ctx
        self.name = name
        self.q = {e: [] for e in ENGINES}

    SELF_SEM = {"vector": "sv", "scalar": "sa", "gpsimd": "sg"}

    def op(self, eng, fn=None, waits=(), inc=None, inc_n=1):
        waits = list(waits)
        ss = self.SELF_SEM.get(eng)
        if ss is not None and fn is not None:
            waits.append((ss, self.ctx.C[ss]))
            if inc is None:
                inc = ss
                inc_n = 1
            else:
                assert inc == ss and inc_n == 1, (eng, inc)
        c = self.ctx.bump(inc, inc_n) if (inc and fn is not None) else None
        self.q[eng].append((fn, tuple(waits), inc, inc_n))
        return c

    def dma(self, out_ap, in_ap, waits=(), sem=None, q="sync"):
        ctx = self.ctx
        nm = ctx.dma_pool[ctx.dma_rr % len(ctx.dma_pool)]
        ctx.dma_rr += 1
        w = list(waits)
        if ctx.C[nm] > 0:
            w.append((nm, ctx.C[nm]))
        self.q[q].append((lambda e, o=out_ap, i=in_ap: e.dma_start(out=o, in_=i),
                          tuple(w), nm, 16))
        return (nm, ctx.bump(nm, 16))

    def run(self):
        ctx = self.ctx
        with ctx.nc.Block() as b:
            for eng in ENGINES:
                ops = self.q[eng]
                if not ops:
                    continue

                def body(e, ops=ops):
                    for fn, waits, inc, inc_n in ops:
                        for wv in waits:
                            if not wv:
                                continue
                            (s, v) = wv
                            if v > 0:
                                e.wait_ge(ctx.sems[s], v)
                        ins = fn(e) if fn is not None else None
                        if inc is not None and ins is not None:
                            ins.then_inc(ctx.sems[inc], inc_n)
                getattr(b, eng)(body)


class Ctx:
    def __init__(self, nc):
        self.nc = nc
        self.es = ExitStack()
        self.sems = {}
        self.C = {}
        for s in ("ld", "st", "sp", "sv", "sa", "sg", "lg"):
            self.sems[s] = self.es.enter_context(nc.semaphore(s))
            self.C[s] = 0
        self.dma_pool = []
        for i in range(12):
            nm = f"dm{i}"
            self.sems[nm] = self.es.enter_context(nc.semaphore(nm))
            self.C[nm] = 0
            self.dma_pool.append(nm)
        self.dma_rr = 0
        self.sb = {}
        self.dram = {}

    def bump(self, s, n=1):
        self.C[s] += n
        return self.C[s]

    def dma_all(self):
        return [(nm, self.C[nm]) for nm in self.dma_pool if self.C[nm] > 0]

    def alloc(self, name, shape, dtype, **kw):
        h = self.es.enter_context(self.nc.sbuf_tensor(name, shape, dtype, **kw))
        self.sb[name] = h
        return h


def declare_io(ctx, debug=False):
    nc = ctx.nc
    d = ctx.dram
    scratch_kind = "ExternalOutput" if debug else "Internal"
    d["xT"] = nc.dram_tensor("xT", [D, T], F32, kind="ExternalInput")
    d["yT"] = nc.dram_tensor("yT", [D, T], F32, kind="ExternalOutput")
    d["wqkv"] = nc.dram_tensor("wqkv", [48, 128, 2048], F32, kind="ExternalInput")
    d["bqkv_l"] = nc.dram_tensor("bqkv_l", [128, 48], F32, kind="ExternalInput")
    d["vbias_r"] = nc.dram_tensor("vbias_r", [1, 2048], F32, kind="ExternalInput")
    d["wout"] = nc.dram_tensor("wout", [16, 128, 2048], F32, kind="ExternalInput")
    d["bout_l"] = nc.dram_tensor("bout_l", [128, 16], F32, kind="ExternalInput")
    d["slopes_r"] = nc.dram_tensor("slopes_r", [1, 16], F32, kind="ExternalInput")
    d["gup_nat"] = nc.dram_tensor("gup_nat", [128, 128, 2048], F32, kind="ExternalInput")
    d["wpj0_nat"] = nc.dram_tensor("wpj0_nat", [16, 128, 8192], F32, kind="ExternalInput")
    for nm in ("wq1", "wk1", "wv1", "wo1"):
        d[nm] = nc.dram_tensor(nm, [16, 128, 2048], F32, kind="ExternalInput")
    d["wfc1"] = nc.dram_tensor("wfc1", [64, 128, 2048], F32, kind="ExternalInput")
    d["wpj1"] = nc.dram_tensor("wpj1", [16, 128, 8192], F32, kind="ExternalInput")
    for nm, shp in CONST_SHAPES.items():
        dt_ = BF16 if nm == "l1mask" else F32
        d[nm] = nc.dram_tensor(nm, shp, dt_, kind="ExternalInput")
    d["xsp"] = nc.dram_tensor("xsp", [KC, 128, T], F32, kind=scratch_kind)
    d["qk_hi"] = nc.dram_tensor("qk_hi", [32, 128, T], BF16, kind=scratch_kind)
    d["qk_lo"] = nc.dram_tensor("qk_lo", [32, 128, T], BF16, kind=scratch_kind)
    d["v_hi"] = nc.dram_tensor("v_hi", [NH, T, HD], BF16, kind=scratch_kind)
    d["v_lo"] = nc.dram_tensor("v_lo", [NH, T, HD], BF16, kind=scratch_kind)
    d["sub_out"] = nc.dram_tensor("sub_out", [KC, 128, T], F32, kind=scratch_kind)
    d["h_f32"] = nc.dram_tensor("h_f32", [64, 128, T], F32, kind=scratch_kind)
    d["hn_hi"] = nc.dram_tensor("hn_hi", [64, 128, T], BF16, kind=scratch_kind)
    d["hn_lo"] = nc.dram_tensor("hn_lo", [64, 128, T], BF16, kind=scratch_kind)
    d["wpj0_t"] = nc.dram_tensor("wpj0_t", [16, 128, 8192], BF16, kind=scratch_kind)
    d["wpj1_t"] = nc.dram_tensor("wpj1_t", [16, 128, 8192], BF16, kind=scratch_kind)
    d["rvec"] = nc.dram_tensor("rvec", [64, 64], F32, kind=scratch_kind)
    d["rv2"] = nc.dram_tensor("rv2", [2, 128], F32, kind=scratch_kind)
    d["rv3"] = nc.dram_tensor("rv3", [64, 32], F32, kind=scratch_kind)
    d["rowsd"] = nc.dram_tensor("rowsd", [4, 2048], F32, kind=scratch_kind)
    d["dbg"] = nc.dram_tensor("dbg", [128, 8192], BF16, kind=scratch_kind)


def alloc_shared(ctx):
    nc = ctx.nc
    ctx.arena = ctx.alloc("arena", [128, KC * T], F32)
    ctx.arena_b = ctx.arena.bitcast(BF16)
    ctx.alloc("c_ident", [128, 128], F32)
    ctx.alloc("c_identb", [128, 128], BF16)
    ctx.alloc("c_ones", [128, 128], F32)
    ctx.alloc("c_cvals", [128, 4], F32)
    ctx.alloc("gcol", [128, 4], F32)
    ctx.alloc("rows", [1, T], F32)
    ctx.alloc("mb", [128, T], F32)
    ctx.alloc("rstdb", [128, T], F32)
    ctx.pb = [ctx.es.enter_context(nc.psum_tensor(f"pb{i}", [128, 512], F32))
              for i in range(8)]
    ctx.pbb = [p.bitcast(BF16) for p in ctx.pb]


def st_c(ctx, c):
    return ctx.arena.ap()[:, c * T:(c + 1) * T]


def ar_hi(ctx, c):
    return ctx.arena_b.ap()[:, c * 2 * T: c * 2 * T + T]


def ar_lo(ctx, c):
    return ctx.arena_b.ap()[:, c * 2 * T + T: (c + 1) * 2 * T]


def rows0(ctx):
    return ctx.sb["rows"].ap()[:, 0:T]


def rsc(ctx):
    return ctx.sb["rstdb"].ap()[0:1, :]



def emit_setup(ctx, load_stream):
    ph = Phase(ctx, "setup")
    d = ctx.dram
    ph.dma(ctx.sb["c_ident"].ap()[:], d["ident"].ap())
    ph.dma(ctx.sb["c_cvals"].ap()[:], d["cvals"].ap())
    ld = ph.dma(ctx.sb["c_ones"].ap()[:], d["ones"].ap())
    if load_stream:
        for c in range(KC):
            ld = ph.dma(st_c(ctx, c), d["xT"].ap()[128 * c:128 * (c + 1), :])
    ph.op("scalar", lambda e: e.copy(ctx.sb["c_identb"].ap()[:], ctx.sb["c_ident"].ap()[:]),
          waits=[ld], inc="sa")
    ph.op("sync", None, waits=ctx.dma_all())
    ph.run()



def emit_stats(ctx, ph, get_chunk, pre_waits, eps, n=D):
    nc = ctx.nc
    ones = ctx.sb["c_ones"]
    sq = ctx.sb["stat_sq"]
    mm_done = [0] * KC
    for c in range(KC):
        ap, w = get_chunk(c)
        w = list(w) + (list(pre_waits) if c == 0 else [])
        if c >= 1:
            w.append(("sp", mm_done[c - 1]))
        ph.op("scalar", lambda e, ap=ap, c=c: e.activation(
            sq.ap()[:], ap, AF.Square), waits=w, inc="sa")
        sq_done = ctx.C["sa"]
        for tt in range(4):
            ph.op("tensor", lambda e, ap=ap, c=c, tt=tt: e.matmul(
                ctx.pb[tt].ap()[0:1, :], ones.ap()[:, 0:1],
                ap[:, 512 * tt:512 * (tt + 1)],
                start=(c == 0), stop=(c == KC - 1)),
                waits=[("sa", sq_done)] if tt == 0 else ())
        for tt in range(4):
            ph.op("tensor", lambda e, c=c, tt=tt: e.matmul(
                ctx.pb[4 + tt].ap()[0:1, :], ones.ap()[:, 0:1],
                sq.ap()[:, 512 * tt:512 * (tt + 1)],
                start=(c == 0), stop=(c == KC - 1)),
                inc="sp" if tt == 3 else None)
        mm_done[c] = ctx.C["sp"]
    allmm = mm_done[KC - 1]
    for tt in range(4):
        ph.op("vector", lambda e, tt=tt: e.tensor_scalar(
            rows0(ctx)[:, 512 * tt:512 * (tt + 1)], ctx.pb[tt].ap()[0:1, :],
            1.0 / n, None, ALU.mult),
            waits=[("sp", allmm)] if tt == 0 else (), inc="sv" if tt == 3 else None)
    mean_done = ctx.C["sv"]
    mb_done = 0
    for tt in range(4):
        ph.op("tensor", lambda e, tt=tt: e.matmul(
            ctx.pb[tt % 2].ap()[:, :], ones.ap()[0:1, :],
            rows0(ctx)[:, 512 * tt:512 * (tt + 1)], start=True, stop=True),
            waits=([("sv", mean_done)] if tt == 0 else
                   ([("sa", mb_done - 1)] if tt >= 2 else ())), inc="sp")
        bd = ctx.C["sp"]
        ph.op("scalar", lambda e, tt=tt: e.copy(
            ctx.sb["mb"].ap()[:, 512 * tt:512 * (tt + 1)], ctx.pb[tt % 2].ap()[:, :]),
            waits=[("sp", bd)], inc="sa")
        mb_done = ctx.C["sa"]
    ph.op("vector", lambda e: e.tensor_mul(rsc(ctx), rows0(ctx), rows0(ctx)),
          waits=[("sa", mb_done)], inc="sv")
    for tt in range(4):
        ph.op("vector", lambda e, tt=tt: e.scalar_tensor_tensor(
            rows0(ctx)[:, 512 * tt:512 * (tt + 1)], ctx.pb[4 + tt].ap()[0:1, :],
            1.0 / n, rsc(ctx)[:, 512 * tt:512 * (tt + 1)], ALU.mult, ALU.subtract),
            inc="sv" if tt == 3 else None)
    var_done = ctx.C["sv"]
    eps_col = {EPS_LN: 0, EPS_RMS: 1, EPS_BIT: 2}[eps]
    ph.op("scalar", lambda e: e.activation(
        rsc(ctx), rows0(ctx), AF.Sqrt,
        bias=ctx.sb["c_cvals"].ap()[0:1, eps_col:eps_col + 1]),
        waits=[("sv", var_done)], inc="sa")
    sdone = ctx.C["sa"]
    ph.op("vector", lambda e: e.reciprocal(rows0(ctx), rsc(ctx)),
          waits=[("sa", sdone)], inc="sv")
    rstd_done = ctx.C["sv"]
    rb_done = 0
    for tt in range(4):
        ph.op("tensor", lambda e, tt=tt: e.matmul(
            ctx.pb[2 + tt % 2].ap()[:, :], ones.ap()[0:1, :],
            rows0(ctx)[:, 512 * tt:512 * (tt + 1)], start=True, stop=True),
            waits=([("sv", rstd_done)] if tt == 0 else
                   ([("sa", rb_done - 1)] if tt >= 2 else ())), inc="sp")
        bd = ctx.C["sp"]
        ph.op("scalar", lambda e, tt=tt: e.copy(
            ctx.sb["rstdb"].ap()[:, 512 * tt:512 * (tt + 1)], ctx.pb[2 + tt % 2].ap()[:, :]),
            waits=[("sp", bd)], inc="sa")
        rb_done = ctx.C["sa"]
    return rb_done



def emit_pre_ln(ctx, split=True, eps=EPS_LN):
    nc = ctx.nc
    d = ctx.dram
    with ExitStack() as es:
        ctx.sb["stat_sq"] = es.enter_context(sbt(nc, "pl_sq", [128, T], F32))
        ut = es.enter_context(sbt(nc, "pl_u", [128, T], F32))
        xnf = es.enter_context(sbt(nc, "pl_xnf", [128, T], F32))

        ph = Phase(ctx, "pre_ln")
        spill_toks = []
        for c in range(KC):
            spill_toks.append(ph.dma(d["xsp"].ap()[c], st_c(ctx, c)))
        bc_done = emit_stats(ctx, ph, lambda c: (st_c(ctx, c), ()), (), eps)
        gd = 0
        for c in range(KC):
            w = [("sa", bc_done)] if c == 0 else [("sg" if split else "sv", gd)]
            if c == 0:
                w += spill_toks
            ph.op("vector", lambda e, c=c: e.tensor_sub(
                ut.ap()[:], st_c(ctx, c), ctx.sb["mb"].ap()[:]), waits=w)
            if split:
                ph.op("vector", lambda e: e.tensor_mul(
                    xnf.ap()[:], ut.ap()[:], ctx.sb["rstdb"].ap()[:]), inc="sv")
                vd = ctx.C["sv"]
                ph.op("scalar", lambda e, c=c: e.copy(ar_hi(ctx, c), xnf.ap()[:]),
                      waits=[("sv", vd)], inc="sa")
                ad = ctx.C["sa"]
                ph.op("gpsimd", lambda e, c=c: e.tensor_sub(
                    ar_lo(ctx, c), xnf.ap()[:], ar_hi(ctx, c)),
                    waits=[("sa", ad)], inc="sg")
                gd = ctx.C["sg"]
            else:
                ph.op("vector", lambda e, c=c: e.tensor_mul(
                    st_c(ctx, c), ut.ap()[:], ctx.sb["rstdb"].ap()[:]), inc="sv")
                gd = ctx.C["sv"]
        ph.op("sync", None, waits=[("sg" if split else "sv", gd)])
        ph.run()
        del ctx.sb["stat_sq"]



def emit_post_ln(ctx, eps=EPS_LN):
    nc = ctx.nc
    d = ctx.dram
    with ExitStack() as es:
        ob = [es.enter_context(sbt(nc, f"po_ob{i}", [128, T], F32)) for i in range(2)]
        ctx.sb["stat_sq"] = es.enter_context(sbt(nc, "po_sq", [128, T], F32))
        xb = [es.enter_context(sbt(nc, f"po_xb{i}", [128, T], F32)) for i in range(2)]

        ph = Phase(ctx, "post_ln")
        lds = [0] * KC
        sp0 = ctx.C["sp"]
        for c in range(KC):
            w = [("sp", sp0 + (c - 2) + 1)] if c >= 2 else []
            lds[c] = ph.dma(ob[c % 2].ap()[:], d["sub_out"].ap()[c], waits=w)

        def get_chunk(c):
            return ob[c % 2].ap()[:], [lds[c]]
        bc_done = emit_stats(ctx, ph, get_chunk, (), eps)
        assert ctx.C["sp"] >= sp0 + KC
        prev = 0
        for c in range(KC):
            l1 = ph.dma(ob[c % 2].ap()[:], d["sub_out"].ap()[c],
                        waits=[("sv", prev)] if c >= 2 else [("sa", bc_done)])
            l2 = ph.dma(xb[c % 2].ap()[:], d["xsp"].ap()[c])
            w = [l1, l2, ("sa", bc_done)]
            ph.op("vector", lambda e, c=c: e.tensor_sub(
                ob[c % 2].ap()[:], ob[c % 2].ap()[:], ctx.sb["mb"].ap()[:]), waits=w)
            ph.op("vector", lambda e, c=c: e.tensor_mul(
                ob[c % 2].ap()[:], ob[c % 2].ap()[:], ctx.sb["rstdb"].ap()[:]))
            ph.op("vector", lambda e, c=c: e.tensor_add(
                st_c(ctx, c), ob[c % 2].ap()[:], xb[c % 2].ap()[:]), inc="sv")
            prev = ctx.C["sv"]
        ph.op("sync", None, waits=[("sv", prev)] + ctx.dma_all())
        ph.run()
        del ctx.sb["stat_sq"]


def emit_output(ctx):
    ph = Phase(ctx, "output")
    d = ctx.dram
    last = 0
    for c in range(KC):
        last = ph.dma(d["yT"].ap()[128 * c:128 * (c + 1), :], st_c(ctx, c))
    ph.op("sync", None, waits=ctx.dma_all())
    ph.run()



def emit_proj_stream(ctx, name, ocs, wdram_of, evac, nkc=KC, rhs_hi=None, rhs_lo=None,
                     nwf=2, nws=2):
    nc = ctx.nc
    with ExitStack() as es:
        wf = [es.enter_context(sbt(nc, f"{name}_wf{i}", [128, nkc * 128], F32))
              for i in range(nwf)]
        whi = [es.enter_context(sbt(nc, f"{name}_wh{i}", [128, nkc * 128], BF16))
               for i in range(nws)]
        wlo = [es.enter_context(sbt(nc, f"{name}_wl{i}", [128, nkc * 128], BF16))
               for i in range(nws)]
        ph = Phase(ctx, name)
        if rhs_hi is None:
            rhs_hi = lambda kc, tt: ar_hi(ctx, kc)[:, 512 * tt:512 * (tt + 1)]
            rhs_lo = lambda kc, tt: ar_lo(ctx, kc)[:, 512 * tt:512 * (tt + 1)]
        free = [(0, 0)] * (len(ocs) + 2)
        split_done = [0] * (len(ocs) + 1)
        ev = evac(ph)
        for j, oc in enumerate(ocs):
            wda = wdram_of(oc)
            ld = ph.dma(wf[j % nwf].ap()[:], wda,
                        waits=[("sg", split_done[j - nwf + 1])] if j >= nwf else ())
            wsp = [ld]
            if j >= nws:
                wsp += [("sa", free[j - nws + 1][0]), ("sv", free[j - nws + 1][1])]
            ph.op("scalar", lambda e, j=j: e.copy(whi[j % nws].ap()[:], wf[j % nwf].ap()[:]),
                  waits=wsp, inc="sa")
            ad = ctx.C["sa"]
            ph.op("gpsimd", lambda e, j=j: e.tensor_sub(
                wlo[j % nws].ap()[:], wf[j % nwf].ap()[:], whi[j % nws].ap()[:]),
                waits=[("sa", ad)], inc="sg")
            split_done[j + 1] = ctx.C["sg"]
            sa_f = sv_f = 0
            for tt in range(4):
                bank = (j % 2) * 4 + tt
                w = [("sg", split_done[j + 1])] if tt == 0 else []
                if j >= 2:
                    w += [("sa", free[j - 1][0]), ("sv", free[j - 1][1])]
                first = True
                for kc in range(nkc):
                    for (wa, ra) in ((whi, rhs_hi), (whi, rhs_lo), (wlo, rhs_hi)):
                        ph.op("tensor", lambda e, j=j, tt=tt, kc=kc, wa=wa, ra=ra,
                              first=first, bank=bank: e.matmul(
                                  ctx.pb[bank].ap()[:, :],
                                  wa[j % nws].ap()[:, 128 * kc:128 * (kc + 1)],
                                  ra(kc, tt), start=first,
                                  stop=(kc == nkc - 1 and wa is wlo)),
                              waits=w if first else ())
                        w = []
                        first = False
                ph.q["tensor"][-1] = (ph.q["tensor"][-1][0], ph.q["tensor"][-1][1], "sp", 1)
                ctx.bump("sp")
                mmd = ctx.C["sp"]
                sa_f, sv_f = ev(j, oc, tt, bank, mmd)
            free[j + 1] = (sa_f, sv_f)
        ph.op("sync", None, waits=ctx.dma_all() + [("sa", ctx.C["sa"]),
                                                   ("sv", ctx.C["sv"])])
        ph.run()



def emit_l0_qkv_qk(ctx):
    d = ctx.dram
    nc = ctx.nc
    with ExitStack() as es:
        ohi = [es.enter_context(sbt(nc, f"qoh{i}", [128, T], BF16)) for i in range(2)]
        olo = [es.enter_context(sbt(nc, f"qol{i}", [128, T], BF16)) for i in range(2)]
        bql = es.enter_context(sbt(nc, "bql", [128, 48], F32))
        bq_done = [0]

        def evac(ph):
            bq_done[0] = ph.dma(bql.ap()[:], d["bqkv_l"].ap())
            st_done = [None] * 40

            def ev(j, oc, tt, bank, mmd):
                w = [("sp", mmd)]
                if tt == 0:
                    w.append(bq_done[0])
                    if j >= 2:
                        w.extend(st_done[j - 2])
                ph.op("scalar", lambda e, j=j, oc=oc, tt=tt, bank=bank: e.activation(
                    ohi[j % 2].ap()[:, 512 * tt:512 * (tt + 1)], ctx.pb[bank].ap()[:, :],
                    AF.Identity, bias=bql.ap()[:, oc:oc + 1]), waits=w, inc="sa")
                ad = ctx.C["sa"]
                ph.op("vector", lambda e, j=j, oc=oc, tt=tt, bank=bank: e.scalar_tensor_tensor(
                    olo[j % 2].ap()[:, 512 * tt:512 * (tt + 1)], ctx.pb[bank].ap()[:, :],
                    bql.ap()[:, oc:oc + 1], ohi[j % 2].ap()[:, 512 * tt:512 * (tt + 1)],
                    ALU.add, ALU.subtract), waits=[("sa", ad)], inc="sv")
                vd = ctx.C["sv"]
                if tt == 3:
                    hi_tok = ph.dma(d["qk_hi"].ap()[oc], ohi[j % 2].ap()[:],
                                    waits=[("sv", vd)])
                    lo_tok = ph.dma(d["qk_lo"].ap()[oc], olo[j % 2].ap()[:])
                    st_done[j] = (hi_tok, lo_tok)
                return ad, vd
            return ev

        emit_proj_stream(ctx, "l0qk", list(range(32)), lambda oc: d["wqkv"].ap()[oc], evac)


def emit_v(ctx, name, wd_of, use_bias):
    d = ctx.dram
    nc = ctx.nc
    with ExitStack() as es:
        wf = es.enter_context(sbt(nc, f"{name}_wf", [128, 2048], F32))
        vwh = es.enter_context(sbt(nc, f"{name}_vwh", [128, 4 * 2048], BF16))
        vwl = es.enter_context(sbt(nc, f"{name}_vwl", [128, 4 * 2048], BF16))
        ub = [es.enter_context(sbt(nc, f"{name}_ub{i}", [128, 512], F32))
              for i in range(2)]
        vh = [es.enter_context(sbt(nc, f"{name}_vh{i}", [128, 512], BF16)) for i in range(2)]
        vl = [es.enter_context(sbt(nc, f"{name}_vl{i}", [128, 512], BF16)) for i in range(2)]
        vbb = ctx.sb["mb"]

        ph = Phase(ctx, name)
        ones = ctx.sb["c_ones"]
        bc_done = 0
        if use_bias:
            bl = ph.dma(rows0(ctx), d["vbias_r"].ap())
            for tt in range(4):
                ph.op("tensor", lambda e, tt=tt: e.matmul(
                    ctx.pb[4 + tt].ap()[:, :], ones.ap()[0:1, :],
                    rows0(ctx)[:, 512 * tt:512 * (tt + 1)], start=True, stop=True),
                    waits=[bl] if tt == 0 else (), inc="sp")
                bd = ctx.C["sp"]
                ph.op("scalar", lambda e, tt=tt: e.copy(
                    vbb.ap()[:, 512 * tt:512 * (tt + 1)], ctx.pb[4 + tt].ap()[:, :]),
                    waits=[("sp", bd)], inc="sa")
                bc_done = ctx.C["sa"]
        vwh3 = vwh.ap().rearrange("p (o f) -> p o f", f=2048)
        vwl3 = vwl.ap().rearrange("p (o f) -> p o f", f=2048)
        ev_free = [(0, 0, 0)] * 20
        vh_st = [None, None]
        vl_st = [None, None]
        sg_d = 0
        for vg in range(4):
            for i in range(4):
                ld = ph.dma(wf.ap()[:], wd_of(vg, i),
                            waits=[("sg", sg_d)] if (vg or i) else ())
                ph.op("scalar", lambda e, i=i: e.copy(
                    vwh.ap()[:, 2048 * i:2048 * (i + 1)], wf.ap()[:]),
                    waits=[ld], inc="sa")
                ad = ctx.C["sa"]
                ph.op("gpsimd", lambda e, i=i: e.tensor_sub(
                    vwl.ap()[:, 2048 * i:2048 * (i + 1)], wf.ap()[:],
                    vwh.ap()[:, 2048 * i:2048 * (i + 1)]),
                    waits=[("sa", ad)], inc="sg")
                sg_d = ctx.C["sg"]
            wsplit = sg_d
            for tc in range(16):
                bank = tc % 2
                w = [("sg", wsplit)] if tc == 0 else []
                if tc >= 2:
                    sa_f, sv_f, sg_f = ev_free[tc - 2]
                    w += [("sa", sa_f), ("sv", sv_f), ("sg", sg_f)]
                first = True
                for kc in range(KC):
                    for (xa, wa3) in ((ar_hi, vwh3), (ar_lo, vwh3), (ar_hi, vwl3)):
                        ph.op("tensor", lambda e, tc=tc, kc=kc, xa=xa, wa3=wa3,
                              first=first, bank=bank: e.matmul(
                                  ctx.pb[bank].ap()[:, :],
                                  xa(ctx, kc)[:, 128 * tc:128 * (tc + 1)],
                                  wa3[:, :, 128 * kc:128 * (kc + 1)],
                                  start=first,
                                  stop=(kc == KC - 1 and xa is ar_hi and wa3 is vwl3)),
                              waits=w if first else ())
                        w = []
                        first = False
                ph.q["tensor"][-1] = (ph.q["tensor"][-1][0], ph.q["tensor"][-1][1], "sp", 1)
                ctx.bump("sp")
                mmd = ctx.C["sp"]
                if use_bias:
                    wv = [("sp", mmd), ("sa", bc_done)]
                    if tc >= 2:
                        wv.append(("sg", ev_free[tc - 2][2]))
                    ph.op("vector", lambda e, tc=tc, bank=bank, vg=vg: e.tensor_add(
                        ub[tc % 2].ap()[:], ctx.pb[bank].ap()[:, :],
                        vbb.ap()[:, 512 * vg:512 * (vg + 1)]),
                        waits=wv, inc="sv")
                    vd = ctx.C["sv"]
                    wv2 = [("sv", vd), vh_st[tc % 2]]
                    ph.op("scalar", lambda e, tc=tc: e.copy(vh[tc % 2].ap()[:],
                                                            ub[tc % 2].ap()[:]),
                          waits=wv2, inc="sa")
                    ad = ctx.C["sa"]
                    ph.op("gpsimd", lambda e, tc=tc: e.tensor_sub(
                        vl[tc % 2].ap()[:], ub[tc % 2].ap()[:], vh[tc % 2].ap()[:]),
                        waits=[("sa", ad), vl_st[tc % 2]], inc="sg")
                    gd = ctx.C["sg"]
                else:
                    w2 = [("sp", mmd), vh_st[tc % 2]]
                    ph.op("scalar", lambda e, tc=tc, bank=bank: e.copy(
                        vh[tc % 2].ap()[:], ctx.pb[bank].ap()[:, :]),
                        waits=w2, inc="sa")
                    ad = ctx.C["sa"]
                    ph.op("vector", lambda e, tc=tc, bank=bank: e.tensor_sub(
                        vl[tc % 2].ap()[:], ctx.pb[bank].ap()[:, :], vh[tc % 2].ap()[:]),
                        waits=[("sa", ad), vl_st[tc % 2]], inc="sv")
                    gd = 0
                vd2 = ctx.C["sv"]
                w = [("sa", ctx.C["sa"]), ("sv", vd2)]
                if use_bias:
                    w.append(("sg", gd))
                vh_st[tc % 2] = ph.dma(
                    d["v_hi"].ap()[4 * vg:4 * (vg + 1), 128 * tc:128 * (tc + 1), :]
                    .rearrange("h t e -> t h e"),
                    vh[tc % 2].ap().rearrange("p (h e) -> p h e", e=128), waits=w)
                vl_st[tc % 2] = ph.dma(
                    d["v_lo"].ap()[4 * vg:4 * (vg + 1), 128 * tc:128 * (tc + 1), :]
                    .rearrange("h t e -> t h e"),
                    vl[tc % 2].ap().rearrange("p (h e) -> p h e", e=128))
                ev_free[tc] = (ctx.C["sa"], ctx.C["sv"], ctx.C["sg"])
            ph.op("sync", None, waits=ctx.dma_all())
        ph.run()


def emit_l0_lightning(ctx):
    d = ctx.dram
    nc = ctx.nc
    with ExitStack() as es:
        qh = es.enter_context(sbt(nc, "lt_qh", [128, T], BF16))
        ql = es.enter_context(sbt(nc, "lt_ql", [128, T], BF16))
        kh = es.enter_context(sbt(nc, "lt_kh", [128, T], BF16))
        kl = es.enter_context(sbt(nc, "lt_kl", [128, T], BF16))
        vbh = es.enter_context(sbt(nc, "lt_vh", [128, T], BF16))
        vbl = es.enter_context(sbt(nc, "lt_vl", [128, T], BF16))
        smf = es.enter_context(sbt(nc, "lt_smf", [128, 128], F32))
        smh = es.enter_context(sbt(nc, "lt_smh", [128, 128], BF16))
        sml = es.enter_context(sbt(nc, "lt_sml", [128, 128], BF16))
        kf = es.enter_context(sbt(nc, "lt_kf", [128, 128], F32))
        khh = es.enter_context(sbt(nc, "lt_khh", [128, 128], BF16))
        khl = es.enter_context(sbt(nc, "lt_khl", [128, 128], BF16))
        qf = es.enter_context(sbt(nc, "lt_qf", [128, 128], F32))
        qtf = es.enter_context(sbt(nc, "lt_qtf", [128, 128], F32))
        qth = es.enter_context(sbt(nc, "lt_qth", [128, 128], BF16))
        qtl = es.enter_context(sbt(nc, "lt_qtl", [128, 128], BF16))
        S = es.enter_context(sbt(nc, "lt_S", [128, 128], F32))
        Sh = es.enter_context(sbt(nc, "lt_Sh", [128, 128], BF16))
        Sl = es.enter_context(sbt(nc, "lt_Sl", [128, 128], BF16))
        dTile = es.enter_context(sbt(nc, "lt_dT", [128, 128], F32))
        decayT = es.enter_context(sbt(nc, "lt_decayT", [128, 128], F32))
        qdecb = es.enter_context(sbt(nc, "lt_qdecb", [128, 128], F32))
        kd2c = es.enter_context(sbt(nc, "lt_kd2c", [128, 1], F32))
        blkc = es.enter_context(sbt(nc, "lt_blkc", [128, 1], F32))
        sneg = es.enter_context(sbt(nc, "lt_sneg", [128, 16], F32))
        cdif = es.enter_context(sbt(nc, "lt_cdif", [128, 128], F32))
        ctri = es.enter_context(sbt(nc, "lt_ctri", [128, 128], F32))
        ciot = es.enter_context(sbt(nc, "lt_ciot", [128, 128], F32))
        cdcl = es.enter_context(sbt(nc, "lt_cdcl", [128, 2], F32))
        srow = es.enter_context(sbt(nc, "lt_srow", [1, 16], F32))

        ph = Phase(ctx, "lightning")
        ones = ctx.sb["c_ones"]
        cld = [ph.dma(cdif.ap()[:], d["diffD"].ap()),
               ph.dma(ctri.ap()[:], d["tri01"].ap()),
               ph.dma(ciot.ap()[:], d["iotaF"].ap()),
               ph.dma(cdcl.ap()[:], d["dcol"].ap())]
        ld = ph.dma(srow.ap()[:], d["slopes_r"].ap())
        ph.op("tensor", lambda e: e.matmul(ctx.pb[0].ap()[:, 0:16], ones.ap()[0:1, :],
                                           srow.ap()[:], start=True, stop=True),
              waits=[ld], inc="sp")
        pd = ctx.C["sp"]
        ph.op("scalar", lambda e: e.activation(sneg.ap()[:], ctx.pb[0].ap()[:, 0:16],
                                               AF.Identity, scale=-1.0),
              waits=[("sp", pd)], inc="sa")
        sneg_done = ctx.C["sa"]

        for h in range(NH):
            head_w = [("sa", sneg_done)] + cld if h == 0 else \
                [("sv", ctx.C["sv"]), ("sg", ctx.C["sg"]), ("sp", ctx.C["sp"])]
            ph.op("scalar", lambda e, h=h: e.activation(
                dTile.ap()[:], cdif.ap()[:], AF.Exp, scale=sneg.ap()[:, h:h + 1]),
                waits=head_w, inc="sa")
            ad = ctx.C["sa"]
            ph.op("vector", lambda e: e.tensor_mul(decayT.ap()[:], dTile.ap()[:],
                                                   ctri.ap()[:]),
                  waits=[("sa", ad)], inc="sv")
            dec_done = ctx.C["sv"]
            ph.op("scalar", lambda e, h=h: e.activation(
                qdecb.ap()[:], ciot.ap()[:], AF.Exp, scale=sneg.ap()[:, h:h + 1]))
            ph.op("scalar", lambda e, h=h: e.activation(
                kd2c.ap()[:], cdcl.ap()[:, 0:1], AF.Exp, scale=sneg.ap()[:, h:h + 1]))
            ph.op("scalar", lambda e, h=h: e.activation(
                blkc.ap()[:], cdcl.ap()[:, 1:2], AF.Exp, scale=sneg.ap()[:, h:h + 1]),
                inc="sa")
            dec2_done = ctx.C["sa"]
            ph.op("vector", lambda e: e.memset(S.ap()[:], 0.0),
                  waits=[("sa", dec2_done)], inc="sv")
            mem_done = ctx.C["sv"]
            ld_toks = [
                ph.dma(qh.ap()[:], d["qk_hi"].ap()[h]),
                ph.dma(ql.ap()[:], d["qk_lo"].ap()[h]),
                ph.dma(kh.ap()[:], d["qk_hi"].ap()[16 + h]),
                ph.dma(kl.ap()[:], d["qk_lo"].ap()[16 + h]),
                ph.dma(vbh.ap().rearrange("p (c e) -> p c e", e=128),
                       d["v_hi"].ap()[h].rearrange("(c p) e -> p c e", p=128)),
                ph.dma(vbl.ap().rearrange("p (c e) -> p c e", e=128),
                       d["v_lo"].ap()[h].rearrange("(c p) e -> p c e", p=128)),
            ]
            Ssplit = (0, 0)
            pe = {}
            for c in range(16):
                cs = slice(128 * c, 128 * (c + 1))
                bs = (c % 2) * 4
                w = ld_toks + [("sv", mem_done), ("sa", dec2_done)] if c == 0 else \
                    [("sv", pe.get(("smf", c - 2), 0))]
                for i, (wa, ra) in enumerate(((kh, qh), (kh, ql), (kl, qh))):
                    ph.op("tensor", lambda e, wa=wa, ra=ra, cs=cs, i=i, bs=bs: e.matmul(
                        ctx.pb[bs].ap()[:, 0:128], wa.ap()[:, cs], ra.ap()[:, cs],
                        start=(i == 0), stop=(i == 2)),
                        waits=w if i == 0 else ())
                    w = []
                ph.q["tensor"][-1] = (ph.q["tensor"][-1][0], ph.q["tensor"][-1][1], "sp", 1)
                ctx.bump("sp")
                sc_done = ctx.C["sp"]
                ph.op("vector", lambda e, bs=bs: e.tensor_mul(
                    smf.ap()[:], ctx.pb[bs].ap()[:, 0:128], decayT.ap()[:]),
                    waits=[("sp", sc_done), ("sa", pe.get(("smh", c - 1), 0))], inc="sv")
                pe[("smf", c)] = ctx.C["sv"]
                ph.op("scalar", lambda e: e.copy(smh.ap()[:], smf.ap()[:]),
                      waits=[("sv", pe[("smf", c)]), ("sp", pe.get(("o", c - 1), 0))],
                      inc="sa")
                pe[("smh", c)] = ctx.C["sa"]
                ph.op("vector", lambda e: e.tensor_sub(sml.ap()[:], smf.ap()[:], smh.ap()[:]),
                      waits=[("sa", pe[("smh", c)]), ("sp", pe.get(("o", c - 1), 0))],
                      inc="sv")
                sm_done = ctx.C["sv"]
                ph.op("vector", lambda e, cs=cs: e.tensor_add(
                    kf.ap()[:], kh.ap()[:, cs], kl.ap()[:, cs]),
                    waits=[("sp", pe.get(("ktr", c - 1), 0))], inc="sv")
                kf_done = ctx.C["sv"]
                ph.op("tensor", lambda e, bs=bs: e.transpose(
                    ctx.pb[bs + 1].ap()[:, 0:128], kf.ap()[:], ctx.sb["c_ident"].ap()[:]),
                    waits=[("sv", kf_done), ("sa", pe.get(("khh", c - 2), 0)),
                           ("sv", pe.get(("khl", c - 2), 0))], inc="sp")
                pe[("ktr", c)] = ctx.C["sp"]
                ph.op("scalar", lambda e, bs=bs: e.activation(
                    khh.ap()[:], ctx.pb[bs + 1].ap()[:, 0:128], AF.Identity,
                    scale=kd2c.ap()[:, 0:1]),
                    waits=[("sp", pe[("ktr", c)]), ("sp", pe.get(("su", c - 1), 0))],
                    inc="sa")
                pe[("khh", c)] = ctx.C["sa"]
                ph.op("vector", lambda e, bs=bs: e.scalar_tensor_tensor(
                    khl.ap()[:], ctx.pb[bs + 1].ap()[:, 0:128], kd2c.ap()[:, 0:1],
                    khh.ap()[:], ALU.mult, ALU.subtract),
                    waits=[("sa", pe[("khh", c)]), ("sp", pe.get(("su", c - 1), 0))],
                    inc="sv")
                pe[("khl", c)] = ctx.C["sv"]
                khat_done = ctx.C["sv"]
                wq = [("sa", pe.get(("qth", c - 1), 0)), ("sg", pe.get(("qtl", c - 1), 0))]
                if c == 0:
                    wq.append(("sa", dec2_done))
                ph.op("vector", lambda e, cs=cs: e.tensor_add(
                    qf.ap()[:], qh.ap()[:, cs], ql.ap()[:, cs]), waits=wq)
                ph.op("vector", lambda e: e.tensor_mul(qtf.ap()[:], qf.ap()[:],
                                                       qdecb.ap()[:]), inc="sv")
                qtf_done = ctx.C["sv"]
                ph.op("scalar", lambda e: e.copy(qth.ap()[:], qtf.ap()[:]),
                      waits=[("sv", qtf_done), ("sp", pe.get(("o", c - 1), 0))], inc="sa")
                pe[("qth", c)] = ctx.C["sa"]
                ph.op("gpsimd", lambda e: e.tensor_sub(qtl.ap()[:], qtf.ap()[:], qth.ap()[:]),
                      waits=[("sa", pe[("qth", c)]), ("sp", pe.get(("o", c - 1), 0))],
                      inc="sg")
                pe[("qtl", c)] = ctx.C["sg"]
                qt_done = ctx.C["sg"]
                mms = [(vbh, smh), (vbh, sml), (vbl, smh)]
                if c > 0:
                    mms += [(Sh, qth), (Sh, qtl), (Sl, qth)]
                w = [("sv", sm_done), ("sg", qt_done),
                     ("sa", pe.get(("oh", c - 2), 0)), ("sv", pe.get(("ol", c - 2), 0))]
                if c > 0:
                    w += [("sa", Ssplit[0]), ("sv", Ssplit[1])]
                for i, (wa, ra) in enumerate(mms):
                    def mmfn(e, wa=wa, ra=ra, i=i, cs=cs, bs=bs, n=len(mms)):
                        lhs = wa.ap()[:, cs] if wa in (vbh, vbl) else wa.ap()[:]
                        return e.matmul(ctx.pb[bs + 2].ap()[:, 0:128], lhs, ra.ap()[:],
                                        start=(i == 0), stop=(i == n - 1))
                    ph.op("tensor", mmfn, waits=w if i == 0 else ())
                    w = []
                ph.q["tensor"][-1] = (ph.q["tensor"][-1][0], ph.q["tensor"][-1][1], "sp", 1)
                ctx.bump("sp")
                pe[("o", c)] = ctx.C["sp"]
                o_done = ctx.C["sp"]
                ph.op("scalar", lambda e, bs=bs, h=h, cs=cs: e.copy(
                    ar_hi(ctx, h)[:, cs], ctx.pb[bs + 2].ap()[:, 0:128]),
                    waits=[("sp", o_done)], inc="sa")
                pe[("oh", c)] = ctx.C["sa"]
                ph.op("vector", lambda e, bs=bs, h=h, cs=cs: e.tensor_sub(
                    ar_lo(ctx, h)[:, cs], ctx.pb[bs + 2].ap()[:, 0:128], ar_hi(ctx, h)[:, cs]),
                    waits=[("sa", pe[("oh", c)])], inc="sv")
                pe[("ol", c)] = ctx.C["sv"]
                w = [("sv", khat_done), ("sv", pe.get(("sustt", c - 2), 0))]
                for i, (wa, ra) in enumerate(((khh, vbh), (khh, vbl), (khl, vbh))):
                    ph.op("tensor", lambda e, wa=wa, ra=ra, i=i, cs=cs, bs=bs: e.matmul(
                        ctx.pb[bs + 3].ap()[:, 0:128], wa.ap()[:], ra.ap()[:, cs],
                        start=(i == 0), stop=(i == 2)),
                        waits=w if i == 0 else ())
                    w = []
                ph.q["tensor"][-1] = (ph.q["tensor"][-1][0], ph.q["tensor"][-1][1], "sp", 1)
                ctx.bump("sp")
                pe[("su", c)] = ctx.C["sp"]
                su_done = ctx.C["sp"]
                ph.op("vector", lambda e, bs=bs: e.scalar_tensor_tensor(
                    S.ap()[:], S.ap()[:], blkc.ap()[:, 0:1], ctx.pb[bs + 3].ap()[:, 0:128],
                    ALU.mult, ALU.add),
                    waits=[("sp", su_done), ("sa", pe.get(("Sh", c - 1), 0))], inc="sv")
                pe[("sustt", c)] = ctx.C["sv"]
                ph.op("scalar", lambda e: e.copy(Sh.ap()[:], S.ap()[:]),
                      waits=[("sv", pe[("sustt", c)])], inc="sa")
                pe[("Sh", c)] = ctx.C["sa"]
                ph.op("vector", lambda e: e.tensor_sub(Sl.ap()[:], S.ap()[:], Sh.ap()[:]),
                      waits=[("sa", pe[("Sh", c)])], inc="sv")
                Ssplit = (pe[("Sh", c)], ctx.C["sv"])
            ph.op("sync", None, waits=[("sp", ctx.C["sp"]), ("sv", ctx.C["sv"]),
                                       ("sa", ctx.C["sa"]), ("sg", ctx.C["sg"])])
        ph.run()


def emit_l0_outproj(ctx):
    d = ctx.dram
    nc = ctx.nc
    with ExitStack() as es:
        ot = [es.enter_context(sbt(nc, f"op_ot{i}", [128, 512], F32)) for i in range(4)]
        bol = es.enter_context(sbt(nc, "op_bol", [128, 16], F32))
        b_done = [0]

        def evac(ph):
            b_done[0] = ph.dma(bol.ap()[:], d["bout_l"].ap())
            st_done = [None] * 70

            def ev(j, oc, tt, bank, mmd):
                ti = (j * 4 + tt) % 4
                w = [("sp", mmd), st_done[max(0, j * 4 + tt - 4)]]
                if j == 0 and tt == 0:
                    w.append(b_done[0])
                ph.op("scalar", lambda e, ti=ti, bank=bank, oc=oc: e.activation(
                    ot[ti].ap()[:], ctx.pb[bank].ap()[:, :], AF.Identity,
                    bias=bol.ap()[:, oc:oc + 1]), waits=w, inc="sa")
                ad = ctx.C["sa"]
                st_done[j * 4 + tt] = ph.dma(
                    d["sub_out"].ap()[oc][:, 512 * tt:512 * (tt + 1)], ot[ti].ap()[:],
                    waits=[("sa", ad)])
                return ad, 0
            return ev

        emit_proj_stream(ctx, "l0op", list(range(16)), lambda oc: d["wout"].ap()[oc], evac)


def emit_l0_attn(ctx):
    emit_pre_ln(ctx, split=True, eps=EPS_LN)
    emit_l0_qkv_qk(ctx)
    emit_v(ctx, "l0v", lambda vg, i: ctx.dram["wqkv"].ap()[32 + 4 * vg + i], use_bias=True)
    emit_l0_lightning(ctx)
    emit_l0_outproj(ctx)
    emit_post_ln(ctx, eps=EPS_LN)



def emit_quant_chain(ctx, ph, wf_ap, wbf, qr, scl, rcp, width, ld_wait, pool_guard):
    ng = width // 64
    ph.op("scalar", lambda e: e.copy(wbf.ap()[:, 0:width], wf_ap),
          waits=ld_wait, inc="sa")
    ad = ctx.C["sa"]
    ph.op("vector", lambda e: e.reduce_sum(
        rcp.ap()[:, 0:ng], wbf.ap()[:, 0:width].rearrange("p (g e) -> p g e", e=64),
        axis=AX.X, apply_absolute_value=True), waits=[("sa", ad)] + list(pool_guard))
    ph.op("vector", lambda e: e.tensor_scalar(
        scl.ap()[:, 0:ng], rcp.ap()[:, 0:ng], 1.0 / 64, 1e-8, ALU.mult, ALU.max))
    ph.op("vector", lambda e: e.reciprocal(rcp.ap()[:, 0:ng], scl.ap()[:, 0:ng]))
    sclb = scl.ap()[:, 0:ng].unsqueeze(2).broadcast_to([128, ng, 64])
    rcpb = rcp.ap()[:, 0:ng].unsqueeze(2).broadcast_to([128, ng, 64])
    w3 = wbf.ap()[:, 0:width].rearrange("p (g e) -> p g e", e=64)
    q3 = qr.ap()[:, 0:width].rearrange("p (g e) -> p g e", e=64)
    ph.op("vector", lambda e: e.tensor_tensor(q3, w3, rcpb, ALU.mult))
    if getattr(ctx, "dbg_quant", False):
        ph.op("vector", lambda e: e.engine_nop(), inc="sv")
        qv = ctx.C["sv"]
        ph.dma(ctx.dram["dbg"].ap()[:, 6144:6144 + width], qr.ap()[:, 0:width],
               waits=[("sv", qv)])
        ph.dma(ctx.dram["dbg"].ap()[:, 6144 - width // 64:6144],
               scl.ap()[:, 0:width // 64])
        ctx.dbg_quant = False
    ph.op("vector", lambda e: e.tensor_scalar(
        wbf.ap()[:, 0:width], qr.ap()[:, 0:width], -0.5, None, ALU.is_lt))
    ph.op("vector", lambda e: e.scalar_tensor_tensor(
        wbf.ap()[:, 0:width], qr.ap()[:, 0:width], 0.5, wbf.ap()[:, 0:width],
        ALU.is_gt, ALU.subtract))
    w3b = wbf.ap()[:, 0:width].rearrange("p (g e) -> p g e", e=64)
    ph.op("vector", lambda e: e.tensor_tensor(q3, w3b, sclb, ALU.mult), inc="sv")
    return ctx.C["sv"]


def emit_l0f_gu(ctx):
    d = ctx.dram
    nc = ctx.nc
    with ExitStack() as es:
        wf = [es.enter_context(sbt(nc, f"gu_wf{i}", [128, 2048], F32)) for i in range(2)]
        wbf = es.enter_context(sbt(nc, "gu_wbf", [128, 2048], BF16))
        qr = es.enter_context(sbt(nc, "gu_qr", [128, 2048], BF16))
        scl = es.enter_context(sbt(nc, "gu_scl", [128, 64], BF16))
        rcp = es.enter_context(sbt(nc, "gu_rcp", [128, 32], F32))
        whT = es.enter_context(sbt(nc, "gu_whT", [128, 2048], BF16))
        silu = es.enter_context(sbt(nc, "gu_silu", [128, 2048], F32))
        hsb = [es.enter_context(sbt(nc, f"gu_h{i}", [128, 2048], F32)) for i in range(2)]
        hsq = es.enter_context(sbt(nc, "gu_hsq", [128, 512], F32))
        ones = ctx.sb["c_ones"]

        ph = Phase(ctx, "l0f_gu")
        prev = {}
        for pi in range(64):
            hstore = 0
            for half, ch in ((0, pi), (1, 64 + pi)):
                j = pi * 2 + half
                ld = ph.dma(wf[j % 2].ap()[:], d["gup_nat"].ap()[ch],
                            waits=[("sa", prev.get(("wbf", j - 2), 0))])
                qd = emit_quant_chain(ctx, ph, wf[j % 2].ap()[:], wbf, qr, scl, rcp,
                                      2048, [ld, ("sv", prev.get(("tev", j - 1), 0))],
                                      [("sv", 0)])
                prev[("wbf", j)] = ctx.C["sa"]
                tevs = {2: prev.get(("tev", j - 1), 0),
                        3: prev.get(("tev", j - 1), 0)}
                for kc in range(16):
                    bnk = 2 + kc % 2
                    wt = [("sv", qd)] if kc == 0 else []
                    wt.append(("sv", tevs[bnk]))
                    ph.op("tensor", lambda e, kc=kc, bnk=bnk: e.transpose(
                        ctx.pbb[bnk].ap()[:, 0:128],
                        qr.ap()[:, 128 * kc:128 * (kc + 1)], ctx.sb["c_identb"].ap()[:]),
                        waits=wt, inc="sp")
                    td = ctx.C["sp"]
                    ph.op("vector", lambda e, kc=kc, bnk=bnk: e.tensor_copy(
                        whT.ap()[:, 128 * kc:128 * (kc + 1)], ctx.pbb[bnk].ap()[:, 0:128]),
                        waits=[("sp", td)], inc="sv")
                    tevs[bnk] = ctx.C["sv"]
                prev[("tev", j)] = ctx.C["sv"]
                tev = ctx.C["sv"]
                for tt in range(4):
                    w = [("sv", tev)] if tt == 0 else []
                    if tt >= 1:
                        w.append(("sa" if half == 0 else "sv", prev.get(("ev", half, tt - 1, pi), 0)))
                    else:
                        w.append(("sa" if half == 0 else "sv", prev.get(("ev", half, 3, pi - 1), 0)))
                    first = True
                    for kc in range(KC):
                        for xa in (ar_hi, ar_lo):
                            ph.op("tensor", lambda e, tt=tt, kc=kc, xa=xa, first=first,
                                  half=half: e.matmul(
                                      ctx.pb[half].ap()[:, :],
                                      whT.ap()[:, 128 * kc:128 * (kc + 1)],
                                      xa(ctx, kc)[:, 512 * tt:512 * (tt + 1)],
                                      start=first, stop=(kc == KC - 1 and xa is ar_lo)),
                                  waits=w if first else ())
                            w = []
                            first = False
                    ph.q["tensor"][-1] = (ph.q["tensor"][-1][0], ph.q["tensor"][-1][1], "sp", 1)
                    ctx.bump("sp")
                    mmd = ctx.C["sp"]
                    if half == 0:
                        ph.op("scalar", lambda e, tt=tt: e.activation(
                            silu.ap()[:, 512 * tt:512 * (tt + 1)], ctx.pb[0].ap()[:, :],
                            AF.Silu),
                            waits=[("sp", mmd), ("sv", prev.get(("ev", 1, tt, pi - 1), 0))],
                            inc="sa")
                        prev[("ev", 0, tt, pi)] = ctx.C["sa"]
                    else:
                        wv = [("sp", mmd), ("sa", prev[("ev", 0, tt, pi)]),
                              prev.get(("hst", pi - 2), None)]
                        ph.op("vector", lambda e, tt=tt, pi=pi: e.tensor_mul(
                            hsb[pi % 2].ap()[:, 512 * tt:512 * (tt + 1)],
                            ctx.pb[1].ap()[:, :], silu.ap()[:, 512 * tt:512 * (tt + 1)]),
                            waits=wv, inc="sv")
                        prev[("ev", 1, tt, pi)] = ctx.C["sv"]
                        hd = ctx.C["sv"]
                        ph.op("scalar", lambda e, tt=tt, pi=pi: e.activation(
                            hsq.ap()[:], hsb[pi % 2].ap()[:, 512 * tt:512 * (tt + 1)],
                            AF.Square), waits=[("sv", hd), ("sp", prev.get("sqlast", 0))],
                            inc="sa")
                        sqd = ctx.C["sa"]
                        ph.op("tensor", lambda e, tt=tt, pi=pi: e.matmul(
                            ctx.pb[4 + tt].ap()[0:1, :], ones.ap()[:, 0:1], hsq.ap()[:],
                            start=(pi == 0), stop=(pi == 63)),
                            waits=[("sa", sqd)], inc="sp")
                        prev["sqlast"] = ctx.C["sp"]
            prev[("hst", pi)] = ph.dma(d["h_f32"].ap()[pi], hsb[pi % 2].ap()[:],
                                       waits=[("sv", prev[("ev", 1, 3, pi)])])
        ph.op("sync", None, waits=ctx.dma_all() + [("sp", ctx.C["sp"]),
                                   ("sv", ctx.C["sv"]), ("sa", ctx.C["sa"])])
        ph.run()


def emit_l0f_hn(ctx):
    d = ctx.dram
    nc = ctx.nc
    ones = ctx.sb["c_ones"]
    with ExitStack() as es:
        hb = [es.enter_context(sbt(nc, f"hn_hb{i}", [128, T], F32)) for i in range(3)]
        hh = [es.enter_context(sbt(nc, f"hn_hh{i}", [128, T], BF16)) for i in range(2)]
        hl = [es.enter_context(sbt(nc, f"hn_hl{i}", [128, T], BF16)) for i in range(2)]
        ph = Phase(ctx, "l0f_hn")
        for tt in range(4):
            ph.op("vector", lambda e, tt=tt: e.tensor_scalar(
                rows0(ctx)[:, 512 * tt:512 * (tt + 1)], ctx.pb[4 + tt].ap()[0:1, :],
                1.0 / 8192.0, None, ALU.mult), inc="sv" if tt == 3 else None)
        md = ctx.C["sv"]
        ph.op("scalar", lambda e: e.activation(
            rsc(ctx), rows0(ctx), AF.Sqrt,
            bias=ctx.sb["c_cvals"].ap()[0:1, 1:2]), waits=[("sv", md)], inc="sa")
        sd = ctx.C["sa"]
        ph.op("vector", lambda e: e.reciprocal(rows0(ctx), rsc(ctx)),
              waits=[("sa", sd)], inc="sv")
        rd = ctx.C["sv"]
        rb = 0
        for tt in range(4):
            ph.op("tensor", lambda e, tt=tt: e.matmul(
                ctx.pb[tt % 2].ap()[:, :], ones.ap()[0:1, :],
                rows0(ctx)[:, 512 * tt:512 * (tt + 1)], start=True, stop=True),
                waits=([("sv", rd)] if tt == 0 else
                       ([("sa", rb - 1)] if tt >= 2 else ())), inc="sp")
            bd = ctx.C["sp"]
            ph.op("scalar", lambda e, tt=tt: e.copy(
                ctx.sb["rstdb"].ap()[:, 512 * tt:512 * (tt + 1)], ctx.pb[tt % 2].ap()[:, :]),
                waits=[("sp", bd)], inc="sa")
            rb = ctx.C["sa"]
        bc = rb
        prev = {}
        for ch in range(64):
            bi = ch % 3
            ld = ph.dma(hb[bi].ap()[:], d["h_f32"].ap()[ch],
                        waits=[prev.get(("lo", ch - 3))])
            ph.op("vector", lambda e, bi=bi: e.tensor_mul(
                hb[bi].ap()[:], hb[bi].ap()[:], ctx.sb["rstdb"].ap()[:]),
                waits=[ld, ("sa", bc)], inc="sv")
            hnfd = ctx.C["sv"]
            ph.op("scalar", lambda e, ch=ch, bi=bi: e.copy(
                hh[ch % 2].ap()[:], hb[bi].ap()[:]),
                waits=[("sv", hnfd)] + list(prev.get(("st", ch - 2), ())),
                inc="sa")
            ad = ctx.C["sa"]
            ph.op("gpsimd", lambda e, ch=ch, bi=bi: e.tensor_sub(
                hl[ch % 2].ap()[:], hb[bi].ap()[:], hh[ch % 2].ap()[:]),
                waits=[("sa", ad)], inc="sg")
            prev[("lo", ch)] = ("sg", ctx.C["sg"])
            hi_tok = ph.dma(d["hn_hi"].ap()[ch], hh[ch % 2].ap()[:],
                            waits=[("sg", ctx.C["sg"])])
            lo_tok = ph.dma(d["hn_lo"].ap()[ch], hl[ch % 2].ap()[:])
            prev[("st", ch)] = (hi_tok, lo_tok)
        ph.op("sync", None, waits=ctx.dma_all())
        ph.run()


def emit_l0f_wpj_quant(ctx):
    d = ctx.dram
    nc = ctx.nc
    with ExitStack() as es:
        wf = es.enter_context(sbt(nc, "pq_wf", [128, 4096], F32))
        wbf = es.enter_context(sbt(nc, "pq_wbf", [128, 4096], BF16))
        qr = es.enter_context(sbt(nc, "pq_qr", [128, 4096], BF16))
        scl = es.enter_context(sbt(nc, "pq_scl", [128, 128], BF16))
        rcp = es.enter_context(sbt(nc, "pq_rcp", [128, 64], F32))
        wt = es.enter_context(sbt(nc, "pq_wt", [128, 8192], BF16))
        ph = Phase(ctx, "l0f_pq")
        prev = {}
        for ch in range(16):
            for hf in range(2):
                j = ch * 2 + hf
                ld = ph.dma(wf.ap()[:], d["wpj0_nat"].ap()[ch][:, 4096 * hf:4096 * (hf + 1)],
                            waits=[("sa", prev.get(("wbf", j - 1), 0))])
                tprev = [t for t in prev.get(("tev", j - 1), ()) if t]
                qd = emit_quant_chain(ctx, ph, wf.ap()[:], wbf, qr, scl, rcp, 4096,
                                      [ld] + tprev,
                                      [("sv", 0)])
                prev[("wbf", j)] = ctx.C["sa"]
                tevs = [prev.get(("tev", j - 1), (None,) * 8)[b] for b in range(8)]
                for kcH in range(32):
                    bnk = kcH % 8
                    eng = "scalar" if bnk < 4 else "vector"
                    sem = "sa" if bnk < 4 else "sv"
                    w = [("sv", qd)] if kcH == 0 else []
                    if tevs[bnk]:
                        w.append(tevs[bnk])
                    ph.op("tensor", lambda e, kcH=kcH, bnk=bnk: e.transpose(
                        ctx.pbb[bnk].ap()[:, 0:128],
                        qr.ap()[:, 128 * kcH:128 * (kcH + 1)], ctx.sb["c_identb"].ap()[:]),
                        waits=w, inc="sp")
                    td = ctx.C["sp"]
                    ev_w = [("sp", td)]
                    if kcH == 0 and hf == 0 and ("st", ch - 1) in prev:
                        ev_w.append(prev[("st", ch - 1)])
                    if eng == "scalar":
                        ph.op(eng, lambda e, kcH=kcH, bnk=bnk, hf=hf: e.copy(
                            wt.ap()[:, 4096 * hf + 128 * kcH:4096 * hf + 128 * (kcH + 1)],
                            ctx.pbb[bnk].ap()[:, 0:128]),
                            waits=ev_w, inc=sem)
                    else:
                        ph.op(eng, lambda e, kcH=kcH, bnk=bnk, hf=hf: e.tensor_copy(
                            wt.ap()[:, 4096 * hf + 128 * kcH:4096 * hf + 128 * (kcH + 1)],
                            ctx.pbb[bnk].ap()[:, 0:128]),
                            waits=ev_w, inc=sem)
                    tevs[bnk] = (sem, ctx.C[sem])
                prev[("tev", j)] = tuple(tevs)
            std = ph.dma(d["wpj0_t"].ap()[ch], wt.ap()[:],
                         waits=list(prev[("tev", ch * 2 + 1)]) +
                               [prev.get(("st", ch - 1), None)])
            prev[("st", ch)] = std
        ph.op("sync", None, waits=ctx.dma_all())
        ph.run()


def emit_ffn_proj(ctx, name, wt_dram, n_oc, sub_scale=None):
    d = ctx.dram
    nc = ctx.nc
    with ExitStack() as es:
        hh = [es.enter_context(sbt(nc, f"{name}_hh{i}", [128, 512], BF16))
              for i in range(3)]
        hl = [es.enter_context(sbt(nc, f"{name}_hl{i}", [128, 512], BF16))
              for i in range(3)]
        ot = [es.enter_context(sbt(nc, f"{name}_ot{i}", [128, 512], F32))
              for i in range(2)]
        ph = Phase(ctx, name)
        nog = n_oc // 8
        prev = {}
        arb = ctx.arena_b
        for og in range(nog):
            wlds = []
            for i in range(8):
                wlds.append(ph.dma(
                    arb.ap()[:, 8192 * i:8192 * (i + 1)], wt_dram.ap()[og * 8 + i],
                    waits=[("sp", prev.get(("mm", og - 1), 0))] if (og and i == 0) else []))
            for tt in range(4):
                for kcH in range(64):
                    jj = tt * 64 + kcH
                    bi = jj % 3
                    gw = [("sp", prev.get(("hn", og, jj - 3), 0))] if jj >= 3 else []
                    l1 = ph.dma(hh[bi].ap()[:], d["hn_hi"].ap()[kcH][:, 512 * tt:512 * (tt + 1)],
                                waits=gw)
                    l2 = ph.dma(hl[bi].ap()[:], d["hn_lo"].ap()[kcH][:, 512 * tt:512 * (tt + 1)])
                    w = [l1, l2]
                    if kcH == 0:
                        w += wlds
                        if tt >= 1:
                            w.append(("sa", prev.get(("ev", og, tt - 1), 0)))
                        elif og >= 1:
                            w.append(("sa", prev.get(("ev", og - 1, 3), 0)))
                    for oc8 in range(8):
                        for hx in (hh, hl):
                            ph.op("tensor", lambda e, oc8=oc8, hx=hx, bi=bi, kcH=kcH: e.matmul(
                                ctx.pb[oc8].ap()[:, :],
                                arb.ap()[:, 8192 * oc8 + 128 * kcH:8192 * oc8 + 128 * (kcH + 1)],
                                hx[bi].ap()[:], start=(kcH == 0 and hx is hh),
                                stop=(kcH == 63 and hx is hl)),
                                waits=w if (oc8 == 0 and hx is hh) else ())
                            w = []
                    ph.q["tensor"][-1] = (ph.q["tensor"][-1][0], ph.q["tensor"][-1][1], "sp", 1)
                    ctx.bump("sp")
                    prev[("hn", og, jj)] = ctx.C["sp"]
                mmd = ctx.C["sp"]
                prev[("mm", og)] = mmd
                for oc8 in range(8):
                    oc = og * 8 + oc8
                    w = [("sp", mmd), prev.get(("st", og, tt, oc8 - 2), None)]
                    if sub_scale is not None:
                        ph.op("scalar", lambda e, oc8=oc8: e.activation(
                            ot[oc8 % 2].ap()[:], ctx.pb[oc8].ap()[:, :], AF.Identity,
                            scale=sub_scale()), waits=w, inc="sa")
                    else:
                        ph.op("scalar", lambda e, oc8=oc8: e.copy(
                            ot[oc8 % 2].ap()[:], ctx.pb[oc8].ap()[:, :]), waits=w, inc="sa")
                    ad = ctx.C["sa"]
                    prev[("st", og, tt, oc8)] = ph.dma(
                        d["sub_out"].ap()[oc][:, 512 * tt:512 * (tt + 1)],
                        ot[oc8 % 2].ap()[:], waits=[("sa", ad)])
                prev[("ev", og, tt)] = ctx.C["sa"]
        ph.op("sync", None, waits=ctx.dma_all())
        ph.run()


def emit_l0_ffn(ctx):
    emit_pre_ln(ctx, split=True, eps=EPS_LN)
    emit_l0f_gu(ctx)
    emit_l0f_hn(ctx)
    emit_l0f_wpj_quant(ctx)
    emit_ffn_proj(ctx, "l0f_pj", ctx.dram["wpj0_t"], 16)
    emit_post_ln(ctx, eps=EPS_LN)



def emit_l1_qkv_qk(ctx):
    d = ctx.dram
    nc = ctx.nc
    with ExitStack() as es:
        qraw = es.enter_context(sbt(nc, "l1qraw", [128, T], F32))
        xsw = es.enter_context(sbt(nc, "l1xsw", [128, T], F32))
        tc = es.enter_context(sbt(nc, "l1tc", [128, T], F32))
        tsn = es.enter_context(sbt(nc, "l1ts", [128, T], F32))
        hi = [es.enter_context(sbt(nc, "l1hi", [128, 512], BF16)) for i in range(2)]
        lo = [es.enter_context(sbt(nc, "l1lo", [128, 512], BF16)) for i in range(2)]
        state = {"mode": None}

        def evac(ph):
            st_done = [None] * 40
            ev_state = {"tbl": None, "sw": 0}

            def ev(j, oc, tt, bank, mmd):
                is_q = oc < 16
                ts = slice(512 * tt, 512 * (tt + 1))
                if state["mode"] != is_q:
                    state["mode"] = is_q
                    cnm = "ropeQC" if is_q else "ropeKC"
                    snm = "ropeQS" if is_q else "ropeKS"
                    w = [("sv", ctx.C["sv"])]
                    tca = ph.dma(tc.ap()[:], d[cnm].ap(), waits=w)
                    ev_state["tbl"] = ph.dma(tsn.ap()[:], d[snm].ap(), waits=[tca])
                w = [("sp", mmd)]
                if tt == 0 and j >= 1:
                    w.append(st_done[j - 1])
                if tt == 0:
                    w.append(("sv", ev_state["sw"]))
                    w.append(("sg", ev_state.get("sg", 0)))
                ph.op("scalar", lambda e, tt=tt, bank=bank: e.copy(
                    qraw.ap()[:, ts], ctx.pb[bank].ap()[:, :]), waits=w, inc="sa")
                qev = ctx.C["sa"]
                if tt == 3:
                    bounce = d["sub_out"].ap()[0]
                    s1 = ph.dma(bounce[0:64, :], qraw.ap()[64:128, :],
                                waits=[("sa", qev)])
                    s1b = ph.dma(bounce[64:128, :], qraw.ap()[0:64, :], waits=[s1])
                    s2 = ph.dma(xsw.ap()[:], bounce, waits=[s1b])
                    ph.op("vector", lambda e: e.tensor_tensor(
                        qraw.ap()[:], qraw.ap()[:], tc.ap()[:], ALU.mult),
                        waits=[s2, ev_state["tbl"]])
                    ph.op("vector", lambda e: e.tensor_tensor(
                        xsw.ap()[:], xsw.ap()[:], tsn.ap()[:], ALU.mult))
                    ph.op("vector", lambda e: e.tensor_add(
                        qraw.ap()[:], qraw.ap()[:], xsw.ap()[:]), inc="sv")
                    ev_state["sw"] = ctx.C["sv"]
                    yd = ctx.C["sv"]
                    hi_st = ev_state.setdefault("hi_st", [None, None])
                    lo_st = ev_state.setdefault("lo_st", [None, None])
                    for t2 in range(4):
                        t2s = slice(512 * t2, 512 * (t2 + 1))
                        ph.op("scalar", lambda e, t2=t2, t2s=t2s: e.copy(
                            hi[t2 % 2].ap()[:], qraw.ap()[:, t2s]),
                            waits=[("sv", yd), hi_st[t2 % 2]], inc="sa")
                        ad = ctx.C["sa"]
                        ph.op("gpsimd", lambda e, t2=t2, t2s=t2s: e.tensor_sub(
                            lo[t2 % 2].ap()[:], qraw.ap()[:, t2s], hi[t2 % 2].ap()[:]),
                            waits=[("sa", ad), lo_st[t2 % 2]], inc="sg")
                        gd = ctx.C["sg"]
                        hi_st[t2 % 2] = ph.dma(d["qk_hi"].ap()[oc][:, t2s],
                                               hi[t2 % 2].ap()[:], waits=[("sg", gd)])
                        lo_st[t2 % 2] = ph.dma(d["qk_lo"].ap()[oc][:, t2s],
                                               lo[t2 % 2].ap()[:])
                    st_done[j] = lo_st[1]
                    ev_state["sg"] = ctx.C["sg"]
                return ctx.C["sa"], ctx.C["sv"]
            return ev

        emit_proj_stream(ctx, "l1qk", list(range(32)),
                         lambda oc: d["wq1"].ap()[oc] if oc < 16 else d["wk1"].ap()[oc - 16],
                         evac, nwf=1, nws=1)


def emit_l1_flash(ctx):
    d = ctx.dram
    nc = ctx.nc
    with ExitStack() as es:
        qh = es.enter_context(sbt(nc, "fl_qh", [128, T], BF16))
        ql = es.enter_context(sbt(nc, "fl_ql", [128, T], BF16))
        kh = es.enter_context(sbt(nc, "fl_kh", [128, T], BF16))
        kl = es.enter_context(sbt(nc, "fl_kl", [128, T], BF16))
        vbh = es.enter_context(sbt(nc, "fl_vh", [128, T], BF16))
        vbl = es.enter_context(sbt(nc, "fl_vl", [128, T], BF16))
        sco = es.enter_context(sbt(nc, "fl_sco", [128, T], F32))
        msk = es.enter_context(sbt(nc, "fl_msk", [128, T], BF16))
        pth = es.enter_context(sbt(nc, "fl_pth", [128, 16 * 256], BF16))
        ptl = es.enter_context(sbt(nc, "fl_ptl", [128, 16 * 256], BF16))
        acc = es.enter_context(sbt(nc, "fl_acc", [128, 8], F32))
        mneg = es.enter_context(sbt(nc, "fl_mn", [128, 2], F32))
        rcol = es.enter_context(sbt(nc, "fl_rc", [128, 2], F32))
        rr = es.enter_context(sbt(nc, "fl_rr", [2, 128], F32))
        rr2 = es.enter_context(sbt(nc, "fl_rr2", [1, 256], F32))
        pbuf = ctx.sb["mb"]
        rcb = ctx.sb["rstdb"]
        otile = ctx.sb["rstdb"].ap()[:, 256:512]
        ph = Phase(ctx, "l1flash")
        mld = ph.dma(msk.ap()[:], d["l1mask"].ap())
        prev = {}
        for h in range(NH):
            w0 = [("sp", ctx.C["sp"]), ("sv", ctx.C["sv"]), ("sa", ctx.C["sa"]),
                  ("sg", ctx.C["sg"])] if h else [mld]
            ph.op("sync", None, waits=w0)
            ld_toks = [
                ph.dma(qh.ap()[:], d["qk_hi"].ap()[h]),
                ph.dma(ql.ap()[:], d["qk_lo"].ap()[h]),
                ph.dma(kh.ap()[:], d["qk_hi"].ap()[16 + h]),
                ph.dma(kl.ap()[:], d["qk_lo"].ap()[16 + h]),
                ph.dma(vbh.ap().rearrange("p (c e) -> p c e", e=128),
                       d["v_hi"].ap()[h].rearrange("(c p) e -> p c e", p=128)),
                ph.dma(vbl.ap().rearrange("p (c e) -> p c e", e=128),
                       d["v_lo"].ap()[h].rearrange("(c p) e -> p c e", p=128)),
            ]
            for g2 in range(8):
                for iq in range(2):
                    qt = 2 * g2 + iq
                    nkt = qt // 4 + 1
                    qs = slice(128 * qt, 128 * (qt + 1))
                    for kt in range(nkt):
                        bank = kt % 2
                        w = list(ld_toks) if (g2 == 0 and iq == 0 and kt == 0) else \
                            [("sa", prev.get(("sev", bank), 0))]
                        for i, (wa, ra) in enumerate(((qh, kh), (qh, kl), (ql, kh))):
                            ph.op("tensor", lambda e, wa=wa, ra=ra, qs=qs, kt=kt, i=i,
                                  bank=bank: e.matmul(
                                      ctx.pb[bank].ap()[:, :], wa.ap()[:, qs],
                                      ra.ap()[:, 512 * kt:512 * (kt + 1)],
                                      start=(i == 0), stop=(i == 2)),
                                  waits=w if i == 0 else ())
                            w = []
                        ph.q["tensor"][-1] = (ph.q["tensor"][-1][0], ph.q["tensor"][-1][1],
                                              "sp", 1)
                        ctx.bump("sp")
                        smd = ctx.C["sp"]
                        wse = [("sp", smd)]
                        if kt == 0:
                            wse.append(("sv", prev.get("mnd", 0)))
                        ph.op("scalar", lambda e, kt=kt, bank=bank: e.copy(
                            sco.ap()[:, 512 * kt:512 * (kt + 1)], ctx.pb[bank].ap()[:, :]),
                            waits=wse, inc="sa")
                        prev[("sev", bank)] = ctx.C["sa"]
                    sc_done = ctx.C["sa"]
                    r = qt % 4
                    ph.op("vector", lambda e, r=r, nkt=nkt: e.tensor_tensor(
                        sco.ap()[:, 512 * (nkt - 1):512 * nkt],
                        sco.ap()[:, 512 * (nkt - 1):512 * nkt],
                        msk.ap()[:, 512 * r:512 * (r + 1)], ALU.add),
                        waits=[("sa", sc_done)])
                    ph.op("vector", lambda e, nkt=nkt, iq=iq: e.reduce_max(
                        mneg.ap()[:, iq:iq + 1], sco.ap()[:, 0:512 * nkt], axis=AX.X),
                        waits=[("sa", prev.get(("exd", iq), 0))])
                    ph.op("vector", lambda e, iq=iq: e.tensor_scalar(
                        mneg.ap()[:, iq:iq + 1], mneg.ap()[:, iq:iq + 1], -1.0, None,
                        ALU.mult), inc="sv")
                    mn_done = ctx.C["sv"]
                    prev["mnd"] = mn_done
                    for kt in range(nkt):
                        wex = [("sv", mn_done), ("sp", prev.get("ptd", 0))] if kt == 0 else ()
                        ph.op("scalar", lambda e, kt=kt, iq=iq: e.activation(
                            pbuf.ap()[:, 512 * kt:512 * (kt + 1)],
                            sco.ap()[:, 512 * kt:512 * (kt + 1)], AF.Exp,
                            bias=mneg.ap()[:, iq:iq + 1],
                            accum_out=acc.ap()[:, 4 * iq + kt:4 * iq + kt + 1]),
                            waits=wex, inc="sa")
                    ex_done = ctx.C["sa"]
                    prev[("exd", iq)] = ex_done
                    ph.op("vector", lambda e, nkt=nkt, iq=iq: e.reduce_sum(
                        rcol.ap()[:, iq:iq + 1], acc.ap()[:, 4 * iq:4 * iq + nkt], axis=AX.X),
                        waits=[("sa", ex_done)])
                    ph.op("vector", lambda e, iq=iq: e.reciprocal(
                        rcol.ap()[:, iq:iq + 1], rcol.ap()[:, iq:iq + 1]), inc="sv")
                    for kc in range(4 * nkt):
                        bnk = 2 + kc % 2
                        w = [("sa", ex_done), ("sa", prev.get(("ptv", bnk), 0)),
                             ("sv", prev.get(("ptlv", bnk), 0))]
                        if kc == 0:
                            w.append(("sp", prev.get("odone", 0)))
                        ph.op("tensor", lambda e, kc=kc, bnk=bnk: e.transpose(
                            ctx.pb[bnk].ap()[:, 0:128],
                            pbuf.ap()[:, 128 * kc:128 * (kc + 1)], ctx.sb["c_ident"].ap()[:]),
                            waits=w, inc="sp")
                        td = ctx.C["sp"]
                        dst = 256 * kc + 128 * iq
                        ph.op("scalar", lambda e, bnk=bnk, dst=dst: e.copy(
                            pth.ap()[:, dst:dst + 128], ctx.pb[bnk].ap()[:, 0:128]),
                            waits=[("sp", td)], inc="sa")
                        prev[("ptv", bnk)] = ctx.C["sa"]
                        ph.op("vector", lambda e, bnk=bnk, dst=dst: e.tensor_sub(
                            ptl.ap()[:, dst:dst + 128], ctx.pb[bnk].ap()[:, 0:128],
                            pth.ap()[:, dst:dst + 128]),
                            waits=[("sa", ctx.C["sa"])], inc="sv")
                        prev[("ptlv", bnk)] = ctx.C["sv"]
                    prev["ptd"] = ctx.C["sp"]
                pt_done_a, pt_done_v = ctx.C["sa"], ctx.C["sv"]
                ph.op("tensor", lambda e: e.transpose(
                    ctx.pb[6].ap()[0:2, 0:128], rcol.ap()[:], ctx.sb["c_ident"].ap()[:]),
                    waits=[("sv", ctx.C["sv"]), ("sa", prev.get("rce", 0))], inc="sp")
                rtd = ctx.C["sp"]
                ph.op("scalar", lambda e: e.copy(rr.ap()[0:2, 0:128],
                                                 ctx.pb[6].ap()[0:2, 0:128]),
                      waits=[("sp", rtd)], inc="sa")
                prev["rce"] = ctx.C["sa"]
                b1 = ph.dma(d["rv2"].ap(), rr.ap()[0:2, 0:128],
                            waits=[("sa", ctx.C["sa"])])
                b2 = ph.dma(rr2.ap()[:],
                            d["rv2"].ap().rearrange("a b -> (a b)").unsqueeze(0),
                            waits=[b1])
                ph.op("tensor", lambda e: e.matmul(
                    ctx.pb[5].ap()[:, 0:256], ctx.sb["c_ones"].ap()[0:1, :],
                    rr2.ap()[:], start=True, stop=True),
                    waits=[b2, ("sa", prev.get("rcb", 0))], inc="sp")
                rbd = ctx.C["sp"]
                ph.op("scalar", lambda e: e.copy(rcb.ap()[:, 0:256],
                                                 ctx.pb[5].ap()[:, 0:256]),
                      waits=[("sp", rbd)], inc="sa")
                prev["rcb"] = ctx.C["sa"]
                rb_done = ctx.C["sa"]
                w = [("sa", pt_done_a), ("sv", pt_done_v),
                     ("sv", prev.get("oev", 0))]
                first = True
                for kc in range(2 * (g2 + 1)):
                    for (wa, ra) in ((vbh, pth), (vbh, ptl), (vbl, pth)):
                        ph.op("tensor", lambda e, wa=wa, ra=ra, kc=kc, first=first,
                              g2=g2: e.matmul(
                                  ctx.pb[4].ap()[:, 0:256], wa.ap()[:, 128 * kc:128 * (kc + 1)],
                                  ra.ap()[:, 256 * kc:256 * (kc + 1)], start=first,
                                  stop=(kc == 2 * g2 + 1 and ra is ptl and False) or
                                       (kc == 2 * g2 + 1 and wa is vbl)),
                              waits=w if first else ())
                        w = []
                        first = False
                ph.q["tensor"][-1] = (ph.q["tensor"][-1][0], ph.q["tensor"][-1][1], "sp", 1)
                ctx.bump("sp")
                o_done = ctx.C["sp"]
                prev["odone"] = o_done
                ph.op("vector", lambda e: e.tensor_tensor(
                    otile, ctx.pb[4].ap()[:, 0:256], rcb.ap()[:, 0:256], ALU.mult),
                    waits=[("sp", o_done), ("sa", rb_done),
                           ("sg", prev.get("olo", 0))], inc="sv")
                ov = ctx.C["sv"]
                prev["oev"] = ov
                ph.op("scalar", lambda e, h=h, g2=g2: e.copy(
                    ar_hi(ctx, h)[:, 256 * g2:256 * (g2 + 1)], otile),
                    waits=[("sv", ov)], inc="sa")
                ad = ctx.C["sa"]
                ph.op("gpsimd", lambda e, h=h, g2=g2: e.tensor_sub(
                    ar_lo(ctx, h)[:, 256 * g2:256 * (g2 + 1)], otile,
                    ar_hi(ctx, h)[:, 256 * g2:256 * (g2 + 1)]),
                    waits=[("sa", ad)], inc="sg")
                prev["olo"] = ctx.C["sg"]
        ph.op("sync", None, waits=[("sg", ctx.C["sg"])] + ctx.dma_all())
        ph.run()


def emit_l1_outproj(ctx):
    d = ctx.dram
    nc = ctx.nc
    with ExitStack() as es:
        ot = [es.enter_context(sbt(nc, "l1op_ot", [128, 512], F32)) for i in range(4)]

        def evac(ph):
            st_done = [0] * 70

            def ev(j, oc, tt, bank, mmd):
                ti = (j * 4 + tt) % 4
                w = [("sp", mmd), st_done[max(0, j * 4 + tt - 4)]]
                ph.op("scalar", lambda e, ti=ti, bank=bank: e.activation(
                    ot[ti].ap()[:], ctx.pb[bank].ap()[:, :], AF.Identity,
                    scale=1.0 / 3.0), waits=w, inc="sa")
                ad = ctx.C["sa"]
                st_done[j * 4 + tt] = ph.dma(
                    d["sub_out"].ap()[oc][:, 512 * tt:512 * (tt + 1)], ot[ti].ap()[:],
                    waits=[("sa", ad)])
                return ad, 0
            return ev

        emit_proj_stream(ctx, "l1op", list(range(16)), lambda oc: d["wo1"].ap()[oc], evac)


def emit_l1_attn(ctx):
    emit_pre_ln(ctx, split=True, eps=EPS_LN)
    emit_l1_qkv_qk(ctx)
    emit_v(ctx, "l1v", lambda vg, i: ctx.dram["wv1"].ap()[4 * vg + i], use_bias=False)
    emit_l1_flash(ctx)
    emit_l1_outproj(ctx)
    emit_post_ln(ctx, eps=EPS_LN)



def emit_pdance(ctx, ph, acc, tr, r32, sb64, sw_ap, out_row, op, guards=()):
    d = ctx.dram
    bounce = d["sub_out"].ap()[2]
    b1 = ph.dma(bounce[0:64, :], acc.ap()[64:128, :], waits=guards)
    b2 = ph.dma(sw_ap[0:64, :], bounce[0:64, :], waits=[b1])
    ph.op("vector", lambda e: e.tensor_tensor(
        acc.ap()[0:64, :], acc.ap()[0:64, :], sw_ap[0:64, :], op),
        waits=[b2] + list(guards), inc="sv")
    v1 = ctx.C["sv"]
    b3 = ph.dma(bounce[0:32, :], acc.ap()[32:64, :], waits=[("sv", v1), b2])
    b4 = ph.dma(sw_ap[0:32, :], bounce[0:32, :], waits=[b3])
    ph.op("vector", lambda e: e.tensor_tensor(
        acc.ap()[0:32, :], acc.ap()[0:32, :], sw_ap[0:32, :], op), waits=[b4])
    ph.op("vector", lambda e: e.transpose(tr.ap()[:], acc.ap()[0:32, :]))
    ph.op("vector", lambda e: e.tensor_reduce(
        r32.ap()[:], tr.ap()[:].rearrange("p (g f) -> p g f", f=32), AX.X, op),
        inc="sv")
    vd = ctx.C["sv"]
    ph.op("tensor", lambda e: e.transpose(
        ctx.pb[7].ap()[0:64, 0:32], r32.ap()[:], ctx.sb["c_ident"].ap()[0:32, 0:32]),
        waits=[("sv", vd), ("sa", getattr(ctx, "_pd_sa", 0))], inc="sp")
    td = ctx.C["sp"]
    ph.op("scalar", lambda e: e.copy(sb64.ap()[:], ctx.pb[7].ap()[0:64, 0:32]),
          waits=[("sp", td)], inc="sa")
    ctx._pd_sa = ctx.C["sa"]
    b5 = ph.dma(d["rv3"].ap(), sb64.ap()[:], waits=[("sa", ctx.C["sa"])])
    return ph.dma(out_row, d["rv3"].ap().rearrange("a b -> (a b)").unsqueeze(0),
                  waits=[b5])


def emit_bit_affine(ctx, ph, mean_row, rstd_row, mx_row, mn_row, r0, r1, r2, cb,
                    pre_waits):
    ones = ctx.sb["c_ones"]
    ph.op("vector", lambda e: e.tensor_sub(r0, mx_row, mean_row), waits=pre_waits)
    ph.op("vector", lambda e: e.tensor_sub(r1, mean_row, mn_row))
    ph.op("vector", lambda e: e.tensor_tensor(r0, r0, r1, ALU.max))
    ph.op("vector", lambda e: e.tensor_tensor(r0, r0, rstd_row, ALU.mult))
    ph.op("vector", lambda e: e.reciprocal(r1, r0))
    ph.op("vector", lambda e: e.tensor_tensor(r1, rstd_row, r1, ALU.mult))
    ph.op("vector", lambda e: e.tensor_scalar(r1, r1, 128.0, None, ALU.mult))
    ph.op("vector", lambda e: e.tensor_tensor(r2, mean_row, r1, ALU.mult))
    ph.op("vector", lambda e: e.tensor_scalar(r0, r0, 1.0 / 128.0, None,
                                              ALU.mult), inc="sv")
    rdone = ctx.C["sv"]
    bc = 0
    for k, (src, dst) in enumerate(((r1, "mb"), (r2, "rstdb"), (r0, None))):
        dtile = ctx.sb[dst] if dst else cb
        for tt in range(4):
            i = k * 4 + tt
            ph.op("tensor", lambda e, src=src, tt=tt, k=k: e.matmul(
                ctx.pb[2 * (k % 2) + tt % 2].ap()[:, :], ones.ap()[0:1, :],
                src[:, 512 * tt:512 * (tt + 1)], start=True, stop=True),
                waits=([("sv", rdone)] if i == 0 else
                       ([("sa", bc - 1)] if i >= 2 else ())), inc="sp")
            bd = ctx.C["sp"]
            ph.op("scalar", lambda e, dtile=dtile, tt=tt, k=k: e.copy(
                dtile.ap()[:, 512 * tt:512 * (tt + 1)],
                ctx.pb[2 * (k % 2) + tt % 2].ap()[:, :]),
                waits=[("sp", bd)], inc="sa")
            bc = ctx.C["sa"]
    return bc


def emit_clip_chain(ctx, ph, src_ap, hi_ap, lo_ap, ut_ap, cb, waits,
                    extra_hi=None, extra_lo=None):
    ph.op("vector", lambda e: e.tensor_tensor(ut_ap, src_ap, ctx.sb["mb"].ap()[:],
                                              ALU.mult), waits=waits)
    ph.op("vector", lambda e: e.tensor_tensor(ut_ap, ut_ap,
                                              ctx.sb["rstdb"].ap()[:], ALU.subtract))
    ph.op("vector", lambda e: e.tensor_scalar(ut_ap, ut_ap, CLIP_LO, CLIP_HI,
                                              ALU.max, ALU.min))
    ph.op("vector", lambda e: e.tensor_tensor(ut_ap, ut_ap, cb.ap()[:],
                                              ALU.mult), inc="sv")
    vd = ctx.C["sv"]
    ph.op("scalar", lambda e: e.copy(hi_ap, ut_ap),
          waits=[("sv", vd), extra_hi], inc="sa")
    ad = ctx.C["sa"]
    ph.op("gpsimd", lambda e: e.tensor_sub(lo_ap, ut_ap, hi_ap),
          waits=[("sa", ad), extra_lo], inc="sg")
    return ctx.C["sg"]


def emit_l1f_xhat(ctx):
    nc = ctx.nc
    d = ctx.dram
    with ExitStack() as es:
        sq = es.enter_context(sbt(nc, "bx_sq", [128, T], F32))
        ctx.sb["stat_sq"] = sq
        mxa = es.enter_context(sbt(nc, "bx_mx", [128, T], F32))
        mna = es.enter_context(sbt(nc, "bx_mn", [128, T], F32))
        tr = es.enter_context(sbt(nc, "bx_tr", [32, T], F32))
        r32 = es.enter_context(sbt(nc, "bx_r32", [32, 64], F32))
        sb64 = es.enter_context(sbt(nc, "bx_s64", [64, 32], F32))
        cb = es.enter_context(sbt(nc, "bx_cb", [128, T], F32))
        ph = Phase(ctx, "l1f_xhat")
        bc_done = emit_stats(ctx, ph, lambda c: (st_c(ctx, c), ()), (), EPS_BIT)
        ph.op("vector", lambda e: e.tensor_copy(mxa.ap()[:], st_c(ctx, 0)))
        ph.op("vector", lambda e: e.tensor_copy(mna.ap()[:], st_c(ctx, 0)))
        for c in range(1, KC):
            ph.op("vector", lambda e, c=c: e.tensor_tensor(
                mxa.ap()[:], mxa.ap()[:], st_c(ctx, c), ALU.max), inc="sv")
            ph.op("vector", lambda e, c=c: e.tensor_tensor(
                mna.ap()[:], mna.ap()[:], st_c(ctx, c), ALU.min), inc="sv")
        mm_done_v = ctx.C["sv"]
        mxt = emit_pdance(ctx, ph, mxa, tr, r32, sb64, cb.ap()[0:64, :],
                          mxa.ap()[0:1, :], ALU.max, guards=[("sv", mm_done_v)])
        mnt = emit_pdance(ctx, ph, mna, tr, r32, sb64, cb.ap()[0:64, :],
                          mna.ap()[0:1, :], ALU.min, guards=[mxt])
        bc2 = emit_bit_affine(ctx, ph, ctx.sb["mb"].ap()[0:1, :], rows0(ctx),
                              mxa.ap()[0:1, :], mna.ap()[0:1, :],
                              tr.ap()[0:1, 0:T], sq.ap()[0:1, 0:T], cb.ap()[0:1, 0:T],
                              cb, [mnt, ("sa", bc_done)])
        gd = 0
        for c in range(KC):
            w = [("sa", bc2)] if c == 0 else [("sg", gd)]
            gd = emit_clip_chain(ctx, ph, st_c(ctx, c), ar_hi(ctx, c), ar_lo(ctx, c),
                                 sq.ap()[:], cb, w)
        ph.op("sync", None, waits=[("sg", gd)] + ctx.dma_all())
        ph.run()
        del ctx.sb["stat_sq"]


def _emit_gamma_w(ctx, ph, wdram, n_oc, numel, wg, col, grow, toks):
    ones = ctx.sb["c_ones"]
    ph.op("vector", lambda e: e.memset(col.ap()[:, 0:1], 0.0))
    nb = len(toks)
    rtoks = [0] * nb
    for oc in range(n_oc):
        ld = ph.dma(wg[oc % nb].ap()[:], wdram(oc),
                    waits=[toks[oc % nb], ("sv", rtoks[oc % nb])])
        toks[oc % nb] = ld
        ph.op("vector", lambda e, oc=oc: e.reduce_sum(
            col.ap()[:, 1:2], wg[oc % nb].ap()[:].unsqueeze(1),
            axis=AX.X, apply_absolute_value=True), waits=[ld])
        ph.op("vector", lambda e: e.tensor_add(col.ap()[:, 0:1], col.ap()[:, 0:1],
                                               col.ap()[:, 1:2]), inc="sv")
        rtoks[oc % nb] = ctx.C["sv"]
    acc_done = ctx.C["sv"]
    ctx._gw_rtoks = rtoks
    ph.op("tensor", lambda e: e.matmul(
        ctx.pb[6].ap()[0:1, 0:1], col.ap()[:, 0:1], ones.ap()[:, 0:1],
        start=True, stop=True), waits=[("sv", acc_done)], inc="sp")
    gd1 = ctx.C["sp"]
    ph.op("vector", lambda e: e.tensor_scalar(
        grow.ap()[:, 0:1], ctx.pb[6].ap()[0:1, 0:1], 1.0 / numel, None,
        ALU.mult), waits=[("sp", gd1)])
    ph.op("vector", lambda e: e.tensor_scalar(
        grow.ap()[:, 1:2], grow.ap()[:, 0:1], EPS_BIT, 0.5, ALU.add, ALU.mult))
    ph.op("vector", lambda e: e.tensor_scalar(
        grow.ap()[:, 2:3], grow.ap()[:, 1:2], -1.0, None, ALU.mult))
    ph.op("vector", lambda e: e.memset(grow.ap()[:, 3:4], 0.0), inc="sv")
    gr_done = ctx.C["sv"]
    ph.op("tensor", lambda e: e.matmul(
        ctx.pb[6].ap()[:, 0:4], ones.ap()[0:1, :], grow.ap()[:, 0:4],
        start=True, stop=True), waits=[("sv", gr_done)], inc="sp")
    gd2 = ctx.C["sp"]
    ph.op("scalar", lambda e: e.copy(ctx.sb["gcol"].ap()[:], ctx.pb[6].ap()[:, 0:4]),
          waits=[("sp", gd2)], inc="sa")
    return ctx.C["sa"]


def emit_l1f_fc(ctx):
    nc = ctx.nc
    d = ctx.dram
    ones = ctx.sb["c_ones"]
    with ExitStack() as es:
        wg = [es.enter_context(sbt(nc, "fc_wg", [128, 2048], F32)) for i in range(2)]
        col = es.enter_context(sbt(nc, "fc_col", [128, 4], F32))
        grow = es.enter_context(sbt(nc, "fc_grow", [1, 4], F32))
        tb1 = es.enter_context(sbt(nc, "fc_tb1", [128, 2048], BF16))
        tb2 = es.enter_context(sbt(nc, "fc_tb2", [128, 2048], BF16))
        h2b = es.enter_context(sbt(nc, "fc_h2b", [128, 1024], F32))
        sq = es.enter_context(sbt(nc, "fc_sq", [128, 512], F32))
        mxa = es.enter_context(sbt(nc, "fc_mx", [128, T], F32))
        mna = es.enter_context(sbt(nc, "fc_mn", [128, T], F32))
        rowm = es.enter_context(sbt(nc, "fc_rowm", [1, T], F32))
        ph = Phase(ctx, "l1f_fc")
        toks = [None, None]
        gcol_done = _emit_gamma_w(ctx, ph, lambda oc: d["wfc1"].ap()[oc], 64,
                                  16777216.0, wg, col, grow, toks)
        prev = {}
        h2st = {}
        wrd = list(ctx._gw_rtoks)
        for half in range(2):
            for oc in range(64):
                ld = ph.dma(wg[oc % 2].ap()[:], d["wfc1"].ap()[oc],
                            waits=[toks[oc % 2], ("sv", wrd[oc % 2])])
                toks[oc % 2] = ld
                ph.op("vector", lambda e, oc=oc: e.tensor_scalar(
                    tb1.ap()[:], wg[oc % 2].ap()[:], ctx.sb["gcol"].ap()[:, 2:3], None,
                    ALU.is_lt), waits=[ld, ("sa", gcol_done),
                                       ("sp", prev.get("mmlast", 0))])
                ph.op("vector", lambda e, oc=oc: e.scalar_tensor_tensor(
                    tb2.ap()[:], wg[oc % 2].ap()[:], ctx.sb["gcol"].ap()[:, 1:2],
                    tb1.ap()[:], ALU.is_gt, ALU.subtract), inc="sv")
                tern_done = ctx.C["sv"]
                wrd[oc % 2] = tern_done
                for t2 in range(2):
                    tt = 2 * half + t2
                    bank = t2
                    w = [("sv", tern_done), ("sa", prev.get(("gel", bank), 0))]
                    first = True
                    for kc in range(KC):
                        for xa in (ar_hi, ar_lo):
                            ph.op("tensor", lambda e, kc=kc, xa=xa, tt=tt, first=first,
                                  bank=bank: e.matmul(
                                      ctx.pb[bank].ap()[:, :],
                                      tb2.ap()[:, 128 * kc:128 * (kc + 1)],
                                      xa(ctx, kc)[:, 512 * tt:512 * (tt + 1)],
                                      start=first, stop=(kc == KC - 1 and xa is ar_lo)),
                                  waits=w if first else ())
                            w = []
                            first = False
                    ph.q["tensor"][-1] = (ph.q["tensor"][-1][0], ph.q["tensor"][-1][1],
                                          "sp", 1)
                    ctx.bump("sp")
                    mmd = ctx.C["sp"]
                    prev["mmlast"] = mmd
                    ph.op("scalar", lambda e, t2=t2, bank=bank: e.activation(
                        h2b.ap()[:, 512 * t2:512 * (t2 + 1)], ctx.pb[bank].ap()[:, :],
                        AF.Gelu, scale=ctx.sb["gcol"].ap()[:, 0:1]),
                        waits=[("sp", mmd), h2st.get("last"),
                               ("sv", prev.get(("mxd", bank), 0)),
                               ("sv", prev.get(("mnd", bank), 0))], inc="sa")
                    prev[("gel", bank)] = ctx.C["sa"]
                    ph.op("scalar", lambda e, t2=t2: e.activation(
                        sq.ap()[:], h2b.ap()[:, 512 * t2:512 * (t2 + 1)], AF.Square),
                        waits=[("sp", prev.get(("sqm", t2), 0))], inc="sa")
                    sqd = ctx.C["sa"]
                    wst = [("sa", prev[("gel", bank)])]
                    if oc == 0:
                        wst.append(("sv", prev.get("rowmd", 0)))
                    ph.op("tensor", lambda e, t2=t2, oc=oc: e.matmul(
                        ctx.pb[2 + t2].ap()[0:1, :], ones.ap()[:, 0:1],
                        h2b.ap()[:, 512 * t2:512 * (t2 + 1)],
                        start=(oc == 0), stop=(oc == 63)), waits=wst)
                    ph.op("tensor", lambda e, t2=t2, oc=oc: e.matmul(
                        ctx.pb[4 + t2].ap()[0:1, :], ones.ap()[:, 0:1], sq.ap()[:],
                        start=(oc == 0), stop=(oc == 63)),
                        waits=[("sa", sqd)], inc="sp")
                    prev[("sqm", t2)] = ctx.C["sp"]
                    ts2 = slice(512 * tt, 512 * (tt + 1))
                    hslc = slice(512 * t2, 512 * (t2 + 1))
                    if oc == 0:
                        ph.op("vector", lambda e, ts2=ts2, hslc=hslc: e.tensor_copy(
                            mxa.ap()[:, ts2], h2b.ap()[:, hslc]),
                            waits=[("sa", prev[("gel", bank)])], inc="sv")
                        ph.op("vector", lambda e, ts2=ts2, hslc=hslc: e.tensor_copy(
                            mna.ap()[:, ts2], h2b.ap()[:, hslc]),
                            waits=[("sa", prev[("gel", bank)])], inc="sv")
                    else:
                        ph.op("vector", lambda e, ts2=ts2, hslc=hslc: e.tensor_tensor(
                            mxa.ap()[:, ts2], mxa.ap()[:, ts2], h2b.ap()[:, hslc],
                            ALU.max), waits=[("sa", prev[("gel", bank)])], inc="sv")
                        ph.op("vector", lambda e, ts2=ts2, hslc=hslc: e.tensor_tensor(
                            mna.ap()[:, ts2], mna.ap()[:, ts2], h2b.ap()[:, hslc],
                            ALU.min), waits=[("sa", prev[("gel", bank)])], inc="sv")
                    prev[("mxd", bank)] = ctx.C["sv"]
                    prev[("mnd", bank)] = ctx.C["sv"]
                h2st["last"] = ph.dma(
                    d["h_f32"].ap()[oc][:, 1024 * half:1024 * (half + 1)], h2b.ap()[:],
                    waits=[("sv", prev[("mxd", 1)]), ("sv", prev[("mnd", 1)])])
            for t2 in range(2):
                wrow = [("sp", prev[("sqm", t2)])]
                if t2 == 0:
                    wrow.append(prev.get("rowst"))
                ph.op("vector", lambda e, t2=t2: e.tensor_copy(
                    rowm.ap()[:, 512 * t2:512 * (t2 + 1)], ctx.pb[2 + t2].ap()[0:1, :]),
                    waits=wrow)
                ph.op("vector", lambda e, t2=t2: e.tensor_copy(
                    rowm.ap()[:, 1024 + 512 * t2:1024 + 512 * (t2 + 1)],
                    ctx.pb[4 + t2].ap()[0:1, :]), inc="sv")
            prev["rowmd"] = ctx.C["sv"]
            prev["rowst"] = ph.dma(d["rowsd"].ap()[half:half + 1, :], rowm.ap()[:],
                                   waits=[("sv", prev["rowmd"])])
        ph.dma(d["sub_out"].ap()[0], mxa.ap()[:], waits=[("sv", ctx.C["sv"])])
        ph.dma(d["sub_out"].ap()[1], mna.ap()[:], waits=[("sg", ctx.C["sg"])])
        ph.op("sync", None, waits=ctx.dma_all())
        ph.run()


def emit_l1f_rows(ctx):
    nc = ctx.nc
    d = ctx.dram
    with ExitStack() as es:
        mxa = es.enter_context(sbt(nc, "fr_mx", [128, T], F32))
        mna = es.enter_context(sbt(nc, "fr_mn", [128, T], F32))
        tr = es.enter_context(sbt(nc, "fr_tr", [32, T], F32))
        r32 = es.enter_context(sbt(nc, "fr_r32", [32, 64], F32))
        sb64 = es.enter_context(sbt(nc, "fr_s64", [64, 32], F32))
        cb = es.enter_context(sbt(nc, "fr_cb", [128, T], F32))
        rowb = es.enter_context(sbt(nc, "fr_rowb", [1, T], F32))
        rowc = es.enter_context(sbt(nc, "fr_rowc", [1, T], F32))
        rowa_ap = cb.ap()[0:1, :]
        ph = Phase(ctx, "l1f_rows")
        l1 = ph.dma(mxa.ap()[:], d["sub_out"].ap()[0])
        l2 = ph.dma(mna.ap()[:], d["sub_out"].ap()[1])
        mxt = emit_pdance(ctx, ph, mxa, tr, r32, sb64, cb.ap()[0:64, :],
                          mxa.ap()[0:1, :], ALU.max, guards=[l1])
        mnt = emit_pdance(ctx, ph, mna, tr, r32, sb64, cb.ap()[0:64, :],
                          mna.ap()[0:1, :], ALU.min, guards=[mxt, l2])
        l3 = ph.dma(rowa_ap[:, 0:1024], d["rowsd"].ap()[0:1, 0:1024], waits=[mnt])
        l4 = ph.dma(rowa_ap[:, 1024:2048], d["rowsd"].ap()[1:2, 0:1024])
        l5 = ph.dma(rowb.ap()[:, 0:1024], d["rowsd"].ap()[0:1, 1024:2048])
        l6 = ph.dma(rowb.ap()[:, 1024:2048], d["rowsd"].ap()[1:2, 1024:2048])
        ph.op("vector", lambda e: e.tensor_scalar(
            rowc.ap()[:], rowa_ap, 1.0 / 8192.0, None, ALU.mult),
            waits=[l3, l4, mnt])
        ph.op("vector", lambda e: e.tensor_tensor(
            rowa_ap, rowc.ap()[:], rowc.ap()[:], ALU.mult), waits=[l5, l6])
        ph.op("vector", lambda e: e.scalar_tensor_tensor(
            rowb.ap()[:], rowb.ap()[:], 1.0 / 8192.0, rowa_ap,
            ALU.mult, ALU.subtract), inc="sv")
        vard = ctx.C["sv"]
        ph.op("scalar", lambda e: e.activation(
            rowa_ap, rowb.ap()[:], AF.Sqrt,
            bias=ctx.sb["c_cvals"].ap()[0:1, 2:3]), waits=[("sv", vard)], inc="sa")
        sqd = ctx.C["sa"]
        ph.op("vector", lambda e: e.reciprocal(rowb.ap()[:], rowa_ap),
              waits=[("sa", sqd)], inc="sv")
        bc2 = emit_bit_affine(ctx, ph, rowc.ap()[:], rowb.ap()[:],
                              mxa.ap()[0:1, :], mna.ap()[0:1, :],
                              tr.ap()[0:1, 0:T], mna.ap()[0:1, :], rowa_ap,
                              cb, [("sv", ctx.C["sv"])])
        ph.dma(d["rvec"].ap().rearrange("a b -> (a b)").unsqueeze(0)[:, 0:T],
               tr.ap()[0:1, 0:T], waits=[("sa", bc2)])
        ph.op("sync", None, waits=ctx.dma_all() + [("sa", ctx.C["sa"])])
        ph.run()


def emit_l1f_xhat2(ctx):
    nc = ctx.nc
    d = ctx.dram
    ones = ctx.sb["c_ones"]
    with ExitStack() as es:
        hb = [es.enter_context(sbt(nc, "x2_hb", [128, T], F32)) for i in range(3)]
        hh = [es.enter_context(sbt(nc, "x2_hh", [128, T], BF16)) for i in range(2)]
        hl = [es.enter_context(sbt(nc, "x2_hl", [128, T], BF16)) for i in range(2)]
        cb = es.enter_context(sbt(nc, "x2_cb", [128, T], F32))
        crow = cb.ap()[0:1, :]
        ph = Phase(ctx, "l1f_xhat2")
        lc = ph.dma(crow,
                    d["rvec"].ap().rearrange("a b -> (a b)").unsqueeze(0)[:, 0:T])
        bcc = 0
        for tt in range(4):
            ph.op("tensor", lambda e, tt=tt: e.matmul(
                ctx.pb[tt % 2].ap()[:, :], ones.ap()[0:1, :],
                crow[:, 512 * tt:512 * (tt + 1)], start=True, stop=True),
                waits=([lc] if tt == 0 else
                       ([("sa", bcc - 1)] if tt >= 2 else ())), inc="sp")
            bd = ctx.C["sp"]
            ph.op("scalar", lambda e, tt=tt: e.copy(
                cb.ap()[:, 512 * tt:512 * (tt + 1)], ctx.pb[tt % 2].ap()[:, :]),
                waits=[("sp", bd)], inc="sa")
            bcc = ctx.C["sa"]
        prev = {}
        for ch in range(64):
            bi = ch % 3
            ld = ph.dma(hb[bi].ap()[:], d["h_f32"].ap()[ch],
                        waits=[prev.get(("gd", ch - 3))])
            w = [ld] + ([("sa", bcc)] if ch == 0 else [])
            gd = emit_clip_chain(ctx, ph, hb[bi].ap()[:], hh[ch % 2].ap()[:],
                                 hl[ch % 2].ap()[:], hb[bi].ap()[:], cb, w,
                                 extra_hi=prev.get(("sth", ch - 2)),
                                 extra_lo=prev.get(("stl", ch - 2)))
            prev[("gd", ch)] = ("sg", gd)
            prev[("sth", ch)] = ph.dma(d["hn_hi"].ap()[ch], hh[ch % 2].ap()[:],
                                       waits=[("sg", gd)])
            prev[("stl", ch)] = ph.dma(d["hn_lo"].ap()[ch], hl[ch % 2].ap()[:])
        ph.op("sync", None, waits=ctx.dma_all())
        ph.run()


def emit_l1f_wpj_tern(ctx):
    nc = ctx.nc
    d = ctx.dram
    with ExitStack() as es:
        wg = [es.enter_context(sbt(nc, "pj_wg", [128, 4096], F32)) for i in range(2)]
        col = es.enter_context(sbt(nc, "pj_col", [128, 4], F32))
        grow = es.enter_context(sbt(nc, "pj_grow", [1, 4], F32))
        tb1 = es.enter_context(sbt(nc, "pj_tb1", [128, 4096], BF16))
        tb2 = es.enter_context(sbt(nc, "pj_tb2", [128, 4096], BF16))
        ph = Phase(ctx, "l1f_pjt")
        toks = [None, None]
        gcol_done = _emit_gamma_w(
            ctx, ph,
            lambda j: d["wpj1"].ap()[j // 2][:, 4096 * (j % 2):4096 * (j % 2 + 1)],
            32, 16777216.0, wg, col, grow, toks)
        prev = {}
        wrd = list(ctx._gw_rtoks)
        for j in range(32):
            ld = ph.dma(wg[j % 2].ap()[:],
                        d["wpj1"].ap()[j // 2][:, 4096 * (j % 2):4096 * (j % 2 + 1)],
                        waits=[toks[j % 2], ("sv", wrd[j % 2])])
            toks[j % 2] = ld
            ph.op("vector", lambda e, j=j: e.tensor_scalar(
                tb1.ap()[:], wg[j % 2].ap()[:], ctx.sb["gcol"].ap()[:, 2:3], None,
                ALU.is_lt), waits=[ld, ("sa", gcol_done), prev.get(("st", j - 1))])
            ph.op("vector", lambda e, j=j: e.scalar_tensor_tensor(
                tb2.ap()[:], wg[j % 2].ap()[:], ctx.sb["gcol"].ap()[:, 1:2],
                tb1.ap()[:], ALU.is_gt, ALU.subtract), inc="sv")
            td = ctx.C["sv"]
            wrd[j % 2] = td
            prev[("st", j)] = ph.dma(
                d["wpj1_t"].ap()[j // 2][:, 4096 * (j % 2):4096 * (j % 2 + 1)],
                tb2.ap()[:], waits=[("sv", td)])
        ph.op("sync", None, waits=ctx.dma_all())
        ph.run()


def emit_l1_ffn(ctx):
    emit_pre_ln(ctx, split=False, eps=EPS_LN)
    emit_l1f_xhat(ctx)
    emit_l1f_fc(ctx)
    emit_l1f_rows(ctx)
    emit_l1f_xhat2(ctx)
    emit_l1f_wpj_tern(ctx)
    emit_ffn_proj(ctx, "l1f_pj", ctx.dram["wpj1_t"], 16,
                  sub_scale=lambda: ctx.sb["gcol"].ap()[:, 0:1])
    emit_post_ln(ctx, eps=EPS_LN)


SUB_BLOCKS = {}


def _register():
    SUB_BLOCKS["l0_attn"] = emit_l0_attn
    SUB_BLOCKS["l0_ffn"] = emit_l0_ffn
    SUB_BLOCKS["l1_attn"] = emit_l1_attn
    SUB_BLOCKS["l1_ffn"] = emit_l1_ffn


_register()


def build_program(parts, debug=False):
    nc = bass.Bass("TRN2", target_bir_lowering=False)
    ctx = Ctx(nc)
    declare_io(ctx, debug=debug)
    alloc_shared(ctx)
    emit_setup(ctx, load_stream=True)
    for p in parts:
        SUB_BLOCKS[p](ctx)
    emit_output(ctx)
    ctx.es.close()
    return nc


ALL_PARTS = ["l0_attn", "l0_ffn", "l1_attn", "l1_ffn"]


def kernel(**inputs):
    per_core = [host_inputs(inputs, c) for c in range(8)]
    nc = build_program(ALL_PARTS)
    res = run_bass_kernel_spmd(nc, per_core, core_ids=list(range(8)))
    return np.ascontiguousarray(
        np.stack([np.asarray(r["yT"]).T for r in res.results]).astype(np.float32))
